# version 02
# baseline (speedup 1.0000x reference)
import numpy as np
from contextlib import ExitStack

import concourse.bass as bass
import concourse.mybir as mybir
from concourse.bass_utils import run_bass_kernel_spmd

F32 = mybir.dt.float32
BF16 = mybir.dt.bfloat16
I32 = mybir.dt.int32
AF = mybir.ActivationFunctionType
ALU = mybir.AluOpType
AX = mybir.AxisListType.X

DM = 1024
B = 2
T = 2048
TOK = B * T
NDT = DM // 128
FC = 512
NFC = TOK // FC
LN_EPS = 1e-5
NH = 16
HD = 64



class Prog:
    def __init__(self, nc, stack):
        self.nc = nc
        self.stack = stack
        self.blk = stack.enter_context(nc.Block())
        self._sems = {}
        self.dguard = {}

    def sem(self, name):
        if name not in self._sems:
            h = self.stack.enter_context(self.nc.semaphore(name))
            self._sems[name] = [h, 0]
        return self._sems[name]

    def inc(self, instr, name, k=1):
        s = self.sem(name)
        instr.then_inc(s[0], k)
        s[1] += k
        return s[1]

    def val(self, name):
        return self.sem(name)[1]

    def wait(self, eng, name, v=None):
        s = self.sem(name)
        eng.wait_ge(s[0], s[1] if v is None else v)

    def fence(self, e, ins, ch):
        v = self.inc(ins, ch, 1)
        self.wait(e, ch, v)

    def sy(self, f):
        self.blk.sync(f)

    def ve(self, f):
        self.blk.vector(f)

    def sc(self, f):
        self.blk.scalar(f)

    def te(self, f):
        self.blk.tensor(f)

    def gp(self, f):
        self.blk.gpsimd(f)

    def barrier(self):
        dchs = [k for k in self._sems if k.startswith("d")]

        def drain(e):
            for ch in dchs:
                self.wait(e, ch)
        self.sy(drain)
        target = self.val("bar") + 5
        for emit, nm in ((self.sy, "sy"), (self.ve, "ve"), (self.sc, "sc"),
                         (self.te, "te"), (self.gp, "gp")):
            def f(e, nm=nm, target=target):
                ins = e.engine_nop() if nm in ("ve", "gp") else e.nop()
                self.inc(ins, "bar", 1)
                self.wait(e, "bar", target)
            emit(f)

    def sbuf(self, st, name, shape, dtype):
        return st.enter_context(self.nc.sbuf_tensor(name, list(shape), dtype))

    def psum(self, st, name, shape, dtype=F32):
        return st.enter_context(self.nc.psum_tensor(name, list(shape), dtype))


_uid = [0]


def uq(base):
    _uid[0] += 1
    return f"{base}{_uid[0]}"



def host_constants():
    c = {}
    c["c_id"] = np.eye(128, dtype=np.float32)
    c["c_ones_col"] = np.ones((128, 1), dtype=np.float32)
    c["c_i1024_col"] = np.full((128, 1), 1.0 / 1024.0, np.float32)
    c["c_eps_col"] = np.full((128, 1), 1e-5, np.float32)
    c["c_ones_row"] = np.ones((1, 128), dtype=np.float32)
    for r in range(4):
        kp = np.arange(128)[:, None]
        qf = np.arange(FC)[None, :]
        c[f"c_dmask{r}"] = (r * 128 + kp <= qf).astype(np.float32)
    c["c_iota_s"] = np.arange(128, dtype=np.float32)[None, :]
    c["c_iota_sm127"] = (np.arange(128, dtype=np.float32) - 127.0)[None, :]
    c["c_iota_t129"] = np.arange(129, dtype=np.float32)[None, :]
    s_ = np.arange(128)[:, None]
    t_ = np.arange(128)[None, :]
    ut = (t_ >= s_).astype(np.float32)
    c["c_utri2"] = np.concatenate([ut, ut], axis=1)
    for dt in range(8):
        hs = np.zeros((128, 16), np.float32)
        for k in range(128):
            hs[k, 2 * dt + k // 64] = 1.0
        c[f"c_hsel{dt}"] = hs
        he = np.zeros((16, 128), np.float32)
        for m in range(128):
            he[2 * dt + m // 64, m] = 1.0
        c[f"c_hexp{dt}"] = he
    return c


def emit_const_load(p, st, aps):
    C = {}
    f32_names = list(aps.keys())
    tensors = {}
    for nm in f32_names:
        shp = aps[nm].shape
        tensors[nm] = p.sbuf(st, "cs_" + nm, list(shp), F32)
        tensors[nm + "_bf"] = p.sbuf(st, "cb_" + nm, list(shp), BF16)

    def syf(e):
        for nm in f32_names:
            ins = e.dma_start(out=tensors[nm][:], in_=aps[nm])
            p.inc(ins, "dw", 16)
    p.sy(syf)
    dt_ = p.val("dw")

    def vf(e):
        p.wait(e, "dw", dt_)
        for nm in f32_names:
            ins = e.tensor_copy(out=tensors[nm + "_bf"][:], in_=tensors[nm][:])
        p.inc(ins, "vet", 1)
    p.ve(vf)
    for nm in f32_names:
        C[nm] = tensors[nm]
        C[nm + "_bf"] = tensors[nm + "_bf"]
    return C



def load_cols(p, st, hbm_ap, name, n=DM, dma="dw"):
    nt = max(1, (n + 127) // 128)
    tcol = p.sbuf(st, name, [128, nt], F32)

    def syf(e):
        p.wait(e, dma)
        for dt in range(nt):
            rows = min(128, n - dt * 128)
            ins = e.dma_start(out=tcol[0:rows, dt:dt + 1],
                              in_=hbm_ap[dt * 128:dt * 128 + rows, 0:1])
            p.inc(ins, dma, 16)
    p.sy(syf)
    tick = p.val(dma)
    return tcol, (lambda dt: tcol[:, dt:dt + 1]), (dma, tick)


def load_weight_bf16(p, st, W_ap, K, name, m0, mw, dtype_in=F32, scale=None,
                     cast_eng="gp", ring=2):
    nk = K // 128
    stage = p.sbuf(st, uq(name + "s"), [128, ring * mw], dtype_in)
    wbf = p.sbuf(st, uq(name + "b"), [128, nk * mw], BF16)
    emit, ch = {"gp": (p.gp, "gpt"), "ve": (p.ve, "vet"), "sc": (p.sc, "sct")}[cast_eng]
    slot_cast = [None] * ring
    for k in range(nk):
        sl = k % ring
        dch = "dwa" if sl == 0 else "dwb"

        guard = slot_cast[sl]
        if guard is not None:
            guard = (ch, guard)
        else:
            guard = p.dguard.get(dch)

        def syf(e, k=k, sl=sl, guard=guard, dch=dch):
            if guard is not None:
                p.wait(e, guard[0], guard[1])
            ins = e.dma_start(out=stage[:, sl * mw:(sl + 1) * mw],
                              in_=W_ap[k * 128:(k + 1) * 128, m0:m0 + mw])
            p.inc(ins, dch, 16)
        p.sy(syf)
        dtick = p.val(dch)

        def cf(e, k=k, sl=sl, dtick=dtick, dch=dch):
            p.wait(e, dch, dtick)
            src_ap = stage[:, sl * mw:(sl + 1) * mw]
            dst_ap = wbf[:, k * mw:(k + 1) * mw]
            if cast_eng == "sc":
                ins = e.activation(out=dst_ap, in_=src_ap, func=AF.Copy,
                                   scale=1.0 if scale is None else scale)
            elif scale is None:
                ins = e.tensor_copy(out=dst_ap, in_=src_ap)
            else:
                ins = e.tensor_scalar_mul(out=dst_ap, in0=src_ap, scalar1=scale)
            p.inc(ins, ch, 1)
        emit(cf)
        slot_cast[sl] = p.val(ch)
    tick = p.val(ch)
    p.dguard["dwa"] = (ch, tick)
    p.dguard["dwb"] = (ch, tick)
    return (lambda k: wbf[:, k * mw:(k + 1) * mw]), (ch, tick)


def mm_stream(p, st, *, name, lhsT_tile, nK, nM, lhs_ready, rhs_hbm, out_cb,
              fw=FC, nF=NFC, mw=128, mh=128):
    rbuf = p.sbuf(st, uq(name + "r"), [128, 2 * nK * fw], BF16)
    psA = p.psum(st, uq(name + "pA"), [128, fw])
    psB = p.psum(st, uq(name + "pB"), [128, fw])
    banks = [psA, psB]
    bank_war = [None, None]
    slot_war = [None, None]
    rhs_tick = [None, None]

    def prefetch(f, slot):
        dch = "dra" if slot == 0 else "drb"
        guard = slot_war[slot] or p.dguard.get(dch)

        def syf(e, f=f, slot=slot, dch=dch, guard=guard):
            if guard is not None:
                p.wait(e, guard[0], guard[1])
            for k in range(nK):
                ins = e.dma_start(
                    out=rbuf[:, (slot * nK + k) * fw:(slot * nK + k + 1) * fw],
                    in_=rhs_hbm(k, f))
                p.inc(ins, dch, 16)
        p.sy(syf)
        rhs_tick[slot] = (dch, p.val(dch))

    prefetch(0, 0)
    bi = 0
    for f in range(nF):
        slot = f & 1
        if f + 1 < nF:
            prefetch(f + 1, 1 - slot)
        for m in range(nM):
            ps = banks[bi]
            war = bank_war[bi]

            def tf(e, f=f, m=m, ps=ps, war=war, slot=slot, rt=rhs_tick[slot]):
                if lhs_ready is not None:
                    p.wait(e, lhs_ready[0], lhs_ready[1])
                p.wait(e, rt[0], rt[1])
                if war is not None:
                    p.wait(e, war[0], war[1])
                for k in range(nK):
                    ins = e.matmul(
                        out=ps[0:mh, 0:fw], lhsT=lhsT_tile(k, m),
                        rhs=rbuf[:, (slot * nK + k) * fw:(slot * nK + k + 1) * fw],
                        start=(k == 0), stop=(k == nK - 1))
                p.inc(ins, "pet", 1)
            p.te(tf)
            bank_war[bi] = out_cb(m, f, ps)
            bi ^= 1
        slot_war[slot] = ("pet", p.val("pet"))
    p.dguard["dra"] = ("pet", p.val("pet"))
    p.dguard["drb"] = ("pet", p.val("pet"))


def evac_to_hbm(p, st, *, name, out_hbm, out_dtype=BF16, bias_col=None,
                scale_col=None, func=None, add_hbm=None, mh=128, extra_ready=None):
    stage = p.sbuf(st, uq(name + "e"), [128, 2 * FC], out_dtype)
    addb = p.sbuf(st, uq(name + "a"), [128, 2 * FC], F32) if add_hbm is not None else None
    slot_ctr = [0]

    def cb(m, f, ps):
        sl = slot_ctr[0] & 1
        slot_ctr[0] += 1
        ssl = slice(sl * FC, (sl + 1) * FC)
        pet_t = p.val("pet")
        if add_hbm is not None:
            def syf(e, m=m, f=f, sl=sl):
                p.wait(e, "vet")
                p.wait(e, "da")
                ins = e.dma_start(out=addb[:, ssl],
                                  in_=add_hbm[m * mh:(m + 1) * mh, f * FC:(f + 1) * FC])
                p.inc(ins, "da", 16)
            p.sy(syf)
            a_t = p.val("da")
        if func is not None:
            def sf(e, m=m, f=f, sl=sl):
                p.wait(e, "pet", pet_t)
                if extra_ready is not None:
                    p.wait(e, extra_ready[0], extra_ready[1])
                p.wait(e, "dx")
                kw = {}
                if bias_col is not None:
                    kw["bias"] = bias_col(m)
                if scale_col is not None:
                    kw["scale"] = scale_col(m)
                ins = e.activation(out=stage[:mh, ssl], in_=ps[:mh, 0:FC],
                                   func=func, **kw)
                p.inc(ins, "sct", 1)
            p.sc(sf)
            done = ("sct", p.val("sct"))
        elif bias_col is not None or scale_col is not None:
            def vfb(e, m=m, f=f, sl=sl):
                p.wait(e, "pet", pet_t)
                if extra_ready is not None:
                    p.wait(e, extra_ready[0], extra_ready[1])
                p.wait(e, "dx")
                if bias_col is not None and scale_col is not None:
                    ins = e.tensor_scalar(out=stage[:mh, ssl], in0=ps[:mh, 0:FC],
                                          scalar1=scale_col(m), scalar2=bias_col(m),
                                          op0=ALU.mult, op1=ALU.add)
                elif bias_col is not None:
                    ins = e.tensor_scalar_add(out=stage[:mh, ssl], in0=ps[:mh, 0:FC],
                                              scalar1=bias_col(m))
                else:
                    ins = e.tensor_scalar_mul(out=stage[:mh, ssl], in0=ps[:mh, 0:FC],
                                              scalar1=scale_col(m))
                p.inc(ins, "vet", 1)
            p.ve(vfb)
            done = ("vet", p.val("vet"))
        elif add_hbm is not None:
            def vf(e, m=m, f=f, sl=sl):
                p.wait(e, "pet", pet_t)
                p.wait(e, "da", a_t)
                p.wait(e, "dx")
                ins = e.tensor_tensor(out=stage[:mh, ssl], in0=ps[:mh, 0:FC],
                                      in1=addb[:mh, ssl], op=ALU.add)
                p.inc(ins, "vet", 1)
            p.ve(vf)
            done = ("vet", p.val("vet"))
        else:
            def vf(e, m=m, f=f, sl=sl):
                p.wait(e, "pet", pet_t)
                p.wait(e, "dx")
                ins = e.tensor_copy(out=stage[:mh, ssl], in_=ps[:mh, 0:FC])
                p.inc(ins, "vet", 1)
            p.ve(vf)
            done = ("vet", p.val("vet"))

        def syf2(e, m=m, f=f, sl=sl):
            p.wait(e, done[0], done[1])
            p.wait(e, "dx")
            ins = e.dma_start(out=out_hbm[m * mh:(m + 1) * mh, f * FC:(f + 1) * FC],
                              in_=stage[:mh, ssl])
            p.inc(ins, "dx", 16)
        p.sy(syf2)
        return done
    return cb



def emit_ln(p, x_hbm, xn_f32_hbm, xn_bf_hbm, C, name, gamma_col=None,
            beta_col=None, src_rdy=None):
    with ExitStack() as ls:
        XT = p.sbuf(ls, name + "xt", [128, NDT * FC], F32)
        SQ = p.sbuf(ls, name + "sq", [128, NDT * FC], BF16)
        XB = p.sbuf(ls, name + "xb", [128, NDT * FC], BF16)
        XN = p.sbuf(ls, name + "xn", [128, NDT * FC], F32)
        XNB = p.sbuf(ls, name + "xnb", [128, NDT * FC], BF16)
        ROWS = p.sbuf(ls, name + "rw", [1, 6 * FC], F32)
        ps_s = p.psum(ls, name + "ps", [1, FC])
        ps_q = p.psum(ls, name + "pq", [1, FC])
        ps_r = p.psum(ls, name + "pr", [128, FC])
        ps_m = p.psum(ls, name + "pm", [128, FC])
        mean, m2, var, sqv, rstd, mrs = (ROWS[:, i * FC:(i + 1) * FC] for i in range(6))

        for j in range(NFC):
            def syf(e, j=j):
                if src_rdy is not None:
                    p.wait(e, src_rdy[0], src_rdy[1])
                p.wait(e, "dx")
                if j >= 1:
                    p.wait(e, "vet")
                for dt in range(NDT):
                    ins = e.dma_start(out=XT[:, dt * FC:(dt + 1) * FC],
                                      in_=x_hbm[dt * 128:(dt + 1) * 128,
                                                j * FC:(j + 1) * FC])
                    p.inc(ins, "dx", 16)
            p.sy(syf)
            dtick = p.val("dx")

            def vf(e, j=j):
                p.wait(e, "dx", dtick)
                if j >= 1:
                    p.wait(e, "pet")
                ins = e.tensor_copy(out=XB[:], in_=XT[:])
                p.inc(ins, "vet", 1)
            p.ve(vf)
            xb_t = p.val("vet")

            def sf(e, j=j):
                p.wait(e, "dx", dtick)
                if j >= 1:
                    p.wait(e, "pet")
                ins = e.activation(out=SQ[:], in_=XT[:], func=AF.Square)
                p.inc(ins, "sct", 1)
            p.sc(sf)
            sq_t = p.val("sct")

            def tf(e, j=j):
                p.wait(e, "vet", xb_t)
                p.wait(e, "sct", sq_t)
                for dt in range(NDT):
                    e.matmul(out=ps_s[0:1, :], lhsT=C["c_i1024_col_bf"][:],
                             rhs=XB[:, dt * FC:(dt + 1) * FC], start=(dt == 0),
                             stop=(dt == NDT - 1))
                for dt in range(NDT):
                    ins = e.matmul(out=ps_q[0:1, :], lhsT=C["c_i1024_col_bf"][:],
                                   rhs=SQ[:, dt * FC:(dt + 1) * FC], start=(dt == 0),
                                   stop=(dt == NDT - 1))
                p.inc(ins, "pet", 1)
            p.te(tf)
            st_t = p.val("pet")

            def vf2(e, j=j):
                p.wait(e, "pet", st_t)
                p.fence(e, e.tensor_copy(out=mean, in_=ps_s[0:1, :]), "vet")
                p.fence(e, e.tensor_tensor(out=m2, in0=mean, in1=mean,
                                           op=ALU.mult), "vet")
                ins = e.tensor_tensor(out=var, in0=ps_q[0:1, :], in1=m2, op=ALU.subtract)
                p.inc(ins, "vet", 1)
            p.ve(vf2)
            v_t = p.val("vet")

            def sf2(e, j=j):
                p.wait(e, "vet", v_t)
                ins = e.activation(out=sqv, in_=var, func=AF.Sqrt, bias=C["c_eps_col"][0:1, 0:1])
                p.inc(ins, "sct", 1)
            p.sc(sf2)
            s_t = p.val("sct")

            def vf3(e, j=j):
                p.wait(e, "sct", s_t)
                p.fence(e, e.reciprocal(out=rstd, in_=sqv), "vet")
                ins = e.tensor_tensor(out=mrs, in0=mean, in1=rstd, op=ALU.mult)
                p.inc(ins, "vet", 1)
            p.ve(vf3)
            r_t = p.val("vet")

            def tf2(e, j=j):
                p.wait(e, "vet", r_t)
                e.matmul(out=ps_r[:, :], lhsT=C["c_ones_row"][:], rhs=rstd,
                         start=True, stop=True)
                ins = e.matmul(out=ps_m[:, :], lhsT=C["c_ones_row"][:], rhs=mrs,
                               start=True, stop=True)
                p.inc(ins, "pet", 1)
            p.te(tf2)
            bc_t = p.val("pet")

            def vf4(e, j=j):
                p.wait(e, "pet", bc_t)
                if j >= 1:
                    p.wait(e, "dx")
                for dt in range(NDT):
                    sl = slice(dt * FC, (dt + 1) * FC)
                    p.fence(e, e.tensor_tensor(out=XN[:, sl], in0=XT[:, sl],
                                               in1=ps_r[:, :], op=ALU.mult), "vet")
                    ins = e.tensor_tensor(out=XN[:, sl], in0=XN[:, sl], in1=ps_m[:, :],
                                          op=ALU.subtract)
                p.inc(ins, "vet", 1)
            p.ve(vf4)
            xn_t = ("vet", p.val("vet"))

            if gamma_col is not None:
                def vf4b(e, j=j, xn_t=xn_t):
                    p.wait(e, xn_t[0], xn_t[1])
                    for dt in range(NDT):
                        sl = slice(dt * FC, (dt + 1) * FC)
                        p.fence(e, e.tensor_scalar(out=XN[:, sl], in0=XN[:, sl],
                                                   scalar1=gamma_col(dt),
                                                   scalar2=beta_col(dt),
                                                   op0=ALU.mult, op1=ALU.add), "vet")
                    ins = e.engine_nop()
                    p.inc(ins, "vet", 1)
                p.ve(vf4b)
                xn_t = ("vet", p.val("vet"))

            def vf5(e, j=j, xn_t=xn_t):
                p.wait(e, xn_t[0], xn_t[1])
                ins = e.tensor_copy(out=XNB[:], in_=XN[:])
                p.inc(ins, "vet", 1)
            p.ve(vf5)
            cb_t = p.val("vet")

            def syf2(e, j=j, xn_t=xn_t):
                p.wait(e, xn_t[0], xn_t[1])
                p.wait(e, "vet", cb_t)
                p.wait(e, "dx")
                if xn_f32_hbm is not None:
                    for dt in range(NDT):
                        ins = e.dma_start(
                            out=xn_f32_hbm[dt * 128:(dt + 1) * 128, j * FC:(j + 1) * FC],
                            in_=XN[:, dt * FC:(dt + 1) * FC])
                        p.inc(ins, "dx", 16)
                for dt in range(NDT):
                    ins = e.dma_start(
                        out=xn_bf_hbm[dt * 128:(dt + 1) * 128, j * FC:(j + 1) * FC],
                        in_=XNB[:, dt * FC:(dt + 1) * FC])
                    p.inc(ins, "dx", 16)
            p.sy(syf2)



def emit_residual_add(p, x_in, a_hbm, b_hbm, x_out):
    CH = 2048
    with ExitStack() as ls:
        X0 = p.sbuf(ls, uq("raX"), [128, CH], F32)
        A0 = p.sbuf(ls, uq("raA"), [128, CH], F32)
        B0 = p.sbuf(ls, uq("raB"), [128, CH], F32)
        O0 = p.sbuf(ls, uq("raO"), [128, CH], F32)
        for dt in range(NDT):
            for jj in range(TOK // CH):
                sl_r = slice(dt * 128, (dt + 1) * 128)
                sl_c = slice(jj * CH, (jj + 1) * CH)

                def syf(e, sl_r=sl_r, sl_c=sl_c):
                    p.wait(e, "vet")
                    p.wait(e, "dx")
                    ins = e.dma_start(out=X0[:], in_=x_in[sl_r, sl_c])
                    p.inc(ins, "dx", 16)
                    ins = e.dma_start(out=A0[:], in_=a_hbm[sl_r, sl_c])
                    p.inc(ins, "dx", 16)
                    ins = e.dma_start(out=B0[:], in_=b_hbm[sl_r, sl_c])
                    p.inc(ins, "dx", 16)
                p.sy(syf)
                d_t = p.val("dx")

                def vf(e, sl_r=sl_r, sl_c=sl_c):
                    p.wait(e, "dx", d_t)
                    p.fence(e, e.tensor_tensor(out=O0[:], in0=X0[:], in1=A0[:],
                                                 op=ALU.add), "vet")
                    ins = e.tensor_tensor(out=O0[:], in0=O0[:], in1=B0[:], op=ALU.add)
                    p.inc(ins, "vet", 1)
                p.ve(vf)
                v_t = p.val("vet")

                def syf2(e, sl_r=sl_r, sl_c=sl_c, v_t=v_t):
                    p.wait(e, "vet", v_t)
                    p.wait(e, "dx")
                    ins = e.dma_start(out=x_out[sl_r, sl_c], in_=O0[:])
                    p.inc(ins, "dx", 16)
                p.sy(syf2)



def emit_mha(p, W, C, scr, bo_ap=None):
    for wn, dst in (("Wq", "q_bf"), ("Wk", "k_bf"), ("Wv", "v_bf")):
        with ExitStack() as ls:
            lt, lrdy = load_weight_bf16(p, ls, W[wn], DM, "w" + wn, 0, DM)
            cb = evac_to_hbm(p, ls, name="e" + wn, out_hbm=scr[dst], out_dtype=BF16)
            mm_stream(p, ls, name="m" + wn,
                      lhsT_tile=lambda k, m, lt=lt: lt(k)[:, m * 128:(m + 1) * 128],
                      nK=NDT, nM=NDT, lhs_ready=lrdy,
                      rhs_hbm=lambda k, f: scr["xn_bf"][k * 128:(k + 1) * 128,
                                                        f * FC:(f + 1) * FC],
                      out_cb=cb)
        p.barrier()

    NKT = T // 128
    with ExitStack() as ls:
        qh = p.sbuf(ls, uq("qh"), [64, T], BF16)
        kh = p.sbuf(ls, uq("kh"), [64, T], BF16)
        vh = p.sbuf(ls, uq("vh"), [64, T], BF16)
        vtok = p.sbuf(ls, uq("vtok"), [128, NKT * 64], BF16)
        sqb = p.sbuf(ls, uq("sqb"), [64, T], BF16)
        rowb = p.sbuf(ls, uq("rowb"), [1, 16], F32)
        bcol = p.sbuf(ls, uq("bcol"), [128, 1], F32)
        expT = p.sbuf(ls, uq("expT"), [128, NKT * FC], BF16)
        etmp = p.sbuf(ls, uq("etmp"), [128, FC], BF16)
        sumr = p.sbuf(ls, uq("sumr"), [1, 2 * FC], F32)
        rrow = p.sbuf(ls, uq("rrow"), [1, FC], BF16)
        rbc = p.sbuf(ls, uq("rbc"), [64, FC], F32)
        ctxs = p.sbuf(ls, uq("ctxs"), [64, 2 * FC], BF16)
        ps_sc = [p.psum(ls, uq("psc"), [128, FC]) for _ in range(2)]
        ps_sum = p.psum(ls, uq("psum"), [1, FC])
        ps_ctx = p.psum(ls, uq("pctx"), [64, FC])
        ps_aux = p.psum(ls, uq("paux"), [128, FC])
        ps_vt = p.psum(ls, uq("pvt"), [128, 64], BF16)

        for b in range(B):
            for h in range(NH):
                r0, c0 = h * HD, b * T
                consumed = ("pet", p.val("pet"))

                def syf(e, r0=r0, c0=c0, consumed=consumed):
                    p.wait(e, consumed[0], consumed[1])
                    p.wait(e, "sct")
                    ins = e.dma_start(out=qh[:], in_=scr["q_bf"][r0:r0 + 64, c0:c0 + T])
                    p.inc(ins, "dr", 16)
                    ins = e.dma_start(out=kh[:], in_=scr["k_bf"][r0:r0 + 64, c0:c0 + T])
                    p.inc(ins, "dr", 16)
                    ins = e.dma_start(out=vh[:], in_=scr["v_bf"][r0:r0 + 64, c0:c0 + T])
                    p.inc(ins, "dr", 16)
                p.sy(syf)
                ld_t = p.val("dr")

                ev_t = p.val("vet")
                for kt in range(NKT):
                    def tf(e, kt=kt, ev_t=ev_t):
                        p.wait(e, "dr", ld_t)
                        p.wait(e, "vet", ev_t)
                        ins = e.transpose(out=ps_vt[:, 0:64],
                                          in_=vh[:, kt * 128:(kt + 1) * 128],
                                          identity=C["c_id_bf"][0:64, 0:64])
                        p.inc(ins, "pet", 1)
                    p.te(tf)
                    tr_t = p.val("pet")

                    def vf(e, kt=kt, tr_t=tr_t):
                        p.wait(e, "pet", tr_t)
                        ins = e.tensor_copy(out=vtok[:, kt * 64:(kt + 1) * 64],
                                            in_=ps_vt[:, 0:64])
                        p.inc(ins, "vet", 1)
                    p.ve(vf)
                    ev_t = p.val("vet")
                vtok_t = p.val("vet")

                def sf(e):
                    p.wait(e, "dr", ld_t)
                    p.wait(e, "pet")
                    ins = e.activation(out=sqb[:], in_=qh[:], func=AF.Square)
                    p.inc(ins, "sct", 1)
                p.sc(sf)
                q2_t = p.val("sct")
                for src, base, rdy in ((0, 0, q2_t), (1, 4, None)):
                    if src == 1:
                        def sf2(e):
                            p.wait(e, "dr", ld_t)
                            p.wait(e, "pet")
                            ins = e.activation(out=sqb[:], in_=kh[:], func=AF.Square)
                            p.inc(ins, "sct", 1)
                        p.sc(sf2)
                        rdy = p.val("sct")
                    for ft in range(4):
                        def tf2(e, ft=ft, rdy=rdy):
                            p.wait(e, "sct", rdy)
                            p.wait(e, "vet")
                            ins = e.matmul(out=ps_sum[0:1, :],
                                           lhsT=C["c_ones_col_bf"][0:64, :],
                                           rhs=sqb[:, ft * FC:(ft + 1) * FC],
                                           start=True, stop=True)
                            p.inc(ins, "pet", 1)
                        p.te(tf2)
                        m_t = p.val("pet")

                        def vf2(e, ft=ft, base=base, m_t=m_t):
                            p.wait(e, "pet", m_t)
                            ins = e.reduce_max(out=rowb[:, base + ft:base + ft + 1],
                                               in_=ps_sum[0:1, :], axis=AX)
                            p.inc(ins, "vet", 1)
                        p.ve(vf2)
                nrm_t = p.val("vet")

                def vf3(e, nrm_t=nrm_t):
                    p.wait(e, "vet", nrm_t)
                    e.reduce_max(out=rowb[:, 8:9], in_=rowb[:, 0:4], axis=AX)
                    ins = e.reduce_max(out=rowb[:, 9:10], in_=rowb[:, 4:8], axis=AX)
                    p.inc(ins, "vet", 1)
                p.ve(vf3)
                mx_t = p.val("vet")

                def sf3(e, mx_t=mx_t):
                    p.wait(e, "vet", mx_t)
                    ins = e.activation(out=rowb[:, 10:12], in_=rowb[:, 8:10],
                                       func=AF.Sqrt)
                    p.inc(ins, "sct", 1)
                p.sc(sf3)
                sq2_t = p.val("sct")

                def vf4(e, sq2_t=sq2_t):
                    p.wait(e, "sct", sq2_t)
                    p.fence(e, e.tensor_tensor(out=rowb[:, 12:13], in0=rowb[:, 10:11],
                                                 in1=rowb[:, 11:12], op=ALU.mult), "vet")
                    ins = e.tensor_scalar_mul(out=rowb[:, 13:14], in0=rowb[:, 12:13],
                                              scalar1=-0.125)
                    p.inc(ins, "vet", 1)
                p.ve(vf4)
                bp_t = p.val("vet")

                def tf3(e, bp_t=bp_t):
                    p.wait(e, "vet", bp_t)
                    p.wait(e, "sct")
                    ins = e.matmul(out=ps_aux[:, 0:1], lhsT=C["c_ones_row"][:],
                                   rhs=rowb[:, 13:14], start=True, stop=True)
                    p.inc(ins, "pet", 1)
                p.te(tf3)
                bcm_t = p.val("pet")

                def vf5(e, bcm_t=bcm_t):
                    p.wait(e, "pet", bcm_t)
                    ins = e.tensor_copy(out=bcol[:], in_=ps_aux[:, 0:1])
                    p.inc(ins, "vet", 1)
                p.ve(vf5)
                bcol_t = p.val("vet")

                for j in range(4):
                    nkt = 4 * (j + 1)
                    exp_done = []
                    for kt in range(nkt):
                        ps = ps_sc[kt & 1]

                        def tf4(e, kt=kt, j=j, ps=ps):
                            p.wait(e, "vet", bcol_t)
                            p.wait(e, "sct")
                            ins = e.matmul(out=ps[:, :],
                                           lhsT=kh[:, kt * 128:(kt + 1) * 128],
                                           rhs=qh[:, j * FC:(j + 1) * FC],
                                           start=True, stop=True)
                            p.inc(ins, "pet", 1)
                        p.te(tf4)
                        sc_t = p.val("pet")
                        diag = kt >= 4 * j

                        def sf4(e, kt=kt, ps=ps, sc_t=sc_t, diag=diag):
                            p.wait(e, "pet", sc_t)
                            if diag:
                                p.wait(e, "vet")
                            tgt = etmp[:, :] if diag else expT[:, kt * FC:(kt + 1) * FC]
                            ins = e.activation(out=tgt, in_=ps[:, :], func=AF.Exp,
                                               scale=0.125, bias=bcol[:, 0:1])
                            p.inc(ins, "sct", 1)
                        p.sc(sf4)
                        e_t = p.val("sct")
                        if diag:
                            def vf6(e, kt=kt, j=j, e_t=e_t):
                                p.wait(e, "sct", e_t)
                                ins = e.tensor_tensor(
                                    out=expT[:, kt * FC:(kt + 1) * FC], in0=etmp[:, :],
                                    in1=C[f"c_dmask{kt - 4 * j}_bf"][:], op=ALU.mult)
                                p.inc(ins, "vet", 1)
                            p.ve(vf6)
                    exp_s, exp_v = p.val("sct"), p.val("vet")

                    def tf5(e, nkt=nkt, exp_s=exp_s, exp_v=exp_v):
                        p.wait(e, "sct", exp_s)
                        p.wait(e, "vet", exp_v)
                        for kt in range(nkt):
                            ins = e.matmul(out=ps_sum[0:1, :],
                                           lhsT=C["c_ones_col_bf"][:],
                                           rhs=expT[:, kt * FC:(kt + 1) * FC],
                                           start=(kt == 0), stop=(kt == nkt - 1))
                        p.inc(ins, "pet", 1)
                    p.te(tf5)
                    su_t = p.val("pet")
                    sl = j & 1

                    def vf7(e, su_t=su_t, sl=sl):
                        p.wait(e, "pet", su_t)
                        p.fence(e, e.tensor_copy(out=sumr[:, 0:FC],
                                                   in_=ps_sum[0:1, :]), "vet")
                        p.fence(e, e.reciprocal(out=sumr[:, FC:2 * FC],
                                                in_=sumr[:, 0:FC]), "vet")
                        ins = e.tensor_copy(out=rrow[:, :], in_=sumr[:, FC:2 * FC])
                        p.inc(ins, "vet", 1)
                    p.ve(vf7)
                    rc_t = p.val("vet")

                    def tf6(e, rc_t=rc_t):
                        p.wait(e, "vet", rc_t)
                        ins = e.matmul(out=ps_aux[0:64, :],
                                       lhsT=C["c_ones_row_bf"][:, 0:64],
                                       rhs=rrow[:, :], start=True, stop=True)
                        p.inc(ins, "pet", 1)
                    p.te(tf6)
                    rb_t = p.val("pet")

                    def vf8(e, rb_t=rb_t):
                        p.wait(e, "pet", rb_t)
                        ins = e.tensor_copy(out=rbc[:, :], in_=ps_aux[0:64, :])
                        p.inc(ins, "vet", 1)
                    p.ve(vf8)
                    rbc_t = p.val("vet")

                    def tf7(e, nkt=nkt, vtok_t=vtok_t, rbc_t=rbc_t):
                        p.wait(e, "vet", rbc_t)
                        for kt in range(nkt):
                            ins = e.matmul(out=ps_ctx[0:64, :],
                                           lhsT=vtok[:, kt * 64:(kt + 1) * 64],
                                           rhs=expT[:, kt * FC:(kt + 1) * FC],
                                           start=(kt == 0), stop=(kt == nkt - 1))
                        p.inc(ins, "pet", 1)
                    p.te(tf7)
                    cx_t = p.val("pet")

                    def vf9(e, cx_t=cx_t, sl=sl):
                        p.wait(e, "pet", cx_t)
                        p.wait(e, "dx")
                        ins = e.tensor_tensor(out=ctxs[:, sl * FC:(sl + 1) * FC],
                                              in0=ps_ctx[0:64, :], in1=rbc[:, :],
                                              op=ALU.mult)
                        p.inc(ins, "vet", 1)
                    p.ve(vf9)
                    cs_t = p.val("vet")

                    def syf2(e, j=j, cs_t=cs_t, sl=sl, r0=r0, c0=c0):
                        p.wait(e, "vet", cs_t)
                        p.wait(e, "dx")
                        ins = e.dma_start(
                            out=scr["ctx_bf"][r0:r0 + 64, c0 + j * FC:c0 + (j + 1) * FC],
                            in_=ctxs[:, sl * FC:(sl + 1) * FC])
                        p.inc(ins, "dx", 16)
                    p.sy(syf2)
    p.barrier()

    with ExitStack() as ls:
        lt, lrdy = load_weight_bf16(p, ls, W["Wo"], DM, "wWo", 0, DM)
        if bo_ap is not None:
            _, bofn, bordy = load_cols(p, ls, bo_ap, uq("bo"))
            cb = evac_to_hbm(p, ls, name="eWo", out_hbm=scr["acc_a"], out_dtype=F32,
                             bias_col=bofn, extra_ready=bordy)
        else:
            cb = evac_to_hbm(p, ls, name="eWo", out_hbm=scr["acc_a"], out_dtype=F32)
        mm_stream(p, ls, name="mWo",
                  lhsT_tile=lambda k, m, lt=lt: lt(k)[:, m * 128:(m + 1) * 128],
                  nK=NDT, nM=NDT, lhs_ready=lrdy,
                  rhs_hbm=lambda k, f: scr["ctx_bf"][k * 128:(k + 1) * 128,
                                                     f * FC:(f + 1) * FC],
                  out_cb=cb)
    p.barrier()



def emit_ffn1(p, W, C, scr):
    H = 3072
    NMH = H // 128
    for wn, dst, act in (("w_gate", "g_bf", True), ("w_up", "u_bf", False)):
        for half in range(2):
            with ExitStack() as ls:
                mw = H // 2
                lt, lrdy = load_weight_bf16(p, ls, W[wn], DM, "wf" + wn, half * mw,
                                            mw, dtype_in=I32, scale=1.0 / 127.0,
                                            cast_eng="sc")
                cb = evac_to_hbm(p, ls, name="ef" + wn + str(half),
                                 out_hbm=scr[dst], out_dtype=BF16)
                mm_stream(p, ls, name="mf" + wn + str(half),
                          lhsT_tile=lambda k, m, lt=lt: lt(k)[:, m * 128:(m + 1) * 128],
                          nK=NDT, nM=NMH // 2, lhs_ready=lrdy,
                          rhs_hbm=lambda k, f: scr["xn_bf"][k * 128:(k + 1) * 128,
                                                           f * FC:(f + 1) * FC],
                          out_cb=lambda m, f, ps, cb=cb, half=half: cb(m + half * (NMH // 2), f, ps))
            p.barrier()
    with ExitStack() as ls:
        G = p.sbuf(ls, uq("fg"), [128, 2048], BF16)
        U = p.sbuf(ls, uq("fu"), [128, 2048], BF16)
        S = p.sbuf(ls, uq("fs"), [128, 2048], BF16)
        Hb = p.sbuf(ls, uq("fh"), [128, 2048], BF16)
        for mt in range(NMH):
            for jj in range(TOK // 2048):
                sl_r = slice(mt * 128, (mt + 1) * 128)
                sl_c = slice(jj * 2048, (jj + 1) * 2048)

                def syf(e, sl_r=sl_r, sl_c=sl_c):
                    p.wait(e, "vet")
                    p.wait(e, "sct")
                    ins = e.dma_start(out=G[:], in_=scr["g_bf"][sl_r, sl_c])
                    p.inc(ins, "dr", 16)
                    ins = e.dma_start(out=U[:], in_=scr["u_bf"][sl_r, sl_c])
                    p.inc(ins, "dr", 16)
                p.sy(syf)
                d_t = p.val("dr")

                def sf(e, d_t=d_t):
                    p.wait(e, "dr", d_t)
                    ins = e.activation(out=S[:], in_=G[:], func=AF.Sigmoid)
                    p.inc(ins, "sct", 1)
                p.sc(sf)
                s_t = p.val("sct")

                def vf(e, s_t=s_t, d_t=d_t):
                    p.wait(e, "sct", s_t)
                    p.wait(e, "dx")
                    p.fence(e, e.tensor_tensor(out=Hb[:], in0=G[:], in1=S[:],
                                                 op=ALU.mult), "vet")
                    ins = e.tensor_tensor(out=Hb[:], in0=Hb[:], in1=U[:], op=ALU.mult)
                    p.inc(ins, "vet", 1)
                p.ve(vf)
                v_t = p.val("vet")

                def syf2(e, sl_r=sl_r, sl_c=sl_c, v_t=v_t):
                    p.wait(e, "vet", v_t)
                    p.wait(e, "dx")
                    ins = e.dma_start(out=scr["h_bf"][sl_r, sl_c], in_=Hb[:])
                    p.inc(ins, "dx", 16)
                p.sy(syf2)
    p.barrier()
    with ExitStack() as ls:
        lt, lrdy = load_weight_bf16(p, ls, W["w_down"], H, "wfd", 0, DM,
                                    dtype_in=I32, scale=1.0 / 127.0, cast_eng="sc")
        cb = evac_to_hbm(p, ls, name="efd", out_hbm=scr["acc_b"], out_dtype=F32)
        mm_stream(p, ls, name="mfd",
                  lhsT_tile=lambda k, m, lt=lt: lt(k)[:, m * 128:(m + 1) * 128],
                  nK=H // 128, nM=NDT, lhs_ready=lrdy,
                  rhs_hbm=lambda k, f: scr["h_bf"][k * 128:(k + 1) * 128,
                                                  f * FC:(f + 1) * FC],
                  out_cb=cb)
    p.barrier()


L1_W = ["L1_attn_Wq", "L1_attn_Wk", "L1_attn_Wv", "L1_attn_Wo", "L1_attn_bo",
        "L1_ffn_w_gate", "L1_ffn_w_up", "L1_ffn_w_down"]
L0_W = ["L0_attn_log_A", "L0_attn_B_param", "L0_attn_C_param", "L0_attn_D_param",
        "L0_attn_log_dt", "L0_attn_ln_gamma", "L0_attn_ln_beta", "L0_attn_W_out",
        "L0_attn_b_out", "L0_ffn_ln_scale", "L0_ffn_ln_bias", "L0_ffn_gate_w",
        "L0_ffn_Wg", "L0_ffn_Wu", "L0_ffn_Wd", "L0_ffn_sWg", "L0_ffn_sWu",
        "L0_ffn_sWd"]
L2_W = ["L2_attn_Wq", "L2_attn_bq", "L2_attn_Wk", "L2_attn_bk", "L2_attn_Wv",
        "L2_attn_bv", "L2_attn_Wo", "L2_attn_bo", "L2_ffn_w1_w", "L2_ffn_w1_s",
        "L2_ffn_w2_w", "L2_ffn_w2_s", "L2_ffn_h1_w", "L2_ffn_h1_s",
        "L2_ffn_h2_w", "L2_ffn_h2_s", "L2_ffn_w3_w", "L2_ffn_w3_s"]
L3_W = ["L3_attn_Wq", "L3_attn_Wk", "L3_attn_Wv", "L3_attn_Wo",
        "L3_ffn_cn_w1", "L3_ffn_cn_b1", "L3_ffn_cn_w2", "L3_ffn_cn_b2",
        "L3_ffn_cn_w3", "L3_ffn_cn_b3", "L3_ffn_cn_w4", "L3_ffn_cn_b4",
        "L3_ffn_gc_w1", "L3_ffn_gc_b1", "L3_ffn_gc_w2", "L3_ffn_gc_b2",
        "L3_ffn_gc_w3", "L3_ffn_gc_b3", "L3_ffn_adaptive_range"]
LAYER_W = {0: L0_W, 1: L1_W, 2: L2_W, 3: L3_W}

WEIGHT_SPECS = {
    "L0_attn_log_A": ([1024, 64], F32), "L0_attn_B_param": ([1024, 64], F32),
    "L0_attn_C_param": ([1024, 64], F32), "L0_attn_D_param": ([1024, 1], F32),
    "L0_attn_log_dt": ([1024, 1], F32), "L0_attn_ln_gamma": ([1024, 1], F32),
    "L0_attn_ln_beta": ([1024, 1], F32), "L0_attn_W_out": ([1024, 1024], F32),
    "L0_attn_b_out": ([1024, 1], F32), "L0_ffn_ln_scale": ([1024, 1], F32),
    "L0_ffn_ln_bias": ([1024, 1], F32), "L0_ffn_gate_w": ([1024, 8], F32),
    "L0_ffn_Wg": ([8, 1024, 2048], F32), "L0_ffn_Wu": ([8, 1024, 2048], F32),
    "L0_ffn_Wd": ([8, 2048, 1024], F32), "L0_ffn_sWg": ([1024, 2048], F32),
    "L0_ffn_sWu": ([1024, 2048], F32), "L0_ffn_sWd": ([2048, 1024], F32),
    "L1_attn_Wq": ([1024, 1024], F32), "L1_attn_Wk": ([1024, 1024], F32),
    "L1_attn_Wv": ([1024, 1024], F32), "L1_attn_Wo": ([1024, 1024], F32),
    "L1_attn_bo": ([1024, 1], F32),
    "L1_ffn_w_gate": ([1024, 3072], I32), "L1_ffn_w_up": ([1024, 3072], I32),
    "L1_ffn_w_down": ([3072, 1024], I32),
    "L2_attn_Wq": ([1024, 1024], F32), "L2_attn_bq": ([1024, 1], F32),
    "L2_attn_Wk": ([1024, 1024], F32), "L2_attn_bk": ([1024, 1], F32),
    "L2_attn_Wv": ([1024, 1024], F32), "L2_attn_bv": ([1024, 1], F32),
    "L2_attn_Wo": ([1024, 1024], F32), "L2_attn_bo": ([1024, 1], F32),
    "L2_ffn_w1_w": ([1024, 4096], F32), "L2_ffn_w1_s": ([4096, 1], F32),
    "L2_ffn_w2_w": ([1024, 4096], F32), "L2_ffn_w2_s": ([4096, 1], F32),
    "L2_ffn_h1_w": ([4096, 4096], F32), "L2_ffn_h1_s": ([4096, 1], F32),
    "L2_ffn_h2_w": ([4096, 4096], F32), "L2_ffn_h2_s": ([4096, 1], F32),
    "L2_ffn_w3_w": ([4096, 1024], F32), "L2_ffn_w3_s": ([1024, 1], F32),
    "L3_attn_Wq": ([1024, 1024], F32), "L3_attn_Wk": ([1024, 1024], F32),
    "L3_attn_Wv": ([1024, 1024], F32), "L3_attn_Wo": ([1024, 1024], F32),
    "L3_ffn_cn_w1": ([1024, 1024], F32), "L3_ffn_cn_b1": ([1024, 1], F32),
    "L3_ffn_cn_w2": ([1024, 512], F32), "L3_ffn_cn_b2": ([512, 1], F32),
    "L3_ffn_cn_w3": ([512, 256], F32), "L3_ffn_cn_b3": ([256, 1], F32),
    "L3_ffn_cn_w4": ([256, 1], F32), "L3_ffn_cn_b4": ([1, 1], F32),
    "L3_ffn_gc_w1": ([1024, 512], F32), "L3_ffn_gc_b1": ([512, 1], F32),
    "L3_ffn_gc_w2": ([512, 256], F32), "L3_ffn_gc_b2": ([256, 1], F32),
    "L3_ffn_gc_w3": ([256, 1], F32), "L3_ffn_gc_b3": ([1, 1], F32),
    "L3_ffn_adaptive_range": ([1, 1], F32),
}

SCRATCH_SPECS = {
    "xn_bf": ([DM, TOK], BF16), "xn_f32": ([DM, TOK], F32),
    "q_bf": ([DM, TOK], BF16), "k_bf": ([DM, TOK], BF16),
    "v_bf": ([DM, TOK], BF16), "ctx_bf": ([DM, TOK], BF16),
    "g_bf": ([4096, TOK], BF16), "u_bf": ([4096, TOK], BF16),
    "h_bf": ([4096, TOK], BF16),
    "acc_a": ([DM, TOK], F32), "acc_b": ([DM, TOK], F32),
    "x_mid0": ([DM, TOK], F32), "x_mid1": ([DM, TOK], F32),
    "x_mid2": ([DM, TOK], F32),
    "w_hbm": ([DM, 64], F32), "cb_hbm": ([DM, 64], F32),
    "xg_f32": ([DM, TOK], F32),
}


def emit_layer(p, li, W, C, scr, x_in, x_out):
    if li == 0:
        _emit_layer0(p, W, C, scr, x_in, x_out)
    elif li == 2:
        _emit_layer2(p, W, C, scr, x_in, x_out)
    elif li == 3:
        _emit_layer3(p, W, C, scr, x_in, x_out)
    elif li == 1:
        emit_ln(p, x_in, None, scr["xn_bf"], C, uq("ln"))
        p.barrier()
        emit_mha(p, {k: W[f"L1_attn_{k}"] for k in ("Wq", "Wk", "Wv", "Wo")},
                 C, scr, bo_ap=W["L1_attn_bo"])
        emit_ffn1(p, {k: W[f"L1_ffn_{k}"] for k in ("w_gate", "w_up", "w_down")},
                  C, scr)
        emit_residual_add(p, x_in, scr["acc_a"], scr["acc_b"], x_out)
        p.barrier()
    else:
        raise NotImplementedError(f"layer {li}")


def build_program(layers):
    nc = bass.Bass("TRN2", target_bir_lowering=False, debug=False)
    x_in = nc.dram_tensor("x_in", [DM, TOK], F32, kind="ExternalInput").ap()
    x_out = nc.dram_tensor("x_out", [DM, TOK], F32, kind="ExternalOutput").ap()
    W = {}
    for li in layers:
        for nm in LAYER_W[li]:
            shp, dt_ = WEIGHT_SPECS[nm]
            W[nm] = nc.dram_tensor(nm, shp, dt_, kind="ExternalInput").ap()
    caps = {}
    for nm, arr in host_constants().items():
        caps[nm] = nc.dram_tensor(nm, list(arr.shape), F32, kind="ExternalInput").ap()
    scr = {}
    for nm, (shp, dt_) in SCRATCH_SPECS.items():
        scr[nm] = nc.dram_tensor("scr_" + nm, shp, dt_, kind="Internal").ap()

    with ExitStack() as stack:
        p = Prog(nc, stack)
        with ExitStack() as cls:
            C = emit_const_load(p, stack, caps)
        p.barrier()
        cur = x_in
        for idx, li in enumerate(layers):
            out = x_out if idx == len(layers) - 1 else scr[f"x_mid{idx}"]
            emit_layer(p, li, W, C, scr, cur, out)
            cur = out

        def fin(e):
            for ch in ("dx", "dr", "dw", "da"):
                if ch in p._sems:
                    p.wait(e, ch)
        p.sy(fin)
    return nc


LAUNCH_GROUPS = [[0], [1], [2], [3]]


def kernel(**inputs):
    w = prep_inputs(inputs)
    x = np.ascontiguousarray(np.asarray(inputs["L0_attn_x"], dtype=np.float32))
    shards = [np.ascontiguousarray(x[2 * c:2 * c + 2].transpose(2, 0, 1)
                                   .reshape(DM, TOK)) for c in range(8)]
    cur = shards
    for group in LAUNCH_GROUPS:
        nc = build_program(group)
        names = ["x_in"] + [nm for li in group for nm in LAYER_W[li]] \
            + list(host_constants().keys())
        in_maps = []
        for c in range(8):
            m = {"x_in": cur[c]}
            for nm in names:
                if nm != "x_in":
                    m[nm] = w[nm]
            in_maps.append(m)
        res = run_bass_kernel_spmd(nc, in_maps, core_ids=list(range(8)))
        cur = [np.asarray(res.results[c]["x_out"]) for c in range(8)]
    full = np.empty((16, 2048, 1024), np.float32)
    for c in range(8):
        full[2 * c:2 * c + 2] = cur[c].reshape(DM, B, T).transpose(1, 2, 0)
    return full


def emit_ffn3(p, W, C, scr):
    FB = 512
    with ExitStack() as ls:
        wts = {}
        for nm, K_, M_ in (("cn_w1", DM, DM), ("cn_w2", DM, 512), ("cn_w3", 512, 256),
                           ("cn_w4", 256, 1), ("gc_w1", DM, 512), ("gc_w2", 512, 256),
                           ("gc_w3", 256, 1)):
            wts[nm] = load_weight_bf16(p, ls, W[nm], K_, "w3" + nm, 0, M_)
        cols = {}
        for nm, n in (("cn_b1", DM), ("cn_b2", 512), ("cn_b3", 256), ("cn_b4", 1),
                      ("gc_b1", 512), ("gc_b2", 256), ("gc_b3", 1)):
            cols[nm] = load_cols(p, ls, W[nm], uq("c3" + nm), n=n)
        arr_c = load_cols(p, ls, W["adaptive_range"], uq("c3ar"), n=1)
        wrdy = [wts[nm][1] for nm in wts] + [cols[nm][2] for nm in cols] + [arr_c[2]]

        def tjoin(e):
            seen = {}
            for ch_, tk_ in wrdy:
                seen[ch_] = max(seen.get(ch_, 0), tk_)
            for ch_, tk_ in seen.items():
                p.wait(e, ch_, tk_)
            ins = e.nop()
            p.inc(ins, "pet", 1)
        p.te(tjoin)

        gcx = p.sbuf(ls, uq("gcx"), [128, NDT * B], F32)
        gcb = p.sbuf(ls, uq("gcb"), [128, NDT * B], BF16)
        XT3 = p.sbuf(ls, uq("xt3"), [128, 2048], F32)
        for dt in range(NDT):
            for b in range(B):
                def syf(e, dt=dt, b=b):
                    p.wait(e, "vet")
                    ins = e.dma_start(out=XT3[:],
                                      in_=scr["xn_f32"][dt * 128:(dt + 1) * 128,
                                                        b * T:(b + 1) * T])
                    p.inc(ins, "dx", 16)
                p.sy(syf)
                d_t = p.val("dx")

                def vf(e, dt=dt, b=b, d_t=d_t):
                    p.wait(e, "dx", d_t)
                    p.fence(e, e.reduce_sum(out=gcx[:, dt * B + b:dt * B + b + 1],
                                            in_=XT3[:], axis=AX), "vet")
                    ins = e.tensor_scalar_mul(
                        out=gcx[:, dt * B + b:dt * B + b + 1],
                        in0=gcx[:, dt * B + b:dt * B + b + 1], scalar1=1.0 / T)
                    p.inc(ins, "vet", 1)
                p.ve(vf)

        def vcast(e):
            p.wait(e, "vet")
            ins = e.tensor_copy(out=gcb[:], in_=gcx[:])
            p.inc(ins, "vet", 1)
        p.ve(vcast)
        gc_t = p.val("vet")

        ps_g = p.psum(ls, uq("psg"), [128, 64])
        h1g = p.sbuf(ls, uq("h1g"), [128, 4 * B], BF16)
        h2g = p.sbuf(ls, uq("h2g"), [128, 2 * B], BF16)
        gfr = p.sbuf(ls, uq("gfr"), [1, 8], F32)

        def net_step(nm, bnm, nK_, nM_, rhs_tensor, out_tensor, act, rdy_t):
            lt, _ = wts[nm]
            for mt in range(nM_):
                def tf(e, mt=mt, rdy_t=rdy_t):
                    p.wait(e, "vet", rdy_t)
                    p.wait(e, "sct")
                    for k in range(nK_):
                        ins = e.matmul(out=ps_g[:, mt * B:(mt + 1) * B],
                                       lhsT=lt(k)[:, mt * 128:(mt + 1) * 128],
                                       rhs=rhs_tensor[:, k * B:(k + 1) * B],
                                       start=(k == 0), stop=(k == nK_ - 1))
                    p.inc(ins, "pet", 1)
                p.te(tf)
                m_t = p.val("pet")

                def sf(e, mt=mt, m_t=m_t):
                    p.wait(e, "sct")
                    p.wait(e, "pet", m_t)
                    ins = e.activation(out=out_tensor[:, mt * B:(mt + 1) * B],
                                       in_=ps_g[:, mt * B:(mt + 1) * B], func=act,
                                       bias=cols[bnm][1](mt))
                    p.inc(ins, "sct", 1)
                p.sc(sf)
            return p.val("sct")

        t1 = net_step("gc_w1", "gc_b1", NDT, 4, gcb, h1g, AF.Gelu, gc_t)
        def vx(e, t1=t1):
            p.wait(e, "sct", t1)
            ins = e.engine_nop()
            p.inc(ins, "vet", 1)
        p.ve(vx)
        t1v = p.val("vet")
        t2 = net_step("gc_w2", "gc_b2", 4, 2, h1g, h2g, AF.Gelu, t1v)
        def vx2(e, t2=t2):
            p.wait(e, "sct", t2)
            ins = e.engine_nop()
            p.inc(ins, "vet", 1)
        p.ve(vx2)
        t2v = p.val("vet")
        gfb = p.sbuf(ls, uq("gfb"), [1, B], F32)
        lt3, _ = wts["gc_w3"]

        def tf_gf(e, t2v=t2v):
            p.wait(e, "vet", t2v)
            for k in range(2):
                ins = e.matmul(out=ps_g[0:1, 32:32 + B], lhsT=lt3(k),
                               rhs=h2g[:, k * B:(k + 1) * B],
                               start=(k == 0), stop=(k == 1))
            p.inc(ins, "pet", 1)
        p.te(tf_gf)
        gf_t = p.val("pet")

        def sf_gf(e, gf_t=gf_t):
            p.wait(e, "pet", gf_t)
            ins = e.activation(out=gfb[:, :], in_=ps_g[0:1, 32:32 + B],
                               func=AF.Sigmoid, bias=cols["gc_b3"][1](0)[0:1, 0:1])
            p.inc(ins, "sct", 1)
        p.sc(sf_gf)
        gfs_t = p.val("sct")

        def vgf(e, gfs_t=gfs_t):
            p.wait(e, gfs_t and "sct", gfs_t)
            for ch_ in wrdy:
                p.wait(e, ch_[0], ch_[1])
            p.fence(e, e.tensor_scalar_mul(out=gfr[:, 0:2], in0=gfb[:, :],
                                           scalar1=0.7), "vet")
            ins = e.tensor_scalar_mul(out=gfr[:, 2:3], in0=arr_c[0][0:1, 0:1],
                                      scalar1=2.0)
            p.inc(ins, "vet", 1)
        p.ve(vgf)
        gfr_t = p.val("vet")

        OUTA = p.sbuf(ls, uq("outA"), [128, NDT * FB], F32)
        OUTB = p.sbuf(ls, uq("outB"), [128, NDT * FB], F32)
        OBF = p.sbuf(ls, uq("obf"), [128, NDT * FB], BF16)
        H1 = p.sbuf(ls, uq("h1c"), [128, NDT * FB], BF16)
        H2 = p.sbuf(ls, uq("h2c"), [128, 4 * FB], BF16)
        H3 = p.sbuf(ls, uq("h3c"), [128, 2 * FB], BF16)
        TFR = p.sbuf(ls, uq("tfr"), [1, FB], F32)
        CPR = p.sbuf(ls, uq("cpr"), [1, FB], BF16)
        TMP = p.sbuf(ls, uq("tmp3"), [128, FB], F32)
        ps_c = [p.psum(ls, uq("psc3"), [128, FB]) for _ in range(2)]
        ps_cp = p.psum(ls, uq("pscp"), [128, FB])

        for blk_i in range(TOK // FB):
            b = blk_i // (T // FB)
            def syf(e, blk_i=blk_i):
                p.wait(e, "vet")
                p.wait(e, "dx")
                for dt in range(NDT):
                    ins = e.dma_start(
                        out=OUTA[:, dt * FB:(dt + 1) * FB],
                        in_=scr["xn_f32"][dt * 128:(dt + 1) * 128,
                                          blk_i * FB:(blk_i + 1) * FB])
                    p.inc(ins, "dx", 16)
            p.sy(syf)
            ld_t = p.val("dx")
            cur, nxt = OUTA, OUTB
            cur_rdy = ("dx", ld_t)

            for cyc in range(3):
                def vc(e, cur=cur, cur_rdy=cur_rdy):
                    p.wait(e, cur_rdy[0], cur_rdy[1])
                    p.wait(e, "pet")
                    ins = e.tensor_copy(out=OBF[:], in_=cur[:])
                    p.inc(ins, "vet", 1)
                p.ve(vc)
                obf_t = p.val("vet")

                def mmloc(lt_nm, b_nm, nK_, nM_, rhs_t, out_t, act, rdy, bank):
                    lt = wts[lt_nm][0]
                    outs = []
                    for mt in range(nM_):
                        ps = ps_c[(bank + mt) % 2]

                        def tf(e, mt=mt, ps=ps, rdy=rdy):
                            p.wait(e, rdy[0], rdy[1])
                            p.wait(e, "sct")
                            for k in range(nK_):
                                ins = e.matmul(out=ps[:, :],
                                               lhsT=lt(k)[:, mt * 128:(mt + 1) * 128],
                                               rhs=rhs_t[:, k * FB:(k + 1) * FB],
                                               start=(k == 0), stop=(k == nK_ - 1))
                            p.inc(ins, "pet", 1)
                        p.te(tf)
                        m_t = p.val("pet")

                        def sf(e, mt=mt, ps=ps, m_t=m_t):
                            p.wait(e, "pet", m_t)
                            ins = e.activation(out=out_t[:, mt * FB:(mt + 1) * FB],
                                               in_=ps[:, :], func=act,
                                               bias=cols[b_nm][1](mt))
                            p.inc(ins, "sct", 1)
                        p.sc(sf)
                    return ("sct", p.val("sct"))

                r1 = mmloc("cn_w1", "cn_b1", NDT, NDT, OBF, H1, AF.Gelu,
                           ("vet", obf_t), 0)
                r2 = mmloc("cn_w2", "cn_b2", NDT, 4, H1, H2, AF.Gelu, r1, 0)
                r3 = mmloc("cn_w3", "cn_b3", 4, 2, H2, H3, AF.Gelu, r2, 0)
                lt4 = wts["cn_w4"][0]

                def tf4(e, r3=r3):
                    p.wait(e, r3[0], r3[1])
                    p.wait(e, "sct")
                    for k in range(2):
                        ins = e.matmul(out=ps_c[0][0:1, :], lhsT=lt4(k),
                                       rhs=H3[:, k * FB:(k + 1) * FB],
                                       start=(k == 0), stop=(k == 1))
                    p.inc(ins, "pet", 1)
                p.te(tf4)
                t4_t = p.val("pet")

                def sf4(e, t4_t=t4_t):
                    p.wait(e, "pet", t4_t)
                    p.wait(e, "vet")
                    ins = e.activation(out=TFR[:, :], in_=ps_c[0][0:1, :],
                                       func=AF.Sigmoid, bias=cols["cn_b4"][1](0)[0:1, 0:1])
                    p.inc(ins, "sct", 1)
                p.sc(sf4)
                tf_t = p.val("sct")

                def vcp(e, b=b, tf_t=tf_t, gfr_t=gfr_t):
                    p.wait(e, "sct", tf_t)
                    p.wait(e, "vet", gfr_t)
                    p.fence(e, e.tensor_scalar(out=TFR[:, :], in0=TFR[:, :],
                                               scalar1=0.3, scalar2=gfr[0:1, b:b + 1],
                                               op0=ALU.mult, op1=ALU.add), "vet")
                    p.fence(e, e.tensor_scalar_add(out=TFR[:, :], in0=TFR[:, :],
                                                   scalar1=-0.5), "vet")
                    p.fence(e, e.tensor_scalar_mul(out=TFR[:, :], in0=TFR[:, :],
                                                   scalar1=gfr[0:1, 2:3]), "vet")
                    p.fence(e, e.tensor_scalar_add(out=TFR[:, :], in0=TFR[:, :],
                                                   scalar1=0.1), "vet")
                    ins = e.tensor_copy(out=CPR[:, :], in_=TFR[:, :])
                    p.inc(ins, "vet", 1)
                p.ve(vcp)
                cp_t = p.val("vet")

                def tbc(e, cp_t=cp_t):
                    p.wait(e, "vet", cp_t)
                    ins = e.matmul(out=ps_cp[:, :], lhsT=C["c_ones_row_bf"][:],
                                   rhs=CPR[:, :], start=True, stop=True)
                    p.inc(ins, "pet", 1)
                p.te(tbc)
                bc_t = p.val("pet")

                TW = [(4, 1.0), (5, 1.0), (6, -1.0), (7, -1.0),
                      (0, -1.0), (1, -1.0), (2, 1.0), (3, 1.0)]
                for dt in range(NDT):
                    src, sg = TW[dt]

                    def vtw(e, dt=dt, src=src, sg=sg, bc_t=bc_t, cur=cur, nxt=nxt):
                        p.wait(e, "pet", bc_t)
                        p.wait(e, "vet")
                        p.fence(e, e.tensor_tensor(
                            out=TMP[:], in0=ps_cp[:, :],
                            in1=cur[:, src * FB:(src + 1) * FB], op=ALU.mult), "vet")
                        ins = e.tensor_tensor(
                            out=nxt[:, dt * FB:(dt + 1) * FB],
                            in0=cur[:, dt * FB:(dt + 1) * FB], in1=TMP[:],
                            op=ALU.subtract if sg < 0 else ALU.add)
                        p.inc(ins, "vet", 1)
                    p.ve(vtw)
                cur_rdy = ("vet", p.val("vet"))
                cur, nxt = nxt, cur

            def syo(e, blk_i=blk_i, cur=cur, cur_rdy=cur_rdy):
                p.wait(e, cur_rdy[0], cur_rdy[1])
                p.wait(e, "dx")
                for dt in range(NDT):
                    ins = e.dma_start(
                        out=scr["acc_b"][dt * 128:(dt + 1) * 128,
                                         blk_i * FB:(blk_i + 1) * FB],
                        in_=cur[:, dt * FB:(dt + 1) * FB])
                    p.inc(ins, "dx", 16)
            p.sy(syo)
    p.barrier()


def _emit_layer3(p, W, C, scr, x_in, x_out):
    emit_ln(p, x_in, scr["xn_f32"], scr["xn_bf"], C, uq("ln"))
    p.barrier()
    emit_mha(p, {k: W[f"L3_attn_{k}"] for k in ("Wq", "Wk", "Wv", "Wo")}, C, scr)
    emit_ffn3(p, {k.replace("L3_ffn_", ""): W[k] for k in L3_W if "ffn" in k}, C, scr)
    emit_residual_add(p, x_in, scr["acc_a"], scr["acc_b"], x_out)
    p.barrier()


def emit_attn2(p, W, C, scr):
    for wn, bn, dst in (("Wq", "bq", "q_bf"), ("Wk", "bk", "k_bf"),
                        ("Wv", "bv", "v_bf")):
        with ExitStack() as ls:
            lt, lrdy = load_weight_bf16(p, ls, W[wn], DM, "w2" + wn, 0, DM)
            _, bfn, brdy = load_cols(p, ls, W[bn], uq("b2" + bn))
            cb = evac_to_hbm(p, ls, name="e2" + wn, out_hbm=scr[dst], out_dtype=BF16,
                             bias_col=bfn, extra_ready=brdy)
            mm_stream(p, ls, name="m2" + wn,
                      lhsT_tile=lambda k, m, lt=lt: lt(k)[:, m * 128:(m + 1) * 128],
                      nK=NDT, nM=NDT, lhs_ready=lrdy,
                      rhs_hbm=lambda k, f: scr["xn_bf"][k * 128:(k + 1) * 128,
                                                        f * FC:(f + 1) * FC],
                      out_cb=cb)
        p.barrier()

    KW = 2052
    with ExitStack() as ls:
        q_sb = p.sbuf(ls, uq("q2"), [128, NDT * T], BF16)
        k_sb = p.sbuf(ls, uq("k2"), [128, NDT * KW], BF16)
        v_sb = p.sbuf(ls, uq("v2"), [128, NDT * KW], BF16)
        qk = p.sbuf(ls, uq("qk2"), [128, NDT * FC], BF16)
        sc = [p.sbuf(ls, uq("sc2"), [16, T], F32) for _ in range(3)]
        mx = p.sbuf(ls, uq("mx2"), [16, T], F32)
        ex = [p.sbuf(ls, uq("ex2"), [16, T], BF16) for _ in range(3)]
        sr = p.sbuf(ls, uq("sr2"), [16, 2 * T], F32)
        att = [p.sbuf(ls, uq("at2"), [16, T], BF16) for _ in range(3)]
        tjs = p.sbuf(ls, uq("tj2"), [128, 3 * FC], F32)
        ybf = p.sbuf(ls, uq("yb2"), [128, FC], BF16)
        ps_s = p.psum(ls, uq("ps2"), [16, FC])
        ps_bc = [p.psum(ls, uq("pb2"), [128, FC]) for _ in range(2)]

        def gf0(e):
            for dt in range(NDT):
                e.memset(k_sb[:, dt * KW:dt * KW + 2], 0.0)
                ins = e.memset(v_sb[:, dt * KW:dt * KW + 2], 0.0)
            p.inc(ins, "gpt", 1)
        p.gp(gf0)
        pad_t = p.val("gpt")

        for b in range(B):
            def syf(e, b=b):
                p.wait(e, "pet")
                p.wait(e, "vet")
                p.wait(e, "gpt")
                for dt in range(NDT):
                    ins = e.dma_start(out=q_sb[:, dt * T:(dt + 1) * T],
                                      in_=scr["q_bf"][dt * 128:(dt + 1) * 128,
                                                      b * T:(b + 1) * T])
                    p.inc(ins, "dr", 16)
                    ins = e.dma_start(out=k_sb[:, dt * KW + 2:dt * KW + 2 + T],
                                      in_=scr["k_bf"][dt * 128:(dt + 1) * 128,
                                                      b * T:(b + 1) * T])
                    p.inc(ins, "dr", 16)
                    ins = e.dma_start(out=v_sb[:, dt * KW + 2:dt * KW + 2 + T],
                                      in_=scr["v_bf"][dt * 128:(dt + 1) * 128,
                                                      b * T:(b + 1) * T])
                    p.inc(ins, "dr", 16)
            p.sy(syf)
            ld_t = p.val("dr")

            for j in range(3):
                for fc in range(T // FC):
                    def vqk(e, j=j, fc=fc, b=b):
                        p.wait(e, "dr", ld_t)
                        p.wait(e, "gpt", pad_t)
                        p.wait(e, "pet")
                        for dt in range(NDT):
                            ins = e.tensor_tensor(
                                out=qk[:, dt * FC:(dt + 1) * FC],
                                in0=q_sb[:, dt * T + fc * FC:dt * T + (fc + 1) * FC],
                                in1=k_sb[:, dt * KW + 2 - j + fc * FC:
                                         dt * KW + 2 - j + (fc + 1) * FC],
                                op=ALU.mult)
                            p.inc(ins, "vet", 1)
                    p.ve(vqk)
                    qk_t = p.val("vet")

                    def tsc(e, j=j, fc=fc, qk_t=qk_t):
                        p.wait(e, "vet", qk_t)
                        for dt in range(NDT):
                            ins = e.matmul(out=ps_s[:, :],
                                           lhsT=C[f"c_hsel{dt}_bf"][:],
                                           rhs=qk[:, dt * FC:(dt + 1) * FC],
                                           start=(dt == 0), stop=(dt == NDT - 1))
                        p.inc(ins, "pet", 1)
                    p.te(tsc)
                    sc_t = p.val("pet")

                    def vev(e, j=j, fc=fc, sc_t=sc_t):
                        p.wait(e, "pet", sc_t)
                        ins = e.tensor_scalar_mul(
                            out=sc[j][:, fc * FC:(fc + 1) * FC],
                            in0=ps_s[:, :], scalar1=0.125)
                        p.inc(ins, "vet", 1)
                    p.ve(vev)

            def vmask(e):
                e.memset(sc[1][:, 0:1], -1e30)
                ins = e.memset(sc[2][:, 0:2], -1e30)
                p.inc(ins, "vet", 1)
            p.ve(vmask)
            scm_t = p.val("vet")

            def vsm(e, scm_t=scm_t):
                p.wait(e, "vet", scm_t)
                p.fence(e, e.tensor_tensor(out=mx[:, :], in0=sc[0][:, :],
                                           in1=sc[1][:, :], op=ALU.max), "vet")
                p.fence(e, e.tensor_tensor(out=mx[:, :], in0=mx[:, :],
                                           in1=sc[2][:, :], op=ALU.max), "vet")
                for j in range(3):
                    p.fence(e, e.tensor_tensor(
                        out=ex[j][:, :], in0=sc[j][:, :],
                        in1=mx[:, :], op=ALU.subtract), "vet")
                ins = e.engine_nop()
                p.inc(ins, "vet", 1)
            p.ve(vsm)
            sub_t = p.val("vet")

            def sexp(e, sub_t=sub_t):
                p.wait(e, "vet", sub_t)
                for j in range(3):
                    ins = e.activation(out=ex[j][:, :],
                                       in_=ex[j][:, :], func=AF.Exp)
                    p.inc(ins, "sct", 1)
            p.sc(sexp)
            exp_t = p.val("sct")

            def vnrm(e, exp_t=exp_t):
                p.wait(e, "sct", exp_t)
                p.fence(e, e.tensor_tensor(out=sr[:, 0:T], in0=ex[0][:, :],
                                           in1=ex[1][:, :], op=ALU.add), "vet")
                p.fence(e, e.tensor_tensor(out=sr[:, 0:T], in0=sr[:, 0:T],
                                           in1=ex[2][:, :], op=ALU.add), "vet")
                p.fence(e, e.reciprocal(out=sr[:, T:2 * T], in_=sr[:, 0:T]), "vet")
                for j in range(3):
                    p.fence(e, e.tensor_tensor(
                        out=att[j][:, :], in0=ex[j][:, :],
                        in1=sr[:, T:2 * T], op=ALU.mult), "vet")
                ins = e.engine_nop()
                p.inc(ins, "vet", 1)
            p.ve(vnrm)
            att_t = p.val("vet")

            for dt in range(NDT):
                for fc in range(T // FC):
                    for j in range(3):
                        ps = ps_bc[j & 1]

                        def tbc(e, j=j, dt=dt, fc=fc, ps=ps, att_t=att_t):
                            p.wait(e, "vet", att_t)
                            p.wait(e, "vet")
                            ins = e.matmul(
                                out=ps[:, :], lhsT=C[f"c_hexp{dt}_bf"][:],
                                rhs=att[j][:, fc * FC:(fc + 1) * FC],
                                start=True, stop=True)
                            p.inc(ins, "pet", 1)
                        p.te(tbc)
                        bc_t = p.val("pet")

                        def vtj(e, j=j, dt=dt, fc=fc, ps=ps, bc_t=bc_t):
                            p.wait(e, "pet", bc_t)
                            ins = e.tensor_tensor(
                                out=tjs[:, j * FC:(j + 1) * FC], in0=ps[:, :],
                                in1=v_sb[:, dt * KW + 2 - j + fc * FC:
                                         dt * KW + 2 - j + (fc + 1) * FC],
                                op=ALU.mult)
                            p.inc(ins, "vet", 1)
                        p.ve(vtj)

                    def vy(e, dt=dt, fc=fc):
                        p.wait(e, "vet")
                        p.wait(e, "dx")
                        p.fence(e, e.tensor_tensor(out=tjs[:, 0:FC], in0=tjs[:, 0:FC],
                                                   in1=tjs[:, FC:2 * FC], op=ALU.add),
                                "vet")
                        ins = e.tensor_tensor(out=ybf[:, :], in0=tjs[:, 0:FC],
                                              in1=tjs[:, 2 * FC:3 * FC], op=ALU.add)
                        p.inc(ins, "vet", 1)
                    p.ve(vy)
                    y_t = p.val("vet")

                    def syy(e, dt=dt, fc=fc, b=b, y_t=y_t):
                        p.wait(e, "vet", y_t)
                        p.wait(e, "dx")
                        ins = e.dma_start(
                            out=scr["ctx_bf"][dt * 128:(dt + 1) * 128,
                                              b * T + fc * FC:b * T + (fc + 1) * FC],
                            in_=ybf[:, :])
                        p.inc(ins, "dx", 16)
                    p.sy(syy)
    p.barrier()

    with ExitStack() as ls:
        lt, lrdy = load_weight_bf16(p, ls, W["Wo"], DM, "w2Wo", 0, DM)
        _, bofn, bordy = load_cols(p, ls, W["bo"], uq("b2bo"))
        cb = evac_to_hbm(p, ls, name="e2Wo", out_hbm=scr["acc_a"], out_dtype=F32,
                         bias_col=bofn, extra_ready=bordy)
        mm_stream(p, ls, name="m2Wo",
                  lhsT_tile=lambda k, m, lt=lt: lt(k)[:, m * 128:(m + 1) * 128],
                  nK=NDT, nM=NDT, lhs_ready=lrdy,
                  rhs_hbm=lambda k, f: scr["ctx_bf"][k * 128:(k + 1) * 128,
                                                     f * FC:(f + 1) * FC],
                  out_cb=cb)
    p.barrier()


def emit_ffn2(p, W, C, scr):
    DH = 4096
    s1c = s2c = None
    for half in range(2):
        with ExitStack() as ls:
            mw = DH // 2
            lt1, r1 = load_weight_bf16(p, ls, W["w1_w"], DM, "w2f1", half * mw, mw)
            lt2, r2 = load_weight_bf16(p, ls, W["w2_w"], DM, "w2f2", half * mw, mw)
            _, s1fn, s1r = load_cols(p, ls, W["w1_s"], uq("s2f1"), n=DH)
            _, s2fn, s2r = load_cols(p, ls, W["w2_s"], uq("s2f2"), n=DH)
            rbuf = p.sbuf(ls, uq("f2r"), [128, 2 * NDT * FC], BF16)
            psA = p.psum(ls, uq("f2A"), [128, FC])
            psB = p.psum(ls, uq("f2B"), [128, FC])
            sig = p.sbuf(ls, uq("f2s"), [128, FC], F32)
            gg = p.sbuf(ls, uq("f2g"), [128, FC], F32)
            uu = p.sbuf(ls, uq("f2u"), [128, FC], F32)
            hb = p.sbuf(ls, uq("f2h"), [128, 2 * FC], BF16)
            slot_war = [None, None]
            rhs_tick = [None, None]

            def prefetch(f, slot):
                dch = "dra" if slot == 0 else "drb"
                guard = slot_war[slot] or p.dguard.get(dch)

                def syf(e, f=f, slot=slot, dch=dch, guard=guard):
                    if guard is not None:
                        p.wait(e, guard[0], guard[1])
                    for k in range(NDT):
                        ins = e.dma_start(
                            out=rbuf[:, (slot * NDT + k) * FC:(slot * NDT + k + 1) * FC],
                            in_=scr["xn_bf"][k * 128:(k + 1) * 128, f * FC:(f + 1) * FC])
                        p.inc(ins, dch, 16)
                p.sy(syf)
                rhs_tick[slot] = (dch, p.val(dch))
            prefetch(0, 0)
            for f in range(NFC):
                slot = f & 1
                if f + 1 < NFC:
                    prefetch(f + 1, 1 - slot)
                for m in range(mw // 128):
                    gm = half * (mw // 128) + m

                    def tf(e, f=f, m=m, slot=slot, rt=rhs_tick[slot]):
                        p.wait(e, r1[0], r1[1])
                        p.wait(e, r2[0], r2[1])
                        p.wait(e, s1r[0], s1r[1])
                        p.wait(e, s2r[0], s2r[1])
                        p.wait(e, rt[0], rt[1])
                        p.wait(e, "vet")
                        p.wait(e, "sct")
                        for k in range(NDT):
                            e.matmul(out=psA[:, :], lhsT=lt1(k)[:, m * 128:(m + 1) * 128],
                                     rhs=rbuf[:, (slot * NDT + k) * FC:(slot * NDT + k + 1) * FC],
                                     start=(k == 0), stop=(k == NDT - 1))
                        for k in range(NDT):
                            ins = e.matmul(out=psB[:, :], lhsT=lt2(k)[:, m * 128:(m + 1) * 128],
                                           rhs=rbuf[:, (slot * NDT + k) * FC:(slot * NDT + k + 1) * FC],
                                           start=(k == 0), stop=(k == NDT - 1))
                        p.inc(ins, "pet", 1)
                    p.te(tf)
                    mm_t = p.val("pet")

                    def sf(e, gm=gm, mm_t=mm_t):
                        p.wait(e, "pet", mm_t)
                        p.wait(e, "vet")
                        ins = e.activation(out=sig[:, :], in_=psA[:, :], func=AF.Sigmoid,
                                           scale=s1fn(gm))
                        p.inc(ins, "sct", 1)
                    p.sc(sf)
                    sg_t = p.val("sct")

                    def vf(e, gm=gm, mm_t=mm_t, sg_t=sg_t, f=f):
                        p.wait(e, "sct", sg_t)
                        p.wait(e, "dx")
                        p.fence(e, e.tensor_scalar_mul(out=gg[:, :], in0=psA[:, :],
                                                       scalar1=s1fn(gm)), "vet")
                        p.fence(e, e.tensor_scalar_mul(out=uu[:, :], in0=psB[:, :],
                                                       scalar1=s2fn(gm)), "vet")
                        p.fence(e, e.tensor_tensor(out=gg[:, :], in0=gg[:, :],
                                                   in1=sig[:, :], op=ALU.mult), "vet")
                        ins = e.tensor_tensor(out=hb[:, (f & 1) * FC:((f & 1) + 1) * FC],
                                              in0=gg[:, :], in1=uu[:, :], op=ALU.mult)
                        p.inc(ins, "vet", 1)
                    p.ve(vf)
                    v_t = p.val("vet")

                    def sy2(e, gm=gm, f=f, v_t=v_t):
                        p.wait(e, "vet", v_t)
                        p.wait(e, "dx")
                        ins = e.dma_start(
                            out=scr["h_bf"][gm * 128:(gm + 1) * 128, f * FC:(f + 1) * FC],
                            in_=hb[:, (f & 1) * FC:((f & 1) + 1) * FC])
                        p.inc(ins, "dx", 16)
                    p.sy(sy2)
                slot_war[slot] = ("pet", p.val("pet"))
            p.dguard["dra"] = ("pet", p.val("pet"))
            p.dguard["drb"] = ("pet", p.val("pet"))
        p.barrier()

    for wn, sn, src_t, dst in (("h1_w", "h1_s", "h_bf", "g_bf"),
                               ("h2_w", "h2_s", "g_bf", "u_bf")):
        for mc in range(8):
            with ExitStack() as ls:
                lt, lrdy = load_weight_bf16(p, ls, W[wn], DH, "w2" + wn + str(mc),
                                            mc * 512, 512)
                _, sfn, srdy = load_cols(p, ls, W[sn], uq("s2" + wn), n=DH)
                cb = evac_to_hbm(p, ls, name="e2" + wn + str(mc), out_hbm=scr[dst],
                                 out_dtype=BF16, func=AF.Gelu,
                                 scale_col=sfn, extra_ready=srdy)
                mm_stream(p, ls, name="m2" + wn + str(mc),
                          lhsT_tile=lambda k, m, lt=lt: lt(k)[:, m * 128:(m + 1) * 128],
                          nK=DH // 128, nM=4, lhs_ready=lrdy,
                          rhs_hbm=lambda k, f, src_t=src_t: scr[src_t][
                              k * 128:(k + 1) * 128, f * FC:(f + 1) * FC],
                          out_cb=lambda m, f, ps, cb=cb, mc=mc: cb(mc * 4 + m, f, ps))
            p.barrier()

    for half in range(2):
        with ExitStack() as ls:
            lt, lrdy = load_weight_bf16(p, ls, W["w3_w"], DH, "w2w3" + str(half),
                                        half * 512, 512)
            _, s3fn, s3rdy = load_cols(p, ls, W["w3_s"], uq("s2w3"), n=DM)
            cb = evac_to_hbm(p, ls, name="e2w3" + str(half), out_hbm=scr["acc_b"],
                             out_dtype=F32, scale_col=s3fn, extra_ready=s3rdy)
            mm_stream(p, ls, name="m2w3" + str(half),
                      lhsT_tile=lambda k, m, lt=lt: lt(k)[:, m * 128:(m + 1) * 128],
                      nK=DH // 128, nM=4, lhs_ready=lrdy,
                      rhs_hbm=lambda k, f: scr["u_bf"][k * 128:(k + 1) * 128,
                                                       f * FC:(f + 1) * FC],
                      out_cb=lambda m, f, ps, cb=cb, half=half: cb(half * 4 + m, f, ps))
        p.barrier()


def _emit_layer2(p, W, C, scr, x_in, x_out):
    emit_ln(p, x_in, None, scr["xn_bf"], C, uq("ln"))
    p.barrier()
    emit_attn2(p, {k.replace("L2_attn_", ""): W[k] for k in L2_W if "attn" in k},
               C, scr)
    emit_ffn2(p, {k.replace("L2_ffn_", ""): W[k] for k in L2_W if "ffn" in k},
              C, scr)
    emit_residual_add(p, x_in, scr["acc_a"], scr["acc_b"], x_out)
    p.barrier()


def emit_ssm(p, W, C, scr):
    with ExitStack() as ls:
        _, gfn, grdy = load_cols(p, ls, W["ln_gamma"], uq("s0g"))
        _, bfn, brdy = load_cols(p, ls, W["ln_beta"], uq("s0b"))

        def join(e):
            p.wait(e, grdy[0], grdy[1])
            p.wait(e, brdy[0], brdy[1])
            ins = e.nop()
            p.inc(ins, "sct", 1)
        p.sc(join)
        emit_ln(p, scr["xn_f32"], None, scr["q_bf"], C, uq("iln"),
                gamma_col=gfn, beta_col=bfn)
    p.barrier()

    sst = ExitStack()
    u_tok = p.sbuf(sst, "u_tok", [128, 32768], BF16)
    with ExitStack() as ls:
        ubuf = p.sbuf(ls, uq("s0u"), [128, TOK], BF16)
        ps_t = p.psum(ls, uq("s0p"), [128, 128], BF16)
        for dt in range(NDT):
            def syf(e, dt=dt):
                p.wait(e, "pet")
                ins = e.dma_start(out=ubuf[:], in_=scr["q_bf"][dt * 128:(dt + 1) * 128, :])
                p.inc(ins, "dr", 16)
            p.sy(syf)
            ld_t = p.val("dr")
            for tt in range(TOK // 128):
                bq, cq = tt // 16, tt % 16

                def tf(e, dt=dt, tt=tt, ld_t=ld_t):
                    p.wait(e, "dr", ld_t)
                    p.wait(e, "vet")
                    ins = e.transpose(out=ps_t[:, :], in_=ubuf[:, tt * 128:(tt + 1) * 128],
                                      identity=C["c_id_bf"][:])
                    p.inc(ins, "pet", 1)
                p.te(tf)
                t_t = p.val("pet")

                def vf(e, dt=dt, bq=bq, cq=cq, t_t=t_t):
                    p.wait(e, "pet", t_t)
                    ins = e.tensor_copy(
                        out=u_tok[:, bq * 16384 + cq * 1024 + dt * 128:
                                  bq * 16384 + cq * 1024 + (dt + 1) * 128],
                        in_=ps_t[:, :])
                    p.inc(ins, "vet", 1)
                p.ve(vf)
    p.barrier()

    with ExitStack() as ls:
        la = p.sbuf(ls, uq("sla"), [128, 64], F32)
        bb = p.sbuf(ls, uq("slb"), [128, 64], F32)
        cc = p.sbuf(ls, uq("slc"), [128, 64], F32)
        wt = p.sbuf(ls, uq("slw"), [128, 64], F32)
        cbt = p.sbuf(ls, uq("slcb"), [128, 64], F32)
        _, dtfn, dtrdy = load_cols(p, ls, W["log_dt"], uq("sldt"))
        dtexp = p.sbuf(ls, uq("slde"), [128, NDT], F32)

        def sdt(e):
            p.wait(e, dtrdy[0], dtrdy[1])
            for dt in range(NDT):
                ins = e.activation(out=dtexp[:, dt:dt + 1], in_=dtfn(dt), func=AF.Exp)
            p.inc(ins, "sct", 1)
        p.sc(sdt)
        dte_t = p.val("sct")

        for dt in range(NDT):
            def syf(e, dt=dt):
                p.wait(e, "sct")
                p.wait(e, "vet")
                for src_ap, dst in ((W["log_A"], la), (W["B_param"], bb),
                                    (W["C_param"], cc)):
                    ins = e.dma_start(out=dst[:], in_=src_ap[dt * 128:(dt + 1) * 128, :])
                    p.inc(ins, "dwa", 16)
            p.sy(syf)
            l_t = p.val("dwa")

            def sf(e, dt=dt, l_t=l_t):
                p.wait(e, "dwa", l_t)
                p.wait(e, "sct", dte_t)
                p.wait(e, "vet")
                p.wait(e, "dx")
                ins = e.activation(out=wt[:], in_=la[:], func=AF.Exp)
                p.inc(ins, "sct", 1)
            p.sc(sf)
            w_t = p.val("sct")

            def vf(e, dt=dt, l_t=l_t, w_t=w_t):
                p.wait(e, "sct", w_t)
                p.wait(e, "dwa", l_t)
                p.wait(e, "dx")
                p.fence(e, e.tensor_scalar_mul(out=wt[:], in0=wt[:],
                                               scalar1=dtexp[:, dt:dt + 1]), "vet")
                p.fence(e, e.tensor_tensor(out=cbt[:], in0=bb[:], in1=cc[:],
                                           op=ALU.mult), "vet")
                ins = e.tensor_scalar_mul(out=cbt[:], in0=cbt[:],
                                          scalar1=dtexp[:, dt:dt + 1])
                p.inc(ins, "vet", 1)
            p.ve(vf)
            v_t = p.val("vet")

            def sy2(e, dt=dt, v_t=v_t):
                p.wait(e, "vet", v_t)
                p.wait(e, "dx")
                ins = e.dma_start(out=scr["w_hbm"][dt * 128:(dt + 1) * 128, :], in_=wt[:])
                p.inc(ins, "dx", 16)
                ins = e.dma_start(out=scr["cb_hbm"][dt * 128:(dt + 1) * 128, :], in_=cbt[:])
                p.inc(ins, "dx", 16)
            p.sy(sy2)
    p.barrier()

    sbufs = ExitStack()
    CBCOL = p.sbuf(sbufs, "CBCOL", [128, 512], F32)
    A128 = p.sbuf(sbufs, "A128", [64, 1024], F32)
    R_all = p.sbuf(sbufs, "R_all", [64, 32768], BF16)
    with ExitStack() as ls:
        stg = p.sbuf(ls, uq("cbs"), [128, 128], F32)
        ps_t2 = p.psum(ls, uq("cbp"), [128, 128])
        cb_v = scr["cb_hbm"].rearrange("(dp two) n -> dp (two n)", two=2)
        w_v = scr["w_hbm"].rearrange("(dp two) n -> dp (two n)", two=2)
        for t4 in range(4):
            def syf(e, t4=t4):
                p.wait(e, "pet")
                ins = e.dma_start(out=stg[:], in_=cb_v[t4 * 128:(t4 + 1) * 128, :])
                p.inc(ins, "dwa", 16)
            p.sy(syf)
            l_t = p.val("dwa")

            def tf(e, t4=t4, l_t=l_t):
                p.wait(e, "dwa", l_t)
                p.wait(e, "vet")
                ins = e.transpose(out=ps_t2[:, :], in_=stg[:], identity=C["c_id"][:])
                p.inc(ins, "pet", 1)
            p.te(tf)
            t_t = p.val("pet")

            def vf(e, t4=t4, t_t=t_t):
                p.wait(e, "pet", t_t)
                ins = e.tensor_copy(out=CBCOL[:, t4 * 128:(t4 + 1) * 128], in_=ps_t2[:, :])
                p.inc(ins, "vet", 1)
            p.ve(vf)
        for dt in range(NDT):
            def syf2(e, dt=dt):
                p.wait(e, "pet")
                ins = e.dma_start(out=stg[:, 0:64],
                                  in_=scr["w_hbm"][dt * 128:(dt + 1) * 128, :])
                p.inc(ins, "dwa", 16)
            p.sy(syf2)
            l_t = p.val("dwa")

            def tf2(e, dt=dt, l_t=l_t):
                p.wait(e, "dwa", l_t)
                p.wait(e, "sct")
                p.wait(e, "vet")
                ins = e.transpose(out=ps_t2[0:64, :], in_=stg[:, 0:64],
                                  identity=C["c_id"][:])
                p.inc(ins, "pet", 1)
            p.te(tf2)
            t_t = p.val("pet")

            def sf2(e, dt=dt, t_t=t_t):
                p.wait(e, "pet", t_t)
                ins = e.activation(out=A128[:, dt * 128:(dt + 1) * 128],
                                   in_=ps_t2[0:64, :], func=AF.Exp, scale=-128.0)
                p.inc(ins, "sct", 1)
            p.sc(sf2)
    p.barrier()

    NW = 16
    with ExitStack() as ls:
        wrow = p.sbuf(ls, uq("wr"), [1, 2 * NW * 128], F32)
        wrowb = p.sbuf(ls, uq("wrb"), [1, 2 * NW * 128], BF16)
        al = p.sbuf(ls, uq("al"), [128, 128], BF16)
        rstg = p.sbuf(ls, uq("rstg"), [64, 512], F32)
        ps_tab = p.psum(ls, uq("ptab"), [128, 132])
        ps_r = [p.psum(ls, uq("pr"), [64, 512]) for _ in range(2)]
        w_flat = scr["w_hbm"].rearrange("d n -> (d n)").rearrange("(one f) -> one f", one=1)

        def wchunk(ci):
            dch = "dwa" if (ci & 1) == 0 else "dwb"
            sl = ci & 1

            def syf(e, ci=ci, dch=dch, sl=sl):
                p.wait(e, "pet")
                ins = e.dma_start(out=wrow[0:1, sl * NW * 128:(sl + 1) * NW * 128],
                                  in_=w_flat[0:1, ci * NW * 128:(ci + 1) * NW * 128])
                p.inc(ins, dch, 16)
            p.sy(syf)
            d_t = p.val(dch)

            def vf(e, ci=ci, dch=dch, sl=sl, d_t=d_t):
                p.wait(e, dch, d_t)
                ins = e.tensor_copy(out=wrowb[0:1, sl * NW * 128:(sl + 1) * NW * 128],
                                    in_=wrow[0:1, sl * NW * 128:(sl + 1) * NW * 128])
                p.inc(ins, "vet", 1)
            p.ve(vf)
            return ("vet", p.val("vet"))

        wc_rdy = wchunk(0)
        uview = u_tok.ap().rearrange("p (b c d) -> p c b d", b=2, c=16, d=1024)
        for dp in range(512):
            ci, off = dp // NW, dp % NW
            if off == 0 and dp + NW < 512:
                nxt_rdy = wchunk(ci + 1)
            if off == 0 and dp > 0:
                wc_rdy = nxt_rdy
            sl = ci & 1

            def tfa(e, dp=dp, sl=sl, off=off, wc_rdy=wc_rdy):
                p.wait(e, wc_rdy[0], wc_rdy[1])
                p.wait(e, "sct")
                ins = e.matmul(out=ps_tab[:, 0:128],
                               lhsT=C["c_iota_sm127_bf"][:],
                               rhs=wrowb[0:1, (sl * NW + off) * 128:(sl * NW + off + 1) * 128],
                               start=True, stop=True)
                p.inc(ins, "pet", 1)
            p.te(tfa)
            ta_t = p.val("pet")

            def sfa(e, dp=dp, ta_t=ta_t):
                p.wait(e, "pet", ta_t)
                p.wait(e, "pet")
                ins = e.activation(out=al[:, :], in_=ps_tab[:, 0:128], func=AF.Exp)
                p.inc(ins, "sct", 1)
            p.sc(sfa)
            ale_t = p.val("sct")

            bank = (dp // 8) & 1
            for dl in range(2):
                d = 2 * dp + dl
                ycol = (d % 16) * 32

                def tfr(e, d=d, dl=dl, ycol=ycol, bank=bank, ale_t=ale_t, dp=dp):
                    p.wait(e, "sct", ale_t)
                    if d % 16 == 0:
                        p.wait(e, "vet")
                    ins = e.matmul(out=ps_r[bank][0:64, ycol:ycol + 32],
                                   lhsT=al[:, dl * 64:(dl + 1) * 64],
                                   rhs=uview[:, :, :, d], start=True, stop=True)
                    p.inc(ins, "pet", 1)
                p.te(tfr)
            if dp % 8 == 7:
                r_t = p.val("pet")
                base = (dp - 7) * 2 * 2

                def vev(e, dp=dp, bank=bank, r_t=r_t):
                    p.wait(e, "pet", r_t)
                    d0 = 2 * (dp - 7)
                    ins = e.tensor_copy(out=R_all[:, d0 * 32:(d0 + 16) * 32],
                                        in_=ps_r[bank][0:64, :])
                    p.inc(ins, "vet", 1)
                p.ve(vev)
    p.barrier()

    with ExitStack() as ls:
        state = p.sbuf(ls, uq("st8"), [64, 1024], F32)
        Rv = R_all.ap().rearrange("p (d c b) -> p d c b", d=1024, c=16, b=2)
        for b in range(B):
            def vz(e, b=b):
                p.wait(e, "vet")
                ins = e.memset(state[:, :], 0.0)
                p.inc(ins, "vet", 1)
            p.ve(vz)
            for c in range(16):
                def vstep(e, b=b, c=c):
                    p.wait(e, "vet")
                    if c > 0:
                        p.fence(e, e.tensor_copy(out=Rv[:, :, c - 1, b],
                                                 in_=state[:, :]), "vet")
                    p.fence(e, e.tensor_tensor(out=state[:, :], in0=state[:, :],
                                               in1=A128[:, :], op=ALU.mult), "vet")
                    ins = e.tensor_tensor(out=state[:, :], in0=state[:, :],
                                          in1=Rv[:, :, c, b], op=ALU.add)
                    p.inc(ins, "vet", 1)
                p.ve(vstep)
    p.barrier()
    with ExitStack() as ls:
        wrow = p.sbuf(ls, uq("wr2"), [1, 2 * 16 * 128], F32)
        wrowb = p.sbuf(ls, uq("wrb2"), [1, 2 * 16 * 128], BF16)
        wrown = p.sbuf(ls, uq("wrn2"), [1, 2 * 16 * 128], BF16)
        aneg = p.sbuf(ls, uq("aneg"), [64, 256], BF16)
        cbax = p.sbuf(ls, uq("cbax"), [64, 2 * 132], BF16)
        cbaxt = p.sbuf(ls, uq("cbaxt"), [64, 2 * 132], BF16)
        kd = p.sbuf(ls, uq("kd"), [128, 256], BF16)
        ystg = p.sbuf(ls, uq("ystg"), [128, 512], BF16)
        ytr = p.sbuf(ls, uq("ytr"), [128, 128], BF16)
        ps_tab = p.psum(ls, uq("ptab2"), [128, 132])
        ps_kd = p.psum(ls, uq("pkd"), [128, 256])
        ps_y = [p.psum(ls, uq("py"), [128, 512]) for _ in range(2)]
        ps_tr = p.psum(ls, uq("ptr"), [128, 128], BF16)
        NW = 16
        w_flat = scr["w_hbm"].rearrange("d n -> (d n)").rearrange("(one f) -> one f", one=1)
        uview = u_tok.ap().rearrange("p (b c d) -> p c b d", b=2, c=16, d=1024)
        Sv = R_all.ap().rearrange("p (d c b) -> p d c b", d=1024, c=16, b=2)

        def wchunk2(ci):
            dch = "dwa" if (ci & 1) == 0 else "dwb"
            sl = ci & 1

            def syf(e, ci=ci, dch=dch, sl=sl):
                p.wait(e, "pet")
                ins = e.dma_start(out=wrow[0:1, sl * NW * 128:(sl + 1) * NW * 128],
                                  in_=w_flat[0:1, ci * NW * 128:(ci + 1) * NW * 128])
                p.inc(ins, dch, 16)
            p.sy(syf)
            d_t = p.val(dch)

            def vf(e, ci=ci, dch=dch, sl=sl, d_t=d_t):
                p.wait(e, dch, d_t)
                p.fence(e, e.tensor_copy(
                    out=wrowb[0:1, sl * NW * 128:(sl + 1) * NW * 128],
                    in_=wrow[0:1, sl * NW * 128:(sl + 1) * NW * 128]), "vet")
                ins = e.tensor_scalar_mul(
                    out=wrown[0:1, sl * NW * 128:(sl + 1) * NW * 128],
                    in0=wrow[0:1, sl * NW * 128:(sl + 1) * NW * 128], scalar1=-1.0)
                p.inc(ins, "vet", 1)
            p.ve(vf)
            return ("vet", p.val("vet"))

        wc_rdy = wchunk2(0)
        for dp in range(512):
            ci, off = dp // NW, dp % NW
            if off == 0 and dp + NW < 512:
                nxt_rdy = wchunk2(ci + 1)
            if off == 0 and dp > 0:
                wc_rdy = nxt_rdy
            sl = ci & 1
            wsl = slice((sl * NW + off) * 128, (sl * NW + off + 1) * 128)

            def tft(e, dp=dp, wsl=wsl, wc_rdy=wc_rdy):
                p.wait(e, wc_rdy[0], wc_rdy[1])
                p.wait(e, "sct")
                ins = e.matmul(out=ps_tab[:, 0:128], lhsT=wrowb[0:1, wsl],
                               rhs=C["c_iota_s_bf"][:], start=True, stop=True)
                p.inc(ins, "pet", 1)
            p.te(tft)
            t1_t = p.val("pet")

            def sf1(e, t1_t=t1_t):
                p.wait(e, "pet", t1_t)
                p.wait(e, "pet")
                e.activation(out=aneg[:, 0:128], in_=ps_tab[0:64, 0:128], func=AF.Exp)
                ins = e.activation(out=aneg[:, 128:256], in_=ps_tab[64:128, 0:128],
                                   func=AF.Exp)
                p.inc(ins, "sct", 1)
            p.sc(sf1)
            an_t = p.val("sct")

            def tft2(e, dp=dp, wsl=wsl, an_t=an_t):
                p.wait(e, "sct", an_t)
                ins = e.matmul(out=ps_tab[:, 0:129], lhsT=wrown[0:1, wsl],
                               rhs=C["c_iota_t129_bf"][:], start=True, stop=True)
                p.inc(ins, "pet", 1)
            p.te(tft2)
            t2_t = p.val("pet")

            def sf2(e, t2_t=t2_t):
                p.wait(e, "pet", t2_t)
                p.wait(e, "vet")
                e.activation(out=cbaxt[:, 0:129], in_=ps_tab[0:64, 0:129], func=AF.Exp)
                ins = e.activation(out=cbaxt[:, 132:261], in_=ps_tab[64:128, 0:129],
                                   func=AF.Exp)
                p.inc(ins, "sct", 1)
            p.sc(sf2)
            cx_t = p.val("sct")

            def vcb(e, dp=dp, cx_t=cx_t):
                p.wait(e, "sct", cx_t)
                p.wait(e, "pet")
                e.tensor_scalar_mul(out=cbax[:, 0:129], in0=cbaxt[:, 0:129],
                                    scalar1=CBCOL[0:64, dp:dp + 1])
                ins = e.tensor_scalar_mul(out=cbax[:, 132:261], in0=cbaxt[:, 132:261],
                                          scalar1=CBCOL[64:128, dp:dp + 1])
                p.inc(ins, "vet", 1)
            p.ve(vcb)
            cb_t = p.val("vet")

            def tkd(e, dp=dp, an_t=an_t, cb_t=cb_t):
                p.wait(e, "sct", an_t)
                p.wait(e, "vet", cb_t)
                p.wait(e, "vet")
                for dl in range(2):
                    ins = e.matmul(out=ps_kd[:, dl * 128:(dl + 1) * 128],
                                   lhsT=aneg[:, dl * 128:(dl + 1) * 128],
                                   rhs=cbax[:, dl * 132:dl * 132 + 128],
                                   start=True, stop=True)
                p.inc(ins, "pet", 1)
            p.te(tkd)
            kb_t = p.val("pet")

            def vkd(e, kb_t=kb_t):
                p.wait(e, "pet", kb_t)
                ins = e.tensor_tensor(out=kd[:, :], in0=ps_kd[:, :],
                                      in1=C["c_utri2_bf"][:], op=ALU.mult)
                p.inc(ins, "vet", 1)
            p.ve(vkd)
            kd_t = p.val("vet")

            bank = (dp // 8) & 1
            for dl in range(2):
                d = 2 * dp + dl
                ycol = (d % 16) * 32

                def tap(e, d=d, dl=dl, ycol=ycol, bank=bank, kd_t=kd_t, dp=dp):
                    p.wait(e, "vet", kd_t)
                    if d % 16 == 0:
                        p.wait(e, "vet")
                    e.matmul(out=ps_y[bank][:, ycol:ycol + 32],
                             lhsT=kd[:, dl * 128:(dl + 1) * 128],
                             rhs=uview[:, :, :, d], start=True, stop=False)
                    ins = e.matmul(out=ps_y[bank][:, ycol + 2:ycol + 32],
                                   lhsT=cbax[0:64, dl * 132 + 1:dl * 132 + 129],
                                   rhs=Sv[:, d, 0:15, :], start=False, stop=True)
                    p.inc(ins, "pet", 1)
                p.te(tap)

            if dp % 8 == 7:
                ya_t = p.val("pet")
                d0 = 2 * (dp - 7)

                def vys(e, dp=dp, bank=bank, ya_t=ya_t):
                    p.wait(e, "pet", ya_t)
                    p.wait(e, "pet")
                    ins = e.tensor_copy(out=ystg[:, :], in_=ps_y[bank][:, :])
                    p.inc(ins, "vet", 1)
                p.ve(vys)
                ys_t = p.val("vet")
                for g in range(4):
                    def ttr(e, g=g, ys_t=ys_t):
                        p.wait(e, "vet", ys_t)
                        p.wait(e, "vet")
                        ins = e.transpose(out=ps_tr[:, :], in_=ystg[:, g * 128:(g + 1) * 128],
                                          identity=C["c_id_bf"][:])
                        p.inc(ins, "pet", 1)
                    p.te(ttr)
                    tr_t = p.val("pet")

                    def vtr(e, g=g, tr_t=tr_t):
                        p.wait(e, "pet", tr_t)
                        p.wait(e, "dx")
                        ins = e.tensor_copy(out=ytr[:, :], in_=ps_tr[:, :])
                        p.inc(ins, "vet", 1)
                    p.ve(vtr)
                    vt_t = p.val("vet")

                    def syy(e, g=g, d0=d0, vt_t=vt_t):
                        p.wait(e, "vet", vt_t)
                        p.wait(e, "dx")
                        dd = d0 + g * 4
                        for di in range(4):
                            yv = scr["v_bf"][dd + di:dd + di + 1, :].rearrange(
                                "one (b c s) -> one c b s", b=2, c=16, s=128)
                            ins = e.dma_start(out=yv,
                                              in_=ytr[di * 32:(di + 1) * 32, :])
                            p.inc(ins, "dx", 16)
                    p.sy(syy)
    p.barrier()
    sbufs.close()
    sst.close()

    with ExitStack() as ls:
        lt, lrdy = load_weight_bf16(p, ls, W["W_out"], DM, "wWo0", 0, DM)
        _, dcol, drdy = load_cols(p, ls, W["D_param"], uq("s2D"))
        _, bocol, bordy = load_cols(p, ls, W["b_out"], uq("s2bo"))
        rbuf = p.sbuf(ls, uq("s2r"), [128, 2 * NDT * FC], BF16)
        ubuf = p.sbuf(ls, uq("s2u"), [128, NDT * FC], BF16)
        ybuf = p.sbuf(ls, uq("s2y"), [128, NDT * FC], BF16)
        stage = p.sbuf(ls, uq("s2e"), [128, 2 * FC], F32)
        axn = p.sbuf(ls, uq("s2x"), [128, 2 * FC], F32)
        psA = p.psum(ls, uq("s2A"), [128, FC])
        psB = p.psum(ls, uq("s2B"), [128, FC])
        banks = [psA, psB]
        bank_war = [None, None]
        rhs_rdy = [None, None]
        slot_war = [None, None]

        def rhs_prep(f, slot):
            dch = "dra" if slot == 0 else "drb"
            guard = slot_war[slot] or p.dguard.get(dch)

            def syf(e, f=f, slot=slot, dch=dch, guard=guard):
                if guard is not None:
                    p.wait(e, guard[0], guard[1])
                p.wait(e, "vet")
                for k in range(NDT):
                    ins = e.dma_start(out=ubuf[:, k * FC:(k + 1) * FC],
                                      in_=scr["q_bf"][k * 128:(k + 1) * 128,
                                                      f * FC:(f + 1) * FC])
                    p.inc(ins, dch, 16)
                    ins = e.dma_start(out=ybuf[:, k * FC:(k + 1) * FC],
                                      in_=scr["v_bf"][k * 128:(k + 1) * 128,
                                                      f * FC:(f + 1) * FC])
                    p.inc(ins, dch, 16)
            p.sy(syf)
            d_t = p.val(dch)

            def vf(e, f=f, slot=slot, dch=dch, d_t=d_t):
                p.wait(e, dch, d_t)
                p.wait(e, drdy[0], drdy[1])
                for k in range(NDT):
                    ins = e.scalar_tensor_tensor(
                        out=rbuf[:, (slot * NDT + k) * FC:(slot * NDT + k + 1) * FC],
                        in0=ubuf[:, k * FC:(k + 1) * FC], scalar=dcol(k),
                        in1=ybuf[:, k * FC:(k + 1) * FC], op0=ALU.mult, op1=ALU.add)
                    p.inc(ins, "vet", 1)
            p.ve(vf)
            rhs_rdy[slot] = ("vet", p.val("vet"))

        rhs_prep(0, 0)
        bi = 0
        for f in range(NFC):
            slot = f & 1
            if f + 1 < NFC:
                rhs_prep(f + 1, 1 - slot)
            for m in range(NDT):
                ps = banks[bi]

                def tf(e, f=f, m=m, ps=ps, war=bank_war[bi], rt=rhs_rdy[slot], slot=slot):
                    p.wait(e, lrdy[0], lrdy[1])
                    p.wait(e, rt[0], rt[1])
                    if war is not None:
                        p.wait(e, war[0], war[1])
                    for k in range(NDT):
                        ins = e.matmul(
                            out=ps[:, :], lhsT=lt(k)[:, m * 128:(m + 1) * 128],
                            rhs=rbuf[:, (slot * NDT + k) * FC:(slot * NDT + k + 1) * FC],
                            start=(k == 0), stop=(k == NDT - 1))
                    p.inc(ins, "pet", 1)
                p.te(tf)
                mm_t = p.val("pet")
                esl = (f * NDT + m) & 1

                def syx(e, m=m, f=f, esl=esl):
                    p.wait(e, "vet")
                    p.wait(e, "da")
                    ins = e.dma_start(out=axn[:, esl * FC:(esl + 1) * FC],
                                      in_=scr["xn_f32"][m * 128:(m + 1) * 128,
                                                        f * FC:(f + 1) * FC])
                    p.inc(ins, "da", 16)
                p.sy(syx)
                ax_t = p.val("da")

                def vf2(e, m=m, f=f, esl=esl, mm_t=mm_t, ax_t=ax_t):
                    p.wait(e, "pet", mm_t)
                    p.wait(e, "da", ax_t)
                    p.wait(e, bordy[0], bordy[1])
                    p.wait(e, "dx")
                    p.fence(e, e.tensor_scalar_add(out=stage[:, esl * FC:(esl + 1) * FC],
                                                   in0=ps[:, :], scalar1=bocol(m)), "vet")
                    ins = e.tensor_tensor(out=stage[:, esl * FC:(esl + 1) * FC],
                                          in0=stage[:, esl * FC:(esl + 1) * FC],
                                          in1=axn[:, esl * FC:(esl + 1) * FC], op=ALU.add)
                    p.inc(ins, "vet", 1)
                p.ve(vf2)
                ev_t = p.val("vet")
                bank_war[bi] = ("vet", ev_t)

                def sy2(e, m=m, f=f, esl=esl, ev_t=ev_t):
                    p.wait(e, "vet", ev_t)
                    p.wait(e, "dx")
                    ins = e.dma_start(out=scr["acc_a"][m * 128:(m + 1) * 128,
                                                       f * FC:(f + 1) * FC],
                                      in_=stage[:, esl * FC:(esl + 1) * FC])
                    p.inc(ins, "dx", 16)
                p.sy(sy2)
                bi ^= 1
            slot_war[slot] = ("pet", p.val("pet"))
        p.dguard["dra"] = ("pet", p.val("pet"))
        p.dguard["drb"] = ("pet", p.val("pet"))
    p.barrier()


def emit_moe(p, W, C, scr):
    EPS = 1e-5
    mst = ExitStack()
    lg_sb = p.sbuf(mst, "lg_sb", [8, TOK], F32)
    w_sb = p.sbuf(mst, "w_sb", [8, TOK], F32)
    with ExitStack() as ls:
        _, sfn, srdy = load_cols(p, ls, W["ln_scale"], uq("m0s"))
        _, bfn, brdy = load_cols(p, ls, W["ln_bias"], uq("m0b"))

        def join(e):
            p.wait(e, srdy[0], srdy[1])
            p.wait(e, brdy[0], brdy[1])
            ins = e.nop()
            p.inc(ins, "sct", 1)
        p.sc(join)
        emit_ln(p, scr["xn_f32"], scr["xg_f32"], scr["u_bf"], C, uq("gln"),
                gamma_col=sfn, beta_col=bfn)
    p.barrier()
    with ExitStack() as ls:
        gw = p.sbuf(ls, uq("mgw"), [128, NDT * 8], F32)
        grb = p.sbuf(ls, uq("mgr"), [128, NDT * FC], F32)
        ps_g = p.psum(ls, uq("mgp"), [8, FC])

        def syg(e):
            p.wait(e, "dwa")
            for k in range(NDT):
                ins = e.dma_start(out=gw[:, k * 8:(k + 1) * 8],
                                  in_=W["gate_w"][k * 128:(k + 1) * 128, :])
                p.inc(ins, "dwa", 16)
        p.sy(syg)
        gw_t = p.val("dwa")
        for f in range(NFC):
            def syr(e, f=f):
                p.wait(e, "pet")
                p.wait(e, "dra")
                for k in range(NDT):
                    ins = e.dma_start(out=grb[:, k * FC:(k + 1) * FC],
                                      in_=scr["xg_f32"][k * 128:(k + 1) * 128,
                                                        f * FC:(f + 1) * FC])
                    p.inc(ins, "dra", 16)
            p.sy(syr)
            r_t = p.val("dra")

            def tg(e, f=f, r_t=r_t):
                p.wait(e, "dwa", gw_t)
                p.wait(e, "dra", r_t)
                p.wait(e, "vet")
                for k in range(NDT):
                    ins = e.matmul(out=ps_g[0:8, :], lhsT=gw[:, k * 8:(k + 1) * 8],
                                   rhs=grb[:, k * FC:(k + 1) * FC],
                                   start=(k == 0), stop=(k == NDT - 1))
                p.inc(ins, "pet", 1)
            p.te(tg)
            g_t = p.val("pet")

            def vg(e, f=f, g_t=g_t):
                p.wait(e, "pet", g_t)
                ins = e.tensor_copy(out=lg_sb[:, f * FC:(f + 1) * FC], in_=ps_g[0:8, :])
                p.inc(ins, "vet", 1)
            p.ve(vg)
        p.dguard["dra"] = ("pet", p.val("pet"))
    p.barrier()

    with ExitStack() as ls:
        lgt = p.sbuf(ls, uq("m1l"), [128, 8], F32)
        prb = p.sbuf(ls, uq("m1p"), [128, 8], F32)
        ge1 = p.sbuf(ls, uq("m1g"), [128, 8], F32)
        msk = p.sbuf(ls, uq("m1m"), [128, 8], F32)
        wtk = p.sbuf(ls, uq("m1w"), [128, 8], F32)
        colb = p.sbuf(ls, uq("m1c"), [128, 8], F32)
        ps_t = p.psum(ls, uq("m1t"), [128, 8])
        ps_b = p.psum(ls, uq("m1b"), [8, 128])
        for tt in range(TOK // 128):
            def tf(e, tt=tt):
                p.wait(e, "vet")
                ins = e.transpose(out=ps_t[:, :], in_=lg_sb[:, tt * 128:(tt + 1) * 128],
                                  identity=C["c_id"][0:8, 0:8])
                p.inc(ins, "pet", 1)
            p.te(tf)
            t_t = p.val("pet")

            def v1(e, tt=tt, t_t=t_t):
                p.wait(e, "pet", t_t)
                p.fence(e, e.tensor_copy(out=lgt[:, :], in_=ps_t[:, :]), "vet")
                p.fence(e, e.reduce_max(out=colb[:, 0:1], in_=lgt[:, :], axis=AX), "vet")
                p.fence(e, e.tensor_scalar_mul(out=colb[:, 1:2], in0=colb[:, 0:1],
                                               scalar1=-1.0), "vet")
                ins = e.engine_nop()
                p.inc(ins, "vet", 1)
            p.ve(v1)
            v1_t = p.val("vet")

            def s1(e, tt=tt, v1_t=v1_t):
                p.wait(e, "vet", v1_t)
                ins = e.activation(out=prb[:, :], in_=lgt[:, :], func=AF.Exp,
                                   bias=colb[:, 1:2])
                p.inc(ins, "sct", 1)
            p.sc(s1)
            s1_t = p.val("sct")

            def v2(e, tt=tt, s1_t=s1_t):
                p.wait(e, "sct", s1_t)
                p.fence(e, e.reduce_sum(out=colb[:, 2:3], in_=prb[:, :], axis=AX), "vet")
                p.fence(e, e.reduce_max(out=colb[:, 3:4], in_=prb[:, :], axis=AX), "vet")
                p.fence(e, e.tensor_scalar(out=ge1[:, :], in0=prb[:, :],
                                           scalar1=colb[:, 3:4], scalar2=None,
                                           op0=ALU.is_ge), "vet")
                p.fence(e, e.tensor_scalar(out=msk[:, :], in0=ge1[:, :], scalar1=-1.0,
                                           scalar2=1.0, op0=ALU.mult, op1=ALU.add), "vet")
                p.fence(e, e.tensor_tensor(out=msk[:, :], in0=prb[:, :], in1=msk[:, :],
                                           op=ALU.mult), "vet")
                p.fence(e, e.reduce_max(out=colb[:, 4:5], in_=msk[:, :], axis=AX), "vet")
                p.fence(e, e.tensor_scalar(out=msk[:, :], in0=msk[:, :],
                                           scalar1=colb[:, 4:5], scalar2=None,
                                           op0=ALU.is_ge), "vet")
                p.fence(e, e.tensor_tensor(out=msk[:, :], in0=msk[:, :], in1=ge1[:, :],
                                           op=ALU.add), "vet")
                p.fence(e, e.tensor_tensor(out=colb[:, 5:6], in0=colb[:, 3:4],
                                           in1=colb[:, 4:5], op=ALU.add), "vet")
                p.fence(e, e.scalar_tensor_tensor(out=colb[:, 5:6], in0=colb[:, 2:3],
                                                  scalar=EPS, in1=colb[:, 5:6],
                                                  op0=ALU.mult, op1=ALU.add), "vet")
                p.fence(e, e.reciprocal(out=colb[:, 6:7], in_=colb[:, 5:6]), "vet")
                p.fence(e, e.tensor_tensor(out=wtk[:, :], in0=prb[:, :], in1=msk[:, :],
                                           op=ALU.mult), "vet")
                ins = e.tensor_scalar_mul(out=wtk[:, :], in0=wtk[:, :],
                                          scalar1=colb[:, 6:7])
                p.inc(ins, "vet", 1)
            p.ve(v2)
            v2_t = p.val("vet")

            def tf2(e, tt=tt, v2_t=v2_t):
                p.wait(e, "vet", v2_t)
                p.wait(e, "vet")
                ins = e.transpose(out=ps_b[:, :], in_=wtk[:, :],
                                  identity=C["c_id"][:])
                p.inc(ins, "pet", 1)
            p.te(tf2)
            t2_t = p.val("pet")

            def v3(e, tt=tt, t2_t=t2_t):
                p.wait(e, "pet", t2_t)
                ins = e.tensor_copy(out=w_sb[:, tt * 128:(tt + 1) * 128],
                                    in_=ps_b[0:8, :])
                p.inc(ins, "vet", 1)
            p.ve(v3)
    p.barrier()

    for ei in range(9):
        if ei == 0:
            wg_ap, wu_ap, wd_ap = W["sWg"], W["sWu"], W["sWd"]
        else:
            wg_ap, wu_ap, wd_ap = (W["Wg"][ei - 1], W["Wu"][ei - 1], W["Wd"][ei - 1])
        IH = 2048
        NMH = IH // 128
        for half in range(2):
            with ExitStack() as ls:
                mw = IH // 2
                lt, lrdy = load_weight_bf16(p, ls, wg_ap, DM, uq("mwg"), half * mw, mw)
                lt2, lrdy2 = load_weight_bf16(p, ls, wu_ap, DM, uq("mwu"), half * mw, mw)
                rbuf = p.sbuf(ls, uq("mr"), [128, 2 * NDT * FC], BF16)
                psA = p.psum(ls, uq("mA"), [128, FC])
                psB = p.psum(ls, uq("mB"), [128, FC])
                sig = p.sbuf(ls, uq("ms"), [128, FC], F32)
                hb = p.sbuf(ls, uq("mh"), [128, 2 * FC], BF16)
                slot_war = [None, None]
                rhs_tick = [None, None]

                def prefetch(f, slot):
                    dch = "dra" if slot == 0 else "drb"
                    guard = slot_war[slot] or p.dguard.get(dch)

                    def syf(e, f=f, slot=slot, dch=dch, guard=guard):
                        if guard is not None:
                            p.wait(e, guard[0], guard[1])
                        for k in range(NDT):
                            ins = e.dma_start(
                                out=rbuf[:, (slot * NDT + k) * FC:(slot * NDT + k + 1) * FC],
                                in_=scr["xn_bf"][k * 128:(k + 1) * 128,
                                                 f * FC:(f + 1) * FC])
                            p.inc(ins, dch, 16)
                    p.sy(syf)
                    rhs_tick[slot] = (dch, p.val(dch))
                prefetch(0, 0)
                for f in range(NFC):
                    slot = f & 1
                    if f + 1 < NFC:
                        prefetch(f + 1, 1 - slot)
                    for m in range(NMH // 2):
                        gm = half * (NMH // 2) + m

                        def tf(e, f=f, m=m, slot=slot, rt=rhs_tick[slot]):
                            p.wait(e, lrdy[0], lrdy[1])
                            p.wait(e, lrdy2[0], lrdy2[1])
                            p.wait(e, rt[0], rt[1])
                            p.wait(e, "vet")
                            p.wait(e, "sct")
                            for k in range(NDT):
                                e.matmul(out=psA[:, :],
                                         lhsT=lt(k)[:, m * 128:(m + 1) * 128],
                                         rhs=rbuf[:, (slot * NDT + k) * FC:(slot * NDT + k + 1) * FC],
                                         start=(k == 0), stop=(k == NDT - 1))
                            for k in range(NDT):
                                ins = e.matmul(out=psB[:, :],
                                               lhsT=lt2(k)[:, m * 128:(m + 1) * 128],
                                               rhs=rbuf[:, (slot * NDT + k) * FC:(slot * NDT + k + 1) * FC],
                                               start=(k == 0), stop=(k == NDT - 1))
                            p.inc(ins, "pet", 1)
                        p.te(tf)
                        mm_t = p.val("pet")

                        def sf(e, mm_t=mm_t):
                            p.wait(e, "pet", mm_t)
                            p.wait(e, "vet")
                            ins = e.activation(out=sig[:, :], in_=psA[:, :],
                                               func=AF.Sigmoid)
                            p.inc(ins, "sct", 1)
                        p.sc(sf)
                        sg_t = p.val("sct")

                        def vf(e, f=f, sg_t=sg_t, mm_t=mm_t):
                            p.wait(e, "sct", sg_t)
                            p.wait(e, "dx")
                            p.fence(e, e.tensor_tensor(out=sig[:, :], in0=psA[:, :],
                                                       in1=sig[:, :], op=ALU.mult), "vet")
                            ins = e.tensor_tensor(out=hb[:, (f & 1) * FC:((f & 1) + 1) * FC],
                                                  in0=sig[:, :], in1=psB[:, :],
                                                  op=ALU.mult)
                            p.inc(ins, "vet", 1)
                        p.ve(vf)
                        v_t = p.val("vet")

                        def sy2(e, gm=gm, f=f, v_t=v_t):
                            p.wait(e, "vet", v_t)
                            p.wait(e, "dx")
                            ins = e.dma_start(
                                out=scr["h_bf"][gm * 128:(gm + 1) * 128,
                                                f * FC:(f + 1) * FC],
                                in_=hb[:, (f & 1) * FC:((f & 1) + 1) * FC])
                            p.inc(ins, "dx", 16)
                        p.sy(sy2)
                    slot_war[slot] = ("pet", p.val("pet"))
                p.dguard["dra"] = ("pet", p.val("pet"))
                p.dguard["drb"] = ("pet", p.val("pet"))
            p.barrier()
        with ExitStack() as ls:
            lt, lrdy = load_weight_bf16(p, ls, wd_ap, IH, uq("mwd"), 0, DM)
            rbuf = p.sbuf(ls, uq("mdr"), [128, 2 * NMH * FC], BF16)
            wrow = p.sbuf(ls, uq("mwr"), [1, FC], F32)
            wbc = p.sbuf(ls, uq("mwb"), [128, FC], F32)
            acb = p.sbuf(ls, uq("mab"), [128, 2 * FC], F32)
            stage = p.sbuf(ls, uq("mst"), [128, 2 * FC], F32)
            psA = p.psum(ls, uq("mdA"), [128, FC])
            psB = p.psum(ls, uq("mdB"), [128, FC])
            ps_w = p.psum(ls, uq("mdW"), [128, FC])
            banks = [psA, psB]
            bank_war = [None, None]
            slot_war = [None, None]
            rhs_tick = [None, None]

            def prefetch(f, slot):
                dch = "dra" if slot == 0 else "drb"
                guard = slot_war[slot] or p.dguard.get(dch)

                def syf(e, f=f, slot=slot, dch=dch, guard=guard):
                    if guard is not None:
                        p.wait(e, guard[0], guard[1])
                    for k in range(NMH):
                        ins = e.dma_start(
                            out=rbuf[:, (slot * NMH + k) * FC:(slot * NMH + k + 1) * FC],
                            in_=scr["h_bf"][k * 128:(k + 1) * 128, f * FC:(f + 1) * FC])
                        p.inc(ins, dch, 16)
                p.sy(syf)
                rhs_tick[slot] = (dch, p.val(dch))
            prefetch(0, 0)
            bi = 0
            for f in range(NFC):
                slot = f & 1
                if f + 1 < NFC:
                    prefetch(f + 1, 1 - slot)
                if ei > 0:
                    def syw(e, f=f, ei=ei):
                        p.wait(e, "pet")
                        ins = e.dma_start(out=wrow[0:1, :],
                                          in_=w_sb[ei - 1:ei, f * FC:(f + 1) * FC])
                        p.inc(ins, "dww", 16)
                    p.sy(syw)
                    wr_t = p.val("dww")

                    def tbc(e, f=f, wr_t=wr_t):
                        p.wait(e, "dww", wr_t)
                        p.wait(e, "vet")
                        ins = e.matmul(out=ps_w[:, :], lhsT=C["c_ones_row"][:],
                                       rhs=wrow[0:1, :], start=True, stop=True)
                        p.inc(ins, "pet", 1)
                    p.te(tbc)
                    bc_t = p.val("pet")

                    def vbc(e, f=f, bc_t=bc_t):
                        p.wait(e, "pet", bc_t)
                        ins = e.tensor_copy(out=wbc[:, :], in_=ps_w[:, :])
                        p.inc(ins, "vet", 1)
                    p.ve(vbc)
                    wbc_t = p.val("vet")
                for m in range(NDT):
                    ps = banks[bi]

                    def tf(e, f=f, m=m, ps=ps, war=bank_war[bi], rt=rhs_tick[slot],
                           slot=slot):
                        p.wait(e, lrdy[0], lrdy[1])
                        p.wait(e, rt[0], rt[1])
                        if war is not None:
                            p.wait(e, war[0], war[1])
                        for k in range(NMH):
                            ins = e.matmul(
                                out=ps[:, :], lhsT=lt(k)[:, m * 128:(m + 1) * 128],
                                rhs=rbuf[:, (slot * NMH + k) * FC:(slot * NMH + k + 1) * FC],
                                start=(k == 0), stop=(k == NMH - 1))
                        p.inc(ins, "pet", 1)
                    p.te(tf)
                    mm_t = p.val("pet")
                    esl = (f * NDT + m) & 1
                    if ei > 0:
                        def sya(e, m=m, f=f, esl=esl):
                            p.wait(e, "vet")
                            p.wait(e, "da")
                            ins = e.dma_start(out=acb[:, esl * FC:(esl + 1) * FC],
                                              in_=scr["acc_b"][m * 128:(m + 1) * 128,
                                                               f * FC:(f + 1) * FC])
                            p.inc(ins, "da", 16)
                        p.sy(sya)
                        ac_t = p.val("da")

                        def vf(e, m=m, f=f, esl=esl, mm_t=mm_t, ac_t=ac_t, ps=ps,
                               wbc_t=wbc_t):
                            p.wait(e, "pet", mm_t)
                            p.wait(e, "da", ac_t)
                            p.wait(e, "vet", wbc_t)
                            p.wait(e, "dx")
                            p.fence(e, e.tensor_tensor(out=stage[:, esl * FC:(esl + 1) * FC],
                                                       in0=ps[:, :], in1=wbc[:, :],
                                                       op=ALU.mult), "vet")
                            ins = e.tensor_tensor(out=stage[:, esl * FC:(esl + 1) * FC],
                                                  in0=stage[:, esl * FC:(esl + 1) * FC],
                                                  in1=acb[:, esl * FC:(esl + 1) * FC],
                                                  op=ALU.add)
                            p.inc(ins, "vet", 1)
                        p.ve(vf)
                    else:
                        def vf(e, m=m, f=f, esl=esl, mm_t=mm_t, ps=ps):
                            p.wait(e, "pet", mm_t)
                            p.wait(e, "dx")
                            ins = e.tensor_copy(out=stage[:, esl * FC:(esl + 1) * FC],
                                                in_=ps[:, :])
                            p.inc(ins, "vet", 1)
                        p.ve(vf)
                    ev_t = p.val("vet")
                    bank_war[bi] = ("vet", ev_t)

                    def sy2(e, m=m, f=f, esl=esl, ev_t=ev_t):
                        p.wait(e, "vet", ev_t)
                        p.wait(e, "dx")
                        ins = e.dma_start(out=scr["acc_b"][m * 128:(m + 1) * 128,
                                                           f * FC:(f + 1) * FC],
                                          in_=stage[:, esl * FC:(esl + 1) * FC])
                        p.inc(ins, "dx", 16)
                    p.sy(sy2)
                    bi ^= 1
                slot_war[slot] = ("pet", p.val("pet"))
            p.dguard["dra"] = ("pet", p.val("pet"))
            p.dguard["drb"] = ("pet", p.val("pet"))
        p.barrier()
    mst.close()


def _emit_layer0(p, W, C, scr, x_in, x_out):
    emit_ln(p, x_in, scr["xn_f32"], scr["xn_bf"], C, uq("ln"))
    p.barrier()
    emit_ssm(p, {k.replace("L0_attn_", ""): W[k] for k in L0_W if "attn" in k},
             C, scr)
    emit_moe(p, {k.replace("L0_ffn_", ""): W[k] for k in L0_W if "ffn" in k},
             C, scr)
    emit_residual_add(p, x_in, scr["acc_a"], scr["acc_b"], x_out)
    p.barrier()


def prep_inputs(inputs):
    t = lambda a: np.ascontiguousarray(np.asarray(a).T)
    a = lambda v: np.ascontiguousarray(np.asarray(v))
    w = {}
    w["L0_attn_log_A"] = a(inputs["L0_attn_log_A"])
    w["L0_attn_B_param"] = a(inputs["L0_attn_B_param"])
    w["L0_attn_C_param"] = a(inputs["L0_attn_C_param"])
    w["L0_attn_D_param"] = a(inputs["L0_attn_D_param"]).reshape(DM, 1)
    w["L0_attn_log_dt"] = a(inputs["L0_attn_log_dt"]).reshape(DM, 1)
    w["L0_attn_ln_gamma"] = a(inputs["L0_attn_ln_gamma"]).reshape(DM, 1)
    w["L0_attn_ln_beta"] = a(inputs["L0_attn_ln_beta"]).reshape(DM, 1)
    w["L0_attn_W_out"] = t(inputs["L0_attn_W_out"])
    w["L0_attn_b_out"] = a(inputs["L0_attn_b_out"]).reshape(DM, 1)
    w["L0_ffn_ln_scale"] = a(inputs["L0_ffn_ln_scale"]).reshape(DM, 1)
    w["L0_ffn_ln_bias"] = a(inputs["L0_ffn_ln_bias"]).reshape(DM, 1)
    w["L0_ffn_gate_w"] = t(inputs["L0_ffn_gate_w"])
    for k in ("Wg", "Wu", "Wd", "sWg", "sWu", "sWd"):
        w[f"L0_ffn_{k}"] = a(inputs[f"L0_ffn_{k}"])
    for k in ("Wq", "Wk", "Wv", "Wo"):
        w[f"L1_attn_{k}"] = a(inputs[f"L1_attn_{k}"])
    w["L1_attn_bo"] = a(inputs["L1_attn_bo"]).reshape(DM, 1)
    w["L1_ffn_w_gate"] = t(inputs["L1_ffn_w_gate"])
    w["L1_ffn_w_up"] = t(inputs["L1_ffn_w_up"])
    w["L1_ffn_w_down"] = t(inputs["L1_ffn_w_down"])
    for k in ("Wq", "Wk", "Wv", "Wo"):
        w[f"L2_attn_{k}"] = t(inputs[f"L2_attn_{k}"])
    for k in ("bq", "bk", "bv", "bo"):
        w[f"L2_attn_{k}"] = a(inputs[f"L2_attn_{k}"]).reshape(DM, 1)
    for k in ("w1_w", "w2_w", "h1_w", "h2_w", "w3_w"):
        w[f"L2_ffn_{k}"] = t(inputs[f"L2_ffn_{k}"])
    for k in ("w1_s", "w2_s", "h1_s", "h2_s", "w3_s"):
        w[f"L2_ffn_{k}"] = a(inputs[f"L2_ffn_{k}"]).reshape(-1, 1)
    for k in ("Wq", "Wk", "Wv", "Wo"):
        w[f"L3_attn_{k}"] = t(inputs[f"L3_attn_{k}"])
    for k in ("cn_w1", "cn_w2", "cn_w3", "cn_w4", "gc_w1", "gc_w2", "gc_w3"):
        w[f"L3_ffn_{k}"] = a(inputs[f"L3_ffn_{k}"])
    for k in ("cn_b1", "cn_b2", "cn_b3", "cn_b4", "gc_b1", "gc_b2", "gc_b3"):
        w[f"L3_ffn_{k}"] = a(inputs[f"L3_ffn_{k}"]).reshape(-1, 1)
    w["L3_ffn_adaptive_range"] = a(inputs["L3_ffn_adaptive_range"]).reshape(1, 1)
    w.update(host_constants())
    return w


# version 03
# speedup vs baseline: 1.0019x; 1.0019x over previous
import numpy as np
from contextlib import ExitStack

import concourse.bass as bass
import concourse.mybir as mybir
from concourse.bass_utils import run_bass_kernel_spmd

F32 = mybir.dt.float32
BF16 = mybir.dt.bfloat16
I32 = mybir.dt.int32
AF = mybir.ActivationFunctionType
ALU = mybir.AluOpType
AX = mybir.AxisListType.X

DM = 1024
B = 2
T = 2048
TOK = B * T
NDT = DM // 128
FC = 512
NFC = TOK // FC
LN_EPS = 1e-5
NH = 16
HD = 64



class Prog:
    def __init__(self, nc, stack):
        self.nc = nc
        self.stack = stack
        self.blk = stack.enter_context(nc.Block())
        self._sems = {}
        self.dguard = {}

    def sem(self, name):
        if name not in self._sems:
            h = self.stack.enter_context(self.nc.semaphore(name))
            self._sems[name] = [h, 0]
        return self._sems[name]

    def inc(self, instr, name, k=1):
        s = self.sem(name)
        instr.then_inc(s[0], k)
        s[1] += k
        return s[1]

    def val(self, name):
        return self.sem(name)[1]

    def wait(self, eng, name, v=None):
        s = self.sem(name)
        eng.wait_ge(s[0], s[1] if v is None else v)

    def fence(self, e, ins, ch):
        v = self.inc(ins, ch, 1)
        self.wait(e, ch, v)

    def sy(self, f):
        self.blk.sync(f)

    def ve(self, f):
        self.blk.vector(f)

    def sc(self, f):
        self.blk.scalar(f)

    def te(self, f):
        self.blk.tensor(f)

    def gp(self, f):
        self.blk.gpsimd(f)

    def barrier(self):
        dchs = [k for k in self._sems if k.startswith("d")]

        def drain(e):
            for ch in dchs:
                self.wait(e, ch)
        self.sy(drain)
        target = self.val("bar") + 5
        for emit, nm in ((self.sy, "sy"), (self.ve, "ve"), (self.sc, "sc"),
                         (self.te, "te"), (self.gp, "gp")):
            def f(e, nm=nm, target=target):
                ins = e.engine_nop() if nm in ("ve", "gp") else e.nop()
                self.inc(ins, "bar", 1)
                self.wait(e, "bar", target)
            emit(f)

    def sbuf(self, st, name, shape, dtype):
        return st.enter_context(self.nc.sbuf_tensor(name, list(shape), dtype))

    def psum(self, st, name, shape, dtype=F32):
        return st.enter_context(self.nc.psum_tensor(name, list(shape), dtype))


_uid = [0]


def uq(base):
    _uid[0] += 1
    return f"{base}{_uid[0]}"



def host_constants():
    c = {}
    c["c_id"] = np.eye(128, dtype=np.float32)
    c["c_ones_col"] = np.ones((128, 1), dtype=np.float32)
    c["c_i1024_col"] = np.full((128, 1), 1.0 / 1024.0, np.float32)
    c["c_eps_col"] = np.full((128, 1), 1e-5, np.float32)
    c["c_ones_row"] = np.ones((1, 128), dtype=np.float32)
    for r in range(4):
        kp = np.arange(128)[:, None]
        qf = np.arange(FC)[None, :]
        c[f"c_dmask{r}"] = (r * 128 + kp <= qf).astype(np.float32)
    c["c_iota_s"] = np.arange(128, dtype=np.float32)[None, :]
    c["c_iota_sm127"] = (np.arange(128, dtype=np.float32) - 127.0)[None, :]
    c["c_iota_t129"] = np.arange(129, dtype=np.float32)[None, :]
    s_ = np.arange(128)[:, None]
    t_ = np.arange(128)[None, :]
    ut = (t_ >= s_).astype(np.float32)
    c["c_utri2"] = np.concatenate([ut, ut], axis=1)
    for dt in range(8):
        hs = np.zeros((128, 16), np.float32)
        for k in range(128):
            hs[k, 2 * dt + k // 64] = 1.0
        c[f"c_hsel{dt}"] = hs
        he = np.zeros((16, 128), np.float32)
        for m in range(128):
            he[2 * dt + m // 64, m] = 1.0
        c[f"c_hexp{dt}"] = he
    return c


def emit_const_load(p, st, aps):
    C = {}
    f32_names = list(aps.keys())
    tensors = {}
    for nm in f32_names:
        shp = aps[nm].shape
        tensors[nm] = p.sbuf(st, "cs_" + nm, list(shp), F32)
        tensors[nm + "_bf"] = p.sbuf(st, "cb_" + nm, list(shp), BF16)

    def syf(e):
        for nm in f32_names:
            ins = e.dma_start(out=tensors[nm][:], in_=aps[nm])
            p.inc(ins, "dw", 16)
    p.sy(syf)
    dt_ = p.val("dw")

    def vf(e):
        p.wait(e, "dw", dt_)
        for nm in f32_names:
            ins = e.tensor_copy(out=tensors[nm + "_bf"][:], in_=tensors[nm][:])
        p.inc(ins, "vet", 1)
    p.ve(vf)
    for nm in f32_names:
        C[nm] = tensors[nm]
        C[nm + "_bf"] = tensors[nm + "_bf"]
    return C



def load_cols(p, st, hbm_ap, name, n=DM, dma="dw"):
    nt = max(1, (n + 127) // 128)
    tcol = p.sbuf(st, name, [128, nt], F32)

    def syf(e):
        p.wait(e, dma)
        for dt in range(nt):
            rows = min(128, n - dt * 128)
            ins = e.dma_start(out=tcol[0:rows, dt:dt + 1],
                              in_=hbm_ap[dt * 128:dt * 128 + rows, 0:1])
            p.inc(ins, dma, 16)
    p.sy(syf)
    tick = p.val(dma)
    return tcol, (lambda dt: tcol[:, dt:dt + 1]), (dma, tick)


def load_weight_bf16(p, st, W_ap, K, name, m0, mw, dtype_in=F32, scale=None,
                     cast_eng="gp", ring=2):
    nk = K // 128
    stage = p.sbuf(st, uq(name + "s"), [128, ring * mw], dtype_in)
    wbf = p.sbuf(st, uq(name + "b"), [128, nk * mw], BF16)
    emit, ch = {"gp": (p.gp, "gpt"), "ve": (p.ve, "vet"), "sc": (p.sc, "sct")}[cast_eng]
    slot_cast = [None] * ring
    for k in range(nk):
        sl = k % ring
        dch = "dwa" if sl == 0 else "dwb"

        guard = slot_cast[sl]
        if guard is not None:
            guard = (ch, guard)
        else:
            guard = p.dguard.get(dch)

        def syf(e, k=k, sl=sl, guard=guard, dch=dch):
            if guard is not None:
                p.wait(e, guard[0], guard[1])
            ins = e.dma_start(out=stage[:, sl * mw:(sl + 1) * mw],
                              in_=W_ap[k * 128:(k + 1) * 128, m0:m0 + mw])
            p.inc(ins, dch, 16)
        p.sy(syf)
        dtick = p.val(dch)

        def cf(e, k=k, sl=sl, dtick=dtick, dch=dch):
            p.wait(e, dch, dtick)
            src_ap = stage[:, sl * mw:(sl + 1) * mw]
            dst_ap = wbf[:, k * mw:(k + 1) * mw]
            if cast_eng == "sc":
                ins = e.activation(out=dst_ap, in_=src_ap, func=AF.Copy,
                                   scale=1.0 if scale is None else scale)
            elif scale is None:
                ins = e.tensor_copy(out=dst_ap, in_=src_ap)
            else:
                ins = e.tensor_scalar_mul(out=dst_ap, in0=src_ap, scalar1=scale)
            p.inc(ins, ch, 1)
        emit(cf)
        slot_cast[sl] = p.val(ch)
    tick = p.val(ch)
    p.dguard["dwa"] = (ch, tick)
    p.dguard["dwb"] = (ch, tick)
    return (lambda k: wbf[:, k * mw:(k + 1) * mw]), (ch, tick)


def mm_stream(p, st, *, name, lhsT_tile, nK, nM, lhs_ready, rhs_hbm, out_cb,
              fw=FC, nF=NFC, mw=128, mh=128):
    rbuf = p.sbuf(st, uq(name + "r"), [128, 2 * nK * fw], BF16)
    psA = p.psum(st, uq(name + "pA"), [128, fw])
    psB = p.psum(st, uq(name + "pB"), [128, fw])
    banks = [psA, psB]
    bank_war = [None, None]
    slot_war = [None, None]
    rhs_tick = [None, None]

    def prefetch(f, slot):
        dch = "dra" if slot == 0 else "drb"
        guard = slot_war[slot] or p.dguard.get(dch)

        def syf(e, f=f, slot=slot, dch=dch, guard=guard):
            if guard is not None:
                p.wait(e, guard[0], guard[1])
            for k in range(nK):
                ins = e.dma_start(
                    out=rbuf[:, (slot * nK + k) * fw:(slot * nK + k + 1) * fw],
                    in_=rhs_hbm(k, f))
                p.inc(ins, dch, 16)
        p.sy(syf)
        rhs_tick[slot] = (dch, p.val(dch))

    prefetch(0, 0)
    bi = 0
    for f in range(nF):
        slot = f & 1
        if f + 1 < nF:
            prefetch(f + 1, 1 - slot)
        for m in range(nM):
            ps = banks[bi]
            war = bank_war[bi]

            def tf(e, f=f, m=m, ps=ps, war=war, slot=slot, rt=rhs_tick[slot]):
                if lhs_ready is not None:
                    p.wait(e, lhs_ready[0], lhs_ready[1])
                p.wait(e, rt[0], rt[1])
                if war is not None:
                    p.wait(e, war[0], war[1])
                for k in range(nK):
                    ins = e.matmul(
                        out=ps[0:mh, 0:fw], lhsT=lhsT_tile(k, m),
                        rhs=rbuf[:, (slot * nK + k) * fw:(slot * nK + k + 1) * fw],
                        start=(k == 0), stop=(k == nK - 1))
                p.inc(ins, "pet", 1)
            p.te(tf)
            bank_war[bi] = out_cb(m, f, ps)
            bi ^= 1
        slot_war[slot] = ("pet", p.val("pet"))
    p.dguard["dra"] = ("pet", p.val("pet"))
    p.dguard["drb"] = ("pet", p.val("pet"))


def evac_to_hbm(p, st, *, name, out_hbm, out_dtype=BF16, bias_col=None,
                scale_col=None, func=None, add_hbm=None, mh=128, extra_ready=None):
    stage = p.sbuf(st, uq(name + "e"), [128, 2 * FC], out_dtype)
    addb = p.sbuf(st, uq(name + "a"), [128, 2 * FC], F32) if add_hbm is not None else None
    slot_ctr = [0]

    def cb(m, f, ps):
        sl = slot_ctr[0] & 1
        slot_ctr[0] += 1
        ssl = slice(sl * FC, (sl + 1) * FC)
        pet_t = p.val("pet")
        if add_hbm is not None:
            def syf(e, m=m, f=f, sl=sl):
                p.wait(e, "vet")
                p.wait(e, "da")
                ins = e.dma_start(out=addb[:, ssl],
                                  in_=add_hbm[m * mh:(m + 1) * mh, f * FC:(f + 1) * FC])
                p.inc(ins, "da", 16)
            p.sy(syf)
            a_t = p.val("da")
        if func is not None:
            def sf(e, m=m, f=f, sl=sl):
                p.wait(e, "pet", pet_t)
                if extra_ready is not None:
                    p.wait(e, extra_ready[0], extra_ready[1])
                p.wait(e, "dx")
                kw = {}
                if bias_col is not None:
                    kw["bias"] = bias_col(m)
                if scale_col is not None:
                    kw["scale"] = scale_col(m)
                ins = e.activation(out=stage[:mh, ssl], in_=ps[:mh, 0:FC],
                                   func=func, **kw)
                p.inc(ins, "sct", 1)
            p.sc(sf)
            done = ("sct", p.val("sct"))
        elif bias_col is not None or scale_col is not None:
            def vfb(e, m=m, f=f, sl=sl):
                p.wait(e, "pet", pet_t)
                if extra_ready is not None:
                    p.wait(e, extra_ready[0], extra_ready[1])
                p.wait(e, "dx")
                if bias_col is not None and scale_col is not None:
                    ins = e.tensor_scalar(out=stage[:mh, ssl], in0=ps[:mh, 0:FC],
                                          scalar1=scale_col(m), scalar2=bias_col(m),
                                          op0=ALU.mult, op1=ALU.add)
                elif bias_col is not None:
                    ins = e.tensor_scalar_add(out=stage[:mh, ssl], in0=ps[:mh, 0:FC],
                                              scalar1=bias_col(m))
                else:
                    ins = e.tensor_scalar_mul(out=stage[:mh, ssl], in0=ps[:mh, 0:FC],
                                              scalar1=scale_col(m))
                p.inc(ins, "vet", 1)
            p.ve(vfb)
            done = ("vet", p.val("vet"))
        elif add_hbm is not None:
            def vf(e, m=m, f=f, sl=sl):
                p.wait(e, "pet", pet_t)
                p.wait(e, "da", a_t)
                p.wait(e, "dx")
                ins = e.tensor_tensor(out=stage[:mh, ssl], in0=ps[:mh, 0:FC],
                                      in1=addb[:mh, ssl], op=ALU.add)
                p.inc(ins, "vet", 1)
            p.ve(vf)
            done = ("vet", p.val("vet"))
        else:
            def vf(e, m=m, f=f, sl=sl):
                p.wait(e, "pet", pet_t)
                p.wait(e, "dx")
                ins = e.tensor_copy(out=stage[:mh, ssl], in_=ps[:mh, 0:FC])
                p.inc(ins, "vet", 1)
            p.ve(vf)
            done = ("vet", p.val("vet"))

        def syf2(e, m=m, f=f, sl=sl):
            p.wait(e, done[0], done[1])
            p.wait(e, "dx")
            ins = e.dma_start(out=out_hbm[m * mh:(m + 1) * mh, f * FC:(f + 1) * FC],
                              in_=stage[:mh, ssl])
            p.inc(ins, "dx", 16)
        p.sy(syf2)
        return done
    return cb



def emit_ln(p, x_hbm, xn_f32_hbm, xn_bf_hbm, C, name, gamma_col=None,
            beta_col=None, src_rdy=None):
    with ExitStack() as ls:
        XT = p.sbuf(ls, name + "xt", [128, NDT * FC], F32)
        SQ = p.sbuf(ls, name + "sq", [128, NDT * FC], BF16)
        XB = p.sbuf(ls, name + "xb", [128, NDT * FC], BF16)
        XN = p.sbuf(ls, name + "xn", [128, NDT * FC], F32)
        XNB = p.sbuf(ls, name + "xnb", [128, NDT * FC], BF16)
        ROWS = p.sbuf(ls, name + "rw", [1, 6 * FC], F32)
        ps_s = p.psum(ls, name + "ps", [1, FC])
        ps_q = p.psum(ls, name + "pq", [1, FC])
        ps_r = p.psum(ls, name + "pr", [128, FC])
        ps_m = p.psum(ls, name + "pm", [128, FC])
        mean, m2, var, sqv, rstd, mrs = (ROWS[:, i * FC:(i + 1) * FC] for i in range(6))

        for j in range(NFC):
            def syf(e, j=j):
                if src_rdy is not None:
                    p.wait(e, src_rdy[0], src_rdy[1])
                p.wait(e, "dx")
                if j >= 1:
                    p.wait(e, "vet")
                for dt in range(NDT):
                    ins = e.dma_start(out=XT[:, dt * FC:(dt + 1) * FC],
                                      in_=x_hbm[dt * 128:(dt + 1) * 128,
                                                j * FC:(j + 1) * FC])
                    p.inc(ins, "dx", 16)
            p.sy(syf)
            dtick = p.val("dx")

            def vf(e, j=j):
                p.wait(e, "dx", dtick)
                if j >= 1:
                    p.wait(e, "pet")
                ins = e.tensor_copy(out=XB[:], in_=XT[:])
                p.inc(ins, "vet", 1)
            p.ve(vf)
            xb_t = p.val("vet")

            def sf(e, j=j):
                p.wait(e, "dx", dtick)
                if j >= 1:
                    p.wait(e, "pet")
                ins = e.activation(out=SQ[:], in_=XT[:], func=AF.Square)
                p.inc(ins, "sct", 1)
            p.sc(sf)
            sq_t = p.val("sct")

            def tf(e, j=j):
                p.wait(e, "vet", xb_t)
                p.wait(e, "sct", sq_t)
                for dt in range(NDT):
                    e.matmul(out=ps_s[0:1, :], lhsT=C["c_i1024_col_bf"][:],
                             rhs=XB[:, dt * FC:(dt + 1) * FC], start=(dt == 0),
                             stop=(dt == NDT - 1))
                for dt in range(NDT):
                    ins = e.matmul(out=ps_q[0:1, :], lhsT=C["c_i1024_col_bf"][:],
                                   rhs=SQ[:, dt * FC:(dt + 1) * FC], start=(dt == 0),
                                   stop=(dt == NDT - 1))
                p.inc(ins, "pet", 1)
            p.te(tf)
            st_t = p.val("pet")

            def vf2(e, j=j):
                p.wait(e, "pet", st_t)
                p.fence(e, e.tensor_copy(out=mean, in_=ps_s[0:1, :]), "vet")
                p.fence(e, e.tensor_tensor(out=m2, in0=mean, in1=mean,
                                           op=ALU.mult), "vet")
                ins = e.tensor_tensor(out=var, in0=ps_q[0:1, :], in1=m2, op=ALU.subtract)
                p.inc(ins, "vet", 1)
            p.ve(vf2)
            v_t = p.val("vet")

            def sf2(e, j=j):
                p.wait(e, "vet", v_t)
                ins = e.activation(out=sqv, in_=var, func=AF.Sqrt, bias=C["c_eps_col"][0:1, 0:1])
                p.inc(ins, "sct", 1)
            p.sc(sf2)
            s_t = p.val("sct")

            def vf3(e, j=j):
                p.wait(e, "sct", s_t)
                p.fence(e, e.reciprocal(out=rstd, in_=sqv), "vet")
                ins = e.tensor_tensor(out=mrs, in0=mean, in1=rstd, op=ALU.mult)
                p.inc(ins, "vet", 1)
            p.ve(vf3)
            r_t = p.val("vet")

            def tf2(e, j=j):
                p.wait(e, "vet", r_t)
                e.matmul(out=ps_r[:, :], lhsT=C["c_ones_row"][:], rhs=rstd,
                         start=True, stop=True)
                ins = e.matmul(out=ps_m[:, :], lhsT=C["c_ones_row"][:], rhs=mrs,
                               start=True, stop=True)
                p.inc(ins, "pet", 1)
            p.te(tf2)
            bc_t = p.val("pet")

            def vf4(e, j=j):
                p.wait(e, "pet", bc_t)
                if j >= 1:
                    p.wait(e, "dx")
                for dt in range(NDT):
                    sl = slice(dt * FC, (dt + 1) * FC)
                    p.fence(e, e.tensor_tensor(out=XN[:, sl], in0=XT[:, sl],
                                               in1=ps_r[:, :], op=ALU.mult), "vet")
                    ins = e.tensor_tensor(out=XN[:, sl], in0=XN[:, sl], in1=ps_m[:, :],
                                          op=ALU.subtract)
                p.inc(ins, "vet", 1)
            p.ve(vf4)
            xn_t = ("vet", p.val("vet"))

            if gamma_col is not None:
                def vf4b(e, j=j, xn_t=xn_t):
                    p.wait(e, xn_t[0], xn_t[1])
                    for dt in range(NDT):
                        sl = slice(dt * FC, (dt + 1) * FC)
                        p.fence(e, e.tensor_scalar(out=XN[:, sl], in0=XN[:, sl],
                                                   scalar1=gamma_col(dt),
                                                   scalar2=beta_col(dt),
                                                   op0=ALU.mult, op1=ALU.add), "vet")
                    ins = e.engine_nop()
                    p.inc(ins, "vet", 1)
                p.ve(vf4b)
                xn_t = ("vet", p.val("vet"))

            def vf5(e, j=j, xn_t=xn_t):
                p.wait(e, xn_t[0], xn_t[1])
                ins = e.tensor_copy(out=XNB[:], in_=XN[:])
                p.inc(ins, "vet", 1)
            p.ve(vf5)
            cb_t = p.val("vet")

            def syf2(e, j=j, xn_t=xn_t):
                p.wait(e, xn_t[0], xn_t[1])
                p.wait(e, "vet", cb_t)
                p.wait(e, "dx")
                if xn_f32_hbm is not None:
                    for dt in range(NDT):
                        ins = e.dma_start(
                            out=xn_f32_hbm[dt * 128:(dt + 1) * 128, j * FC:(j + 1) * FC],
                            in_=XN[:, dt * FC:(dt + 1) * FC])
                        p.inc(ins, "dx", 16)
                for dt in range(NDT):
                    ins = e.dma_start(
                        out=xn_bf_hbm[dt * 128:(dt + 1) * 128, j * FC:(j + 1) * FC],
                        in_=XNB[:, dt * FC:(dt + 1) * FC])
                    p.inc(ins, "dx", 16)
            p.sy(syf2)



def emit_residual_add(p, x_in, a_hbm, b_hbm, x_out):
    CH = 2048
    with ExitStack() as ls:
        X0 = p.sbuf(ls, uq("raX"), [128, CH], F32)
        A0 = p.sbuf(ls, uq("raA"), [128, CH], F32)
        B0 = p.sbuf(ls, uq("raB"), [128, CH], F32)
        O0 = p.sbuf(ls, uq("raO"), [128, CH], F32)
        for dt in range(NDT):
            for jj in range(TOK // CH):
                sl_r = slice(dt * 128, (dt + 1) * 128)
                sl_c = slice(jj * CH, (jj + 1) * CH)

                def syf(e, sl_r=sl_r, sl_c=sl_c):
                    p.wait(e, "vet")
                    p.wait(e, "dx")
                    ins = e.dma_start(out=X0[:], in_=x_in[sl_r, sl_c])
                    p.inc(ins, "dx", 16)
                    ins = e.dma_start(out=A0[:], in_=a_hbm[sl_r, sl_c])
                    p.inc(ins, "dx", 16)
                    ins = e.dma_start(out=B0[:], in_=b_hbm[sl_r, sl_c])
                    p.inc(ins, "dx", 16)
                p.sy(syf)
                d_t = p.val("dx")

                def vf(e, sl_r=sl_r, sl_c=sl_c):
                    p.wait(e, "dx", d_t)
                    p.fence(e, e.tensor_tensor(out=O0[:], in0=X0[:], in1=A0[:],
                                                 op=ALU.add), "vet")
                    ins = e.tensor_tensor(out=O0[:], in0=O0[:], in1=B0[:], op=ALU.add)
                    p.inc(ins, "vet", 1)
                p.ve(vf)
                v_t = p.val("vet")

                def syf2(e, sl_r=sl_r, sl_c=sl_c, v_t=v_t):
                    p.wait(e, "vet", v_t)
                    p.wait(e, "dx")
                    ins = e.dma_start(out=x_out[sl_r, sl_c], in_=O0[:])
                    p.inc(ins, "dx", 16)
                p.sy(syf2)



def emit_mha(p, W, C, scr, bo_ap=None):
    for wn, dst in (("Wq", "q_bf"), ("Wk", "k_bf"), ("Wv", "v_bf")):
        with ExitStack() as ls:
            lt, lrdy = load_weight_bf16(p, ls, W[wn], DM, "w" + wn, 0, DM)
            cb = evac_to_hbm(p, ls, name="e" + wn, out_hbm=scr[dst], out_dtype=BF16)
            mm_stream(p, ls, name="m" + wn,
                      lhsT_tile=lambda k, m, lt=lt: lt(k)[:, m * 128:(m + 1) * 128],
                      nK=NDT, nM=NDT, lhs_ready=lrdy,
                      rhs_hbm=lambda k, f: scr["xn_bf"][k * 128:(k + 1) * 128,
                                                        f * FC:(f + 1) * FC],
                      out_cb=cb)
        p.barrier()

    NKT = T // 128
    with ExitStack() as ls:
        qh = p.sbuf(ls, uq("qh"), [64, T], BF16)
        kh = p.sbuf(ls, uq("kh"), [64, T], BF16)
        vh = p.sbuf(ls, uq("vh"), [64, T], BF16)
        vtok = p.sbuf(ls, uq("vtok"), [128, NKT * 64], BF16)
        sqb = p.sbuf(ls, uq("sqb"), [64, T], BF16)
        rowb = p.sbuf(ls, uq("rowb"), [1, 16], F32)
        bcol = p.sbuf(ls, uq("bcol"), [128, 1], F32)
        expT = p.sbuf(ls, uq("expT"), [128, NKT * FC], BF16)
        etmp = p.sbuf(ls, uq("etmp"), [128, FC], BF16)
        sumr = p.sbuf(ls, uq("sumr"), [1, 2 * FC], F32)
        rrow = p.sbuf(ls, uq("rrow"), [1, FC], BF16)
        rbc = p.sbuf(ls, uq("rbc"), [64, FC], F32)
        ctxs = p.sbuf(ls, uq("ctxs"), [64, 2 * FC], BF16)
        ps_sc = [p.psum(ls, uq("psc"), [128, FC]) for _ in range(2)]
        ps_sum = p.psum(ls, uq("psum"), [1, FC])
        ps_ctx = p.psum(ls, uq("pctx"), [64, FC])
        ps_aux = p.psum(ls, uq("paux"), [128, FC])
        ps_vt = p.psum(ls, uq("pvt"), [128, 64], BF16)

        for b in range(B):
            for h in range(NH):
                r0, c0 = h * HD, b * T
                consumed = ("pet", p.val("pet"))

                def syf(e, r0=r0, c0=c0, consumed=consumed):
                    p.wait(e, consumed[0], consumed[1])
                    p.wait(e, "sct")
                    ins = e.dma_start(out=qh[:], in_=scr["q_bf"][r0:r0 + 64, c0:c0 + T])
                    p.inc(ins, "dr", 16)
                    ins = e.dma_start(out=kh[:], in_=scr["k_bf"][r0:r0 + 64, c0:c0 + T])
                    p.inc(ins, "dr", 16)
                    ins = e.dma_start(out=vh[:], in_=scr["v_bf"][r0:r0 + 64, c0:c0 + T])
                    p.inc(ins, "dr", 16)
                p.sy(syf)
                ld_t = p.val("dr")

                ev_t = p.val("vet")
                for kt in range(NKT):
                    def tf(e, kt=kt, ev_t=ev_t):
                        p.wait(e, "dr", ld_t)
                        p.wait(e, "vet", ev_t)
                        ins = e.transpose(out=ps_vt[:, 0:64],
                                          in_=vh[:, kt * 128:(kt + 1) * 128],
                                          identity=C["c_id_bf"][0:64, 0:64])
                        p.inc(ins, "pet", 1)
                    p.te(tf)
                    tr_t = p.val("pet")

                    def vf(e, kt=kt, tr_t=tr_t):
                        p.wait(e, "pet", tr_t)
                        ins = e.tensor_copy(out=vtok[:, kt * 64:(kt + 1) * 64],
                                            in_=ps_vt[:, 0:64])
                        p.inc(ins, "vet", 1)
                    p.ve(vf)
                    ev_t = p.val("vet")
                vtok_t = p.val("vet")

                def sf(e):
                    p.wait(e, "dr", ld_t)
                    p.wait(e, "pet")
                    ins = e.activation(out=sqb[:], in_=qh[:], func=AF.Square)
                    p.inc(ins, "sct", 1)
                p.sc(sf)
                q2_t = p.val("sct")
                for src, base, rdy in ((0, 0, q2_t), (1, 4, None)):
                    if src == 1:
                        def sf2(e):
                            p.wait(e, "dr", ld_t)
                            p.wait(e, "pet")
                            ins = e.activation(out=sqb[:], in_=kh[:], func=AF.Square)
                            p.inc(ins, "sct", 1)
                        p.sc(sf2)
                        rdy = p.val("sct")
                    for ft in range(4):
                        def tf2(e, ft=ft, rdy=rdy):
                            p.wait(e, "sct", rdy)
                            p.wait(e, "vet")
                            ins = e.matmul(out=ps_sum[0:1, :],
                                           lhsT=C["c_ones_col_bf"][0:64, :],
                                           rhs=sqb[:, ft * FC:(ft + 1) * FC],
                                           start=True, stop=True)
                            p.inc(ins, "pet", 1)
                        p.te(tf2)
                        m_t = p.val("pet")

                        def vf2(e, ft=ft, base=base, m_t=m_t):
                            p.wait(e, "pet", m_t)
                            ins = e.reduce_max(out=rowb[:, base + ft:base + ft + 1],
                                               in_=ps_sum[0:1, :], axis=AX)
                            p.inc(ins, "vet", 1)
                        p.ve(vf2)
                nrm_t = p.val("vet")

                def vf3(e, nrm_t=nrm_t):
                    p.wait(e, "vet", nrm_t)
                    e.reduce_max(out=rowb[:, 8:9], in_=rowb[:, 0:4], axis=AX)
                    ins = e.reduce_max(out=rowb[:, 9:10], in_=rowb[:, 4:8], axis=AX)
                    p.inc(ins, "vet", 1)
                p.ve(vf3)
                mx_t = p.val("vet")

                def sf3(e, mx_t=mx_t):
                    p.wait(e, "vet", mx_t)
                    ins = e.activation(out=rowb[:, 10:12], in_=rowb[:, 8:10],
                                       func=AF.Sqrt)
                    p.inc(ins, "sct", 1)
                p.sc(sf3)
                sq2_t = p.val("sct")

                def vf4(e, sq2_t=sq2_t):
                    p.wait(e, "sct", sq2_t)
                    p.fence(e, e.tensor_tensor(out=rowb[:, 12:13], in0=rowb[:, 10:11],
                                                 in1=rowb[:, 11:12], op=ALU.mult), "vet")
                    ins = e.tensor_scalar_mul(out=rowb[:, 13:14], in0=rowb[:, 12:13],
                                              scalar1=-0.125)
                    p.inc(ins, "vet", 1)
                p.ve(vf4)
                bp_t = p.val("vet")

                def tf3(e, bp_t=bp_t):
                    p.wait(e, "vet", bp_t)
                    p.wait(e, "sct")
                    ins = e.matmul(out=ps_aux[:, 0:1], lhsT=C["c_ones_row"][:],
                                   rhs=rowb[:, 13:14], start=True, stop=True)
                    p.inc(ins, "pet", 1)
                p.te(tf3)
                bcm_t = p.val("pet")

                def vf5(e, bcm_t=bcm_t):
                    p.wait(e, "pet", bcm_t)
                    ins = e.tensor_copy(out=bcol[:], in_=ps_aux[:, 0:1])
                    p.inc(ins, "vet", 1)
                p.ve(vf5)
                bcol_t = p.val("vet")

                for j in range(4):
                    nkt = 4 * (j + 1)
                    exp_done = []
                    for kt in range(nkt):
                        ps = ps_sc[kt & 1]

                        def tf4(e, kt=kt, j=j, ps=ps):
                            p.wait(e, "vet", bcol_t)
                            p.wait(e, "sct")
                            ins = e.matmul(out=ps[:, :],
                                           lhsT=kh[:, kt * 128:(kt + 1) * 128],
                                           rhs=qh[:, j * FC:(j + 1) * FC],
                                           start=True, stop=True)
                            p.inc(ins, "pet", 1)
                        p.te(tf4)
                        sc_t = p.val("pet")
                        diag = kt >= 4 * j

                        def sf4(e, kt=kt, ps=ps, sc_t=sc_t, diag=diag):
                            p.wait(e, "pet", sc_t)
                            if diag:
                                p.wait(e, "vet")
                            tgt = etmp[:, :] if diag else expT[:, kt * FC:(kt + 1) * FC]
                            ins = e.activation(out=tgt, in_=ps[:, :], func=AF.Exp,
                                               scale=0.125, bias=bcol[:, 0:1])
                            p.inc(ins, "sct", 1)
                        p.sc(sf4)
                        e_t = p.val("sct")
                        if diag:
                            def vf6(e, kt=kt, j=j, e_t=e_t):
                                p.wait(e, "sct", e_t)
                                ins = e.tensor_tensor(
                                    out=expT[:, kt * FC:(kt + 1) * FC], in0=etmp[:, :],
                                    in1=C[f"c_dmask{kt - 4 * j}_bf"][:], op=ALU.mult)
                                p.inc(ins, "vet", 1)
                            p.ve(vf6)
                    exp_s, exp_v = p.val("sct"), p.val("vet")

                    def tf5(e, nkt=nkt, exp_s=exp_s, exp_v=exp_v):
                        p.wait(e, "sct", exp_s)
                        p.wait(e, "vet", exp_v)
                        for kt in range(nkt):
                            ins = e.matmul(out=ps_sum[0:1, :],
                                           lhsT=C["c_ones_col_bf"][:],
                                           rhs=expT[:, kt * FC:(kt + 1) * FC],
                                           start=(kt == 0), stop=(kt == nkt - 1))
                        p.inc(ins, "pet", 1)
                    p.te(tf5)
                    su_t = p.val("pet")
                    sl = j & 1

                    def vf7(e, su_t=su_t, sl=sl):
                        p.wait(e, "pet", su_t)
                        p.fence(e, e.tensor_copy(out=sumr[:, 0:FC],
                                                   in_=ps_sum[0:1, :]), "vet")
                        p.fence(e, e.reciprocal(out=sumr[:, FC:2 * FC],
                                                in_=sumr[:, 0:FC]), "vet")
                        ins = e.tensor_copy(out=rrow[:, :], in_=sumr[:, FC:2 * FC])
                        p.inc(ins, "vet", 1)
                    p.ve(vf7)
                    rc_t = p.val("vet")

                    def tf6(e, rc_t=rc_t):
                        p.wait(e, "vet", rc_t)
                        ins = e.matmul(out=ps_aux[0:64, :],
                                       lhsT=C["c_ones_row_bf"][:, 0:64],
                                       rhs=rrow[:, :], start=True, stop=True)
                        p.inc(ins, "pet", 1)
                    p.te(tf6)
                    rb_t = p.val("pet")

                    def vf8(e, rb_t=rb_t):
                        p.wait(e, "pet", rb_t)
                        ins = e.tensor_copy(out=rbc[:, :], in_=ps_aux[0:64, :])
                        p.inc(ins, "vet", 1)
                    p.ve(vf8)
                    rbc_t = p.val("vet")

                    def tf7(e, nkt=nkt, vtok_t=vtok_t, rbc_t=rbc_t):
                        p.wait(e, "vet", rbc_t)
                        for kt in range(nkt):
                            ins = e.matmul(out=ps_ctx[0:64, :],
                                           lhsT=vtok[:, kt * 64:(kt + 1) * 64],
                                           rhs=expT[:, kt * FC:(kt + 1) * FC],
                                           start=(kt == 0), stop=(kt == nkt - 1))
                        p.inc(ins, "pet", 1)
                    p.te(tf7)
                    cx_t = p.val("pet")

                    def vf9(e, cx_t=cx_t, sl=sl):
                        p.wait(e, "pet", cx_t)
                        p.wait(e, "dx")
                        ins = e.tensor_tensor(out=ctxs[:, sl * FC:(sl + 1) * FC],
                                              in0=ps_ctx[0:64, :], in1=rbc[:, :],
                                              op=ALU.mult)
                        p.inc(ins, "vet", 1)
                    p.ve(vf9)
                    cs_t = p.val("vet")

                    def syf2(e, j=j, cs_t=cs_t, sl=sl, r0=r0, c0=c0):
                        p.wait(e, "vet", cs_t)
                        p.wait(e, "dx")
                        ins = e.dma_start(
                            out=scr["ctx_bf"][r0:r0 + 64, c0 + j * FC:c0 + (j + 1) * FC],
                            in_=ctxs[:, sl * FC:(sl + 1) * FC])
                        p.inc(ins, "dx", 16)
                    p.sy(syf2)
    p.barrier()

    with ExitStack() as ls:
        lt, lrdy = load_weight_bf16(p, ls, W["Wo"], DM, "wWo", 0, DM)
        if bo_ap is not None:
            _, bofn, bordy = load_cols(p, ls, bo_ap, uq("bo"))
            cb = evac_to_hbm(p, ls, name="eWo", out_hbm=scr["acc_a"], out_dtype=F32,
                             bias_col=bofn, extra_ready=bordy)
        else:
            cb = evac_to_hbm(p, ls, name="eWo", out_hbm=scr["acc_a"], out_dtype=F32)
        mm_stream(p, ls, name="mWo",
                  lhsT_tile=lambda k, m, lt=lt: lt(k)[:, m * 128:(m + 1) * 128],
                  nK=NDT, nM=NDT, lhs_ready=lrdy,
                  rhs_hbm=lambda k, f: scr["ctx_bf"][k * 128:(k + 1) * 128,
                                                     f * FC:(f + 1) * FC],
                  out_cb=cb)
    p.barrier()



def emit_ffn1(p, W, C, scr):
    H = 3072
    NMH = H // 128
    for wn, dst, act in (("w_gate", "g_bf", True), ("w_up", "u_bf", False)):
        for half in range(2):
            with ExitStack() as ls:
                mw = H // 2
                lt, lrdy = load_weight_bf16(p, ls, W[wn], DM, "wf" + wn, half * mw,
                                            mw, dtype_in=I32, scale=1.0 / 127.0,
                                            cast_eng="sc")
                cb = evac_to_hbm(p, ls, name="ef" + wn + str(half),
                                 out_hbm=scr[dst], out_dtype=BF16)
                mm_stream(p, ls, name="mf" + wn + str(half),
                          lhsT_tile=lambda k, m, lt=lt: lt(k)[:, m * 128:(m + 1) * 128],
                          nK=NDT, nM=NMH // 2, lhs_ready=lrdy,
                          rhs_hbm=lambda k, f: scr["xn_bf"][k * 128:(k + 1) * 128,
                                                           f * FC:(f + 1) * FC],
                          out_cb=lambda m, f, ps, cb=cb, half=half: cb(m + half * (NMH // 2), f, ps))
            p.barrier()
    with ExitStack() as ls:
        G = p.sbuf(ls, uq("fg"), [128, 2048], BF16)
        U = p.sbuf(ls, uq("fu"), [128, 2048], BF16)
        S = p.sbuf(ls, uq("fs"), [128, 2048], BF16)
        Hb = p.sbuf(ls, uq("fh"), [128, 2048], BF16)
        for mt in range(NMH):
            for jj in range(TOK // 2048):
                sl_r = slice(mt * 128, (mt + 1) * 128)
                sl_c = slice(jj * 2048, (jj + 1) * 2048)

                def syf(e, sl_r=sl_r, sl_c=sl_c):
                    p.wait(e, "vet")
                    p.wait(e, "sct")
                    ins = e.dma_start(out=G[:], in_=scr["g_bf"][sl_r, sl_c])
                    p.inc(ins, "dr", 16)
                    ins = e.dma_start(out=U[:], in_=scr["u_bf"][sl_r, sl_c])
                    p.inc(ins, "dr", 16)
                p.sy(syf)
                d_t = p.val("dr")

                def sf(e, d_t=d_t):
                    p.wait(e, "dr", d_t)
                    ins = e.activation(out=S[:], in_=G[:], func=AF.Sigmoid)
                    p.inc(ins, "sct", 1)
                p.sc(sf)
                s_t = p.val("sct")

                def vf(e, s_t=s_t, d_t=d_t):
                    p.wait(e, "sct", s_t)
                    p.wait(e, "dx")
                    p.fence(e, e.tensor_tensor(out=Hb[:], in0=G[:], in1=S[:],
                                                 op=ALU.mult), "vet")
                    ins = e.tensor_tensor(out=Hb[:], in0=Hb[:], in1=U[:], op=ALU.mult)
                    p.inc(ins, "vet", 1)
                p.ve(vf)
                v_t = p.val("vet")

                def syf2(e, sl_r=sl_r, sl_c=sl_c, v_t=v_t):
                    p.wait(e, "vet", v_t)
                    p.wait(e, "dx")
                    ins = e.dma_start(out=scr["h_bf"][sl_r, sl_c], in_=Hb[:])
                    p.inc(ins, "dx", 16)
                p.sy(syf2)
    p.barrier()
    with ExitStack() as ls:
        lt, lrdy = load_weight_bf16(p, ls, W["w_down"], H, "wfd", 0, DM,
                                    dtype_in=I32, scale=1.0 / 127.0, cast_eng="sc")
        cb = evac_to_hbm(p, ls, name="efd", out_hbm=scr["acc_b"], out_dtype=F32)
        mm_stream(p, ls, name="mfd",
                  lhsT_tile=lambda k, m, lt=lt: lt(k)[:, m * 128:(m + 1) * 128],
                  nK=H // 128, nM=NDT, lhs_ready=lrdy,
                  rhs_hbm=lambda k, f: scr["h_bf"][k * 128:(k + 1) * 128,
                                                  f * FC:(f + 1) * FC],
                  out_cb=cb)
    p.barrier()


L1_W = ["L1_attn_Wq", "L1_attn_Wk", "L1_attn_Wv", "L1_attn_Wo", "L1_attn_bo",
        "L1_ffn_w_gate", "L1_ffn_w_up", "L1_ffn_w_down"]
L0_W = ["L0_attn_log_A", "L0_attn_B_param", "L0_attn_C_param", "L0_attn_D_param",
        "L0_attn_log_dt", "L0_attn_ln_gamma", "L0_attn_ln_beta", "L0_attn_W_out",
        "L0_attn_b_out", "L0_ffn_ln_scale", "L0_ffn_ln_bias", "L0_ffn_gate_w",
        "L0_ffn_Wg", "L0_ffn_Wu", "L0_ffn_Wd", "L0_ffn_sWg", "L0_ffn_sWu",
        "L0_ffn_sWd"]
L2_W = ["L2_attn_Wq", "L2_attn_bq", "L2_attn_Wk", "L2_attn_bk", "L2_attn_Wv",
        "L2_attn_bv", "L2_attn_Wo", "L2_attn_bo", "L2_ffn_w1_w", "L2_ffn_w1_s",
        "L2_ffn_w2_w", "L2_ffn_w2_s", "L2_ffn_h1_w", "L2_ffn_h1_s",
        "L2_ffn_h2_w", "L2_ffn_h2_s", "L2_ffn_w3_w", "L2_ffn_w3_s"]
L3_W = ["L3_attn_Wq", "L3_attn_Wk", "L3_attn_Wv", "L3_attn_Wo",
        "L3_ffn_cn_w1", "L3_ffn_cn_b1", "L3_ffn_cn_w2", "L3_ffn_cn_b2",
        "L3_ffn_cn_w3", "L3_ffn_cn_b3", "L3_ffn_cn_w4", "L3_ffn_cn_b4",
        "L3_ffn_gc_w1", "L3_ffn_gc_b1", "L3_ffn_gc_w2", "L3_ffn_gc_b2",
        "L3_ffn_gc_w3", "L3_ffn_gc_b3", "L3_ffn_adaptive_range"]
LAYER_W = {0: L0_W, 1: L1_W, 2: L2_W, 3: L3_W}

WEIGHT_SPECS = {
    "L0_attn_log_A": ([1024, 64], F32), "L0_attn_B_param": ([1024, 64], F32),
    "L0_attn_C_param": ([1024, 64], F32), "L0_attn_D_param": ([1024, 1], F32),
    "L0_attn_log_dt": ([1024, 1], F32), "L0_attn_ln_gamma": ([1024, 1], F32),
    "L0_attn_ln_beta": ([1024, 1], F32), "L0_attn_W_out": ([1024, 1024], F32),
    "L0_attn_b_out": ([1024, 1], F32), "L0_ffn_ln_scale": ([1024, 1], F32),
    "L0_ffn_ln_bias": ([1024, 1], F32), "L0_ffn_gate_w": ([1024, 8], F32),
    "L0_ffn_Wg": ([8, 1024, 2048], F32), "L0_ffn_Wu": ([8, 1024, 2048], F32),
    "L0_ffn_Wd": ([8, 2048, 1024], F32), "L0_ffn_sWg": ([1024, 2048], F32),
    "L0_ffn_sWu": ([1024, 2048], F32), "L0_ffn_sWd": ([2048, 1024], F32),
    "L1_attn_Wq": ([1024, 1024], F32), "L1_attn_Wk": ([1024, 1024], F32),
    "L1_attn_Wv": ([1024, 1024], F32), "L1_attn_Wo": ([1024, 1024], F32),
    "L1_attn_bo": ([1024, 1], F32),
    "L1_ffn_w_gate": ([1024, 3072], I32), "L1_ffn_w_up": ([1024, 3072], I32),
    "L1_ffn_w_down": ([3072, 1024], I32),
    "L2_attn_Wq": ([1024, 1024], F32), "L2_attn_bq": ([1024, 1], F32),
    "L2_attn_Wk": ([1024, 1024], F32), "L2_attn_bk": ([1024, 1], F32),
    "L2_attn_Wv": ([1024, 1024], F32), "L2_attn_bv": ([1024, 1], F32),
    "L2_attn_Wo": ([1024, 1024], F32), "L2_attn_bo": ([1024, 1], F32),
    "L2_ffn_w1_w": ([1024, 4096], F32), "L2_ffn_w1_s": ([4096, 1], F32),
    "L2_ffn_w2_w": ([1024, 4096], F32), "L2_ffn_w2_s": ([4096, 1], F32),
    "L2_ffn_h1_w": ([4096, 4096], F32), "L2_ffn_h1_s": ([4096, 1], F32),
    "L2_ffn_h2_w": ([4096, 4096], F32), "L2_ffn_h2_s": ([4096, 1], F32),
    "L2_ffn_w3_w": ([4096, 1024], F32), "L2_ffn_w3_s": ([1024, 1], F32),
    "L3_attn_Wq": ([1024, 1024], F32), "L3_attn_Wk": ([1024, 1024], F32),
    "L3_attn_Wv": ([1024, 1024], F32), "L3_attn_Wo": ([1024, 1024], F32),
    "L3_ffn_cn_w1": ([1024, 1024], F32), "L3_ffn_cn_b1": ([1024, 1], F32),
    "L3_ffn_cn_w2": ([1024, 512], F32), "L3_ffn_cn_b2": ([512, 1], F32),
    "L3_ffn_cn_w3": ([512, 256], F32), "L3_ffn_cn_b3": ([256, 1], F32),
    "L3_ffn_cn_w4": ([256, 1], F32), "L3_ffn_cn_b4": ([1, 1], F32),
    "L3_ffn_gc_w1": ([1024, 512], F32), "L3_ffn_gc_b1": ([512, 1], F32),
    "L3_ffn_gc_w2": ([512, 256], F32), "L3_ffn_gc_b2": ([256, 1], F32),
    "L3_ffn_gc_w3": ([256, 1], F32), "L3_ffn_gc_b3": ([1, 1], F32),
    "L3_ffn_adaptive_range": ([1, 1], F32),
}

SCRATCH_SPECS = {
    "xn_bf": ([DM, TOK], BF16), "xn_f32": ([DM, TOK], F32),
    "q_bf": ([DM, TOK], BF16), "k_bf": ([DM, TOK], BF16),
    "v_bf": ([DM, TOK], BF16), "ctx_bf": ([DM, TOK], BF16),
    "g_bf": ([4096, TOK], BF16), "u_bf": ([4096, TOK], BF16),
    "h_bf": ([4096, TOK], BF16),
    "acc_a": ([DM, TOK], F32), "acc_b": ([DM, TOK], F32),
    "x_mid0": ([DM, TOK], F32), "x_mid1": ([DM, TOK], F32),
    "x_mid2": ([DM, TOK], F32),
    "w_hbm": ([DM, 64], F32), "cb_hbm": ([DM, 64], F32),
    "xg_f32": ([DM, TOK], F32),
}


def emit_layer(p, li, W, C, scr, x_in, x_out):
    if li == 0:
        _emit_layer0(p, W, C, scr, x_in, x_out)
    elif li == 2:
        _emit_layer2(p, W, C, scr, x_in, x_out)
    elif li == 3:
        _emit_layer3(p, W, C, scr, x_in, x_out)
    elif li == 1:
        emit_ln(p, x_in, None, scr["xn_bf"], C, uq("ln"))
        p.barrier()
        emit_mha(p, {k: W[f"L1_attn_{k}"] for k in ("Wq", "Wk", "Wv", "Wo")},
                 C, scr, bo_ap=W["L1_attn_bo"])
        emit_ffn1(p, {k: W[f"L1_ffn_{k}"] for k in ("w_gate", "w_up", "w_down")},
                  C, scr)
        emit_residual_add(p, x_in, scr["acc_a"], scr["acc_b"], x_out)
        p.barrier()
    else:
        raise NotImplementedError(f"layer {li}")


def build_program(layers):
    nc = bass.Bass("TRN2", target_bir_lowering=False, debug=False)
    x_in = nc.dram_tensor("x_in", [DM, TOK], F32, kind="ExternalInput").ap()
    x_out = nc.dram_tensor("x_out", [DM, TOK], F32, kind="ExternalOutput").ap()
    W = {}
    for li in layers:
        for nm in LAYER_W[li]:
            shp, dt_ = WEIGHT_SPECS[nm]
            W[nm] = nc.dram_tensor(nm, shp, dt_, kind="ExternalInput").ap()
    caps = {}
    for nm, arr in host_constants().items():
        caps[nm] = nc.dram_tensor(nm, list(arr.shape), F32, kind="ExternalInput").ap()
    scr = {}
    for nm, (shp, dt_) in SCRATCH_SPECS.items():
        scr[nm] = nc.dram_tensor("scr_" + nm, shp, dt_, kind="Internal").ap()

    with ExitStack() as stack:
        p = Prog(nc, stack)
        with ExitStack() as cls:
            C = emit_const_load(p, stack, caps)
        p.barrier()
        cur = x_in
        for idx, li in enumerate(layers):
            out = x_out if idx == len(layers) - 1 else scr[f"x_mid{idx}"]
            emit_layer(p, li, W, C, scr, cur, out)
            cur = out

        def fin(e):
            for ch in ("dx", "dr", "dw", "da"):
                if ch in p._sems:
                    p.wait(e, ch)
        p.sy(fin)
    return nc


LAUNCH_GROUPS = [[0, 1, 2, 3]]


def kernel(**inputs):
    w = prep_inputs(inputs)
    x = np.ascontiguousarray(np.asarray(inputs["L0_attn_x"], dtype=np.float32))
    shards = [np.ascontiguousarray(x[2 * c:2 * c + 2].transpose(2, 0, 1)
                                   .reshape(DM, TOK)) for c in range(8)]
    cur = shards
    for group in LAUNCH_GROUPS:
        nc = build_program(group)
        names = ["x_in"] + [nm for li in group for nm in LAYER_W[li]] \
            + list(host_constants().keys())
        in_maps = []
        for c in range(8):
            m = {"x_in": cur[c]}
            for nm in names:
                if nm != "x_in":
                    m[nm] = w[nm]
            in_maps.append(m)
        res = run_bass_kernel_spmd(nc, in_maps, core_ids=list(range(8)))
        cur = [np.asarray(res.results[c]["x_out"]) for c in range(8)]
    full = np.empty((16, 2048, 1024), np.float32)
    for c in range(8):
        full[2 * c:2 * c + 2] = cur[c].reshape(DM, B, T).transpose(1, 2, 0)
    return full


def emit_ffn3(p, W, C, scr):
    FB = 512
    with ExitStack() as ls:
        wts = {}
        for nm, K_, M_ in (("cn_w1", DM, DM), ("cn_w2", DM, 512), ("cn_w3", 512, 256),
                           ("cn_w4", 256, 1), ("gc_w1", DM, 512), ("gc_w2", 512, 256),
                           ("gc_w3", 256, 1)):
            wts[nm] = load_weight_bf16(p, ls, W[nm], K_, "w3" + nm, 0, M_)
        cols = {}
        for nm, n in (("cn_b1", DM), ("cn_b2", 512), ("cn_b3", 256), ("cn_b4", 1),
                      ("gc_b1", 512), ("gc_b2", 256), ("gc_b3", 1)):
            cols[nm] = load_cols(p, ls, W[nm], uq("c3" + nm), n=n)
        arr_c = load_cols(p, ls, W["adaptive_range"], uq("c3ar"), n=1)
        wrdy = [wts[nm][1] for nm in wts] + [cols[nm][2] for nm in cols] + [arr_c[2]]

        def tjoin(e):
            seen = {}
            for ch_, tk_ in wrdy:
                seen[ch_] = max(seen.get(ch_, 0), tk_)
            for ch_, tk_ in seen.items():
                p.wait(e, ch_, tk_)
            ins = e.nop()
            p.inc(ins, "pet", 1)
        p.te(tjoin)

        gcx = p.sbuf(ls, uq("gcx"), [128, NDT * B], F32)
        gcb = p.sbuf(ls, uq("gcb"), [128, NDT * B], BF16)
        XT3 = p.sbuf(ls, uq("xt3"), [128, 2048], F32)
        for dt in range(NDT):
            for b in range(B):
                def syf(e, dt=dt, b=b):
                    p.wait(e, "vet")
                    ins = e.dma_start(out=XT3[:],
                                      in_=scr["xn_f32"][dt * 128:(dt + 1) * 128,
                                                        b * T:(b + 1) * T])
                    p.inc(ins, "dx", 16)
                p.sy(syf)
                d_t = p.val("dx")

                def vf(e, dt=dt, b=b, d_t=d_t):
                    p.wait(e, "dx", d_t)
                    p.fence(e, e.reduce_sum(out=gcx[:, dt * B + b:dt * B + b + 1],
                                            in_=XT3[:], axis=AX), "vet")
                    ins = e.tensor_scalar_mul(
                        out=gcx[:, dt * B + b:dt * B + b + 1],
                        in0=gcx[:, dt * B + b:dt * B + b + 1], scalar1=1.0 / T)
                    p.inc(ins, "vet", 1)
                p.ve(vf)

        def vcast(e):
            p.wait(e, "vet")
            ins = e.tensor_copy(out=gcb[:], in_=gcx[:])
            p.inc(ins, "vet", 1)
        p.ve(vcast)
        gc_t = p.val("vet")

        ps_g = p.psum(ls, uq("psg"), [128, 64])
        h1g = p.sbuf(ls, uq("h1g"), [128, 4 * B], BF16)
        h2g = p.sbuf(ls, uq("h2g"), [128, 2 * B], BF16)
        gfr = p.sbuf(ls, uq("gfr"), [1, 8], F32)

        def net_step(nm, bnm, nK_, nM_, rhs_tensor, out_tensor, act, rdy_t):
            lt, _ = wts[nm]
            for mt in range(nM_):
                def tf(e, mt=mt, rdy_t=rdy_t):
                    p.wait(e, "vet", rdy_t)
                    p.wait(e, "sct")
                    for k in range(nK_):
                        ins = e.matmul(out=ps_g[:, mt * B:(mt + 1) * B],
                                       lhsT=lt(k)[:, mt * 128:(mt + 1) * 128],
                                       rhs=rhs_tensor[:, k * B:(k + 1) * B],
                                       start=(k == 0), stop=(k == nK_ - 1))
                    p.inc(ins, "pet", 1)
                p.te(tf)
                m_t = p.val("pet")

                def sf(e, mt=mt, m_t=m_t):
                    p.wait(e, "sct")
                    p.wait(e, "pet", m_t)
                    ins = e.activation(out=out_tensor[:, mt * B:(mt + 1) * B],
                                       in_=ps_g[:, mt * B:(mt + 1) * B], func=act,
                                       bias=cols[bnm][1](mt))
                    p.inc(ins, "sct", 1)
                p.sc(sf)
            return p.val("sct")

        t1 = net_step("gc_w1", "gc_b1", NDT, 4, gcb, h1g, AF.Gelu, gc_t)
        def vx(e, t1=t1):
            p.wait(e, "sct", t1)
            ins = e.engine_nop()
            p.inc(ins, "vet", 1)
        p.ve(vx)
        t1v = p.val("vet")
        t2 = net_step("gc_w2", "gc_b2", 4, 2, h1g, h2g, AF.Gelu, t1v)
        def vx2(e, t2=t2):
            p.wait(e, "sct", t2)
            ins = e.engine_nop()
            p.inc(ins, "vet", 1)
        p.ve(vx2)
        t2v = p.val("vet")
        gfb = p.sbuf(ls, uq("gfb"), [1, B], F32)
        lt3, _ = wts["gc_w3"]

        def tf_gf(e, t2v=t2v):
            p.wait(e, "vet", t2v)
            for k in range(2):
                ins = e.matmul(out=ps_g[0:1, 32:32 + B], lhsT=lt3(k),
                               rhs=h2g[:, k * B:(k + 1) * B],
                               start=(k == 0), stop=(k == 1))
            p.inc(ins, "pet", 1)
        p.te(tf_gf)
        gf_t = p.val("pet")

        def sf_gf(e, gf_t=gf_t):
            p.wait(e, "pet", gf_t)
            ins = e.activation(out=gfb[:, :], in_=ps_g[0:1, 32:32 + B],
                               func=AF.Sigmoid, bias=cols["gc_b3"][1](0)[0:1, 0:1])
            p.inc(ins, "sct", 1)
        p.sc(sf_gf)
        gfs_t = p.val("sct")

        def vgf(e, gfs_t=gfs_t):
            p.wait(e, gfs_t and "sct", gfs_t)
            for ch_ in wrdy:
                p.wait(e, ch_[0], ch_[1])
            p.fence(e, e.tensor_scalar_mul(out=gfr[:, 0:2], in0=gfb[:, :],
                                           scalar1=0.7), "vet")
            ins = e.tensor_scalar_mul(out=gfr[:, 2:3], in0=arr_c[0][0:1, 0:1],
                                      scalar1=2.0)
            p.inc(ins, "vet", 1)
        p.ve(vgf)
        gfr_t = p.val("vet")

        OUTA = p.sbuf(ls, uq("outA"), [128, NDT * FB], F32)
        OUTB = p.sbuf(ls, uq("outB"), [128, NDT * FB], F32)
        OBF = p.sbuf(ls, uq("obf"), [128, NDT * FB], BF16)
        H1 = p.sbuf(ls, uq("h1c"), [128, NDT * FB], BF16)
        H2 = p.sbuf(ls, uq("h2c"), [128, 4 * FB], BF16)
        H3 = p.sbuf(ls, uq("h3c"), [128, 2 * FB], BF16)
        TFR = p.sbuf(ls, uq("tfr"), [1, FB], F32)
        CPR = p.sbuf(ls, uq("cpr"), [1, FB], BF16)
        TMP = p.sbuf(ls, uq("tmp3"), [128, FB], F32)
        ps_c = [p.psum(ls, uq("psc3"), [128, FB]) for _ in range(2)]
        ps_cp = p.psum(ls, uq("pscp"), [128, FB])

        for blk_i in range(TOK // FB):
            b = blk_i // (T // FB)
            def syf(e, blk_i=blk_i):
                p.wait(e, "vet")
                p.wait(e, "dx")
                for dt in range(NDT):
                    ins = e.dma_start(
                        out=OUTA[:, dt * FB:(dt + 1) * FB],
                        in_=scr["xn_f32"][dt * 128:(dt + 1) * 128,
                                          blk_i * FB:(blk_i + 1) * FB])
                    p.inc(ins, "dx", 16)
            p.sy(syf)
            ld_t = p.val("dx")
            cur, nxt = OUTA, OUTB
            cur_rdy = ("dx", ld_t)

            for cyc in range(3):
                def vc(e, cur=cur, cur_rdy=cur_rdy):
                    p.wait(e, cur_rdy[0], cur_rdy[1])
                    p.wait(e, "pet")
                    ins = e.tensor_copy(out=OBF[:], in_=cur[:])
                    p.inc(ins, "vet", 1)
                p.ve(vc)
                obf_t = p.val("vet")

                def mmloc(lt_nm, b_nm, nK_, nM_, rhs_t, out_t, act, rdy, bank):
                    lt = wts[lt_nm][0]
                    outs = []
                    for mt in range(nM_):
                        ps = ps_c[(bank + mt) % 2]

                        def tf(e, mt=mt, ps=ps, rdy=rdy):
                            p.wait(e, rdy[0], rdy[1])
                            p.wait(e, "sct")
                            for k in range(nK_):
                                ins = e.matmul(out=ps[:, :],
                                               lhsT=lt(k)[:, mt * 128:(mt + 1) * 128],
                                               rhs=rhs_t[:, k * FB:(k + 1) * FB],
                                               start=(k == 0), stop=(k == nK_ - 1))
                            p.inc(ins, "pet", 1)
                        p.te(tf)
                        m_t = p.val("pet")

                        def sf(e, mt=mt, ps=ps, m_t=m_t):
                            p.wait(e, "pet", m_t)
                            ins = e.activation(out=out_t[:, mt * FB:(mt + 1) * FB],
                                               in_=ps[:, :], func=act,
                                               bias=cols[b_nm][1](mt))
                            p.inc(ins, "sct", 1)
                        p.sc(sf)
                    return ("sct", p.val("sct"))

                r1 = mmloc("cn_w1", "cn_b1", NDT, NDT, OBF, H1, AF.Gelu,
                           ("vet", obf_t), 0)
                r2 = mmloc("cn_w2", "cn_b2", NDT, 4, H1, H2, AF.Gelu, r1, 0)
                r3 = mmloc("cn_w3", "cn_b3", 4, 2, H2, H3, AF.Gelu, r2, 0)
                lt4 = wts["cn_w4"][0]

                def tf4(e, r3=r3):
                    p.wait(e, r3[0], r3[1])
                    p.wait(e, "sct")
                    for k in range(2):
                        ins = e.matmul(out=ps_c[0][0:1, :], lhsT=lt4(k),
                                       rhs=H3[:, k * FB:(k + 1) * FB],
                                       start=(k == 0), stop=(k == 1))
                    p.inc(ins, "pet", 1)
                p.te(tf4)
                t4_t = p.val("pet")

                def sf4(e, t4_t=t4_t):
                    p.wait(e, "pet", t4_t)
                    p.wait(e, "vet")
                    ins = e.activation(out=TFR[:, :], in_=ps_c[0][0:1, :],
                                       func=AF.Sigmoid, bias=cols["cn_b4"][1](0)[0:1, 0:1])
                    p.inc(ins, "sct", 1)
                p.sc(sf4)
                tf_t = p.val("sct")

                def vcp(e, b=b, tf_t=tf_t, gfr_t=gfr_t):
                    p.wait(e, "sct", tf_t)
                    p.wait(e, "vet", gfr_t)
                    p.fence(e, e.tensor_scalar(out=TFR[:, :], in0=TFR[:, :],
                                               scalar1=0.3, scalar2=gfr[0:1, b:b + 1],
                                               op0=ALU.mult, op1=ALU.add), "vet")
                    p.fence(e, e.tensor_scalar_add(out=TFR[:, :], in0=TFR[:, :],
                                                   scalar1=-0.5), "vet")
                    p.fence(e, e.tensor_scalar_mul(out=TFR[:, :], in0=TFR[:, :],
                                                   scalar1=gfr[0:1, 2:3]), "vet")
                    p.fence(e, e.tensor_scalar_add(out=TFR[:, :], in0=TFR[:, :],
                                                   scalar1=0.1), "vet")
                    ins = e.tensor_copy(out=CPR[:, :], in_=TFR[:, :])
                    p.inc(ins, "vet", 1)
                p.ve(vcp)
                cp_t = p.val("vet")

                def tbc(e, cp_t=cp_t):
                    p.wait(e, "vet", cp_t)
                    ins = e.matmul(out=ps_cp[:, :], lhsT=C["c_ones_row_bf"][:],
                                   rhs=CPR[:, :], start=True, stop=True)
                    p.inc(ins, "pet", 1)
                p.te(tbc)
                bc_t = p.val("pet")

                TW = [(4, 1.0), (5, 1.0), (6, -1.0), (7, -1.0),
                      (0, -1.0), (1, -1.0), (2, 1.0), (3, 1.0)]
                for dt in range(NDT):
                    src, sg = TW[dt]

                    def vtw(e, dt=dt, src=src, sg=sg, bc_t=bc_t, cur=cur, nxt=nxt):
                        p.wait(e, "pet", bc_t)
                        p.wait(e, "vet")
                        p.fence(e, e.tensor_tensor(
                            out=TMP[:], in0=ps_cp[:, :],
                            in1=cur[:, src * FB:(src + 1) * FB], op=ALU.mult), "vet")
                        ins = e.tensor_tensor(
                            out=nxt[:, dt * FB:(dt + 1) * FB],
                            in0=cur[:, dt * FB:(dt + 1) * FB], in1=TMP[:],
                            op=ALU.subtract if sg < 0 else ALU.add)
                        p.inc(ins, "vet", 1)
                    p.ve(vtw)
                cur_rdy = ("vet", p.val("vet"))
                cur, nxt = nxt, cur

            def syo(e, blk_i=blk_i, cur=cur, cur_rdy=cur_rdy):
                p.wait(e, cur_rdy[0], cur_rdy[1])
                p.wait(e, "dx")
                for dt in range(NDT):
                    ins = e.dma_start(
                        out=scr["acc_b"][dt * 128:(dt + 1) * 128,
                                         blk_i * FB:(blk_i + 1) * FB],
                        in_=cur[:, dt * FB:(dt + 1) * FB])
                    p.inc(ins, "dx", 16)
            p.sy(syo)
    p.barrier()


def _emit_layer3(p, W, C, scr, x_in, x_out):
    emit_ln(p, x_in, scr["xn_f32"], scr["xn_bf"], C, uq("ln"))
    p.barrier()
    emit_mha(p, {k: W[f"L3_attn_{k}"] for k in ("Wq", "Wk", "Wv", "Wo")}, C, scr)
    emit_ffn3(p, {k.replace("L3_ffn_", ""): W[k] for k in L3_W if "ffn" in k}, C, scr)
    emit_residual_add(p, x_in, scr["acc_a"], scr["acc_b"], x_out)
    p.barrier()


def emit_attn2(p, W, C, scr):
    for wn, bn, dst in (("Wq", "bq", "q_bf"), ("Wk", "bk", "k_bf"),
                        ("Wv", "bv", "v_bf")):
        with ExitStack() as ls:
            lt, lrdy = load_weight_bf16(p, ls, W[wn], DM, "w2" + wn, 0, DM)
            _, bfn, brdy = load_cols(p, ls, W[bn], uq("b2" + bn))
            cb = evac_to_hbm(p, ls, name="e2" + wn, out_hbm=scr[dst], out_dtype=BF16,
                             bias_col=bfn, extra_ready=brdy)
            mm_stream(p, ls, name="m2" + wn,
                      lhsT_tile=lambda k, m, lt=lt: lt(k)[:, m * 128:(m + 1) * 128],
                      nK=NDT, nM=NDT, lhs_ready=lrdy,
                      rhs_hbm=lambda k, f: scr["xn_bf"][k * 128:(k + 1) * 128,
                                                        f * FC:(f + 1) * FC],
                      out_cb=cb)
        p.barrier()

    KW = 2052
    with ExitStack() as ls:
        q_sb = p.sbuf(ls, uq("q2"), [128, NDT * T], BF16)
        k_sb = p.sbuf(ls, uq("k2"), [128, NDT * KW], BF16)
        v_sb = p.sbuf(ls, uq("v2"), [128, NDT * KW], BF16)
        qk = p.sbuf(ls, uq("qk2"), [128, NDT * FC], BF16)
        sc = [p.sbuf(ls, uq("sc2"), [16, T], F32) for _ in range(3)]
        mx = p.sbuf(ls, uq("mx2"), [16, T], F32)
        ex = [p.sbuf(ls, uq("ex2"), [16, T], BF16) for _ in range(3)]
        sr = p.sbuf(ls, uq("sr2"), [16, 2 * T], F32)
        att = [p.sbuf(ls, uq("at2"), [16, T], BF16) for _ in range(3)]
        tjs = p.sbuf(ls, uq("tj2"), [128, 3 * FC], F32)
        ybf = p.sbuf(ls, uq("yb2"), [128, FC], BF16)
        ps_s = p.psum(ls, uq("ps2"), [16, FC])
        ps_bc = [p.psum(ls, uq("pb2"), [128, FC]) for _ in range(2)]

        def gf0(e):
            for dt in range(NDT):
                e.memset(k_sb[:, dt * KW:dt * KW + 2], 0.0)
                ins = e.memset(v_sb[:, dt * KW:dt * KW + 2], 0.0)
            p.inc(ins, "gpt", 1)
        p.gp(gf0)
        pad_t = p.val("gpt")

        for b in range(B):
            def syf(e, b=b):
                p.wait(e, "pet")
                p.wait(e, "vet")
                p.wait(e, "gpt")
                for dt in range(NDT):
                    ins = e.dma_start(out=q_sb[:, dt * T:(dt + 1) * T],
                                      in_=scr["q_bf"][dt * 128:(dt + 1) * 128,
                                                      b * T:(b + 1) * T])
                    p.inc(ins, "dr", 16)
                    ins = e.dma_start(out=k_sb[:, dt * KW + 2:dt * KW + 2 + T],
                                      in_=scr["k_bf"][dt * 128:(dt + 1) * 128,
                                                      b * T:(b + 1) * T])
                    p.inc(ins, "dr", 16)
                    ins = e.dma_start(out=v_sb[:, dt * KW + 2:dt * KW + 2 + T],
                                      in_=scr["v_bf"][dt * 128:(dt + 1) * 128,
                                                      b * T:(b + 1) * T])
                    p.inc(ins, "dr", 16)
            p.sy(syf)
            ld_t = p.val("dr")

            for j in range(3):
                for fc in range(T // FC):
                    def vqk(e, j=j, fc=fc, b=b):
                        p.wait(e, "dr", ld_t)
                        p.wait(e, "gpt", pad_t)
                        p.wait(e, "pet")
                        for dt in range(NDT):
                            ins = e.tensor_tensor(
                                out=qk[:, dt * FC:(dt + 1) * FC],
                                in0=q_sb[:, dt * T + fc * FC:dt * T + (fc + 1) * FC],
                                in1=k_sb[:, dt * KW + 2 - j + fc * FC:
                                         dt * KW + 2 - j + (fc + 1) * FC],
                                op=ALU.mult)
                            p.inc(ins, "vet", 1)
                    p.ve(vqk)
                    qk_t = p.val("vet")

                    def tsc(e, j=j, fc=fc, qk_t=qk_t):
                        p.wait(e, "vet", qk_t)
                        for dt in range(NDT):
                            ins = e.matmul(out=ps_s[:, :],
                                           lhsT=C[f"c_hsel{dt}_bf"][:],
                                           rhs=qk[:, dt * FC:(dt + 1) * FC],
                                           start=(dt == 0), stop=(dt == NDT - 1))
                        p.inc(ins, "pet", 1)
                    p.te(tsc)
                    sc_t = p.val("pet")

                    def vev(e, j=j, fc=fc, sc_t=sc_t):
                        p.wait(e, "pet", sc_t)
                        ins = e.tensor_scalar_mul(
                            out=sc[j][:, fc * FC:(fc + 1) * FC],
                            in0=ps_s[:, :], scalar1=0.125)
                        p.inc(ins, "vet", 1)
                    p.ve(vev)

            def vmask(e):
                e.memset(sc[1][:, 0:1], -1e30)
                ins = e.memset(sc[2][:, 0:2], -1e30)
                p.inc(ins, "vet", 1)
            p.ve(vmask)
            scm_t = p.val("vet")

            def vsm(e, scm_t=scm_t):
                p.wait(e, "vet", scm_t)
                p.fence(e, e.tensor_tensor(out=mx[:, :], in0=sc[0][:, :],
                                           in1=sc[1][:, :], op=ALU.max), "vet")
                p.fence(e, e.tensor_tensor(out=mx[:, :], in0=mx[:, :],
                                           in1=sc[2][:, :], op=ALU.max), "vet")
                for j in range(3):
                    p.fence(e, e.tensor_tensor(
                        out=ex[j][:, :], in0=sc[j][:, :],
                        in1=mx[:, :], op=ALU.subtract), "vet")
                ins = e.engine_nop()
                p.inc(ins, "vet", 1)
            p.ve(vsm)
            sub_t = p.val("vet")

            def sexp(e, sub_t=sub_t):
                p.wait(e, "vet", sub_t)
                for j in range(3):
                    ins = e.activation(out=ex[j][:, :],
                                       in_=ex[j][:, :], func=AF.Exp)
                    p.inc(ins, "sct", 1)
            p.sc(sexp)
            exp_t = p.val("sct")

            def vnrm(e, exp_t=exp_t):
                p.wait(e, "sct", exp_t)
                p.fence(e, e.tensor_tensor(out=sr[:, 0:T], in0=ex[0][:, :],
                                           in1=ex[1][:, :], op=ALU.add), "vet")
                p.fence(e, e.tensor_tensor(out=sr[:, 0:T], in0=sr[:, 0:T],
                                           in1=ex[2][:, :], op=ALU.add), "vet")
                p.fence(e, e.reciprocal(out=sr[:, T:2 * T], in_=sr[:, 0:T]), "vet")
                for j in range(3):
                    p.fence(e, e.tensor_tensor(
                        out=att[j][:, :], in0=ex[j][:, :],
                        in1=sr[:, T:2 * T], op=ALU.mult), "vet")
                ins = e.engine_nop()
                p.inc(ins, "vet", 1)
            p.ve(vnrm)
            att_t = p.val("vet")

            for dt in range(NDT):
                for fc in range(T // FC):
                    for j in range(3):
                        ps = ps_bc[j & 1]

                        def tbc(e, j=j, dt=dt, fc=fc, ps=ps, att_t=att_t):
                            p.wait(e, "vet", att_t)
                            p.wait(e, "vet")
                            ins = e.matmul(
                                out=ps[:, :], lhsT=C[f"c_hexp{dt}_bf"][:],
                                rhs=att[j][:, fc * FC:(fc + 1) * FC],
                                start=True, stop=True)
                            p.inc(ins, "pet", 1)
                        p.te(tbc)
                        bc_t = p.val("pet")

                        def vtj(e, j=j, dt=dt, fc=fc, ps=ps, bc_t=bc_t):
                            p.wait(e, "pet", bc_t)
                            ins = e.tensor_tensor(
                                out=tjs[:, j * FC:(j + 1) * FC], in0=ps[:, :],
                                in1=v_sb[:, dt * KW + 2 - j + fc * FC:
                                         dt * KW + 2 - j + (fc + 1) * FC],
                                op=ALU.mult)
                            p.inc(ins, "vet", 1)
                        p.ve(vtj)

                    def vy(e, dt=dt, fc=fc):
                        p.wait(e, "vet")
                        p.wait(e, "dx")
                        p.fence(e, e.tensor_tensor(out=tjs[:, 0:FC], in0=tjs[:, 0:FC],
                                                   in1=tjs[:, FC:2 * FC], op=ALU.add),
                                "vet")
                        ins = e.tensor_tensor(out=ybf[:, :], in0=tjs[:, 0:FC],
                                              in1=tjs[:, 2 * FC:3 * FC], op=ALU.add)
                        p.inc(ins, "vet", 1)
                    p.ve(vy)
                    y_t = p.val("vet")

                    def syy(e, dt=dt, fc=fc, b=b, y_t=y_t):
                        p.wait(e, "vet", y_t)
                        p.wait(e, "dx")
                        ins = e.dma_start(
                            out=scr["ctx_bf"][dt * 128:(dt + 1) * 128,
                                              b * T + fc * FC:b * T + (fc + 1) * FC],
                            in_=ybf[:, :])
                        p.inc(ins, "dx", 16)
                    p.sy(syy)
    p.barrier()

    with ExitStack() as ls:
        lt, lrdy = load_weight_bf16(p, ls, W["Wo"], DM, "w2Wo", 0, DM)
        _, bofn, bordy = load_cols(p, ls, W["bo"], uq("b2bo"))
        cb = evac_to_hbm(p, ls, name="e2Wo", out_hbm=scr["acc_a"], out_dtype=F32,
                         bias_col=bofn, extra_ready=bordy)
        mm_stream(p, ls, name="m2Wo",
                  lhsT_tile=lambda k, m, lt=lt: lt(k)[:, m * 128:(m + 1) * 128],
                  nK=NDT, nM=NDT, lhs_ready=lrdy,
                  rhs_hbm=lambda k, f: scr["ctx_bf"][k * 128:(k + 1) * 128,
                                                     f * FC:(f + 1) * FC],
                  out_cb=cb)
    p.barrier()


def emit_ffn2(p, W, C, scr):
    DH = 4096
    s1c = s2c = None
    for half in range(2):
        with ExitStack() as ls:
            mw = DH // 2
            lt1, r1 = load_weight_bf16(p, ls, W["w1_w"], DM, "w2f1", half * mw, mw)
            lt2, r2 = load_weight_bf16(p, ls, W["w2_w"], DM, "w2f2", half * mw, mw)
            _, s1fn, s1r = load_cols(p, ls, W["w1_s"], uq("s2f1"), n=DH)
            _, s2fn, s2r = load_cols(p, ls, W["w2_s"], uq("s2f2"), n=DH)
            rbuf = p.sbuf(ls, uq("f2r"), [128, 2 * NDT * FC], BF16)
            psA = p.psum(ls, uq("f2A"), [128, FC])
            psB = p.psum(ls, uq("f2B"), [128, FC])
            sig = p.sbuf(ls, uq("f2s"), [128, FC], F32)
            gg = p.sbuf(ls, uq("f2g"), [128, FC], F32)
            uu = p.sbuf(ls, uq("f2u"), [128, FC], F32)
            hb = p.sbuf(ls, uq("f2h"), [128, 2 * FC], BF16)
            slot_war = [None, None]
            rhs_tick = [None, None]

            def prefetch(f, slot):
                dch = "dra" if slot == 0 else "drb"
                guard = slot_war[slot] or p.dguard.get(dch)

                def syf(e, f=f, slot=slot, dch=dch, guard=guard):
                    if guard is not None:
                        p.wait(e, guard[0], guard[1])
                    for k in range(NDT):
                        ins = e.dma_start(
                            out=rbuf[:, (slot * NDT + k) * FC:(slot * NDT + k + 1) * FC],
                            in_=scr["xn_bf"][k * 128:(k + 1) * 128, f * FC:(f + 1) * FC])
                        p.inc(ins, dch, 16)
                p.sy(syf)
                rhs_tick[slot] = (dch, p.val(dch))
            prefetch(0, 0)
            for f in range(NFC):
                slot = f & 1
                if f + 1 < NFC:
                    prefetch(f + 1, 1 - slot)
                for m in range(mw // 128):
                    gm = half * (mw // 128) + m

                    def tf(e, f=f, m=m, slot=slot, rt=rhs_tick[slot]):
                        p.wait(e, r1[0], r1[1])
                        p.wait(e, r2[0], r2[1])
                        p.wait(e, s1r[0], s1r[1])
                        p.wait(e, s2r[0], s2r[1])
                        p.wait(e, rt[0], rt[1])
                        p.wait(e, "vet")
                        p.wait(e, "sct")
                        for k in range(NDT):
                            e.matmul(out=psA[:, :], lhsT=lt1(k)[:, m * 128:(m + 1) * 128],
                                     rhs=rbuf[:, (slot * NDT + k) * FC:(slot * NDT + k + 1) * FC],
                                     start=(k == 0), stop=(k == NDT - 1))
                        for k in range(NDT):
                            ins = e.matmul(out=psB[:, :], lhsT=lt2(k)[:, m * 128:(m + 1) * 128],
                                           rhs=rbuf[:, (slot * NDT + k) * FC:(slot * NDT + k + 1) * FC],
                                           start=(k == 0), stop=(k == NDT - 1))
                        p.inc(ins, "pet", 1)
                    p.te(tf)
                    mm_t = p.val("pet")

                    def sf(e, gm=gm, mm_t=mm_t):
                        p.wait(e, "pet", mm_t)
                        p.wait(e, "vet")
                        ins = e.activation(out=sig[:, :], in_=psA[:, :], func=AF.Sigmoid,
                                           scale=s1fn(gm))
                        p.inc(ins, "sct", 1)
                    p.sc(sf)
                    sg_t = p.val("sct")

                    def vf(e, gm=gm, mm_t=mm_t, sg_t=sg_t, f=f):
                        p.wait(e, "sct", sg_t)
                        p.wait(e, "dx")
                        p.fence(e, e.tensor_scalar_mul(out=gg[:, :], in0=psA[:, :],
                                                       scalar1=s1fn(gm)), "vet")
                        p.fence(e, e.tensor_scalar_mul(out=uu[:, :], in0=psB[:, :],
                                                       scalar1=s2fn(gm)), "vet")
                        p.fence(e, e.tensor_tensor(out=gg[:, :], in0=gg[:, :],
                                                   in1=sig[:, :], op=ALU.mult), "vet")
                        ins = e.tensor_tensor(out=hb[:, (f & 1) * FC:((f & 1) + 1) * FC],
                                              in0=gg[:, :], in1=uu[:, :], op=ALU.mult)
                        p.inc(ins, "vet", 1)
                    p.ve(vf)
                    v_t = p.val("vet")

                    def sy2(e, gm=gm, f=f, v_t=v_t):
                        p.wait(e, "vet", v_t)
                        p.wait(e, "dx")
                        ins = e.dma_start(
                            out=scr["h_bf"][gm * 128:(gm + 1) * 128, f * FC:(f + 1) * FC],
                            in_=hb[:, (f & 1) * FC:((f & 1) + 1) * FC])
                        p.inc(ins, "dx", 16)
                    p.sy(sy2)
                slot_war[slot] = ("pet", p.val("pet"))
            p.dguard["dra"] = ("pet", p.val("pet"))
            p.dguard["drb"] = ("pet", p.val("pet"))
        p.barrier()

    for wn, sn, src_t, dst in (("h1_w", "h1_s", "h_bf", "g_bf"),
                               ("h2_w", "h2_s", "g_bf", "u_bf")):
        for mc in range(8):
            with ExitStack() as ls:
                lt, lrdy = load_weight_bf16(p, ls, W[wn], DH, "w2" + wn + str(mc),
                                            mc * 512, 512)
                _, sfn, srdy = load_cols(p, ls, W[sn], uq("s2" + wn), n=DH)
                cb = evac_to_hbm(p, ls, name="e2" + wn + str(mc), out_hbm=scr[dst],
                                 out_dtype=BF16, func=AF.Gelu,
                                 scale_col=sfn, extra_ready=srdy)
                mm_stream(p, ls, name="m2" + wn + str(mc),
                          lhsT_tile=lambda k, m, lt=lt: lt(k)[:, m * 128:(m + 1) * 128],
                          nK=DH // 128, nM=4, lhs_ready=lrdy,
                          rhs_hbm=lambda k, f, src_t=src_t: scr[src_t][
                              k * 128:(k + 1) * 128, f * FC:(f + 1) * FC],
                          out_cb=lambda m, f, ps, cb=cb, mc=mc: cb(mc * 4 + m, f, ps))
            p.barrier()

    for half in range(2):
        with ExitStack() as ls:
            lt, lrdy = load_weight_bf16(p, ls, W["w3_w"], DH, "w2w3" + str(half),
                                        half * 512, 512)
            _, s3fn, s3rdy = load_cols(p, ls, W["w3_s"], uq("s2w3"), n=DM)
            cb = evac_to_hbm(p, ls, name="e2w3" + str(half), out_hbm=scr["acc_b"],
                             out_dtype=F32, scale_col=s3fn, extra_ready=s3rdy)
            mm_stream(p, ls, name="m2w3" + str(half),
                      lhsT_tile=lambda k, m, lt=lt: lt(k)[:, m * 128:(m + 1) * 128],
                      nK=DH // 128, nM=4, lhs_ready=lrdy,
                      rhs_hbm=lambda k, f: scr["u_bf"][k * 128:(k + 1) * 128,
                                                       f * FC:(f + 1) * FC],
                      out_cb=lambda m, f, ps, cb=cb, half=half: cb(half * 4 + m, f, ps))
        p.barrier()


def _emit_layer2(p, W, C, scr, x_in, x_out):
    emit_ln(p, x_in, None, scr["xn_bf"], C, uq("ln"))
    p.barrier()
    emit_attn2(p, {k.replace("L2_attn_", ""): W[k] for k in L2_W if "attn" in k},
               C, scr)
    emit_ffn2(p, {k.replace("L2_ffn_", ""): W[k] for k in L2_W if "ffn" in k},
              C, scr)
    emit_residual_add(p, x_in, scr["acc_a"], scr["acc_b"], x_out)
    p.barrier()


def emit_ssm(p, W, C, scr):
    with ExitStack() as ls:
        _, gfn, grdy = load_cols(p, ls, W["ln_gamma"], uq("s0g"))
        _, bfn, brdy = load_cols(p, ls, W["ln_beta"], uq("s0b"))

        def join(e):
            p.wait(e, grdy[0], grdy[1])
            p.wait(e, brdy[0], brdy[1])
            ins = e.nop()
            p.inc(ins, "sct", 1)
        p.sc(join)
        emit_ln(p, scr["xn_f32"], None, scr["q_bf"], C, uq("iln"),
                gamma_col=gfn, beta_col=bfn)
    p.barrier()

    sst = ExitStack()
    u_tok = p.sbuf(sst, "u_tok", [128, 32768], BF16)
    with ExitStack() as ls:
        ubuf = p.sbuf(ls, uq("s0u"), [128, TOK], BF16)
        ps_t = p.psum(ls, uq("s0p"), [128, 128], BF16)
        for dt in range(NDT):
            def syf(e, dt=dt):
                p.wait(e, "pet")
                ins = e.dma_start(out=ubuf[:], in_=scr["q_bf"][dt * 128:(dt + 1) * 128, :])
                p.inc(ins, "dr", 16)
            p.sy(syf)
            ld_t = p.val("dr")
            for tt in range(TOK // 128):
                bq, cq = tt // 16, tt % 16

                def tf(e, dt=dt, tt=tt, ld_t=ld_t):
                    p.wait(e, "dr", ld_t)
                    p.wait(e, "vet")
                    ins = e.transpose(out=ps_t[:, :], in_=ubuf[:, tt * 128:(tt + 1) * 128],
                                      identity=C["c_id_bf"][:])
                    p.inc(ins, "pet", 1)
                p.te(tf)
                t_t = p.val("pet")

                def vf(e, dt=dt, bq=bq, cq=cq, t_t=t_t):
                    p.wait(e, "pet", t_t)
                    ins = e.tensor_copy(
                        out=u_tok[:, bq * 16384 + cq * 1024 + dt * 128:
                                  bq * 16384 + cq * 1024 + (dt + 1) * 128],
                        in_=ps_t[:, :])
                    p.inc(ins, "vet", 1)
                p.ve(vf)
    p.barrier()

    with ExitStack() as ls:
        la = p.sbuf(ls, uq("sla"), [128, 64], F32)
        bb = p.sbuf(ls, uq("slb"), [128, 64], F32)
        cc = p.sbuf(ls, uq("slc"), [128, 64], F32)
        wt = p.sbuf(ls, uq("slw"), [128, 64], F32)
        cbt = p.sbuf(ls, uq("slcb"), [128, 64], F32)
        _, dtfn, dtrdy = load_cols(p, ls, W["log_dt"], uq("sldt"))
        dtexp = p.sbuf(ls, uq("slde"), [128, NDT], F32)

        def sdt(e):
            p.wait(e, dtrdy[0], dtrdy[1])
            for dt in range(NDT):
                ins = e.activation(out=dtexp[:, dt:dt + 1], in_=dtfn(dt), func=AF.Exp)
            p.inc(ins, "sct", 1)
        p.sc(sdt)
        dte_t = p.val("sct")

        for dt in range(NDT):
            def syf(e, dt=dt):
                p.wait(e, "sct")
                p.wait(e, "vet")
                for src_ap, dst in ((W["log_A"], la), (W["B_param"], bb),
                                    (W["C_param"], cc)):
                    ins = e.dma_start(out=dst[:], in_=src_ap[dt * 128:(dt + 1) * 128, :])
                    p.inc(ins, "dwa", 16)
            p.sy(syf)
            l_t = p.val("dwa")

            def sf(e, dt=dt, l_t=l_t):
                p.wait(e, "dwa", l_t)
                p.wait(e, "sct", dte_t)
                p.wait(e, "vet")
                p.wait(e, "dx")
                ins = e.activation(out=wt[:], in_=la[:], func=AF.Exp)
                p.inc(ins, "sct", 1)
            p.sc(sf)
            w_t = p.val("sct")

            def vf(e, dt=dt, l_t=l_t, w_t=w_t):
                p.wait(e, "sct", w_t)
                p.wait(e, "dwa", l_t)
                p.wait(e, "dx")
                p.fence(e, e.tensor_scalar_mul(out=wt[:], in0=wt[:],
                                               scalar1=dtexp[:, dt:dt + 1]), "vet")
                p.fence(e, e.tensor_tensor(out=cbt[:], in0=bb[:], in1=cc[:],
                                           op=ALU.mult), "vet")
                ins = e.tensor_scalar_mul(out=cbt[:], in0=cbt[:],
                                          scalar1=dtexp[:, dt:dt + 1])
                p.inc(ins, "vet", 1)
            p.ve(vf)
            v_t = p.val("vet")

            def sy2(e, dt=dt, v_t=v_t):
                p.wait(e, "vet", v_t)
                p.wait(e, "dx")
                ins = e.dma_start(out=scr["w_hbm"][dt * 128:(dt + 1) * 128, :], in_=wt[:])
                p.inc(ins, "dx", 16)
                ins = e.dma_start(out=scr["cb_hbm"][dt * 128:(dt + 1) * 128, :], in_=cbt[:])
                p.inc(ins, "dx", 16)
            p.sy(sy2)
    p.barrier()

    sbufs = ExitStack()
    CBCOL = p.sbuf(sbufs, "CBCOL", [128, 512], F32)
    A128 = p.sbuf(sbufs, "A128", [64, 1024], F32)
    R_all = p.sbuf(sbufs, "R_all", [64, 32768], BF16)
    with ExitStack() as ls:
        stg = p.sbuf(ls, uq("cbs"), [128, 128], F32)
        ps_t2 = p.psum(ls, uq("cbp"), [128, 128])
        cb_v = scr["cb_hbm"].rearrange("(dp two) n -> dp (two n)", two=2)
        w_v = scr["w_hbm"].rearrange("(dp two) n -> dp (two n)", two=2)
        for t4 in range(4):
            def syf(e, t4=t4):
                p.wait(e, "pet")
                ins = e.dma_start(out=stg[:], in_=cb_v[t4 * 128:(t4 + 1) * 128, :])
                p.inc(ins, "dwa", 16)
            p.sy(syf)
            l_t = p.val("dwa")

            def tf(e, t4=t4, l_t=l_t):
                p.wait(e, "dwa", l_t)
                p.wait(e, "vet")
                ins = e.transpose(out=ps_t2[:, :], in_=stg[:], identity=C["c_id"][:])
                p.inc(ins, "pet", 1)
            p.te(tf)
            t_t = p.val("pet")

            def vf(e, t4=t4, t_t=t_t):
                p.wait(e, "pet", t_t)
                ins = e.tensor_copy(out=CBCOL[:, t4 * 128:(t4 + 1) * 128], in_=ps_t2[:, :])
                p.inc(ins, "vet", 1)
            p.ve(vf)
        for dt in range(NDT):
            def syf2(e, dt=dt):
                p.wait(e, "pet")
                ins = e.dma_start(out=stg[:, 0:64],
                                  in_=scr["w_hbm"][dt * 128:(dt + 1) * 128, :])
                p.inc(ins, "dwa", 16)
            p.sy(syf2)
            l_t = p.val("dwa")

            def tf2(e, dt=dt, l_t=l_t):
                p.wait(e, "dwa", l_t)
                p.wait(e, "sct")
                p.wait(e, "vet")
                ins = e.transpose(out=ps_t2[0:64, :], in_=stg[:, 0:64],
                                  identity=C["c_id"][:])
                p.inc(ins, "pet", 1)
            p.te(tf2)
            t_t = p.val("pet")

            def sf2(e, dt=dt, t_t=t_t):
                p.wait(e, "pet", t_t)
                ins = e.activation(out=A128[:, dt * 128:(dt + 1) * 128],
                                   in_=ps_t2[0:64, :], func=AF.Exp, scale=-128.0)
                p.inc(ins, "sct", 1)
            p.sc(sf2)
    p.barrier()

    NW = 16
    with ExitStack() as ls:
        wrow = p.sbuf(ls, uq("wr"), [1, 2 * NW * 128], F32)
        wrowb = p.sbuf(ls, uq("wrb"), [1, 2 * NW * 128], BF16)
        al = p.sbuf(ls, uq("al"), [128, 128], BF16)
        rstg = p.sbuf(ls, uq("rstg"), [64, 512], F32)
        ps_tab = p.psum(ls, uq("ptab"), [128, 132])
        ps_r = [p.psum(ls, uq("pr"), [64, 512]) for _ in range(2)]
        w_flat = scr["w_hbm"].rearrange("d n -> (d n)").rearrange("(one f) -> one f", one=1)

        def wchunk(ci):
            dch = "dwa" if (ci & 1) == 0 else "dwb"
            sl = ci & 1

            def syf(e, ci=ci, dch=dch, sl=sl):
                p.wait(e, "pet")
                ins = e.dma_start(out=wrow[0:1, sl * NW * 128:(sl + 1) * NW * 128],
                                  in_=w_flat[0:1, ci * NW * 128:(ci + 1) * NW * 128])
                p.inc(ins, dch, 16)
            p.sy(syf)
            d_t = p.val(dch)

            def vf(e, ci=ci, dch=dch, sl=sl, d_t=d_t):
                p.wait(e, dch, d_t)
                ins = e.tensor_copy(out=wrowb[0:1, sl * NW * 128:(sl + 1) * NW * 128],
                                    in_=wrow[0:1, sl * NW * 128:(sl + 1) * NW * 128])
                p.inc(ins, "vet", 1)
            p.ve(vf)
            return ("vet", p.val("vet"))

        wc_rdy = wchunk(0)
        uview = u_tok.ap().rearrange("p (b c d) -> p c b d", b=2, c=16, d=1024)
        for dp in range(512):
            ci, off = dp // NW, dp % NW
            if off == 0 and dp + NW < 512:
                nxt_rdy = wchunk(ci + 1)
            if off == 0 and dp > 0:
                wc_rdy = nxt_rdy
            sl = ci & 1

            def tfa(e, dp=dp, sl=sl, off=off, wc_rdy=wc_rdy):
                p.wait(e, wc_rdy[0], wc_rdy[1])
                p.wait(e, "sct")
                ins = e.matmul(out=ps_tab[:, 0:128],
                               lhsT=C["c_iota_sm127_bf"][:],
                               rhs=wrowb[0:1, (sl * NW + off) * 128:(sl * NW + off + 1) * 128],
                               start=True, stop=True)
                p.inc(ins, "pet", 1)
            p.te(tfa)
            ta_t = p.val("pet")

            def sfa(e, dp=dp, ta_t=ta_t):
                p.wait(e, "pet", ta_t)
                p.wait(e, "pet")
                ins = e.activation(out=al[:, :], in_=ps_tab[:, 0:128], func=AF.Exp)
                p.inc(ins, "sct", 1)
            p.sc(sfa)
            ale_t = p.val("sct")

            bank = (dp // 8) & 1
            for dl in range(2):
                d = 2 * dp + dl
                ycol = (d % 16) * 32

                def tfr(e, d=d, dl=dl, ycol=ycol, bank=bank, ale_t=ale_t, dp=dp):
                    p.wait(e, "sct", ale_t)
                    if d % 16 == 0:
                        p.wait(e, "vet")
                    ins = e.matmul(out=ps_r[bank][0:64, ycol:ycol + 32],
                                   lhsT=al[:, dl * 64:(dl + 1) * 64],
                                   rhs=uview[:, :, :, d], start=True, stop=True)
                    p.inc(ins, "pet", 1)
                p.te(tfr)
            if dp % 8 == 7:
                r_t = p.val("pet")
                base = (dp - 7) * 2 * 2

                def vev(e, dp=dp, bank=bank, r_t=r_t):
                    p.wait(e, "pet", r_t)
                    d0 = 2 * (dp - 7)
                    ins = e.tensor_copy(out=R_all[:, d0 * 32:(d0 + 16) * 32],
                                        in_=ps_r[bank][0:64, :])
                    p.inc(ins, "vet", 1)
                p.ve(vev)
    p.barrier()

    with ExitStack() as ls:
        state = p.sbuf(ls, uq("st8"), [64, 1024], F32)
        Rv = R_all.ap().rearrange("p (d c b) -> p d c b", d=1024, c=16, b=2)
        for b in range(B):
            def vz(e, b=b):
                p.wait(e, "vet")
                ins = e.memset(state[:, :], 0.0)
                p.inc(ins, "vet", 1)
            p.ve(vz)
            for c in range(16):
                def vstep(e, b=b, c=c):
                    p.wait(e, "vet")
                    if c > 0:
                        p.fence(e, e.tensor_copy(out=Rv[:, :, c - 1, b],
                                                 in_=state[:, :]), "vet")
                    p.fence(e, e.tensor_tensor(out=state[:, :], in0=state[:, :],
                                               in1=A128[:, :], op=ALU.mult), "vet")
                    ins = e.tensor_tensor(out=state[:, :], in0=state[:, :],
                                          in1=Rv[:, :, c, b], op=ALU.add)
                    p.inc(ins, "vet", 1)
                p.ve(vstep)
    p.barrier()
    with ExitStack() as ls:
        wrow = p.sbuf(ls, uq("wr2"), [1, 2 * 16 * 128], F32)
        wrowb = p.sbuf(ls, uq("wrb2"), [1, 2 * 16 * 128], BF16)
        wrown = p.sbuf(ls, uq("wrn2"), [1, 2 * 16 * 128], BF16)
        aneg = p.sbuf(ls, uq("aneg"), [64, 256], BF16)
        cbax = p.sbuf(ls, uq("cbax"), [64, 2 * 132], BF16)
        cbaxt = p.sbuf(ls, uq("cbaxt"), [64, 2 * 132], BF16)
        kd = p.sbuf(ls, uq("kd"), [128, 256], BF16)
        ystg = p.sbuf(ls, uq("ystg"), [128, 512], BF16)
        ytr = p.sbuf(ls, uq("ytr"), [128, 128], BF16)
        ps_tab = p.psum(ls, uq("ptab2"), [128, 132])
        ps_kd = p.psum(ls, uq("pkd"), [128, 256])
        ps_y = [p.psum(ls, uq("py"), [128, 512]) for _ in range(2)]
        ps_tr = p.psum(ls, uq("ptr"), [128, 128], BF16)
        NW = 16
        w_flat = scr["w_hbm"].rearrange("d n -> (d n)").rearrange("(one f) -> one f", one=1)
        uview = u_tok.ap().rearrange("p (b c d) -> p c b d", b=2, c=16, d=1024)
        Sv = R_all.ap().rearrange("p (d c b) -> p d c b", d=1024, c=16, b=2)

        def wchunk2(ci):
            dch = "dwa" if (ci & 1) == 0 else "dwb"
            sl = ci & 1

            def syf(e, ci=ci, dch=dch, sl=sl):
                p.wait(e, "pet")
                ins = e.dma_start(out=wrow[0:1, sl * NW * 128:(sl + 1) * NW * 128],
                                  in_=w_flat[0:1, ci * NW * 128:(ci + 1) * NW * 128])
                p.inc(ins, dch, 16)
            p.sy(syf)
            d_t = p.val(dch)

            def vf(e, ci=ci, dch=dch, sl=sl, d_t=d_t):
                p.wait(e, dch, d_t)
                p.fence(e, e.tensor_copy(
                    out=wrowb[0:1, sl * NW * 128:(sl + 1) * NW * 128],
                    in_=wrow[0:1, sl * NW * 128:(sl + 1) * NW * 128]), "vet")
                ins = e.tensor_scalar_mul(
                    out=wrown[0:1, sl * NW * 128:(sl + 1) * NW * 128],
                    in0=wrow[0:1, sl * NW * 128:(sl + 1) * NW * 128], scalar1=-1.0)
                p.inc(ins, "vet", 1)
            p.ve(vf)
            return ("vet", p.val("vet"))

        wc_rdy = wchunk2(0)
        for dp in range(512):
            ci, off = dp // NW, dp % NW
            if off == 0 and dp + NW < 512:
                nxt_rdy = wchunk2(ci + 1)
            if off == 0 and dp > 0:
                wc_rdy = nxt_rdy
            sl = ci & 1
            wsl = slice((sl * NW + off) * 128, (sl * NW + off + 1) * 128)

            def tft(e, dp=dp, wsl=wsl, wc_rdy=wc_rdy):
                p.wait(e, wc_rdy[0], wc_rdy[1])
                p.wait(e, "sct")
                ins = e.matmul(out=ps_tab[:, 0:128], lhsT=wrowb[0:1, wsl],
                               rhs=C["c_iota_s_bf"][:], start=True, stop=True)
                p.inc(ins, "pet", 1)
            p.te(tft)
            t1_t = p.val("pet")

            def sf1(e, t1_t=t1_t):
                p.wait(e, "pet", t1_t)
                p.wait(e, "pet")
                e.activation(out=aneg[:, 0:128], in_=ps_tab[0:64, 0:128], func=AF.Exp)
                ins = e.activation(out=aneg[:, 128:256], in_=ps_tab[64:128, 0:128],
                                   func=AF.Exp)
                p.inc(ins, "sct", 1)
            p.sc(sf1)
            an_t = p.val("sct")

            def tft2(e, dp=dp, wsl=wsl, an_t=an_t):
                p.wait(e, "sct", an_t)
                ins = e.matmul(out=ps_tab[:, 0:129], lhsT=wrown[0:1, wsl],
                               rhs=C["c_iota_t129_bf"][:], start=True, stop=True)
                p.inc(ins, "pet", 1)
            p.te(tft2)
            t2_t = p.val("pet")

            def sf2(e, t2_t=t2_t):
                p.wait(e, "pet", t2_t)
                p.wait(e, "vet")
                e.activation(out=cbaxt[:, 0:129], in_=ps_tab[0:64, 0:129], func=AF.Exp)
                ins = e.activation(out=cbaxt[:, 132:261], in_=ps_tab[64:128, 0:129],
                                   func=AF.Exp)
                p.inc(ins, "sct", 1)
            p.sc(sf2)
            cx_t = p.val("sct")

            def vcb(e, dp=dp, cx_t=cx_t):
                p.wait(e, "sct", cx_t)
                p.wait(e, "pet")
                e.tensor_scalar_mul(out=cbax[:, 0:129], in0=cbaxt[:, 0:129],
                                    scalar1=CBCOL[0:64, dp:dp + 1])
                ins = e.tensor_scalar_mul(out=cbax[:, 132:261], in0=cbaxt[:, 132:261],
                                          scalar1=CBCOL[64:128, dp:dp + 1])
                p.inc(ins, "vet", 1)
            p.ve(vcb)
            cb_t = p.val("vet")

            def tkd(e, dp=dp, an_t=an_t, cb_t=cb_t):
                p.wait(e, "sct", an_t)
                p.wait(e, "vet", cb_t)
                p.wait(e, "vet")
                for dl in range(2):
                    ins = e.matmul(out=ps_kd[:, dl * 128:(dl + 1) * 128],
                                   lhsT=aneg[:, dl * 128:(dl + 1) * 128],
                                   rhs=cbax[:, dl * 132:dl * 132 + 128],
                                   start=True, stop=True)
                p.inc(ins, "pet", 1)
            p.te(tkd)
            kb_t = p.val("pet")

            def vkd(e, kb_t=kb_t):
                p.wait(e, "pet", kb_t)
                ins = e.tensor_tensor(out=kd[:, :], in0=ps_kd[:, :],
                                      in1=C["c_utri2_bf"][:], op=ALU.mult)
                p.inc(ins, "vet", 1)
            p.ve(vkd)
            kd_t = p.val("vet")

            bank = (dp // 8) & 1
            for dl in range(2):
                d = 2 * dp + dl
                ycol = (d % 16) * 32

                def tap(e, d=d, dl=dl, ycol=ycol, bank=bank, kd_t=kd_t, dp=dp):
                    p.wait(e, "vet", kd_t)
                    if d % 16 == 0:
                        p.wait(e, "vet")
                    e.matmul(out=ps_y[bank][:, ycol:ycol + 32],
                             lhsT=kd[:, dl * 128:(dl + 1) * 128],
                             rhs=uview[:, :, :, d], start=True, stop=False)
                    ins = e.matmul(out=ps_y[bank][:, ycol + 2:ycol + 32],
                                   lhsT=cbax[0:64, dl * 132 + 1:dl * 132 + 129],
                                   rhs=Sv[:, d, 0:15, :], start=False, stop=True)
                    p.inc(ins, "pet", 1)
                p.te(tap)

            if dp % 8 == 7:
                ya_t = p.val("pet")
                d0 = 2 * (dp - 7)

                def vys(e, dp=dp, bank=bank, ya_t=ya_t):
                    p.wait(e, "pet", ya_t)
                    p.wait(e, "pet")
                    ins = e.tensor_copy(out=ystg[:, :], in_=ps_y[bank][:, :])
                    p.inc(ins, "vet", 1)
                p.ve(vys)
                ys_t = p.val("vet")
                for g in range(4):
                    def ttr(e, g=g, ys_t=ys_t):
                        p.wait(e, "vet", ys_t)
                        p.wait(e, "vet")
                        ins = e.transpose(out=ps_tr[:, :], in_=ystg[:, g * 128:(g + 1) * 128],
                                          identity=C["c_id_bf"][:])
                        p.inc(ins, "pet", 1)
                    p.te(ttr)
                    tr_t = p.val("pet")

                    def vtr(e, g=g, tr_t=tr_t):
                        p.wait(e, "pet", tr_t)
                        p.wait(e, "dx")
                        ins = e.tensor_copy(out=ytr[:, :], in_=ps_tr[:, :])
                        p.inc(ins, "vet", 1)
                    p.ve(vtr)
                    vt_t = p.val("vet")

                    def syy(e, g=g, d0=d0, vt_t=vt_t):
                        p.wait(e, "vet", vt_t)
                        p.wait(e, "dx")
                        dd = d0 + g * 4
                        for di in range(4):
                            yv = scr["v_bf"][dd + di:dd + di + 1, :].rearrange(
                                "one (b c s) -> one c b s", b=2, c=16, s=128)
                            ins = e.dma_start(out=yv,
                                              in_=ytr[di * 32:(di + 1) * 32, :])
                            p.inc(ins, "dx", 16)
                    p.sy(syy)
    p.barrier()
    sbufs.close()
    sst.close()

    with ExitStack() as ls:
        lt, lrdy = load_weight_bf16(p, ls, W["W_out"], DM, "wWo0", 0, DM)
        _, dcol, drdy = load_cols(p, ls, W["D_param"], uq("s2D"))
        _, bocol, bordy = load_cols(p, ls, W["b_out"], uq("s2bo"))
        rbuf = p.sbuf(ls, uq("s2r"), [128, 2 * NDT * FC], BF16)
        ubuf = p.sbuf(ls, uq("s2u"), [128, NDT * FC], BF16)
        ybuf = p.sbuf(ls, uq("s2y"), [128, NDT * FC], BF16)
        stage = p.sbuf(ls, uq("s2e"), [128, 2 * FC], F32)
        axn = p.sbuf(ls, uq("s2x"), [128, 2 * FC], F32)
        psA = p.psum(ls, uq("s2A"), [128, FC])
        psB = p.psum(ls, uq("s2B"), [128, FC])
        banks = [psA, psB]
        bank_war = [None, None]
        rhs_rdy = [None, None]
        slot_war = [None, None]

        def rhs_prep(f, slot):
            dch = "dra" if slot == 0 else "drb"
            guard = slot_war[slot] or p.dguard.get(dch)

            def syf(e, f=f, slot=slot, dch=dch, guard=guard):
                if guard is not None:
                    p.wait(e, guard[0], guard[1])
                p.wait(e, "vet")
                for k in range(NDT):
                    ins = e.dma_start(out=ubuf[:, k * FC:(k + 1) * FC],
                                      in_=scr["q_bf"][k * 128:(k + 1) * 128,
                                                      f * FC:(f + 1) * FC])
                    p.inc(ins, dch, 16)
                    ins = e.dma_start(out=ybuf[:, k * FC:(k + 1) * FC],
                                      in_=scr["v_bf"][k * 128:(k + 1) * 128,
                                                      f * FC:(f + 1) * FC])
                    p.inc(ins, dch, 16)
            p.sy(syf)
            d_t = p.val(dch)

            def vf(e, f=f, slot=slot, dch=dch, d_t=d_t):
                p.wait(e, dch, d_t)
                p.wait(e, drdy[0], drdy[1])
                for k in range(NDT):
                    ins = e.scalar_tensor_tensor(
                        out=rbuf[:, (slot * NDT + k) * FC:(slot * NDT + k + 1) * FC],
                        in0=ubuf[:, k * FC:(k + 1) * FC], scalar=dcol(k),
                        in1=ybuf[:, k * FC:(k + 1) * FC], op0=ALU.mult, op1=ALU.add)
                    p.inc(ins, "vet", 1)
            p.ve(vf)
            rhs_rdy[slot] = ("vet", p.val("vet"))

        rhs_prep(0, 0)
        bi = 0
        for f in range(NFC):
            slot = f & 1
            if f + 1 < NFC:
                rhs_prep(f + 1, 1 - slot)
            for m in range(NDT):
                ps = banks[bi]

                def tf(e, f=f, m=m, ps=ps, war=bank_war[bi], rt=rhs_rdy[slot], slot=slot):
                    p.wait(e, lrdy[0], lrdy[1])
                    p.wait(e, rt[0], rt[1])
                    if war is not None:
                        p.wait(e, war[0], war[1])
                    for k in range(NDT):
                        ins = e.matmul(
                            out=ps[:, :], lhsT=lt(k)[:, m * 128:(m + 1) * 128],
                            rhs=rbuf[:, (slot * NDT + k) * FC:(slot * NDT + k + 1) * FC],
                            start=(k == 0), stop=(k == NDT - 1))
                    p.inc(ins, "pet", 1)
                p.te(tf)
                mm_t = p.val("pet")
                esl = (f * NDT + m) & 1

                def syx(e, m=m, f=f, esl=esl):
                    p.wait(e, "vet")
                    p.wait(e, "da")
                    ins = e.dma_start(out=axn[:, esl * FC:(esl + 1) * FC],
                                      in_=scr["xn_f32"][m * 128:(m + 1) * 128,
                                                        f * FC:(f + 1) * FC])
                    p.inc(ins, "da", 16)
                p.sy(syx)
                ax_t = p.val("da")

                def vf2(e, m=m, f=f, esl=esl, mm_t=mm_t, ax_t=ax_t):
                    p.wait(e, "pet", mm_t)
                    p.wait(e, "da", ax_t)
                    p.wait(e, bordy[0], bordy[1])
                    p.wait(e, "dx")
                    p.fence(e, e.tensor_scalar_add(out=stage[:, esl * FC:(esl + 1) * FC],
                                                   in0=ps[:, :], scalar1=bocol(m)), "vet")
                    ins = e.tensor_tensor(out=stage[:, esl * FC:(esl + 1) * FC],
                                          in0=stage[:, esl * FC:(esl + 1) * FC],
                                          in1=axn[:, esl * FC:(esl + 1) * FC], op=ALU.add)
                    p.inc(ins, "vet", 1)
                p.ve(vf2)
                ev_t = p.val("vet")
                bank_war[bi] = ("vet", ev_t)

                def sy2(e, m=m, f=f, esl=esl, ev_t=ev_t):
                    p.wait(e, "vet", ev_t)
                    p.wait(e, "dx")
                    ins = e.dma_start(out=scr["acc_a"][m * 128:(m + 1) * 128,
                                                       f * FC:(f + 1) * FC],
                                      in_=stage[:, esl * FC:(esl + 1) * FC])
                    p.inc(ins, "dx", 16)
                p.sy(sy2)
                bi ^= 1
            slot_war[slot] = ("pet", p.val("pet"))
        p.dguard["dra"] = ("pet", p.val("pet"))
        p.dguard["drb"] = ("pet", p.val("pet"))
    p.barrier()


def emit_moe(p, W, C, scr):
    EPS = 1e-5
    mst = ExitStack()
    lg_sb = p.sbuf(mst, "lg_sb", [8, TOK], F32)
    w_sb = p.sbuf(mst, "w_sb", [8, TOK], F32)
    with ExitStack() as ls:
        _, sfn, srdy = load_cols(p, ls, W["ln_scale"], uq("m0s"))
        _, bfn, brdy = load_cols(p, ls, W["ln_bias"], uq("m0b"))

        def join(e):
            p.wait(e, srdy[0], srdy[1])
            p.wait(e, brdy[0], brdy[1])
            ins = e.nop()
            p.inc(ins, "sct", 1)
        p.sc(join)
        emit_ln(p, scr["xn_f32"], scr["xg_f32"], scr["u_bf"], C, uq("gln"),
                gamma_col=sfn, beta_col=bfn)
    p.barrier()
    with ExitStack() as ls:
        gw = p.sbuf(ls, uq("mgw"), [128, NDT * 8], F32)
        grb = p.sbuf(ls, uq("mgr"), [128, NDT * FC], F32)
        ps_g = p.psum(ls, uq("mgp"), [8, FC])

        def syg(e):
            p.wait(e, "dwa")
            for k in range(NDT):
                ins = e.dma_start(out=gw[:, k * 8:(k + 1) * 8],
                                  in_=W["gate_w"][k * 128:(k + 1) * 128, :])
                p.inc(ins, "dwa", 16)
        p.sy(syg)
        gw_t = p.val("dwa")
        for f in range(NFC):
            def syr(e, f=f):
                p.wait(e, "pet")
                p.wait(e, "dra")
                for k in range(NDT):
                    ins = e.dma_start(out=grb[:, k * FC:(k + 1) * FC],
                                      in_=scr["xg_f32"][k * 128:(k + 1) * 128,
                                                        f * FC:(f + 1) * FC])
                    p.inc(ins, "dra", 16)
            p.sy(syr)
            r_t = p.val("dra")

            def tg(e, f=f, r_t=r_t):
                p.wait(e, "dwa", gw_t)
                p.wait(e, "dra", r_t)
                p.wait(e, "vet")
                for k in range(NDT):
                    ins = e.matmul(out=ps_g[0:8, :], lhsT=gw[:, k * 8:(k + 1) * 8],
                                   rhs=grb[:, k * FC:(k + 1) * FC],
                                   start=(k == 0), stop=(k == NDT - 1))
                p.inc(ins, "pet", 1)
            p.te(tg)
            g_t = p.val("pet")

            def vg(e, f=f, g_t=g_t):
                p.wait(e, "pet", g_t)
                ins = e.tensor_copy(out=lg_sb[:, f * FC:(f + 1) * FC], in_=ps_g[0:8, :])
                p.inc(ins, "vet", 1)
            p.ve(vg)
        p.dguard["dra"] = ("pet", p.val("pet"))
    p.barrier()

    with ExitStack() as ls:
        lgt = p.sbuf(ls, uq("m1l"), [128, 8], F32)
        prb = p.sbuf(ls, uq("m1p"), [128, 8], F32)
        ge1 = p.sbuf(ls, uq("m1g"), [128, 8], F32)
        msk = p.sbuf(ls, uq("m1m"), [128, 8], F32)
        wtk = p.sbuf(ls, uq("m1w"), [128, 8], F32)
        colb = p.sbuf(ls, uq("m1c"), [128, 8], F32)
        ps_t = p.psum(ls, uq("m1t"), [128, 8])
        ps_b = p.psum(ls, uq("m1b"), [8, 128])
        for tt in range(TOK // 128):
            def tf(e, tt=tt):
                p.wait(e, "vet")
                ins = e.transpose(out=ps_t[:, :], in_=lg_sb[:, tt * 128:(tt + 1) * 128],
                                  identity=C["c_id"][0:8, 0:8])
                p.inc(ins, "pet", 1)
            p.te(tf)
            t_t = p.val("pet")

            def v1(e, tt=tt, t_t=t_t):
                p.wait(e, "pet", t_t)
                p.fence(e, e.tensor_copy(out=lgt[:, :], in_=ps_t[:, :]), "vet")
                p.fence(e, e.reduce_max(out=colb[:, 0:1], in_=lgt[:, :], axis=AX), "vet")
                p.fence(e, e.tensor_scalar_mul(out=colb[:, 1:2], in0=colb[:, 0:1],
                                               scalar1=-1.0), "vet")
                ins = e.engine_nop()
                p.inc(ins, "vet", 1)
            p.ve(v1)
            v1_t = p.val("vet")

            def s1(e, tt=tt, v1_t=v1_t):
                p.wait(e, "vet", v1_t)
                ins = e.activation(out=prb[:, :], in_=lgt[:, :], func=AF.Exp,
                                   bias=colb[:, 1:2])
                p.inc(ins, "sct", 1)
            p.sc(s1)
            s1_t = p.val("sct")

            def v2(e, tt=tt, s1_t=s1_t):
                p.wait(e, "sct", s1_t)
                p.fence(e, e.reduce_sum(out=colb[:, 2:3], in_=prb[:, :], axis=AX), "vet")
                p.fence(e, e.reduce_max(out=colb[:, 3:4], in_=prb[:, :], axis=AX), "vet")
                p.fence(e, e.tensor_scalar(out=ge1[:, :], in0=prb[:, :],
                                           scalar1=colb[:, 3:4], scalar2=None,
                                           op0=ALU.is_ge), "vet")
                p.fence(e, e.tensor_scalar(out=msk[:, :], in0=ge1[:, :], scalar1=-1.0,
                                           scalar2=1.0, op0=ALU.mult, op1=ALU.add), "vet")
                p.fence(e, e.tensor_tensor(out=msk[:, :], in0=prb[:, :], in1=msk[:, :],
                                           op=ALU.mult), "vet")
                p.fence(e, e.reduce_max(out=colb[:, 4:5], in_=msk[:, :], axis=AX), "vet")
                p.fence(e, e.tensor_scalar(out=msk[:, :], in0=msk[:, :],
                                           scalar1=colb[:, 4:5], scalar2=None,
                                           op0=ALU.is_ge), "vet")
                p.fence(e, e.tensor_tensor(out=msk[:, :], in0=msk[:, :], in1=ge1[:, :],
                                           op=ALU.add), "vet")
                p.fence(e, e.tensor_tensor(out=colb[:, 5:6], in0=colb[:, 3:4],
                                           in1=colb[:, 4:5], op=ALU.add), "vet")
                p.fence(e, e.scalar_tensor_tensor(out=colb[:, 5:6], in0=colb[:, 2:3],
                                                  scalar=EPS, in1=colb[:, 5:6],
                                                  op0=ALU.mult, op1=ALU.add), "vet")
                p.fence(e, e.reciprocal(out=colb[:, 6:7], in_=colb[:, 5:6]), "vet")
                p.fence(e, e.tensor_tensor(out=wtk[:, :], in0=prb[:, :], in1=msk[:, :],
                                           op=ALU.mult), "vet")
                ins = e.tensor_scalar_mul(out=wtk[:, :], in0=wtk[:, :],
                                          scalar1=colb[:, 6:7])
                p.inc(ins, "vet", 1)
            p.ve(v2)
            v2_t = p.val("vet")

            def tf2(e, tt=tt, v2_t=v2_t):
                p.wait(e, "vet", v2_t)
                p.wait(e, "vet")
                ins = e.transpose(out=ps_b[:, :], in_=wtk[:, :],
                                  identity=C["c_id"][:])
                p.inc(ins, "pet", 1)
            p.te(tf2)
            t2_t = p.val("pet")

            def v3(e, tt=tt, t2_t=t2_t):
                p.wait(e, "pet", t2_t)
                ins = e.tensor_copy(out=w_sb[:, tt * 128:(tt + 1) * 128],
                                    in_=ps_b[0:8, :])
                p.inc(ins, "vet", 1)
            p.ve(v3)
    p.barrier()

    for ei in range(9):
        if ei == 0:
            wg_ap, wu_ap, wd_ap = W["sWg"], W["sWu"], W["sWd"]
        else:
            wg_ap, wu_ap, wd_ap = (W["Wg"][ei - 1], W["Wu"][ei - 1], W["Wd"][ei - 1])
        IH = 2048
        NMH = IH // 128
        for half in range(2):
            with ExitStack() as ls:
                mw = IH // 2
                lt, lrdy = load_weight_bf16(p, ls, wg_ap, DM, uq("mwg"), half * mw, mw)
                lt2, lrdy2 = load_weight_bf16(p, ls, wu_ap, DM, uq("mwu"), half * mw, mw)
                rbuf = p.sbuf(ls, uq("mr"), [128, 2 * NDT * FC], BF16)
                psA = p.psum(ls, uq("mA"), [128, FC])
                psB = p.psum(ls, uq("mB"), [128, FC])
                sig = p.sbuf(ls, uq("ms"), [128, FC], F32)
                hb = p.sbuf(ls, uq("mh"), [128, 2 * FC], BF16)
                slot_war = [None, None]
                rhs_tick = [None, None]

                def prefetch(f, slot):
                    dch = "dra" if slot == 0 else "drb"
                    guard = slot_war[slot] or p.dguard.get(dch)

                    def syf(e, f=f, slot=slot, dch=dch, guard=guard):
                        if guard is not None:
                            p.wait(e, guard[0], guard[1])
                        for k in range(NDT):
                            ins = e.dma_start(
                                out=rbuf[:, (slot * NDT + k) * FC:(slot * NDT + k + 1) * FC],
                                in_=scr["xn_bf"][k * 128:(k + 1) * 128,
                                                 f * FC:(f + 1) * FC])
                            p.inc(ins, dch, 16)
                    p.sy(syf)
                    rhs_tick[slot] = (dch, p.val(dch))
                prefetch(0, 0)
                for f in range(NFC):
                    slot = f & 1
                    if f + 1 < NFC:
                        prefetch(f + 1, 1 - slot)
                    for m in range(NMH // 2):
                        gm = half * (NMH // 2) + m

                        def tf(e, f=f, m=m, slot=slot, rt=rhs_tick[slot]):
                            p.wait(e, lrdy[0], lrdy[1])
                            p.wait(e, lrdy2[0], lrdy2[1])
                            p.wait(e, rt[0], rt[1])
                            p.wait(e, "vet")
                            p.wait(e, "sct")
                            for k in range(NDT):
                                e.matmul(out=psA[:, :],
                                         lhsT=lt(k)[:, m * 128:(m + 1) * 128],
                                         rhs=rbuf[:, (slot * NDT + k) * FC:(slot * NDT + k + 1) * FC],
                                         start=(k == 0), stop=(k == NDT - 1))
                            for k in range(NDT):
                                ins = e.matmul(out=psB[:, :],
                                               lhsT=lt2(k)[:, m * 128:(m + 1) * 128],
                                               rhs=rbuf[:, (slot * NDT + k) * FC:(slot * NDT + k + 1) * FC],
                                               start=(k == 0), stop=(k == NDT - 1))
                            p.inc(ins, "pet", 1)
                        p.te(tf)
                        mm_t = p.val("pet")

                        def sf(e, mm_t=mm_t):
                            p.wait(e, "pet", mm_t)
                            p.wait(e, "vet")
                            ins = e.activation(out=sig[:, :], in_=psA[:, :],
                                               func=AF.Sigmoid)
                            p.inc(ins, "sct", 1)
                        p.sc(sf)
                        sg_t = p.val("sct")

                        def vf(e, f=f, sg_t=sg_t, mm_t=mm_t):
                            p.wait(e, "sct", sg_t)
                            p.wait(e, "dx")
                            p.fence(e, e.tensor_tensor(out=sig[:, :], in0=psA[:, :],
                                                       in1=sig[:, :], op=ALU.mult), "vet")
                            ins = e.tensor_tensor(out=hb[:, (f & 1) * FC:((f & 1) + 1) * FC],
                                                  in0=sig[:, :], in1=psB[:, :],
                                                  op=ALU.mult)
                            p.inc(ins, "vet", 1)
                        p.ve(vf)
                        v_t = p.val("vet")

                        def sy2(e, gm=gm, f=f, v_t=v_t):
                            p.wait(e, "vet", v_t)
                            p.wait(e, "dx")
                            ins = e.dma_start(
                                out=scr["h_bf"][gm * 128:(gm + 1) * 128,
                                                f * FC:(f + 1) * FC],
                                in_=hb[:, (f & 1) * FC:((f & 1) + 1) * FC])
                            p.inc(ins, "dx", 16)
                        p.sy(sy2)
                    slot_war[slot] = ("pet", p.val("pet"))
                p.dguard["dra"] = ("pet", p.val("pet"))
                p.dguard["drb"] = ("pet", p.val("pet"))
            p.barrier()
        with ExitStack() as ls:
            lt, lrdy = load_weight_bf16(p, ls, wd_ap, IH, uq("mwd"), 0, DM)
            rbuf = p.sbuf(ls, uq("mdr"), [128, 2 * NMH * FC], BF16)
            wrow = p.sbuf(ls, uq("mwr"), [1, FC], F32)
            wbc = p.sbuf(ls, uq("mwb"), [128, FC], F32)
            acb = p.sbuf(ls, uq("mab"), [128, 2 * FC], F32)
            stage = p.sbuf(ls, uq("mst"), [128, 2 * FC], F32)
            psA = p.psum(ls, uq("mdA"), [128, FC])
            psB = p.psum(ls, uq("mdB"), [128, FC])
            ps_w = p.psum(ls, uq("mdW"), [128, FC])
            banks = [psA, psB]
            bank_war = [None, None]
            slot_war = [None, None]
            rhs_tick = [None, None]

            def prefetch(f, slot):
                dch = "dra" if slot == 0 else "drb"
                guard = slot_war[slot] or p.dguard.get(dch)

                def syf(e, f=f, slot=slot, dch=dch, guard=guard):
                    if guard is not None:
                        p.wait(e, guard[0], guard[1])
                    for k in range(NMH):
                        ins = e.dma_start(
                            out=rbuf[:, (slot * NMH + k) * FC:(slot * NMH + k + 1) * FC],
                            in_=scr["h_bf"][k * 128:(k + 1) * 128, f * FC:(f + 1) * FC])
                        p.inc(ins, dch, 16)
                p.sy(syf)
                rhs_tick[slot] = (dch, p.val(dch))
            prefetch(0, 0)
            bi = 0
            for f in range(NFC):
                slot = f & 1
                if f + 1 < NFC:
                    prefetch(f + 1, 1 - slot)
                if ei > 0:
                    def syw(e, f=f, ei=ei):
                        p.wait(e, "pet")
                        ins = e.dma_start(out=wrow[0:1, :],
                                          in_=w_sb[ei - 1:ei, f * FC:(f + 1) * FC])
                        p.inc(ins, "dww", 16)
                    p.sy(syw)
                    wr_t = p.val("dww")

                    def tbc(e, f=f, wr_t=wr_t):
                        p.wait(e, "dww", wr_t)
                        p.wait(e, "vet")
                        ins = e.matmul(out=ps_w[:, :], lhsT=C["c_ones_row"][:],
                                       rhs=wrow[0:1, :], start=True, stop=True)
                        p.inc(ins, "pet", 1)
                    p.te(tbc)
                    bc_t = p.val("pet")

                    def vbc(e, f=f, bc_t=bc_t):
                        p.wait(e, "pet", bc_t)
                        ins = e.tensor_copy(out=wbc[:, :], in_=ps_w[:, :])
                        p.inc(ins, "vet", 1)
                    p.ve(vbc)
                    wbc_t = p.val("vet")
                for m in range(NDT):
                    ps = banks[bi]

                    def tf(e, f=f, m=m, ps=ps, war=bank_war[bi], rt=rhs_tick[slot],
                           slot=slot):
                        p.wait(e, lrdy[0], lrdy[1])
                        p.wait(e, rt[0], rt[1])
                        if war is not None:
                            p.wait(e, war[0], war[1])
                        for k in range(NMH):
                            ins = e.matmul(
                                out=ps[:, :], lhsT=lt(k)[:, m * 128:(m + 1) * 128],
                                rhs=rbuf[:, (slot * NMH + k) * FC:(slot * NMH + k + 1) * FC],
                                start=(k == 0), stop=(k == NMH - 1))
                        p.inc(ins, "pet", 1)
                    p.te(tf)
                    mm_t = p.val("pet")
                    esl = (f * NDT + m) & 1
                    if ei > 0:
                        def sya(e, m=m, f=f, esl=esl):
                            p.wait(e, "vet")
                            p.wait(e, "da")
                            ins = e.dma_start(out=acb[:, esl * FC:(esl + 1) * FC],
                                              in_=scr["acc_b"][m * 128:(m + 1) * 128,
                                                               f * FC:(f + 1) * FC])
                            p.inc(ins, "da", 16)
                        p.sy(sya)
                        ac_t = p.val("da")

                        def vf(e, m=m, f=f, esl=esl, mm_t=mm_t, ac_t=ac_t, ps=ps,
                               wbc_t=wbc_t):
                            p.wait(e, "pet", mm_t)
                            p.wait(e, "da", ac_t)
                            p.wait(e, "vet", wbc_t)
                            p.wait(e, "dx")
                            p.fence(e, e.tensor_tensor(out=stage[:, esl * FC:(esl + 1) * FC],
                                                       in0=ps[:, :], in1=wbc[:, :],
                                                       op=ALU.mult), "vet")
                            ins = e.tensor_tensor(out=stage[:, esl * FC:(esl + 1) * FC],
                                                  in0=stage[:, esl * FC:(esl + 1) * FC],
                                                  in1=acb[:, esl * FC:(esl + 1) * FC],
                                                  op=ALU.add)
                            p.inc(ins, "vet", 1)
                        p.ve(vf)
                    else:
                        def vf(e, m=m, f=f, esl=esl, mm_t=mm_t, ps=ps):
                            p.wait(e, "pet", mm_t)
                            p.wait(e, "dx")
                            ins = e.tensor_copy(out=stage[:, esl * FC:(esl + 1) * FC],
                                                in_=ps[:, :])
                            p.inc(ins, "vet", 1)
                        p.ve(vf)
                    ev_t = p.val("vet")
                    bank_war[bi] = ("vet", ev_t)

                    def sy2(e, m=m, f=f, esl=esl, ev_t=ev_t):
                        p.wait(e, "vet", ev_t)
                        p.wait(e, "dx")
                        ins = e.dma_start(out=scr["acc_b"][m * 128:(m + 1) * 128,
                                                           f * FC:(f + 1) * FC],
                                          in_=stage[:, esl * FC:(esl + 1) * FC])
                        p.inc(ins, "dx", 16)
                    p.sy(sy2)
                    bi ^= 1
                slot_war[slot] = ("pet", p.val("pet"))
            p.dguard["dra"] = ("pet", p.val("pet"))
            p.dguard["drb"] = ("pet", p.val("pet"))
        p.barrier()
    mst.close()


def _emit_layer0(p, W, C, scr, x_in, x_out):
    emit_ln(p, x_in, scr["xn_f32"], scr["xn_bf"], C, uq("ln"))
    p.barrier()
    emit_ssm(p, {k.replace("L0_attn_", ""): W[k] for k in L0_W if "attn" in k},
             C, scr)
    emit_moe(p, {k.replace("L0_ffn_", ""): W[k] for k in L0_W if "ffn" in k},
             C, scr)
    emit_residual_add(p, x_in, scr["acc_a"], scr["acc_b"], x_out)
    p.barrier()


def prep_inputs(inputs):
    t = lambda a: np.ascontiguousarray(np.asarray(a).T)
    a = lambda v: np.ascontiguousarray(np.asarray(v))
    w = {}
    w["L0_attn_log_A"] = a(inputs["L0_attn_log_A"])
    w["L0_attn_B_param"] = a(inputs["L0_attn_B_param"])
    w["L0_attn_C_param"] = a(inputs["L0_attn_C_param"])
    w["L0_attn_D_param"] = a(inputs["L0_attn_D_param"]).reshape(DM, 1)
    w["L0_attn_log_dt"] = a(inputs["L0_attn_log_dt"]).reshape(DM, 1)
    w["L0_attn_ln_gamma"] = a(inputs["L0_attn_ln_gamma"]).reshape(DM, 1)
    w["L0_attn_ln_beta"] = a(inputs["L0_attn_ln_beta"]).reshape(DM, 1)
    w["L0_attn_W_out"] = t(inputs["L0_attn_W_out"])
    w["L0_attn_b_out"] = a(inputs["L0_attn_b_out"]).reshape(DM, 1)
    w["L0_ffn_ln_scale"] = a(inputs["L0_ffn_ln_scale"]).reshape(DM, 1)
    w["L0_ffn_ln_bias"] = a(inputs["L0_ffn_ln_bias"]).reshape(DM, 1)
    w["L0_ffn_gate_w"] = t(inputs["L0_ffn_gate_w"])
    for k in ("Wg", "Wu", "Wd", "sWg", "sWu", "sWd"):
        w[f"L0_ffn_{k}"] = a(inputs[f"L0_ffn_{k}"])
    for k in ("Wq", "Wk", "Wv", "Wo"):
        w[f"L1_attn_{k}"] = a(inputs[f"L1_attn_{k}"])
    w["L1_attn_bo"] = a(inputs["L1_attn_bo"]).reshape(DM, 1)
    w["L1_ffn_w_gate"] = t(inputs["L1_ffn_w_gate"])
    w["L1_ffn_w_up"] = t(inputs["L1_ffn_w_up"])
    w["L1_ffn_w_down"] = t(inputs["L1_ffn_w_down"])
    for k in ("Wq", "Wk", "Wv", "Wo"):
        w[f"L2_attn_{k}"] = t(inputs[f"L2_attn_{k}"])
    for k in ("bq", "bk", "bv", "bo"):
        w[f"L2_attn_{k}"] = a(inputs[f"L2_attn_{k}"]).reshape(DM, 1)
    for k in ("w1_w", "w2_w", "h1_w", "h2_w", "w3_w"):
        w[f"L2_ffn_{k}"] = t(inputs[f"L2_ffn_{k}"])
    for k in ("w1_s", "w2_s", "h1_s", "h2_s", "w3_s"):
        w[f"L2_ffn_{k}"] = a(inputs[f"L2_ffn_{k}"]).reshape(-1, 1)
    for k in ("Wq", "Wk", "Wv", "Wo"):
        w[f"L3_attn_{k}"] = t(inputs[f"L3_attn_{k}"])
    for k in ("cn_w1", "cn_w2", "cn_w3", "cn_w4", "gc_w1", "gc_w2", "gc_w3"):
        w[f"L3_ffn_{k}"] = a(inputs[f"L3_ffn_{k}"])
    for k in ("cn_b1", "cn_b2", "cn_b3", "cn_b4", "gc_b1", "gc_b2", "gc_b3"):
        w[f"L3_ffn_{k}"] = a(inputs[f"L3_ffn_{k}"]).reshape(-1, 1)
    w["L3_ffn_adaptive_range"] = a(inputs["L3_ffn_adaptive_range"]).reshape(1, 1)
    w.update(host_constants())
    return w


# version 04
# speedup vs baseline: 1.0467x; 1.0448x over previous
import numpy as np
from contextlib import ExitStack

import concourse.bass as bass
import concourse.mybir as mybir
from concourse.bass_utils import run_bass_kernel_spmd

F32 = mybir.dt.float32
BF16 = mybir.dt.bfloat16
I32 = mybir.dt.int32
AF = mybir.ActivationFunctionType
ALU = mybir.AluOpType
AX = mybir.AxisListType.X

DM = 1024
B = 2
T = 2048
TOK = B * T
NDT = DM // 128
FC = 512
NFC = TOK // FC
LN_EPS = 1e-5
NH = 16
HD = 64



class Prog:
    def __init__(self, nc, stack):
        self.nc = nc
        self.stack = stack
        self.blk = stack.enter_context(nc.Block())
        self._sems = {}
        self.dguard = {}

    def sem(self, name):
        if name not in self._sems:
            h = self.stack.enter_context(self.nc.semaphore(name))
            self._sems[name] = [h, 0]
        return self._sems[name]

    def inc(self, instr, name, k=1):
        s = self.sem(name)
        instr.then_inc(s[0], k)
        s[1] += k
        return s[1]

    def val(self, name):
        return self.sem(name)[1]

    def wait(self, eng, name, v=None):
        s = self.sem(name)
        eng.wait_ge(s[0], s[1] if v is None else v)

    def fence(self, e, ins, ch):
        v = self.inc(ins, ch, 1)
        self.wait(e, ch, v)

    def dma_pair(self):
        if not hasattr(self, "_pair_ctr"):
            self._pair_ctr = 0
        i = self._pair_ctr % 8
        self._pair_ctr += 1
        return (f"dp{i}a", f"dp{i}b")

    def sy(self, f):
        self.blk.sync(f)

    def ve(self, f):
        self.blk.vector(f)

    def sc(self, f):
        self.blk.scalar(f)

    def te(self, f):
        self.blk.tensor(f)

    def gp(self, f):
        self.blk.gpsimd(f)

    def barrier(self):
        dchs = [k for k in self._sems if k.startswith("d")]

        def drain(e):
            for ch in dchs:
                self.wait(e, ch)
        self.sy(drain)
        target = self.val("bar") + 5
        for emit, nm in ((self.sy, "sy"), (self.ve, "ve"), (self.sc, "sc"),
                         (self.te, "te"), (self.gp, "gp")):
            def f(e, nm=nm, target=target):
                ins = e.engine_nop() if nm in ("ve", "gp") else e.nop()
                self.inc(ins, "bar", 1)
                self.wait(e, "bar", target)
            emit(f)

    def sbuf(self, st, name, shape, dtype):
        return st.enter_context(self.nc.sbuf_tensor(name, list(shape), dtype))

    def psum(self, st, name, shape, dtype=F32):
        return st.enter_context(self.nc.psum_tensor(name, list(shape), dtype))


_uid = [0]


def uq(base):
    _uid[0] += 1
    return f"{base}{_uid[0]}"



def host_constants():
    c = {}
    c["c_id"] = np.eye(128, dtype=np.float32)
    c["c_ones_col"] = np.ones((128, 1), dtype=np.float32)
    c["c_i1024_col"] = np.full((128, 1), 1.0 / 1024.0, np.float32)
    c["c_eps_col"] = np.full((128, 1), 1e-5, np.float32)
    c["c_ones_row"] = np.ones((1, 128), dtype=np.float32)
    for r in range(4):
        kp = np.arange(128)[:, None]
        qf = np.arange(FC)[None, :]
        c[f"c_dmask{r}"] = (r * 128 + kp <= qf).astype(np.float32)
    c["c_iota_s"] = np.arange(128, dtype=np.float32)[None, :]
    c["c_iota_sm127"] = (np.arange(128, dtype=np.float32) - 127.0)[None, :]
    c["c_iota_t129"] = np.arange(129, dtype=np.float32)[None, :]
    s_ = np.arange(128)[:, None]
    t_ = np.arange(128)[None, :]
    ut = (t_ >= s_).astype(np.float32)
    c["c_utri2"] = np.concatenate([ut, ut], axis=1)
    for dt in range(8):
        hs = np.zeros((128, 16), np.float32)
        for k in range(128):
            hs[k, 2 * dt + k // 64] = 1.0
        c[f"c_hsel{dt}"] = hs
        he = np.zeros((16, 128), np.float32)
        for m in range(128):
            he[2 * dt + m // 64, m] = 1.0
        c[f"c_hexp{dt}"] = he
    return c


def emit_const_load(p, st, aps):
    C = {}
    f32_names = list(aps.keys())
    tensors = {}
    for nm in f32_names:
        shp = aps[nm].shape
        tensors[nm] = p.sbuf(st, "cs_" + nm, list(shp), F32)
        tensors[nm + "_bf"] = p.sbuf(st, "cb_" + nm, list(shp), BF16)

    def syf(e):
        for nm in f32_names:
            ins = e.dma_start(out=tensors[nm][:], in_=aps[nm])
            p.inc(ins, "dw", 16)
    p.sy(syf)
    dt_ = p.val("dw")

    def vf(e):
        p.wait(e, "dw", dt_)
        for nm in f32_names:
            ins = e.tensor_copy(out=tensors[nm + "_bf"][:], in_=tensors[nm][:])
        p.inc(ins, "vet", 1)
    p.ve(vf)
    for nm in f32_names:
        C[nm] = tensors[nm]
        C[nm + "_bf"] = tensors[nm + "_bf"]
    return C



def load_cols(p, st, hbm_ap, name, n=DM, dma="dw"):
    nt = max(1, (n + 127) // 128)
    tcol = p.sbuf(st, name, [128, nt], F32)

    def syf(e):
        p.wait(e, dma)
        for dt in range(nt):
            rows = min(128, n - dt * 128)
            ins = e.dma_start(out=tcol[0:rows, dt:dt + 1],
                              in_=hbm_ap[dt * 128:dt * 128 + rows, 0:1])
            p.inc(ins, dma, 16)
    p.sy(syf)
    tick = p.val(dma)
    return tcol, (lambda dt: tcol[:, dt:dt + 1]), (dma, tick)


def load_weight_bf16(p, st, W_ap, K, name, m0, mw, dtype_in=F32, scale=None,
                     cast_eng="gp", ring=2):
    nk = K // 128
    stage = p.sbuf(st, uq(name + "s"), [128, ring * mw], dtype_in)
    wbf = p.sbuf(st, uq(name + "b"), [128, nk * mw], BF16)
    emit, ch = {"gp": (p.gp, "gpt"), "ve": (p.ve, "vet"), "sc": (p.sc, "sct")}[cast_eng]
    slot_cast = [None] * ring
    for k in range(nk):
        sl = k % ring
        dch = "dwa" if sl == 0 else "dwb"

        guard = slot_cast[sl]
        if guard is not None:
            guard = (ch, guard)
        else:
            guard = p.dguard.get(dch)

        def syf(e, k=k, sl=sl, guard=guard, dch=dch):
            if guard is not None:
                p.wait(e, guard[0], guard[1])
            ins = e.dma_start(out=stage[:, sl * mw:(sl + 1) * mw],
                              in_=W_ap[k * 128:(k + 1) * 128, m0:m0 + mw])
            p.inc(ins, dch, 16)
        p.sy(syf)
        dtick = p.val(dch)

        def cf(e, k=k, sl=sl, dtick=dtick, dch=dch):
            p.wait(e, dch, dtick)
            src_ap = stage[:, sl * mw:(sl + 1) * mw]
            dst_ap = wbf[:, k * mw:(k + 1) * mw]
            if cast_eng == "sc":
                ins = e.activation(out=dst_ap, in_=src_ap, func=AF.Copy,
                                   scale=1.0 if scale is None else scale)
            elif scale is None:
                ins = e.tensor_copy(out=dst_ap, in_=src_ap)
            else:
                ins = e.tensor_scalar_mul(out=dst_ap, in0=src_ap, scalar1=scale)
            p.inc(ins, ch, 1)
        emit(cf)
        slot_cast[sl] = p.val(ch)
    tick = p.val(ch)
    p.dguard["dwa"] = (ch, tick)
    p.dguard["dwb"] = (ch, tick)
    return (lambda k: wbf[:, k * mw:(k + 1) * mw]), (ch, tick)


def mm_stream(p, st, *, name, lhsT_tile, nK, nM, lhs_ready, rhs_hbm, out_cb,
              fw=FC, nF=NFC, mw=128, mh=128):
    rbuf = p.sbuf(st, uq(name + "r"), [128, 2 * nK * fw], BF16)
    psA = p.psum(st, uq(name + "pA"), [128, fw])
    psB = p.psum(st, uq(name + "pB"), [128, fw])
    banks = [psA, psB]
    bank_war = [None, None]
    slot_war = [None, None]
    rhs_tick = [None, None]

    def prefetch(f, slot):
        dch = "dra" if slot == 0 else "drb"
        guard = slot_war[slot] or p.dguard.get(dch)

        def syf(e, f=f, slot=slot, dch=dch, guard=guard):
            if guard is not None:
                p.wait(e, guard[0], guard[1])
            for k in range(nK):
                ins = e.dma_start(
                    out=rbuf[:, (slot * nK + k) * fw:(slot * nK + k + 1) * fw],
                    in_=rhs_hbm(k, f))
                p.inc(ins, dch, 16)
        p.sy(syf)
        rhs_tick[slot] = (dch, p.val(dch))

    prefetch(0, 0)
    bi = 0
    for f in range(nF):
        slot = f & 1
        if f + 1 < nF:
            prefetch(f + 1, 1 - slot)
        for m in range(nM):
            ps = banks[bi]
            war = bank_war[bi]

            def tf(e, f=f, m=m, ps=ps, war=war, slot=slot, rt=rhs_tick[slot]):
                if lhs_ready is not None:
                    p.wait(e, lhs_ready[0], lhs_ready[1])
                p.wait(e, rt[0], rt[1])
                if war is not None:
                    p.wait(e, war[0], war[1])
                for k in range(nK):
                    ins = e.matmul(
                        out=ps[0:mh, 0:fw], lhsT=lhsT_tile(k, m),
                        rhs=rbuf[:, (slot * nK + k) * fw:(slot * nK + k + 1) * fw],
                        start=(k == 0), stop=(k == nK - 1))
                p.inc(ins, "pet", 1)
            p.te(tf)
            bank_war[bi] = out_cb(m, f, ps)
            bi ^= 1
        slot_war[slot] = ("pet", p.val("pet"))
    p.dguard["dra"] = ("pet", p.val("pet"))
    p.dguard["drb"] = ("pet", p.val("pet"))


def evac_to_hbm(p, st, *, name, out_hbm, out_dtype=BF16, bias_col=None,
                scale_col=None, func=None, add_hbm=None, mh=128, extra_ready=None):
    stage = p.sbuf(st, uq(name + "e"), [128, 2 * FC], out_dtype)
    chA, chB = p.dma_pair()
    store_tick = {0: p.dguard.get(chA), 1: p.dguard.get(chB)}
    if add_hbm is not None:
        addb = p.sbuf(st, uq(name + "a"), [128, 2 * FC], F32)
        chC, chD = p.dma_pair()
        add_tick = {0: p.dguard.get(chC), 1: p.dguard.get(chD)}
        add_done = {0: None, 1: None}
    slot_ctr = [0]

    def cb(m, f, ps):
        sl = slot_ctr[0] & 1
        slot_ctr[0] += 1
        ssl = slice(sl * FC, (sl + 1) * FC)
        ch = chA if sl == 0 else chB
        pet_t = p.val("pet")
        if add_hbm is not None:
            cha = chC if sl == 0 else chD

            def syf(e, m=m, f=f, sl=sl, cha=cha, war=add_done[sl], g=add_tick[sl]):
                if war is not None:
                    p.wait(e, war[0], war[1])
                elif g is not None:
                    p.wait(e, g[0], g[1])
                ins = e.dma_start(out=addb[:, ssl],
                                  in_=add_hbm[m * mh:(m + 1) * mh, f * FC:(f + 1) * FC])
                p.inc(ins, cha, 16)
            p.sy(syf)
            a_t = (cha, p.val(cha))
        if func is not None or (add_hbm is None and (bias_col is not None or
                                                     scale_col is not None)):
            use_scalar = func is not None

            def sf(e, m=m, f=f, sl=sl, st_t=store_tick[sl]):
                p.wait(e, "pet", pet_t)
                if extra_ready is not None:
                    p.wait(e, extra_ready[0], extra_ready[1])
                if st_t is not None:
                    p.wait(e, st_t[0], st_t[1])
                if use_scalar:
                    kw = {}
                    if bias_col is not None:
                        kw["bias"] = bias_col(m)
                    if scale_col is not None:
                        kw["scale"] = scale_col(m)
                    ins = e.activation(out=stage[:mh, ssl], in_=ps[:mh, 0:FC],
                                       func=func, **kw)
                    p.inc(ins, "sct", 1)
                else:
                    if bias_col is not None and scale_col is not None:
                        ins = e.tensor_scalar(out=stage[:mh, ssl], in0=ps[:mh, 0:FC],
                                              scalar1=scale_col(m), scalar2=bias_col(m),
                                              op0=ALU.mult, op1=ALU.add)
                    elif bias_col is not None:
                        ins = e.tensor_scalar_add(out=stage[:mh, ssl], in0=ps[:mh, 0:FC],
                                                  scalar1=bias_col(m))
                    else:
                        ins = e.tensor_scalar_mul(out=stage[:mh, ssl], in0=ps[:mh, 0:FC],
                                                  scalar1=scale_col(m))
                    p.inc(ins, "vet", 1)
            if use_scalar:
                p.sc(sf)
                done = ("sct", p.val("sct"))
            else:
                p.ve(sf)
                done = ("vet", p.val("vet"))
        elif add_hbm is not None:
            def vf(e, m=m, f=f, sl=sl, st_t=store_tick[sl], a_t=a_t):
                p.wait(e, "pet", pet_t)
                p.wait(e, a_t[0], a_t[1])
                if st_t is not None:
                    p.wait(e, st_t[0], st_t[1])
                ins = e.tensor_tensor(out=stage[:mh, ssl], in0=ps[:mh, 0:FC],
                                      in1=addb[:mh, ssl], op=ALU.add)
                p.inc(ins, "vet", 1)
            p.ve(vf)
            done = ("vet", p.val("vet"))
        else:
            def vf(e, m=m, f=f, sl=sl, st_t=store_tick[sl]):
                p.wait(e, "pet", pet_t)
                if st_t is not None:
                    p.wait(e, st_t[0], st_t[1])
                ins = e.tensor_copy(out=stage[:mh, ssl], in_=ps[:mh, 0:FC])
                p.inc(ins, "vet", 1)
            p.ve(vf)
            done = ("vet", p.val("vet"))
        if add_hbm is not None:
            add_done[sl] = done

        def syf2(e, m=m, f=f, sl=sl, ch=ch, done=done):
            p.wait(e, done[0], done[1])
            ins = e.dma_start(out=out_hbm[m * mh:(m + 1) * mh, f * FC:(f + 1) * FC],
                              in_=stage[:mh, ssl])
            p.inc(ins, ch, 16)
        p.sy(syf2)
        store_tick[sl] = (ch, p.val(ch))
        p.dguard[ch] = store_tick[sl]
        if add_hbm is not None:
            p.dguard[chA if sl == 0 else chB] = store_tick[sl]
            p.dguard[cha] = a_t
        return done
    return cb



def emit_ln(p, x_hbm, xn_f32_hbm, xn_bf_hbm, C, name, gamma_col=None,
            beta_col=None, src_rdy=None):
    with ExitStack() as ls:
        XT = p.sbuf(ls, name + "xt", [128, NDT * FC], F32)
        SQ = p.sbuf(ls, name + "sq", [128, NDT * FC], BF16)
        XB = p.sbuf(ls, name + "xb", [128, NDT * FC], BF16)
        XN = p.sbuf(ls, name + "xn", [128, NDT * FC], F32)
        XNB = p.sbuf(ls, name + "xnb", [128, NDT * FC], BF16)
        ROWS = p.sbuf(ls, name + "rw", [1, 6 * FC], F32)
        ps_s = p.psum(ls, name + "ps", [1, FC])
        ps_q = p.psum(ls, name + "pq", [1, FC])
        ps_r = p.psum(ls, name + "pr", [128, FC])
        ps_m = p.psum(ls, name + "pm", [128, FC])
        mean, m2, var, sqv, rstd, mrs = (ROWS[:, i * FC:(i + 1) * FC] for i in range(6))

        for j in range(NFC):
            def syf(e, j=j):
                if src_rdy is not None:
                    p.wait(e, src_rdy[0], src_rdy[1])
                p.wait(e, "dx")
                if j >= 1:
                    p.wait(e, "vet")
                for dt in range(NDT):
                    ins = e.dma_start(out=XT[:, dt * FC:(dt + 1) * FC],
                                      in_=x_hbm[dt * 128:(dt + 1) * 128,
                                                j * FC:(j + 1) * FC])
                    p.inc(ins, "dx", 16)
            p.sy(syf)
            dtick = p.val("dx")

            def vf(e, j=j):
                p.wait(e, "dx", dtick)
                if j >= 1:
                    p.wait(e, "pet")
                ins = e.tensor_copy(out=XB[:], in_=XT[:])
                p.inc(ins, "vet", 1)
            p.ve(vf)
            xb_t = p.val("vet")

            def sf(e, j=j):
                p.wait(e, "dx", dtick)
                if j >= 1:
                    p.wait(e, "pet")
                ins = e.activation(out=SQ[:], in_=XT[:], func=AF.Square)
                p.inc(ins, "sct", 1)
            p.sc(sf)
            sq_t = p.val("sct")

            def tf(e, j=j):
                p.wait(e, "vet", xb_t)
                p.wait(e, "sct", sq_t)
                for dt in range(NDT):
                    e.matmul(out=ps_s[0:1, :], lhsT=C["c_i1024_col_bf"][:],
                             rhs=XB[:, dt * FC:(dt + 1) * FC], start=(dt == 0),
                             stop=(dt == NDT - 1))
                for dt in range(NDT):
                    ins = e.matmul(out=ps_q[0:1, :], lhsT=C["c_i1024_col_bf"][:],
                                   rhs=SQ[:, dt * FC:(dt + 1) * FC], start=(dt == 0),
                                   stop=(dt == NDT - 1))
                p.inc(ins, "pet", 1)
            p.te(tf)
            st_t = p.val("pet")

            def vf2(e, j=j):
                p.wait(e, "pet", st_t)
                p.fence(e, e.tensor_copy(out=mean, in_=ps_s[0:1, :]), "vet")
                p.fence(e, e.tensor_tensor(out=m2, in0=mean, in1=mean,
                                           op=ALU.mult), "vet")
                ins = e.tensor_tensor(out=var, in0=ps_q[0:1, :], in1=m2, op=ALU.subtract)
                p.inc(ins, "vet", 1)
            p.ve(vf2)
            v_t = p.val("vet")

            def sf2(e, j=j):
                p.wait(e, "vet", v_t)
                ins = e.activation(out=sqv, in_=var, func=AF.Sqrt, bias=C["c_eps_col"][0:1, 0:1])
                p.inc(ins, "sct", 1)
            p.sc(sf2)
            s_t = p.val("sct")

            def vf3(e, j=j):
                p.wait(e, "sct", s_t)
                p.fence(e, e.reciprocal(out=rstd, in_=sqv), "vet")
                ins = e.tensor_tensor(out=mrs, in0=mean, in1=rstd, op=ALU.mult)
                p.inc(ins, "vet", 1)
            p.ve(vf3)
            r_t = p.val("vet")

            def tf2(e, j=j):
                p.wait(e, "vet", r_t)
                e.matmul(out=ps_r[:, :], lhsT=C["c_ones_row"][:], rhs=rstd,
                         start=True, stop=True)
                ins = e.matmul(out=ps_m[:, :], lhsT=C["c_ones_row"][:], rhs=mrs,
                               start=True, stop=True)
                p.inc(ins, "pet", 1)
            p.te(tf2)
            bc_t = p.val("pet")

            def vf4(e, j=j):
                p.wait(e, "pet", bc_t)
                if j >= 1:
                    p.wait(e, "dx")
                for dt in range(NDT):
                    sl = slice(dt * FC, (dt + 1) * FC)
                    p.fence(e, e.tensor_tensor(out=XN[:, sl], in0=XT[:, sl],
                                               in1=ps_r[:, :], op=ALU.mult), "vet")
                    ins = e.tensor_tensor(out=XN[:, sl], in0=XN[:, sl], in1=ps_m[:, :],
                                          op=ALU.subtract)
                p.inc(ins, "vet", 1)
            p.ve(vf4)
            xn_t = ("vet", p.val("vet"))

            if gamma_col is not None:
                def vf4b(e, j=j, xn_t=xn_t):
                    p.wait(e, xn_t[0], xn_t[1])
                    for dt in range(NDT):
                        sl = slice(dt * FC, (dt + 1) * FC)
                        p.fence(e, e.tensor_scalar(out=XN[:, sl], in0=XN[:, sl],
                                                   scalar1=gamma_col(dt),
                                                   scalar2=beta_col(dt),
                                                   op0=ALU.mult, op1=ALU.add), "vet")
                    ins = e.engine_nop()
                    p.inc(ins, "vet", 1)
                p.ve(vf4b)
                xn_t = ("vet", p.val("vet"))

            def vf5(e, j=j, xn_t=xn_t):
                p.wait(e, xn_t[0], xn_t[1])
                ins = e.tensor_copy(out=XNB[:], in_=XN[:])
                p.inc(ins, "vet", 1)
            p.ve(vf5)
            cb_t = p.val("vet")

            def syf2(e, j=j, xn_t=xn_t):
                p.wait(e, xn_t[0], xn_t[1])
                p.wait(e, "vet", cb_t)
                p.wait(e, "dx")
                if xn_f32_hbm is not None:
                    for dt in range(NDT):
                        ins = e.dma_start(
                            out=xn_f32_hbm[dt * 128:(dt + 1) * 128, j * FC:(j + 1) * FC],
                            in_=XN[:, dt * FC:(dt + 1) * FC])
                        p.inc(ins, "dx", 16)
                for dt in range(NDT):
                    ins = e.dma_start(
                        out=xn_bf_hbm[dt * 128:(dt + 1) * 128, j * FC:(j + 1) * FC],
                        in_=XNB[:, dt * FC:(dt + 1) * FC])
                    p.inc(ins, "dx", 16)
            p.sy(syf2)



def emit_residual_add(p, x_in, a_hbm, b_hbm, x_out):
    CH = 2048
    with ExitStack() as ls:
        X0 = p.sbuf(ls, uq("raX"), [128, CH], F32)
        A0 = p.sbuf(ls, uq("raA"), [128, CH], F32)
        B0 = p.sbuf(ls, uq("raB"), [128, CH], F32)
        O0 = p.sbuf(ls, uq("raO"), [128, CH], F32)
        for dt in range(NDT):
            for jj in range(TOK // CH):
                sl_r = slice(dt * 128, (dt + 1) * 128)
                sl_c = slice(jj * CH, (jj + 1) * CH)

                def syf(e, sl_r=sl_r, sl_c=sl_c):
                    p.wait(e, "vet")
                    p.wait(e, "dx")
                    ins = e.dma_start(out=X0[:], in_=x_in[sl_r, sl_c])
                    p.inc(ins, "dx", 16)
                    ins = e.dma_start(out=A0[:], in_=a_hbm[sl_r, sl_c])
                    p.inc(ins, "dx", 16)
                    ins = e.dma_start(out=B0[:], in_=b_hbm[sl_r, sl_c])
                    p.inc(ins, "dx", 16)
                p.sy(syf)
                d_t = p.val("dx")

                def vf(e, sl_r=sl_r, sl_c=sl_c):
                    p.wait(e, "dx", d_t)
                    p.fence(e, e.tensor_tensor(out=O0[:], in0=X0[:], in1=A0[:],
                                                 op=ALU.add), "vet")
                    ins = e.tensor_tensor(out=O0[:], in0=O0[:], in1=B0[:], op=ALU.add)
                    p.inc(ins, "vet", 1)
                p.ve(vf)
                v_t = p.val("vet")

                def syf2(e, sl_r=sl_r, sl_c=sl_c, v_t=v_t):
                    p.wait(e, "vet", v_t)
                    p.wait(e, "dx")
                    ins = e.dma_start(out=x_out[sl_r, sl_c], in_=O0[:])
                    p.inc(ins, "dx", 16)
                p.sy(syf2)



def emit_mha(p, W, C, scr, bo_ap=None):
    for wn, dst in (("Wq", "q_bf"), ("Wk", "k_bf"), ("Wv", "v_bf")):
        with ExitStack() as ls:
            lt, lrdy = load_weight_bf16(p, ls, W[wn], DM, "w" + wn, 0, DM)
            cb = evac_to_hbm(p, ls, name="e" + wn, out_hbm=scr[dst], out_dtype=BF16)
            mm_stream(p, ls, name="m" + wn,
                      lhsT_tile=lambda k, m, lt=lt: lt(k)[:, m * 128:(m + 1) * 128],
                      nK=NDT, nM=NDT, lhs_ready=lrdy,
                      rhs_hbm=lambda k, f: scr["xn_bf"][k * 128:(k + 1) * 128,
                                                        f * FC:(f + 1) * FC],
                      out_cb=cb)
        p.barrier()

    NKT = T // 128
    with ExitStack() as ls:
        qh = p.sbuf(ls, uq("qh"), [64, T], BF16)
        kh = p.sbuf(ls, uq("kh"), [64, T], BF16)
        vh = p.sbuf(ls, uq("vh"), [64, T], BF16)
        vtok = p.sbuf(ls, uq("vtok"), [128, NKT * 64], BF16)
        sqb = p.sbuf(ls, uq("sqb"), [64, T], BF16)
        rowb = p.sbuf(ls, uq("rowb"), [1, 16], F32)
        bcol = p.sbuf(ls, uq("bcol"), [128, 1], F32)
        expT = p.sbuf(ls, uq("expT"), [128, NKT * FC], BF16)
        etmp = p.sbuf(ls, uq("etmp"), [128, FC], BF16)
        sumr = p.sbuf(ls, uq("sumr"), [1, 2 * FC], F32)
        rrow = p.sbuf(ls, uq("rrow"), [1, FC], BF16)
        rbc = p.sbuf(ls, uq("rbc"), [64, FC], F32)
        ctxs = p.sbuf(ls, uq("ctxs"), [64, 2 * FC], BF16)
        ps_sc = [p.psum(ls, uq("psc"), [128, FC]) for _ in range(2)]
        ps_sum = p.psum(ls, uq("psum"), [1, FC])
        ps_ctx = p.psum(ls, uq("pctx"), [64, FC])
        ps_aux = p.psum(ls, uq("paux"), [128, FC])
        ps_vt = p.psum(ls, uq("pvt"), [128, 64], BF16)

        for b in range(B):
            for h in range(NH):
                r0, c0 = h * HD, b * T
                consumed = ("pet", p.val("pet"))

                def syf(e, r0=r0, c0=c0, consumed=consumed):
                    p.wait(e, consumed[0], consumed[1])
                    p.wait(e, "sct")
                    ins = e.dma_start(out=qh[:], in_=scr["q_bf"][r0:r0 + 64, c0:c0 + T])
                    p.inc(ins, "dr", 16)
                    ins = e.dma_start(out=kh[:], in_=scr["k_bf"][r0:r0 + 64, c0:c0 + T])
                    p.inc(ins, "dr", 16)
                    ins = e.dma_start(out=vh[:], in_=scr["v_bf"][r0:r0 + 64, c0:c0 + T])
                    p.inc(ins, "dr", 16)
                p.sy(syf)
                ld_t = p.val("dr")

                ev_t = p.val("vet")
                for kt in range(NKT):
                    def tf(e, kt=kt, ev_t=ev_t):
                        p.wait(e, "dr", ld_t)
                        p.wait(e, "vet", ev_t)
                        ins = e.transpose(out=ps_vt[:, 0:64],
                                          in_=vh[:, kt * 128:(kt + 1) * 128],
                                          identity=C["c_id_bf"][0:64, 0:64])
                        p.inc(ins, "pet", 1)
                    p.te(tf)
                    tr_t = p.val("pet")

                    def vf(e, kt=kt, tr_t=tr_t):
                        p.wait(e, "pet", tr_t)
                        ins = e.tensor_copy(out=vtok[:, kt * 64:(kt + 1) * 64],
                                            in_=ps_vt[:, 0:64])
                        p.inc(ins, "vet", 1)
                    p.ve(vf)
                    ev_t = p.val("vet")
                vtok_t = p.val("vet")

                def sf(e):
                    p.wait(e, "dr", ld_t)
                    p.wait(e, "pet")
                    ins = e.activation(out=sqb[:], in_=qh[:], func=AF.Square)
                    p.inc(ins, "sct", 1)
                p.sc(sf)
                q2_t = p.val("sct")
                for src, base, rdy in ((0, 0, q2_t), (1, 4, None)):
                    if src == 1:
                        def sf2(e):
                            p.wait(e, "dr", ld_t)
                            p.wait(e, "pet")
                            ins = e.activation(out=sqb[:], in_=kh[:], func=AF.Square)
                            p.inc(ins, "sct", 1)
                        p.sc(sf2)
                        rdy = p.val("sct")
                    for ft in range(4):
                        def tf2(e, ft=ft, rdy=rdy):
                            p.wait(e, "sct", rdy)
                            p.wait(e, "vet")
                            ins = e.matmul(out=ps_sum[0:1, :],
                                           lhsT=C["c_ones_col_bf"][0:64, :],
                                           rhs=sqb[:, ft * FC:(ft + 1) * FC],
                                           start=True, stop=True)
                            p.inc(ins, "pet", 1)
                        p.te(tf2)
                        m_t = p.val("pet")

                        def vf2(e, ft=ft, base=base, m_t=m_t):
                            p.wait(e, "pet", m_t)
                            ins = e.reduce_max(out=rowb[:, base + ft:base + ft + 1],
                                               in_=ps_sum[0:1, :], axis=AX)
                            p.inc(ins, "vet", 1)
                        p.ve(vf2)
                nrm_t = p.val("vet")

                def vf3(e, nrm_t=nrm_t):
                    p.wait(e, "vet", nrm_t)
                    e.reduce_max(out=rowb[:, 8:9], in_=rowb[:, 0:4], axis=AX)
                    ins = e.reduce_max(out=rowb[:, 9:10], in_=rowb[:, 4:8], axis=AX)
                    p.inc(ins, "vet", 1)
                p.ve(vf3)
                mx_t = p.val("vet")

                def sf3(e, mx_t=mx_t):
                    p.wait(e, "vet", mx_t)
                    ins = e.activation(out=rowb[:, 10:12], in_=rowb[:, 8:10],
                                       func=AF.Sqrt)
                    p.inc(ins, "sct", 1)
                p.sc(sf3)
                sq2_t = p.val("sct")

                def vf4(e, sq2_t=sq2_t):
                    p.wait(e, "sct", sq2_t)
                    p.fence(e, e.tensor_tensor(out=rowb[:, 12:13], in0=rowb[:, 10:11],
                                                 in1=rowb[:, 11:12], op=ALU.mult), "vet")
                    ins = e.tensor_scalar_mul(out=rowb[:, 13:14], in0=rowb[:, 12:13],
                                              scalar1=-0.125)
                    p.inc(ins, "vet", 1)
                p.ve(vf4)
                bp_t = p.val("vet")

                def tf3(e, bp_t=bp_t):
                    p.wait(e, "vet", bp_t)
                    p.wait(e, "sct")
                    ins = e.matmul(out=ps_aux[:, 0:1], lhsT=C["c_ones_row"][:],
                                   rhs=rowb[:, 13:14], start=True, stop=True)
                    p.inc(ins, "pet", 1)
                p.te(tf3)
                bcm_t = p.val("pet")

                def vf5(e, bcm_t=bcm_t):
                    p.wait(e, "pet", bcm_t)
                    ins = e.tensor_copy(out=bcol[:], in_=ps_aux[:, 0:1])
                    p.inc(ins, "vet", 1)
                p.ve(vf5)
                bcol_t = p.val("vet")

                for j in range(4):
                    nkt = 4 * (j + 1)
                    exp_done = []
                    for kt in range(nkt):
                        ps = ps_sc[kt & 1]

                        def tf4(e, kt=kt, j=j, ps=ps):
                            p.wait(e, "vet", bcol_t)
                            p.wait(e, "sct")
                            ins = e.matmul(out=ps[:, :],
                                           lhsT=kh[:, kt * 128:(kt + 1) * 128],
                                           rhs=qh[:, j * FC:(j + 1) * FC],
                                           start=True, stop=True)
                            p.inc(ins, "pet", 1)
                        p.te(tf4)
                        sc_t = p.val("pet")
                        diag = kt >= 4 * j

                        def sf4(e, kt=kt, ps=ps, sc_t=sc_t, diag=diag):
                            p.wait(e, "pet", sc_t)
                            if diag:
                                p.wait(e, "vet")
                            tgt = etmp[:, :] if diag else expT[:, kt * FC:(kt + 1) * FC]
                            ins = e.activation(out=tgt, in_=ps[:, :], func=AF.Exp,
                                               scale=0.125, bias=bcol[:, 0:1])
                            p.inc(ins, "sct", 1)
                        p.sc(sf4)
                        e_t = p.val("sct")
                        if diag:
                            def vf6(e, kt=kt, j=j, e_t=e_t):
                                p.wait(e, "sct", e_t)
                                ins = e.tensor_tensor(
                                    out=expT[:, kt * FC:(kt + 1) * FC], in0=etmp[:, :],
                                    in1=C[f"c_dmask{kt - 4 * j}_bf"][:], op=ALU.mult)
                                p.inc(ins, "vet", 1)
                            p.ve(vf6)
                    exp_s, exp_v = p.val("sct"), p.val("vet")

                    def tf5(e, nkt=nkt, exp_s=exp_s, exp_v=exp_v):
                        p.wait(e, "sct", exp_s)
                        p.wait(e, "vet", exp_v)
                        for kt in range(nkt):
                            ins = e.matmul(out=ps_sum[0:1, :],
                                           lhsT=C["c_ones_col_bf"][:],
                                           rhs=expT[:, kt * FC:(kt + 1) * FC],
                                           start=(kt == 0), stop=(kt == nkt - 1))
                        p.inc(ins, "pet", 1)
                    p.te(tf5)
                    su_t = p.val("pet")
                    sl = j & 1

                    def vf7(e, su_t=su_t, sl=sl):
                        p.wait(e, "pet", su_t)
                        p.fence(e, e.tensor_copy(out=sumr[:, 0:FC],
                                                   in_=ps_sum[0:1, :]), "vet")
                        p.fence(e, e.reciprocal(out=sumr[:, FC:2 * FC],
                                                in_=sumr[:, 0:FC]), "vet")
                        ins = e.tensor_copy(out=rrow[:, :], in_=sumr[:, FC:2 * FC])
                        p.inc(ins, "vet", 1)
                    p.ve(vf7)
                    rc_t = p.val("vet")

                    def tf6(e, rc_t=rc_t):
                        p.wait(e, "vet", rc_t)
                        ins = e.matmul(out=ps_aux[0:64, :],
                                       lhsT=C["c_ones_row_bf"][:, 0:64],
                                       rhs=rrow[:, :], start=True, stop=True)
                        p.inc(ins, "pet", 1)
                    p.te(tf6)
                    rb_t = p.val("pet")

                    def vf8(e, rb_t=rb_t):
                        p.wait(e, "pet", rb_t)
                        ins = e.tensor_copy(out=rbc[:, :], in_=ps_aux[0:64, :])
                        p.inc(ins, "vet", 1)
                    p.ve(vf8)
                    rbc_t = p.val("vet")

                    def tf7(e, nkt=nkt, vtok_t=vtok_t, rbc_t=rbc_t):
                        p.wait(e, "vet", rbc_t)
                        for kt in range(nkt):
                            ins = e.matmul(out=ps_ctx[0:64, :],
                                           lhsT=vtok[:, kt * 64:(kt + 1) * 64],
                                           rhs=expT[:, kt * FC:(kt + 1) * FC],
                                           start=(kt == 0), stop=(kt == nkt - 1))
                        p.inc(ins, "pet", 1)
                    p.te(tf7)
                    cx_t = p.val("pet")

                    def vf9(e, cx_t=cx_t, sl=sl):
                        p.wait(e, "pet", cx_t)
                        p.wait(e, "dx")
                        ins = e.tensor_tensor(out=ctxs[:, sl * FC:(sl + 1) * FC],
                                              in0=ps_ctx[0:64, :], in1=rbc[:, :],
                                              op=ALU.mult)
                        p.inc(ins, "vet", 1)
                    p.ve(vf9)
                    cs_t = p.val("vet")

                    def syf2(e, j=j, cs_t=cs_t, sl=sl, r0=r0, c0=c0):
                        p.wait(e, "vet", cs_t)
                        p.wait(e, "dx")
                        ins = e.dma_start(
                            out=scr["ctx_bf"][r0:r0 + 64, c0 + j * FC:c0 + (j + 1) * FC],
                            in_=ctxs[:, sl * FC:(sl + 1) * FC])
                        p.inc(ins, "dx", 16)
                    p.sy(syf2)
    p.barrier()

    with ExitStack() as ls:
        lt, lrdy = load_weight_bf16(p, ls, W["Wo"], DM, "wWo", 0, DM)
        if bo_ap is not None:
            _, bofn, bordy = load_cols(p, ls, bo_ap, uq("bo"))
            cb = evac_to_hbm(p, ls, name="eWo", out_hbm=scr["acc_a"], out_dtype=F32,
                             bias_col=bofn, extra_ready=bordy)
        else:
            cb = evac_to_hbm(p, ls, name="eWo", out_hbm=scr["acc_a"], out_dtype=F32)
        mm_stream(p, ls, name="mWo",
                  lhsT_tile=lambda k, m, lt=lt: lt(k)[:, m * 128:(m + 1) * 128],
                  nK=NDT, nM=NDT, lhs_ready=lrdy,
                  rhs_hbm=lambda k, f: scr["ctx_bf"][k * 128:(k + 1) * 128,
                                                     f * FC:(f + 1) * FC],
                  out_cb=cb)
    p.barrier()



def emit_ffn1(p, W, C, scr):
    H = 3072
    NMH = H // 128
    for wn, dst, act in (("w_gate", "g_bf", True), ("w_up", "u_bf", False)):
        for half in range(2):
            with ExitStack() as ls:
                mw = H // 2
                lt, lrdy = load_weight_bf16(p, ls, W[wn], DM, "wf" + wn, half * mw,
                                            mw, dtype_in=I32, scale=1.0 / 127.0,
                                            cast_eng="sc")
                cb = evac_to_hbm(p, ls, name="ef" + wn + str(half),
                                 out_hbm=scr[dst], out_dtype=BF16)
                mm_stream(p, ls, name="mf" + wn + str(half),
                          lhsT_tile=lambda k, m, lt=lt: lt(k)[:, m * 128:(m + 1) * 128],
                          nK=NDT, nM=NMH // 2, lhs_ready=lrdy,
                          rhs_hbm=lambda k, f: scr["xn_bf"][k * 128:(k + 1) * 128,
                                                           f * FC:(f + 1) * FC],
                          out_cb=lambda m, f, ps, cb=cb, half=half: cb(m + half * (NMH // 2), f, ps))
            p.barrier()
    with ExitStack() as ls:
        G = p.sbuf(ls, uq("fg"), [128, 2 * 2048], BF16)
        U = p.sbuf(ls, uq("fu"), [128, 2 * 2048], BF16)
        S = p.sbuf(ls, uq("fs"), [128, 2 * 2048], BF16)
        Hb = p.sbuf(ls, uq("fh"), [128, 2 * 2048], BF16)
        chA, chB = p.dma_pair()
        chC, chD = p.dma_pair()
        ld_tick = {0: p.dguard.get(chA), 1: p.dguard.get(chB)}
        st_tick = {0: p.dguard.get(chC), 1: p.dguard.get(chD)}
        use_tick = {0: None, 1: None}
        it = [0]
        for mt in range(NMH):
            for jj in range(TOK // 2048):
                sl = it[0] & 1
                it[0] += 1
                bsl = slice(sl * 2048, (sl + 1) * 2048)
                sl_r = slice(mt * 128, (mt + 1) * 128)
                sl_c = slice(jj * 2048, (jj + 1) * 2048)
                ch = chA if sl == 0 else chB
                chs = chC if sl == 0 else chD

                def syf(e, sl_r=sl_r, sl_c=sl_c, bsl=bsl, ch=ch, war=use_tick[sl]):
                    if war is not None:
                        p.wait(e, war[0], war[1])
                    ins = e.dma_start(out=G[:, bsl], in_=scr["g_bf"][sl_r, sl_c])
                    p.inc(ins, ch, 16)
                    ins = e.dma_start(out=U[:, bsl], in_=scr["u_bf"][sl_r, sl_c])
                    p.inc(ins, ch, 16)
                p.sy(syf)
                d_t = (ch, p.val(ch))

                def sf(e, d_t=d_t, bsl=bsl):
                    p.wait(e, d_t[0], d_t[1])
                    p.wait(e, "vet")
                    ins = e.activation(out=S[:, bsl], in_=G[:, bsl], func=AF.Sigmoid)
                    p.inc(ins, "sct", 1)
                p.sc(sf)
                s_t = p.val("sct")

                def vf(e, s_t=s_t, d_t=d_t, bsl=bsl, st_t=st_tick[sl]):
                    p.wait(e, "sct", s_t)
                    if st_t is not None:
                        p.wait(e, st_t[0], st_t[1])
                    p.fence(e, e.tensor_tensor(out=Hb[:, bsl], in0=G[:, bsl],
                                               in1=S[:, bsl], op=ALU.mult), "vet")
                    ins = e.tensor_tensor(out=Hb[:, bsl], in0=Hb[:, bsl],
                                          in1=U[:, bsl], op=ALU.mult)
                    p.inc(ins, "vet", 1)
                p.ve(vf)
                v_t = p.val("vet")
                use_tick[sl] = ("vet", v_t)

                def syf2(e, sl_r=sl_r, sl_c=sl_c, v_t=v_t, bsl=bsl, chs=chs):
                    p.wait(e, "vet", v_t)
                    ins = e.dma_start(out=scr["h_bf"][sl_r, sl_c], in_=Hb[:, bsl])
                    p.inc(ins, chs, 16)
                p.sy(syf2)
                st_tick[sl] = (chs, p.val(chs))
                p.dguard[ch] = use_tick[sl]
                p.dguard[chs] = st_tick[sl]
    p.barrier()
    with ExitStack() as ls:
        lt, lrdy = load_weight_bf16(p, ls, W["w_down"], H, "wfd", 0, DM,
                                    dtype_in=I32, scale=1.0 / 127.0, cast_eng="sc")
        cb = evac_to_hbm(p, ls, name="efd", out_hbm=scr["acc_b"], out_dtype=F32)
        mm_stream(p, ls, name="mfd",
                  lhsT_tile=lambda k, m, lt=lt: lt(k)[:, m * 128:(m + 1) * 128],
                  nK=H // 128, nM=NDT, lhs_ready=lrdy,
                  rhs_hbm=lambda k, f: scr["h_bf"][k * 128:(k + 1) * 128,
                                                  f * FC:(f + 1) * FC],
                  out_cb=cb)
    p.barrier()


L1_W = ["L1_attn_Wq", "L1_attn_Wk", "L1_attn_Wv", "L1_attn_Wo", "L1_attn_bo",
        "L1_ffn_w_gate", "L1_ffn_w_up", "L1_ffn_w_down"]
L0_W = ["L0_attn_log_A", "L0_attn_B_param", "L0_attn_C_param", "L0_attn_D_param",
        "L0_attn_log_dt", "L0_attn_ln_gamma", "L0_attn_ln_beta", "L0_attn_W_out",
        "L0_attn_b_out", "L0_ffn_ln_scale", "L0_ffn_ln_bias", "L0_ffn_gate_w",
        "L0_ffn_Wg", "L0_ffn_Wu", "L0_ffn_Wd", "L0_ffn_sWg", "L0_ffn_sWu",
        "L0_ffn_sWd"]
L2_W = ["L2_attn_Wq", "L2_attn_bq", "L2_attn_Wk", "L2_attn_bk", "L2_attn_Wv",
        "L2_attn_bv", "L2_attn_Wo", "L2_attn_bo", "L2_ffn_w1_w", "L2_ffn_w1_s",
        "L2_ffn_w2_w", "L2_ffn_w2_s", "L2_ffn_h1_w", "L2_ffn_h1_s",
        "L2_ffn_h2_w", "L2_ffn_h2_s", "L2_ffn_w3_w", "L2_ffn_w3_s"]
L3_W = ["L3_attn_Wq", "L3_attn_Wk", "L3_attn_Wv", "L3_attn_Wo",
        "L3_ffn_cn_w1", "L3_ffn_cn_b1", "L3_ffn_cn_w2", "L3_ffn_cn_b2",
        "L3_ffn_cn_w3", "L3_ffn_cn_b3", "L3_ffn_cn_w4", "L3_ffn_cn_b4",
        "L3_ffn_gc_w1", "L3_ffn_gc_b1", "L3_ffn_gc_w2", "L3_ffn_gc_b2",
        "L3_ffn_gc_w3", "L3_ffn_gc_b3", "L3_ffn_adaptive_range"]
LAYER_W = {0: L0_W, 1: L1_W, 2: L2_W, 3: L3_W}

WEIGHT_SPECS = {
    "L0_attn_log_A": ([1024, 64], F32), "L0_attn_B_param": ([1024, 64], F32),
    "L0_attn_C_param": ([1024, 64], F32), "L0_attn_D_param": ([1024, 1], F32),
    "L0_attn_log_dt": ([1024, 1], F32), "L0_attn_ln_gamma": ([1024, 1], F32),
    "L0_attn_ln_beta": ([1024, 1], F32), "L0_attn_W_out": ([1024, 1024], F32),
    "L0_attn_b_out": ([1024, 1], F32), "L0_ffn_ln_scale": ([1024, 1], F32),
    "L0_ffn_ln_bias": ([1024, 1], F32), "L0_ffn_gate_w": ([1024, 8], F32),
    "L0_ffn_Wg": ([8, 1024, 2048], F32), "L0_ffn_Wu": ([8, 1024, 2048], F32),
    "L0_ffn_Wd": ([8, 2048, 1024], F32), "L0_ffn_sWg": ([1024, 2048], F32),
    "L0_ffn_sWu": ([1024, 2048], F32), "L0_ffn_sWd": ([2048, 1024], F32),
    "L1_attn_Wq": ([1024, 1024], F32), "L1_attn_Wk": ([1024, 1024], F32),
    "L1_attn_Wv": ([1024, 1024], F32), "L1_attn_Wo": ([1024, 1024], F32),
    "L1_attn_bo": ([1024, 1], F32),
    "L1_ffn_w_gate": ([1024, 3072], I32), "L1_ffn_w_up": ([1024, 3072], I32),
    "L1_ffn_w_down": ([3072, 1024], I32),
    "L2_attn_Wq": ([1024, 1024], F32), "L2_attn_bq": ([1024, 1], F32),
    "L2_attn_Wk": ([1024, 1024], F32), "L2_attn_bk": ([1024, 1], F32),
    "L2_attn_Wv": ([1024, 1024], F32), "L2_attn_bv": ([1024, 1], F32),
    "L2_attn_Wo": ([1024, 1024], F32), "L2_attn_bo": ([1024, 1], F32),
    "L2_ffn_w1_w": ([1024, 4096], F32), "L2_ffn_w1_s": ([4096, 1], F32),
    "L2_ffn_w2_w": ([1024, 4096], F32), "L2_ffn_w2_s": ([4096, 1], F32),
    "L2_ffn_h1_w": ([4096, 4096], F32), "L2_ffn_h1_s": ([4096, 1], F32),
    "L2_ffn_h2_w": ([4096, 4096], F32), "L2_ffn_h2_s": ([4096, 1], F32),
    "L2_ffn_w3_w": ([4096, 1024], F32), "L2_ffn_w3_s": ([1024, 1], F32),
    "L3_attn_Wq": ([1024, 1024], F32), "L3_attn_Wk": ([1024, 1024], F32),
    "L3_attn_Wv": ([1024, 1024], F32), "L3_attn_Wo": ([1024, 1024], F32),
    "L3_ffn_cn_w1": ([1024, 1024], F32), "L3_ffn_cn_b1": ([1024, 1], F32),
    "L3_ffn_cn_w2": ([1024, 512], F32), "L3_ffn_cn_b2": ([512, 1], F32),
    "L3_ffn_cn_w3": ([512, 256], F32), "L3_ffn_cn_b3": ([256, 1], F32),
    "L3_ffn_cn_w4": ([256, 1], F32), "L3_ffn_cn_b4": ([1, 1], F32),
    "L3_ffn_gc_w1": ([1024, 512], F32), "L3_ffn_gc_b1": ([512, 1], F32),
    "L3_ffn_gc_w2": ([512, 256], F32), "L3_ffn_gc_b2": ([256, 1], F32),
    "L3_ffn_gc_w3": ([256, 1], F32), "L3_ffn_gc_b3": ([1, 1], F32),
    "L3_ffn_adaptive_range": ([1, 1], F32),
}

SCRATCH_SPECS = {
    "xn_bf": ([DM, TOK], BF16), "xn_f32": ([DM, TOK], F32),
    "q_bf": ([DM, TOK], BF16), "k_bf": ([DM, TOK], BF16),
    "v_bf": ([DM, TOK], BF16), "ctx_bf": ([DM, TOK], BF16),
    "g_bf": ([4096, TOK], BF16), "u_bf": ([4096, TOK], BF16),
    "h_bf": ([4096, TOK], BF16),
    "acc_a": ([DM, TOK], F32), "acc_b": ([DM, TOK], F32),
    "x_mid0": ([DM, TOK], F32), "x_mid1": ([DM, TOK], F32),
    "x_mid2": ([DM, TOK], F32),
    "w_hbm": ([DM, 64], F32), "cb_hbm": ([DM, 64], F32),
    "xg_f32": ([DM, TOK], F32),
}


def emit_layer(p, li, W, C, scr, x_in, x_out):
    if li == 0:
        _emit_layer0(p, W, C, scr, x_in, x_out)
    elif li == 2:
        _emit_layer2(p, W, C, scr, x_in, x_out)
    elif li == 3:
        _emit_layer3(p, W, C, scr, x_in, x_out)
    elif li == 1:
        emit_ln(p, x_in, None, scr["xn_bf"], C, uq("ln"))
        p.barrier()
        emit_mha(p, {k: W[f"L1_attn_{k}"] for k in ("Wq", "Wk", "Wv", "Wo")},
                 C, scr, bo_ap=W["L1_attn_bo"])
        emit_ffn1(p, {k: W[f"L1_ffn_{k}"] for k in ("w_gate", "w_up", "w_down")},
                  C, scr)
        emit_residual_add(p, x_in, scr["acc_a"], scr["acc_b"], x_out)
        p.barrier()
    else:
        raise NotImplementedError(f"layer {li}")


def build_program(layers):
    nc = bass.Bass("TRN2", target_bir_lowering=False, debug=False)
    x_in = nc.dram_tensor("x_in", [DM, TOK], F32, kind="ExternalInput").ap()
    x_out = nc.dram_tensor("x_out", [DM, TOK], F32, kind="ExternalOutput").ap()
    W = {}
    for li in layers:
        for nm in LAYER_W[li]:
            shp, dt_ = WEIGHT_SPECS[nm]
            W[nm] = nc.dram_tensor(nm, shp, dt_, kind="ExternalInput").ap()
    caps = {}
    for nm, arr in host_constants().items():
        caps[nm] = nc.dram_tensor(nm, list(arr.shape), F32, kind="ExternalInput").ap()
    scr = {}
    for nm, (shp, dt_) in SCRATCH_SPECS.items():
        scr[nm] = nc.dram_tensor("scr_" + nm, shp, dt_, kind="Internal").ap()

    with ExitStack() as stack:
        p = Prog(nc, stack)
        with ExitStack() as cls:
            C = emit_const_load(p, stack, caps)
        p.barrier()
        cur = x_in
        for idx, li in enumerate(layers):
            out = x_out if idx == len(layers) - 1 else scr[f"x_mid{idx}"]
            emit_layer(p, li, W, C, scr, cur, out)
            cur = out

        def fin(e):
            for ch in ("dx", "dr", "dw", "da"):
                if ch in p._sems:
                    p.wait(e, ch)
        p.sy(fin)
    return nc


LAUNCH_GROUPS = [[0, 1, 2, 3]]


def kernel(**inputs):
    w = prep_inputs(inputs)
    x = np.ascontiguousarray(np.asarray(inputs["L0_attn_x"], dtype=np.float32))
    shards = [np.ascontiguousarray(x[2 * c:2 * c + 2].transpose(2, 0, 1)
                                   .reshape(DM, TOK)) for c in range(8)]
    cur = shards
    for group in LAUNCH_GROUPS:
        nc = build_program(group)
        names = ["x_in"] + [nm for li in group for nm in LAYER_W[li]] \
            + list(host_constants().keys())
        in_maps = []
        for c in range(8):
            m = {"x_in": cur[c]}
            for nm in names:
                if nm != "x_in":
                    m[nm] = w[nm]
            in_maps.append(m)
        res = run_bass_kernel_spmd(nc, in_maps, core_ids=list(range(8)))
        cur = [np.asarray(res.results[c]["x_out"]) for c in range(8)]
    full = np.empty((16, 2048, 1024), np.float32)
    for c in range(8):
        full[2 * c:2 * c + 2] = cur[c].reshape(DM, B, T).transpose(1, 2, 0)
    return full


def emit_ffn3(p, W, C, scr):
    FB = 512
    with ExitStack() as ls:
        wts = {}
        for nm, K_, M_ in (("cn_w1", DM, DM), ("cn_w2", DM, 512), ("cn_w3", 512, 256),
                           ("cn_w4", 256, 1), ("gc_w1", DM, 512), ("gc_w2", 512, 256),
                           ("gc_w3", 256, 1)):
            wts[nm] = load_weight_bf16(p, ls, W[nm], K_, "w3" + nm, 0, M_)
        cols = {}
        for nm, n in (("cn_b1", DM), ("cn_b2", 512), ("cn_b3", 256), ("cn_b4", 1),
                      ("gc_b1", 512), ("gc_b2", 256), ("gc_b3", 1)):
            cols[nm] = load_cols(p, ls, W[nm], uq("c3" + nm), n=n)
        arr_c = load_cols(p, ls, W["adaptive_range"], uq("c3ar"), n=1)
        wrdy = [wts[nm][1] for nm in wts] + [cols[nm][2] for nm in cols] + [arr_c[2]]

        def tjoin(e):
            seen = {}
            for ch_, tk_ in wrdy:
                seen[ch_] = max(seen.get(ch_, 0), tk_)
            for ch_, tk_ in seen.items():
                p.wait(e, ch_, tk_)
            ins = e.nop()
            p.inc(ins, "pet", 1)
        p.te(tjoin)

        gcx = p.sbuf(ls, uq("gcx"), [128, NDT * B], F32)
        gcb = p.sbuf(ls, uq("gcb"), [128, NDT * B], BF16)
        XT3 = p.sbuf(ls, uq("xt3"), [128, 2048], F32)
        for dt in range(NDT):
            for b in range(B):
                def syf(e, dt=dt, b=b):
                    p.wait(e, "vet")
                    ins = e.dma_start(out=XT3[:],
                                      in_=scr["xn_f32"][dt * 128:(dt + 1) * 128,
                                                        b * T:(b + 1) * T])
                    p.inc(ins, "dx", 16)
                p.sy(syf)
                d_t = p.val("dx")

                def vf(e, dt=dt, b=b, d_t=d_t):
                    p.wait(e, "dx", d_t)
                    p.fence(e, e.reduce_sum(out=gcx[:, dt * B + b:dt * B + b + 1],
                                            in_=XT3[:], axis=AX), "vet")
                    ins = e.tensor_scalar_mul(
                        out=gcx[:, dt * B + b:dt * B + b + 1],
                        in0=gcx[:, dt * B + b:dt * B + b + 1], scalar1=1.0 / T)
                    p.inc(ins, "vet", 1)
                p.ve(vf)

        def vcast(e):
            p.wait(e, "vet")
            ins = e.tensor_copy(out=gcb[:], in_=gcx[:])
            p.inc(ins, "vet", 1)
        p.ve(vcast)
        gc_t = p.val("vet")

        ps_g = p.psum(ls, uq("psg"), [128, 64])
        h1g = p.sbuf(ls, uq("h1g"), [128, 4 * B], BF16)
        h2g = p.sbuf(ls, uq("h2g"), [128, 2 * B], BF16)
        gfr = p.sbuf(ls, uq("gfr"), [1, 8], F32)

        def net_step(nm, bnm, nK_, nM_, rhs_tensor, out_tensor, act, rdy_t):
            lt, _ = wts[nm]
            for mt in range(nM_):
                def tf(e, mt=mt, rdy_t=rdy_t):
                    p.wait(e, "vet", rdy_t)
                    p.wait(e, "sct")
                    for k in range(nK_):
                        ins = e.matmul(out=ps_g[:, mt * B:(mt + 1) * B],
                                       lhsT=lt(k)[:, mt * 128:(mt + 1) * 128],
                                       rhs=rhs_tensor[:, k * B:(k + 1) * B],
                                       start=(k == 0), stop=(k == nK_ - 1))
                    p.inc(ins, "pet", 1)
                p.te(tf)
                m_t = p.val("pet")

                def sf(e, mt=mt, m_t=m_t):
                    p.wait(e, "sct")
                    p.wait(e, "pet", m_t)
                    ins = e.activation(out=out_tensor[:, mt * B:(mt + 1) * B],
                                       in_=ps_g[:, mt * B:(mt + 1) * B], func=act,
                                       bias=cols[bnm][1](mt))
                    p.inc(ins, "sct", 1)
                p.sc(sf)
            return p.val("sct")

        t1 = net_step("gc_w1", "gc_b1", NDT, 4, gcb, h1g, AF.Gelu, gc_t)
        def vx(e, t1=t1):
            p.wait(e, "sct", t1)
            ins = e.engine_nop()
            p.inc(ins, "vet", 1)
        p.ve(vx)
        t1v = p.val("vet")
        t2 = net_step("gc_w2", "gc_b2", 4, 2, h1g, h2g, AF.Gelu, t1v)
        def vx2(e, t2=t2):
            p.wait(e, "sct", t2)
            ins = e.engine_nop()
            p.inc(ins, "vet", 1)
        p.ve(vx2)
        t2v = p.val("vet")
        gfb = p.sbuf(ls, uq("gfb"), [1, B], F32)
        lt3, _ = wts["gc_w3"]

        def tf_gf(e, t2v=t2v):
            p.wait(e, "vet", t2v)
            for k in range(2):
                ins = e.matmul(out=ps_g[0:1, 32:32 + B], lhsT=lt3(k),
                               rhs=h2g[:, k * B:(k + 1) * B],
                               start=(k == 0), stop=(k == 1))
            p.inc(ins, "pet", 1)
        p.te(tf_gf)
        gf_t = p.val("pet")

        def sf_gf(e, gf_t=gf_t):
            p.wait(e, "pet", gf_t)
            ins = e.activation(out=gfb[:, :], in_=ps_g[0:1, 32:32 + B],
                               func=AF.Sigmoid, bias=cols["gc_b3"][1](0)[0:1, 0:1])
            p.inc(ins, "sct", 1)
        p.sc(sf_gf)
        gfs_t = p.val("sct")

        def vgf(e, gfs_t=gfs_t):
            p.wait(e, gfs_t and "sct", gfs_t)
            for ch_ in wrdy:
                p.wait(e, ch_[0], ch_[1])
            p.fence(e, e.tensor_scalar_mul(out=gfr[:, 0:2], in0=gfb[:, :],
                                           scalar1=0.7), "vet")
            ins = e.tensor_scalar_mul(out=gfr[:, 2:3], in0=arr_c[0][0:1, 0:1],
                                      scalar1=2.0)
            p.inc(ins, "vet", 1)
        p.ve(vgf)
        gfr_t = p.val("vet")

        OUTA = p.sbuf(ls, uq("outA"), [128, NDT * FB], F32)
        OUTB = p.sbuf(ls, uq("outB"), [128, NDT * FB], F32)
        OBF = p.sbuf(ls, uq("obf"), [128, NDT * FB], BF16)
        H1 = p.sbuf(ls, uq("h1c"), [128, NDT * FB], BF16)
        H2 = p.sbuf(ls, uq("h2c"), [128, 4 * FB], BF16)
        H3 = p.sbuf(ls, uq("h3c"), [128, 2 * FB], BF16)
        TFR = p.sbuf(ls, uq("tfr"), [1, FB], F32)
        CPR = p.sbuf(ls, uq("cpr"), [1, FB], BF16)
        TMP = p.sbuf(ls, uq("tmp3"), [128, FB], F32)
        ps_c = [p.psum(ls, uq("psc3"), [128, FB]) for _ in range(2)]
        ps_cp = p.psum(ls, uq("pscp"), [128, FB])

        for blk_i in range(TOK // FB):
            b = blk_i // (T // FB)
            def syf(e, blk_i=blk_i):
                p.wait(e, "vet")
                p.wait(e, "dx")
                for dt in range(NDT):
                    ins = e.dma_start(
                        out=OUTA[:, dt * FB:(dt + 1) * FB],
                        in_=scr["xn_f32"][dt * 128:(dt + 1) * 128,
                                          blk_i * FB:(blk_i + 1) * FB])
                    p.inc(ins, "dx", 16)
            p.sy(syf)
            ld_t = p.val("dx")
            cur, nxt = OUTA, OUTB
            cur_rdy = ("dx", ld_t)

            for cyc in range(3):
                def vc(e, cur=cur, cur_rdy=cur_rdy):
                    p.wait(e, cur_rdy[0], cur_rdy[1])
                    p.wait(e, "pet")
                    ins = e.tensor_copy(out=OBF[:], in_=cur[:])
                    p.inc(ins, "vet", 1)
                p.ve(vc)
                obf_t = p.val("vet")

                def mmloc(lt_nm, b_nm, nK_, nM_, rhs_t, out_t, act, rdy, bank):
                    lt = wts[lt_nm][0]
                    outs = []
                    for mt in range(nM_):
                        ps = ps_c[(bank + mt) % 2]

                        def tf(e, mt=mt, ps=ps, rdy=rdy):
                            p.wait(e, rdy[0], rdy[1])
                            p.wait(e, "sct")
                            for k in range(nK_):
                                ins = e.matmul(out=ps[:, :],
                                               lhsT=lt(k)[:, mt * 128:(mt + 1) * 128],
                                               rhs=rhs_t[:, k * FB:(k + 1) * FB],
                                               start=(k == 0), stop=(k == nK_ - 1))
                            p.inc(ins, "pet", 1)
                        p.te(tf)
                        m_t = p.val("pet")

                        def sf(e, mt=mt, ps=ps, m_t=m_t):
                            p.wait(e, "pet", m_t)
                            ins = e.activation(out=out_t[:, mt * FB:(mt + 1) * FB],
                                               in_=ps[:, :], func=act,
                                               bias=cols[b_nm][1](mt))
                            p.inc(ins, "sct", 1)
                        p.sc(sf)
                    return ("sct", p.val("sct"))

                r1 = mmloc("cn_w1", "cn_b1", NDT, NDT, OBF, H1, AF.Gelu,
                           ("vet", obf_t), 0)
                r2 = mmloc("cn_w2", "cn_b2", NDT, 4, H1, H2, AF.Gelu, r1, 0)
                r3 = mmloc("cn_w3", "cn_b3", 4, 2, H2, H3, AF.Gelu, r2, 0)
                lt4 = wts["cn_w4"][0]

                def tf4(e, r3=r3):
                    p.wait(e, r3[0], r3[1])
                    p.wait(e, "sct")
                    for k in range(2):
                        ins = e.matmul(out=ps_c[0][0:1, :], lhsT=lt4(k),
                                       rhs=H3[:, k * FB:(k + 1) * FB],
                                       start=(k == 0), stop=(k == 1))
                    p.inc(ins, "pet", 1)
                p.te(tf4)
                t4_t = p.val("pet")

                def sf4(e, t4_t=t4_t):
                    p.wait(e, "pet", t4_t)
                    p.wait(e, "vet")
                    ins = e.activation(out=TFR[:, :], in_=ps_c[0][0:1, :],
                                       func=AF.Sigmoid, bias=cols["cn_b4"][1](0)[0:1, 0:1])
                    p.inc(ins, "sct", 1)
                p.sc(sf4)
                tf_t = p.val("sct")

                def vcp(e, b=b, tf_t=tf_t, gfr_t=gfr_t):
                    p.wait(e, "sct", tf_t)
                    p.wait(e, "vet", gfr_t)
                    p.fence(e, e.tensor_scalar(out=TFR[:, :], in0=TFR[:, :],
                                               scalar1=0.3, scalar2=gfr[0:1, b:b + 1],
                                               op0=ALU.mult, op1=ALU.add), "vet")
                    p.fence(e, e.tensor_scalar_add(out=TFR[:, :], in0=TFR[:, :],
                                                   scalar1=-0.5), "vet")
                    p.fence(e, e.tensor_scalar_mul(out=TFR[:, :], in0=TFR[:, :],
                                                   scalar1=gfr[0:1, 2:3]), "vet")
                    p.fence(e, e.tensor_scalar_add(out=TFR[:, :], in0=TFR[:, :],
                                                   scalar1=0.1), "vet")
                    ins = e.tensor_copy(out=CPR[:, :], in_=TFR[:, :])
                    p.inc(ins, "vet", 1)
                p.ve(vcp)
                cp_t = p.val("vet")

                def tbc(e, cp_t=cp_t):
                    p.wait(e, "vet", cp_t)
                    ins = e.matmul(out=ps_cp[:, :], lhsT=C["c_ones_row_bf"][:],
                                   rhs=CPR[:, :], start=True, stop=True)
                    p.inc(ins, "pet", 1)
                p.te(tbc)
                bc_t = p.val("pet")

                TW = [(4, 1.0), (5, 1.0), (6, -1.0), (7, -1.0),
                      (0, -1.0), (1, -1.0), (2, 1.0), (3, 1.0)]
                for dt in range(NDT):
                    src, sg = TW[dt]

                    def vtw(e, dt=dt, src=src, sg=sg, bc_t=bc_t, cur=cur, nxt=nxt):
                        p.wait(e, "pet", bc_t)
                        p.wait(e, "vet")
                        p.fence(e, e.tensor_tensor(
                            out=TMP[:], in0=ps_cp[:, :],
                            in1=cur[:, src * FB:(src + 1) * FB], op=ALU.mult), "vet")
                        ins = e.tensor_tensor(
                            out=nxt[:, dt * FB:(dt + 1) * FB],
                            in0=cur[:, dt * FB:(dt + 1) * FB], in1=TMP[:],
                            op=ALU.subtract if sg < 0 else ALU.add)
                        p.inc(ins, "vet", 1)
                    p.ve(vtw)
                cur_rdy = ("vet", p.val("vet"))
                cur, nxt = nxt, cur

            def syo(e, blk_i=blk_i, cur=cur, cur_rdy=cur_rdy):
                p.wait(e, cur_rdy[0], cur_rdy[1])
                p.wait(e, "dx")
                for dt in range(NDT):
                    ins = e.dma_start(
                        out=scr["acc_b"][dt * 128:(dt + 1) * 128,
                                         blk_i * FB:(blk_i + 1) * FB],
                        in_=cur[:, dt * FB:(dt + 1) * FB])
                    p.inc(ins, "dx", 16)
            p.sy(syo)
    p.barrier()


def _emit_layer3(p, W, C, scr, x_in, x_out):
    emit_ln(p, x_in, scr["xn_f32"], scr["xn_bf"], C, uq("ln"))
    p.barrier()
    emit_mha(p, {k: W[f"L3_attn_{k}"] for k in ("Wq", "Wk", "Wv", "Wo")}, C, scr)
    emit_ffn3(p, {k.replace("L3_ffn_", ""): W[k] for k in L3_W if "ffn" in k}, C, scr)
    emit_residual_add(p, x_in, scr["acc_a"], scr["acc_b"], x_out)
    p.barrier()


def emit_attn2(p, W, C, scr):
    for wn, bn, dst in (("Wq", "bq", "q_bf"), ("Wk", "bk", "k_bf"),
                        ("Wv", "bv", "v_bf")):
        with ExitStack() as ls:
            lt, lrdy = load_weight_bf16(p, ls, W[wn], DM, "w2" + wn, 0, DM)
            _, bfn, brdy = load_cols(p, ls, W[bn], uq("b2" + bn))
            cb = evac_to_hbm(p, ls, name="e2" + wn, out_hbm=scr[dst], out_dtype=BF16,
                             bias_col=bfn, extra_ready=brdy)
            mm_stream(p, ls, name="m2" + wn,
                      lhsT_tile=lambda k, m, lt=lt: lt(k)[:, m * 128:(m + 1) * 128],
                      nK=NDT, nM=NDT, lhs_ready=lrdy,
                      rhs_hbm=lambda k, f: scr["xn_bf"][k * 128:(k + 1) * 128,
                                                        f * FC:(f + 1) * FC],
                      out_cb=cb)
        p.barrier()

    KW = 2052
    with ExitStack() as ls:
        q_sb = p.sbuf(ls, uq("q2"), [128, NDT * T], BF16)
        k_sb = p.sbuf(ls, uq("k2"), [128, NDT * KW], BF16)
        v_sb = p.sbuf(ls, uq("v2"), [128, NDT * KW], BF16)
        qk = p.sbuf(ls, uq("qk2"), [128, NDT * FC], BF16)
        sc = [p.sbuf(ls, uq("sc2"), [16, T], F32) for _ in range(3)]
        mx = p.sbuf(ls, uq("mx2"), [16, T], F32)
        ex = [p.sbuf(ls, uq("ex2"), [16, T], BF16) for _ in range(3)]
        sr = p.sbuf(ls, uq("sr2"), [16, 2 * T], F32)
        att = [p.sbuf(ls, uq("at2"), [16, T], BF16) for _ in range(3)]
        tjs = p.sbuf(ls, uq("tj2"), [128, 3 * FC], F32)
        ybf = p.sbuf(ls, uq("yb2"), [128, FC], BF16)
        ps_s = p.psum(ls, uq("ps2"), [16, FC])
        ps_bc = [p.psum(ls, uq("pb2"), [128, FC]) for _ in range(2)]

        def gf0(e):
            for dt in range(NDT):
                e.memset(k_sb[:, dt * KW:dt * KW + 2], 0.0)
                ins = e.memset(v_sb[:, dt * KW:dt * KW + 2], 0.0)
            p.inc(ins, "gpt", 1)
        p.gp(gf0)
        pad_t = p.val("gpt")

        for b in range(B):
            def syf(e, b=b):
                p.wait(e, "pet")
                p.wait(e, "vet")
                p.wait(e, "gpt")
                for dt in range(NDT):
                    ins = e.dma_start(out=q_sb[:, dt * T:(dt + 1) * T],
                                      in_=scr["q_bf"][dt * 128:(dt + 1) * 128,
                                                      b * T:(b + 1) * T])
                    p.inc(ins, "dr", 16)
                    ins = e.dma_start(out=k_sb[:, dt * KW + 2:dt * KW + 2 + T],
                                      in_=scr["k_bf"][dt * 128:(dt + 1) * 128,
                                                      b * T:(b + 1) * T])
                    p.inc(ins, "dr", 16)
                    ins = e.dma_start(out=v_sb[:, dt * KW + 2:dt * KW + 2 + T],
                                      in_=scr["v_bf"][dt * 128:(dt + 1) * 128,
                                                      b * T:(b + 1) * T])
                    p.inc(ins, "dr", 16)
            p.sy(syf)
            ld_t = p.val("dr")

            for j in range(3):
                for fc in range(T // FC):
                    def vqk(e, j=j, fc=fc, b=b):
                        p.wait(e, "dr", ld_t)
                        p.wait(e, "gpt", pad_t)
                        p.wait(e, "pet")
                        for dt in range(NDT):
                            ins = e.tensor_tensor(
                                out=qk[:, dt * FC:(dt + 1) * FC],
                                in0=q_sb[:, dt * T + fc * FC:dt * T + (fc + 1) * FC],
                                in1=k_sb[:, dt * KW + 2 - j + fc * FC:
                                         dt * KW + 2 - j + (fc + 1) * FC],
                                op=ALU.mult)
                            p.inc(ins, "vet", 1)
                    p.ve(vqk)
                    qk_t = p.val("vet")

                    def tsc(e, j=j, fc=fc, qk_t=qk_t):
                        p.wait(e, "vet", qk_t)
                        for dt in range(NDT):
                            ins = e.matmul(out=ps_s[:, :],
                                           lhsT=C[f"c_hsel{dt}_bf"][:],
                                           rhs=qk[:, dt * FC:(dt + 1) * FC],
                                           start=(dt == 0), stop=(dt == NDT - 1))
                        p.inc(ins, "pet", 1)
                    p.te(tsc)
                    sc_t = p.val("pet")

                    def vev(e, j=j, fc=fc, sc_t=sc_t):
                        p.wait(e, "pet", sc_t)
                        ins = e.tensor_scalar_mul(
                            out=sc[j][:, fc * FC:(fc + 1) * FC],
                            in0=ps_s[:, :], scalar1=0.125)
                        p.inc(ins, "vet", 1)
                    p.ve(vev)

            def vmask(e):
                e.memset(sc[1][:, 0:1], -1e30)
                ins = e.memset(sc[2][:, 0:2], -1e30)
                p.inc(ins, "vet", 1)
            p.ve(vmask)
            scm_t = p.val("vet")

            def vsm(e, scm_t=scm_t):
                p.wait(e, "vet", scm_t)
                p.fence(e, e.tensor_tensor(out=mx[:, :], in0=sc[0][:, :],
                                           in1=sc[1][:, :], op=ALU.max), "vet")
                p.fence(e, e.tensor_tensor(out=mx[:, :], in0=mx[:, :],
                                           in1=sc[2][:, :], op=ALU.max), "vet")
                for j in range(3):
                    p.fence(e, e.tensor_tensor(
                        out=ex[j][:, :], in0=sc[j][:, :],
                        in1=mx[:, :], op=ALU.subtract), "vet")
                ins = e.engine_nop()
                p.inc(ins, "vet", 1)
            p.ve(vsm)
            sub_t = p.val("vet")

            def sexp(e, sub_t=sub_t):
                p.wait(e, "vet", sub_t)
                for j in range(3):
                    ins = e.activation(out=ex[j][:, :],
                                       in_=ex[j][:, :], func=AF.Exp)
                    p.inc(ins, "sct", 1)
            p.sc(sexp)
            exp_t = p.val("sct")

            def vnrm(e, exp_t=exp_t):
                p.wait(e, "sct", exp_t)
                p.fence(e, e.tensor_tensor(out=sr[:, 0:T], in0=ex[0][:, :],
                                           in1=ex[1][:, :], op=ALU.add), "vet")
                p.fence(e, e.tensor_tensor(out=sr[:, 0:T], in0=sr[:, 0:T],
                                           in1=ex[2][:, :], op=ALU.add), "vet")
                p.fence(e, e.reciprocal(out=sr[:, T:2 * T], in_=sr[:, 0:T]), "vet")
                for j in range(3):
                    p.fence(e, e.tensor_tensor(
                        out=att[j][:, :], in0=ex[j][:, :],
                        in1=sr[:, T:2 * T], op=ALU.mult), "vet")
                ins = e.engine_nop()
                p.inc(ins, "vet", 1)
            p.ve(vnrm)
            att_t = p.val("vet")

            for dt in range(NDT):
                for fc in range(T // FC):
                    for j in range(3):
                        ps = ps_bc[j & 1]

                        def tbc(e, j=j, dt=dt, fc=fc, ps=ps, att_t=att_t):
                            p.wait(e, "vet", att_t)
                            p.wait(e, "vet")
                            ins = e.matmul(
                                out=ps[:, :], lhsT=C[f"c_hexp{dt}_bf"][:],
                                rhs=att[j][:, fc * FC:(fc + 1) * FC],
                                start=True, stop=True)
                            p.inc(ins, "pet", 1)
                        p.te(tbc)
                        bc_t = p.val("pet")

                        def vtj(e, j=j, dt=dt, fc=fc, ps=ps, bc_t=bc_t):
                            p.wait(e, "pet", bc_t)
                            ins = e.tensor_tensor(
                                out=tjs[:, j * FC:(j + 1) * FC], in0=ps[:, :],
                                in1=v_sb[:, dt * KW + 2 - j + fc * FC:
                                         dt * KW + 2 - j + (fc + 1) * FC],
                                op=ALU.mult)
                            p.inc(ins, "vet", 1)
                        p.ve(vtj)

                    def vy(e, dt=dt, fc=fc):
                        p.wait(e, "vet")
                        p.wait(e, "dx")
                        p.fence(e, e.tensor_tensor(out=tjs[:, 0:FC], in0=tjs[:, 0:FC],
                                                   in1=tjs[:, FC:2 * FC], op=ALU.add),
                                "vet")
                        ins = e.tensor_tensor(out=ybf[:, :], in0=tjs[:, 0:FC],
                                              in1=tjs[:, 2 * FC:3 * FC], op=ALU.add)
                        p.inc(ins, "vet", 1)
                    p.ve(vy)
                    y_t = p.val("vet")

                    def syy(e, dt=dt, fc=fc, b=b, y_t=y_t):
                        p.wait(e, "vet", y_t)
                        p.wait(e, "dx")
                        ins = e.dma_start(
                            out=scr["ctx_bf"][dt * 128:(dt + 1) * 128,
                                              b * T + fc * FC:b * T + (fc + 1) * FC],
                            in_=ybf[:, :])
                        p.inc(ins, "dx", 16)
                    p.sy(syy)
    p.barrier()

    with ExitStack() as ls:
        lt, lrdy = load_weight_bf16(p, ls, W["Wo"], DM, "w2Wo", 0, DM)
        _, bofn, bordy = load_cols(p, ls, W["bo"], uq("b2bo"))
        cb = evac_to_hbm(p, ls, name="e2Wo", out_hbm=scr["acc_a"], out_dtype=F32,
                         bias_col=bofn, extra_ready=bordy)
        mm_stream(p, ls, name="m2Wo",
                  lhsT_tile=lambda k, m, lt=lt: lt(k)[:, m * 128:(m + 1) * 128],
                  nK=NDT, nM=NDT, lhs_ready=lrdy,
                  rhs_hbm=lambda k, f: scr["ctx_bf"][k * 128:(k + 1) * 128,
                                                     f * FC:(f + 1) * FC],
                  out_cb=cb)
    p.barrier()


def emit_ffn2(p, W, C, scr):
    DH = 4096
    s1c = s2c = None
    for half in range(2):
        with ExitStack() as ls:
            mw = DH // 2
            lt1, r1 = load_weight_bf16(p, ls, W["w1_w"], DM, "w2f1", half * mw, mw)
            lt2, r2 = load_weight_bf16(p, ls, W["w2_w"], DM, "w2f2", half * mw, mw)
            _, s1fn, s1r = load_cols(p, ls, W["w1_s"], uq("s2f1"), n=DH)
            _, s2fn, s2r = load_cols(p, ls, W["w2_s"], uq("s2f2"), n=DH)
            rbuf = p.sbuf(ls, uq("f2r"), [128, 2 * NDT * FC], BF16)
            psA = p.psum(ls, uq("f2A"), [128, FC])
            psB = p.psum(ls, uq("f2B"), [128, FC])
            sig = p.sbuf(ls, uq("f2s"), [128, FC], F32)
            gg = p.sbuf(ls, uq("f2g"), [128, FC], F32)
            uu = p.sbuf(ls, uq("f2u"), [128, FC], F32)
            hb = p.sbuf(ls, uq("f2h"), [128, 2 * FC], BF16)
            chS, chS2 = p.dma_pair()
            hst_tick = {0: p.dguard.get(chS), 1: p.dguard.get(chS2)}
            slot_war = [None, None]
            rhs_tick = [None, None]

            def prefetch(f, slot):
                dch = "dra" if slot == 0 else "drb"
                guard = slot_war[slot] or p.dguard.get(dch)

                def syf(e, f=f, slot=slot, dch=dch, guard=guard):
                    if guard is not None:
                        p.wait(e, guard[0], guard[1])
                    for k in range(NDT):
                        ins = e.dma_start(
                            out=rbuf[:, (slot * NDT + k) * FC:(slot * NDT + k + 1) * FC],
                            in_=scr["xn_bf"][k * 128:(k + 1) * 128, f * FC:(f + 1) * FC])
                        p.inc(ins, dch, 16)
                p.sy(syf)
                rhs_tick[slot] = (dch, p.val(dch))
            prefetch(0, 0)
            for f in range(NFC):
                slot = f & 1
                if f + 1 < NFC:
                    prefetch(f + 1, 1 - slot)
                for m in range(mw // 128):
                    gm = half * (mw // 128) + m

                    def tf(e, f=f, m=m, slot=slot, rt=rhs_tick[slot]):
                        p.wait(e, r1[0], r1[1])
                        p.wait(e, r2[0], r2[1])
                        p.wait(e, s1r[0], s1r[1])
                        p.wait(e, s2r[0], s2r[1])
                        p.wait(e, rt[0], rt[1])
                        p.wait(e, "vet")
                        p.wait(e, "sct")
                        for k in range(NDT):
                            e.matmul(out=psA[:, :], lhsT=lt1(k)[:, m * 128:(m + 1) * 128],
                                     rhs=rbuf[:, (slot * NDT + k) * FC:(slot * NDT + k + 1) * FC],
                                     start=(k == 0), stop=(k == NDT - 1))
                        for k in range(NDT):
                            ins = e.matmul(out=psB[:, :], lhsT=lt2(k)[:, m * 128:(m + 1) * 128],
                                           rhs=rbuf[:, (slot * NDT + k) * FC:(slot * NDT + k + 1) * FC],
                                           start=(k == 0), stop=(k == NDT - 1))
                        p.inc(ins, "pet", 1)
                    p.te(tf)
                    mm_t = p.val("pet")

                    def sf(e, gm=gm, mm_t=mm_t):
                        p.wait(e, "pet", mm_t)
                        p.wait(e, "vet")
                        ins = e.activation(out=sig[:, :], in_=psA[:, :], func=AF.Sigmoid,
                                           scale=s1fn(gm))
                        p.inc(ins, "sct", 1)
                    p.sc(sf)
                    sg_t = p.val("sct")

                    hsl = f & 1
                    hch = chS if hsl == 0 else chS2

                    def vf(e, gm=gm, mm_t=mm_t, sg_t=sg_t, f=f, hsl=hsl,
                           st_t=hst_tick[hsl]):
                        p.wait(e, "sct", sg_t)
                        if st_t is not None:
                            p.wait(e, st_t[0], st_t[1])
                        p.fence(e, e.tensor_scalar_mul(out=gg[:, :], in0=psA[:, :],
                                                       scalar1=s1fn(gm)), "vet")
                        p.fence(e, e.tensor_scalar_mul(out=uu[:, :], in0=psB[:, :],
                                                       scalar1=s2fn(gm)), "vet")
                        p.fence(e, e.tensor_tensor(out=gg[:, :], in0=gg[:, :],
                                                   in1=sig[:, :], op=ALU.mult), "vet")
                        ins = e.tensor_tensor(out=hb[:, hsl * FC:(hsl + 1) * FC],
                                              in0=gg[:, :], in1=uu[:, :], op=ALU.mult)
                        p.inc(ins, "vet", 1)
                    p.ve(vf)
                    v_t = p.val("vet")

                    def sy2(e, gm=gm, f=f, v_t=v_t, hsl=hsl, hch=hch):
                        p.wait(e, "vet", v_t)
                        ins = e.dma_start(
                            out=scr["h_bf"][gm * 128:(gm + 1) * 128, f * FC:(f + 1) * FC],
                            in_=hb[:, hsl * FC:(hsl + 1) * FC])
                        p.inc(ins, hch, 16)
                    p.sy(sy2)
                    hst_tick[hsl] = (hch, p.val(hch))
                    p.dguard[hch] = hst_tick[hsl]
                slot_war[slot] = ("pet", p.val("pet"))
            p.dguard["dra"] = ("pet", p.val("pet"))
            p.dguard["drb"] = ("pet", p.val("pet"))
        p.barrier()

    for wn, sn, src_t, dst in (("h1_w", "h1_s", "h_bf", "g_bf"),
                               ("h2_w", "h2_s", "g_bf", "u_bf")):
        for mc in range(8):
            with ExitStack() as ls:
                lt, lrdy = load_weight_bf16(p, ls, W[wn], DH, "w2" + wn + str(mc),
                                            mc * 512, 512)
                _, sfn, srdy = load_cols(p, ls, W[sn], uq("s2" + wn), n=DH)
                cb = evac_to_hbm(p, ls, name="e2" + wn + str(mc), out_hbm=scr[dst],
                                 out_dtype=BF16, func=AF.Gelu,
                                 scale_col=sfn, extra_ready=srdy)
                mm_stream(p, ls, name="m2" + wn + str(mc),
                          lhsT_tile=lambda k, m, lt=lt: lt(k)[:, m * 128:(m + 1) * 128],
                          nK=DH // 128, nM=4, lhs_ready=lrdy,
                          rhs_hbm=lambda k, f, src_t=src_t: scr[src_t][
                              k * 128:(k + 1) * 128, f * FC:(f + 1) * FC],
                          out_cb=lambda m, f, ps, cb=cb, mc=mc: cb(mc * 4 + m, f, ps))
            p.barrier()

    for half in range(2):
        with ExitStack() as ls:
            lt, lrdy = load_weight_bf16(p, ls, W["w3_w"], DH, "w2w3" + str(half),
                                        half * 512, 512)
            _, s3fn, s3rdy = load_cols(p, ls, W["w3_s"], uq("s2w3"), n=DM)
            cb = evac_to_hbm(p, ls, name="e2w3" + str(half), out_hbm=scr["acc_b"],
                             out_dtype=F32, scale_col=s3fn, extra_ready=s3rdy)
            mm_stream(p, ls, name="m2w3" + str(half),
                      lhsT_tile=lambda k, m, lt=lt: lt(k)[:, m * 128:(m + 1) * 128],
                      nK=DH // 128, nM=4, lhs_ready=lrdy,
                      rhs_hbm=lambda k, f: scr["u_bf"][k * 128:(k + 1) * 128,
                                                       f * FC:(f + 1) * FC],
                      out_cb=lambda m, f, ps, cb=cb, half=half: cb(half * 4 + m, f, ps))
        p.barrier()


def _emit_layer2(p, W, C, scr, x_in, x_out):
    emit_ln(p, x_in, None, scr["xn_bf"], C, uq("ln"))
    p.barrier()
    emit_attn2(p, {k.replace("L2_attn_", ""): W[k] for k in L2_W if "attn" in k},
               C, scr)
    emit_ffn2(p, {k.replace("L2_ffn_", ""): W[k] for k in L2_W if "ffn" in k},
              C, scr)
    emit_residual_add(p, x_in, scr["acc_a"], scr["acc_b"], x_out)
    p.barrier()


def emit_ssm(p, W, C, scr):
    with ExitStack() as ls:
        _, gfn, grdy = load_cols(p, ls, W["ln_gamma"], uq("s0g"))
        _, bfn, brdy = load_cols(p, ls, W["ln_beta"], uq("s0b"))

        def join(e):
            p.wait(e, grdy[0], grdy[1])
            p.wait(e, brdy[0], brdy[1])
            ins = e.nop()
            p.inc(ins, "sct", 1)
        p.sc(join)
        emit_ln(p, scr["xn_f32"], None, scr["q_bf"], C, uq("iln"),
                gamma_col=gfn, beta_col=bfn)
    p.barrier()

    sst = ExitStack()
    u_tok = p.sbuf(sst, "u_tok", [128, 32768], BF16)
    with ExitStack() as ls:
        ubuf = p.sbuf(ls, uq("s0u"), [128, TOK], BF16)
        ps_t = p.psum(ls, uq("s0p"), [128, 128], BF16)
        for dt in range(NDT):
            def syf(e, dt=dt):
                p.wait(e, "pet")
                ins = e.dma_start(out=ubuf[:], in_=scr["q_bf"][dt * 128:(dt + 1) * 128, :])
                p.inc(ins, "dr", 16)
            p.sy(syf)
            ld_t = p.val("dr")
            for tt in range(TOK // 128):
                bq, cq = tt // 16, tt % 16

                def tf(e, dt=dt, tt=tt, ld_t=ld_t):
                    p.wait(e, "dr", ld_t)
                    p.wait(e, "vet")
                    ins = e.transpose(out=ps_t[:, :], in_=ubuf[:, tt * 128:(tt + 1) * 128],
                                      identity=C["c_id_bf"][:])
                    p.inc(ins, "pet", 1)
                p.te(tf)
                t_t = p.val("pet")

                def vf(e, dt=dt, bq=bq, cq=cq, t_t=t_t):
                    p.wait(e, "pet", t_t)
                    ins = e.tensor_copy(
                        out=u_tok[:, bq * 16384 + cq * 1024 + dt * 128:
                                  bq * 16384 + cq * 1024 + (dt + 1) * 128],
                        in_=ps_t[:, :])
                    p.inc(ins, "vet", 1)
                p.ve(vf)
    p.barrier()

    with ExitStack() as ls:
        la = p.sbuf(ls, uq("sla"), [128, 64], F32)
        bb = p.sbuf(ls, uq("slb"), [128, 64], F32)
        cc = p.sbuf(ls, uq("slc"), [128, 64], F32)
        wt = p.sbuf(ls, uq("slw"), [128, 64], F32)
        cbt = p.sbuf(ls, uq("slcb"), [128, 64], F32)
        _, dtfn, dtrdy = load_cols(p, ls, W["log_dt"], uq("sldt"))
        dtexp = p.sbuf(ls, uq("slde"), [128, NDT], F32)

        def sdt(e):
            p.wait(e, dtrdy[0], dtrdy[1])
            for dt in range(NDT):
                ins = e.activation(out=dtexp[:, dt:dt + 1], in_=dtfn(dt), func=AF.Exp)
            p.inc(ins, "sct", 1)
        p.sc(sdt)
        dte_t = p.val("sct")

        for dt in range(NDT):
            def syf(e, dt=dt):
                p.wait(e, "sct")
                p.wait(e, "vet")
                for src_ap, dst in ((W["log_A"], la), (W["B_param"], bb),
                                    (W["C_param"], cc)):
                    ins = e.dma_start(out=dst[:], in_=src_ap[dt * 128:(dt + 1) * 128, :])
                    p.inc(ins, "dwa", 16)
            p.sy(syf)
            l_t = p.val("dwa")

            def sf(e, dt=dt, l_t=l_t):
                p.wait(e, "dwa", l_t)
                p.wait(e, "sct", dte_t)
                p.wait(e, "vet")
                p.wait(e, "dx")
                ins = e.activation(out=wt[:], in_=la[:], func=AF.Exp)
                p.inc(ins, "sct", 1)
            p.sc(sf)
            w_t = p.val("sct")

            def vf(e, dt=dt, l_t=l_t, w_t=w_t):
                p.wait(e, "sct", w_t)
                p.wait(e, "dwa", l_t)
                p.wait(e, "dx")
                p.fence(e, e.tensor_scalar_mul(out=wt[:], in0=wt[:],
                                               scalar1=dtexp[:, dt:dt + 1]), "vet")
                p.fence(e, e.tensor_tensor(out=cbt[:], in0=bb[:], in1=cc[:],
                                           op=ALU.mult), "vet")
                ins = e.tensor_scalar_mul(out=cbt[:], in0=cbt[:],
                                          scalar1=dtexp[:, dt:dt + 1])
                p.inc(ins, "vet", 1)
            p.ve(vf)
            v_t = p.val("vet")

            def sy2(e, dt=dt, v_t=v_t):
                p.wait(e, "vet", v_t)
                p.wait(e, "dx")
                ins = e.dma_start(out=scr["w_hbm"][dt * 128:(dt + 1) * 128, :], in_=wt[:])
                p.inc(ins, "dx", 16)
                ins = e.dma_start(out=scr["cb_hbm"][dt * 128:(dt + 1) * 128, :], in_=cbt[:])
                p.inc(ins, "dx", 16)
            p.sy(sy2)
    p.barrier()

    sbufs = ExitStack()
    CBCOL = p.sbuf(sbufs, "CBCOL", [128, 512], F32)
    A128 = p.sbuf(sbufs, "A128", [64, 1024], F32)
    R_all = p.sbuf(sbufs, "R_all", [64, 32768], BF16)
    with ExitStack() as ls:
        stg = p.sbuf(ls, uq("cbs"), [128, 128], F32)
        ps_t2 = p.psum(ls, uq("cbp"), [128, 128])
        cb_v = scr["cb_hbm"].rearrange("(dp two) n -> dp (two n)", two=2)
        w_v = scr["w_hbm"].rearrange("(dp two) n -> dp (two n)", two=2)
        for t4 in range(4):
            def syf(e, t4=t4):
                p.wait(e, "pet")
                ins = e.dma_start(out=stg[:], in_=cb_v[t4 * 128:(t4 + 1) * 128, :])
                p.inc(ins, "dwa", 16)
            p.sy(syf)
            l_t = p.val("dwa")

            def tf(e, t4=t4, l_t=l_t):
                p.wait(e, "dwa", l_t)
                p.wait(e, "vet")
                ins = e.transpose(out=ps_t2[:, :], in_=stg[:], identity=C["c_id"][:])
                p.inc(ins, "pet", 1)
            p.te(tf)
            t_t = p.val("pet")

            def vf(e, t4=t4, t_t=t_t):
                p.wait(e, "pet", t_t)
                ins = e.tensor_copy(out=CBCOL[:, t4 * 128:(t4 + 1) * 128], in_=ps_t2[:, :])
                p.inc(ins, "vet", 1)
            p.ve(vf)
        for dt in range(NDT):
            def syf2(e, dt=dt):
                p.wait(e, "pet")
                ins = e.dma_start(out=stg[:, 0:64],
                                  in_=scr["w_hbm"][dt * 128:(dt + 1) * 128, :])
                p.inc(ins, "dwa", 16)
            p.sy(syf2)
            l_t = p.val("dwa")

            def tf2(e, dt=dt, l_t=l_t):
                p.wait(e, "dwa", l_t)
                p.wait(e, "sct")
                p.wait(e, "vet")
                ins = e.transpose(out=ps_t2[0:64, :], in_=stg[:, 0:64],
                                  identity=C["c_id"][:])
                p.inc(ins, "pet", 1)
            p.te(tf2)
            t_t = p.val("pet")

            def sf2(e, dt=dt, t_t=t_t):
                p.wait(e, "pet", t_t)
                ins = e.activation(out=A128[:, dt * 128:(dt + 1) * 128],
                                   in_=ps_t2[0:64, :], func=AF.Exp, scale=-128.0)
                p.inc(ins, "sct", 1)
            p.sc(sf2)
    p.barrier()

    NW = 16
    with ExitStack() as ls:
        wrow = p.sbuf(ls, uq("wr"), [1, 2 * NW * 128], F32)
        wrowb = p.sbuf(ls, uq("wrb"), [1, 2 * NW * 128], BF16)
        al = p.sbuf(ls, uq("al"), [128, 128], BF16)
        rstg = p.sbuf(ls, uq("rstg"), [64, 512], F32)
        ps_tab = p.psum(ls, uq("ptab"), [128, 132])
        ps_r = [p.psum(ls, uq("pr"), [64, 512]) for _ in range(2)]
        w_flat = scr["w_hbm"].rearrange("d n -> (d n)").rearrange("(one f) -> one f", one=1)

        def wchunk(ci):
            dch = "dwa" if (ci & 1) == 0 else "dwb"
            sl = ci & 1

            def syf(e, ci=ci, dch=dch, sl=sl):
                p.wait(e, "pet")
                ins = e.dma_start(out=wrow[0:1, sl * NW * 128:(sl + 1) * NW * 128],
                                  in_=w_flat[0:1, ci * NW * 128:(ci + 1) * NW * 128])
                p.inc(ins, dch, 16)
            p.sy(syf)
            d_t = p.val(dch)

            def vf(e, ci=ci, dch=dch, sl=sl, d_t=d_t):
                p.wait(e, dch, d_t)
                ins = e.tensor_copy(out=wrowb[0:1, sl * NW * 128:(sl + 1) * NW * 128],
                                    in_=wrow[0:1, sl * NW * 128:(sl + 1) * NW * 128])
                p.inc(ins, "vet", 1)
            p.ve(vf)
            return ("vet", p.val("vet"))

        wc_rdy = wchunk(0)
        uview = u_tok.ap().rearrange("p (b c d) -> p c b d", b=2, c=16, d=1024)
        for dp in range(512):
            ci, off = dp // NW, dp % NW
            if off == 0 and dp + NW < 512:
                nxt_rdy = wchunk(ci + 1)
            if off == 0 and dp > 0:
                wc_rdy = nxt_rdy
            sl = ci & 1

            def tfa(e, dp=dp, sl=sl, off=off, wc_rdy=wc_rdy):
                p.wait(e, wc_rdy[0], wc_rdy[1])
                p.wait(e, "sct")
                ins = e.matmul(out=ps_tab[:, 0:128],
                               lhsT=C["c_iota_sm127_bf"][:],
                               rhs=wrowb[0:1, (sl * NW + off) * 128:(sl * NW + off + 1) * 128],
                               start=True, stop=True)
                p.inc(ins, "pet", 1)
            p.te(tfa)
            ta_t = p.val("pet")

            def sfa(e, dp=dp, ta_t=ta_t):
                p.wait(e, "pet", ta_t)
                p.wait(e, "pet")
                ins = e.activation(out=al[:, :], in_=ps_tab[:, 0:128], func=AF.Exp)
                p.inc(ins, "sct", 1)
            p.sc(sfa)
            ale_t = p.val("sct")

            bank = (dp // 8) & 1
            for dl in range(2):
                d = 2 * dp + dl
                ycol = (d % 16) * 32

                def tfr(e, d=d, dl=dl, ycol=ycol, bank=bank, ale_t=ale_t, dp=dp):
                    p.wait(e, "sct", ale_t)
                    if d % 16 == 0:
                        p.wait(e, "vet")
                    ins = e.matmul(out=ps_r[bank][0:64, ycol:ycol + 32],
                                   lhsT=al[:, dl * 64:(dl + 1) * 64],
                                   rhs=uview[:, :, :, d], start=True, stop=True)
                    p.inc(ins, "pet", 1)
                p.te(tfr)
            if dp % 8 == 7:
                r_t = p.val("pet")
                base = (dp - 7) * 2 * 2

                def vev(e, dp=dp, bank=bank, r_t=r_t):
                    p.wait(e, "pet", r_t)
                    d0 = 2 * (dp - 7)
                    ins = e.tensor_copy(out=R_all[:, d0 * 32:(d0 + 16) * 32],
                                        in_=ps_r[bank][0:64, :])
                    p.inc(ins, "vet", 1)
                p.ve(vev)
    p.barrier()

    with ExitStack() as ls:
        state = p.sbuf(ls, uq("st8"), [64, 1024], F32)
        Rv = R_all.ap().rearrange("p (d c b) -> p d c b", d=1024, c=16, b=2)
        for b in range(B):
            def vz(e, b=b):
                p.wait(e, "vet")
                ins = e.memset(state[:, :], 0.0)
                p.inc(ins, "vet", 1)
            p.ve(vz)
            for c in range(16):
                def vstep(e, b=b, c=c):
                    p.wait(e, "vet")
                    if c > 0:
                        p.fence(e, e.tensor_copy(out=Rv[:, :, c - 1, b],
                                                 in_=state[:, :]), "vet")
                    p.fence(e, e.tensor_tensor(out=state[:, :], in0=state[:, :],
                                               in1=A128[:, :], op=ALU.mult), "vet")
                    ins = e.tensor_tensor(out=state[:, :], in0=state[:, :],
                                          in1=Rv[:, :, c, b], op=ALU.add)
                    p.inc(ins, "vet", 1)
                p.ve(vstep)
    p.barrier()
    with ExitStack() as ls:
        wrow = p.sbuf(ls, uq("wr2"), [1, 2 * 16 * 128], F32)
        wrowb = p.sbuf(ls, uq("wrb2"), [1, 2 * 16 * 128], BF16)
        wrown = p.sbuf(ls, uq("wrn2"), [1, 2 * 16 * 128], BF16)
        aneg = p.sbuf(ls, uq("aneg"), [64, 256], BF16)
        cbax = p.sbuf(ls, uq("cbax"), [64, 2 * 132], BF16)
        cbaxt = p.sbuf(ls, uq("cbaxt"), [64, 2 * 132], BF16)
        kd = p.sbuf(ls, uq("kd"), [128, 256], BF16)
        ystg = p.sbuf(ls, uq("ystg"), [128, 512], BF16)
        ytr = p.sbuf(ls, uq("ytr"), [128, 128], BF16)
        ps_tab = p.psum(ls, uq("ptab2"), [128, 132])
        ps_kd = p.psum(ls, uq("pkd"), [128, 256])
        ps_y = [p.psum(ls, uq("py"), [128, 512]) for _ in range(2)]
        ps_tr = p.psum(ls, uq("ptr"), [128, 128], BF16)
        NW = 16
        w_flat = scr["w_hbm"].rearrange("d n -> (d n)").rearrange("(one f) -> one f", one=1)
        uview = u_tok.ap().rearrange("p (b c d) -> p c b d", b=2, c=16, d=1024)
        Sv = R_all.ap().rearrange("p (d c b) -> p d c b", d=1024, c=16, b=2)

        def wchunk2(ci):
            dch = "dwa" if (ci & 1) == 0 else "dwb"
            sl = ci & 1

            def syf(e, ci=ci, dch=dch, sl=sl):
                p.wait(e, "pet")
                ins = e.dma_start(out=wrow[0:1, sl * NW * 128:(sl + 1) * NW * 128],
                                  in_=w_flat[0:1, ci * NW * 128:(ci + 1) * NW * 128])
                p.inc(ins, dch, 16)
            p.sy(syf)
            d_t = p.val(dch)

            def vf(e, ci=ci, dch=dch, sl=sl, d_t=d_t):
                p.wait(e, dch, d_t)
                p.fence(e, e.tensor_copy(
                    out=wrowb[0:1, sl * NW * 128:(sl + 1) * NW * 128],
                    in_=wrow[0:1, sl * NW * 128:(sl + 1) * NW * 128]), "vet")
                ins = e.tensor_scalar_mul(
                    out=wrown[0:1, sl * NW * 128:(sl + 1) * NW * 128],
                    in0=wrow[0:1, sl * NW * 128:(sl + 1) * NW * 128], scalar1=-1.0)
                p.inc(ins, "vet", 1)
            p.ve(vf)
            return ("vet", p.val("vet"))

        wc_rdy = wchunk2(0)
        for dp in range(512):
            ci, off = dp // NW, dp % NW
            if off == 0 and dp + NW < 512:
                nxt_rdy = wchunk2(ci + 1)
            if off == 0 and dp > 0:
                wc_rdy = nxt_rdy
            sl = ci & 1
            wsl = slice((sl * NW + off) * 128, (sl * NW + off + 1) * 128)

            def tft(e, dp=dp, wsl=wsl, wc_rdy=wc_rdy):
                p.wait(e, wc_rdy[0], wc_rdy[1])
                p.wait(e, "sct")
                ins = e.matmul(out=ps_tab[:, 0:128], lhsT=wrowb[0:1, wsl],
                               rhs=C["c_iota_s_bf"][:], start=True, stop=True)
                p.inc(ins, "pet", 1)
            p.te(tft)
            t1_t = p.val("pet")

            def sf1(e, t1_t=t1_t):
                p.wait(e, "pet", t1_t)
                p.wait(e, "pet")
                e.activation(out=aneg[:, 0:128], in_=ps_tab[0:64, 0:128], func=AF.Exp)
                ins = e.activation(out=aneg[:, 128:256], in_=ps_tab[64:128, 0:128],
                                   func=AF.Exp)
                p.inc(ins, "sct", 1)
            p.sc(sf1)
            an_t = p.val("sct")

            def tft2(e, dp=dp, wsl=wsl, an_t=an_t):
                p.wait(e, "sct", an_t)
                ins = e.matmul(out=ps_tab[:, 0:129], lhsT=wrown[0:1, wsl],
                               rhs=C["c_iota_t129_bf"][:], start=True, stop=True)
                p.inc(ins, "pet", 1)
            p.te(tft2)
            t2_t = p.val("pet")

            def sf2(e, t2_t=t2_t):
                p.wait(e, "pet", t2_t)
                p.wait(e, "vet")
                e.activation(out=cbaxt[:, 0:129], in_=ps_tab[0:64, 0:129], func=AF.Exp)
                ins = e.activation(out=cbaxt[:, 132:261], in_=ps_tab[64:128, 0:129],
                                   func=AF.Exp)
                p.inc(ins, "sct", 1)
            p.sc(sf2)
            cx_t = p.val("sct")

            def vcb(e, dp=dp, cx_t=cx_t):
                p.wait(e, "sct", cx_t)
                p.wait(e, "pet")
                e.tensor_scalar_mul(out=cbax[:, 0:129], in0=cbaxt[:, 0:129],
                                    scalar1=CBCOL[0:64, dp:dp + 1])
                ins = e.tensor_scalar_mul(out=cbax[:, 132:261], in0=cbaxt[:, 132:261],
                                          scalar1=CBCOL[64:128, dp:dp + 1])
                p.inc(ins, "vet", 1)
            p.ve(vcb)
            cb_t = p.val("vet")

            def tkd(e, dp=dp, an_t=an_t, cb_t=cb_t):
                p.wait(e, "sct", an_t)
                p.wait(e, "vet", cb_t)
                p.wait(e, "vet")
                for dl in range(2):
                    ins = e.matmul(out=ps_kd[:, dl * 128:(dl + 1) * 128],
                                   lhsT=aneg[:, dl * 128:(dl + 1) * 128],
                                   rhs=cbax[:, dl * 132:dl * 132 + 128],
                                   start=True, stop=True)
                p.inc(ins, "pet", 1)
            p.te(tkd)
            kb_t = p.val("pet")

            def vkd(e, kb_t=kb_t):
                p.wait(e, "pet", kb_t)
                ins = e.tensor_tensor(out=kd[:, :], in0=ps_kd[:, :],
                                      in1=C["c_utri2_bf"][:], op=ALU.mult)
                p.inc(ins, "vet", 1)
            p.ve(vkd)
            kd_t = p.val("vet")

            bank = (dp // 8) & 1
            for dl in range(2):
                d = 2 * dp + dl
                ycol = (d % 16) * 32

                def tap(e, d=d, dl=dl, ycol=ycol, bank=bank, kd_t=kd_t, dp=dp):
                    p.wait(e, "vet", kd_t)
                    if d % 16 == 0:
                        p.wait(e, "vet")
                    e.matmul(out=ps_y[bank][:, ycol:ycol + 32],
                             lhsT=kd[:, dl * 128:(dl + 1) * 128],
                             rhs=uview[:, :, :, d], start=True, stop=False)
                    ins = e.matmul(out=ps_y[bank][:, ycol + 2:ycol + 32],
                                   lhsT=cbax[0:64, dl * 132 + 1:dl * 132 + 129],
                                   rhs=Sv[:, d, 0:15, :], start=False, stop=True)
                    p.inc(ins, "pet", 1)
                p.te(tap)

            if dp % 8 == 7:
                ya_t = p.val("pet")
                d0 = 2 * (dp - 7)

                def vys(e, dp=dp, bank=bank, ya_t=ya_t):
                    p.wait(e, "pet", ya_t)
                    p.wait(e, "pet")
                    ins = e.tensor_copy(out=ystg[:, :], in_=ps_y[bank][:, :])
                    p.inc(ins, "vet", 1)
                p.ve(vys)
                ys_t = p.val("vet")
                for g in range(4):
                    def ttr(e, g=g, ys_t=ys_t):
                        p.wait(e, "vet", ys_t)
                        p.wait(e, "vet")
                        ins = e.transpose(out=ps_tr[:, :], in_=ystg[:, g * 128:(g + 1) * 128],
                                          identity=C["c_id_bf"][:])
                        p.inc(ins, "pet", 1)
                    p.te(ttr)
                    tr_t = p.val("pet")

                    def vtr(e, g=g, tr_t=tr_t):
                        p.wait(e, "pet", tr_t)
                        p.wait(e, "dx")
                        ins = e.tensor_copy(out=ytr[:, :], in_=ps_tr[:, :])
                        p.inc(ins, "vet", 1)
                    p.ve(vtr)
                    vt_t = p.val("vet")

                    def syy(e, g=g, d0=d0, vt_t=vt_t):
                        p.wait(e, "vet", vt_t)
                        p.wait(e, "dx")
                        dd = d0 + g * 4
                        for di in range(4):
                            yv = scr["v_bf"][dd + di:dd + di + 1, :].rearrange(
                                "one (b c s) -> one c b s", b=2, c=16, s=128)
                            ins = e.dma_start(out=yv,
                                              in_=ytr[di * 32:(di + 1) * 32, :])
                            p.inc(ins, "dx", 16)
                    p.sy(syy)
    p.barrier()
    sbufs.close()
    sst.close()

    with ExitStack() as ls:
        lt, lrdy = load_weight_bf16(p, ls, W["W_out"], DM, "wWo0", 0, DM)
        _, dcol, drdy = load_cols(p, ls, W["D_param"], uq("s2D"))
        _, bocol, bordy = load_cols(p, ls, W["b_out"], uq("s2bo"))
        rbuf = p.sbuf(ls, uq("s2r"), [128, 2 * NDT * FC], BF16)
        ubuf = p.sbuf(ls, uq("s2u"), [128, NDT * FC], BF16)
        ybuf = p.sbuf(ls, uq("s2y"), [128, NDT * FC], BF16)
        stage = p.sbuf(ls, uq("s2e"), [128, 2 * FC], F32)
        axn = p.sbuf(ls, uq("s2x"), [128, 2 * FC], F32)
        chL, chL2 = p.dma_pair()
        chO, chO2 = p.dma_pair()
        xn_tick = {0: p.dguard.get(chL), 1: p.dguard.get(chL2)}
        ost_tick = {0: p.dguard.get(chO), 1: p.dguard.get(chO2)}
        xn_used = {0: None, 1: None}
        psA = p.psum(ls, uq("s2A"), [128, FC])
        psB = p.psum(ls, uq("s2B"), [128, FC])
        banks = [psA, psB]
        bank_war = [None, None]
        rhs_rdy = [None, None]
        slot_war = [None, None]

        def rhs_prep(f, slot):
            dch = "dra" if slot == 0 else "drb"
            guard = slot_war[slot] or p.dguard.get(dch)

            def syf(e, f=f, slot=slot, dch=dch, guard=guard):
                if guard is not None:
                    p.wait(e, guard[0], guard[1])
                p.wait(e, "vet")
                for k in range(NDT):
                    ins = e.dma_start(out=ubuf[:, k * FC:(k + 1) * FC],
                                      in_=scr["q_bf"][k * 128:(k + 1) * 128,
                                                      f * FC:(f + 1) * FC])
                    p.inc(ins, dch, 16)
                    ins = e.dma_start(out=ybuf[:, k * FC:(k + 1) * FC],
                                      in_=scr["v_bf"][k * 128:(k + 1) * 128,
                                                      f * FC:(f + 1) * FC])
                    p.inc(ins, dch, 16)
            p.sy(syf)
            d_t = p.val(dch)

            def vf(e, f=f, slot=slot, dch=dch, d_t=d_t):
                p.wait(e, dch, d_t)
                p.wait(e, drdy[0], drdy[1])
                for k in range(NDT):
                    ins = e.scalar_tensor_tensor(
                        out=rbuf[:, (slot * NDT + k) * FC:(slot * NDT + k + 1) * FC],
                        in0=ubuf[:, k * FC:(k + 1) * FC], scalar=dcol(k),
                        in1=ybuf[:, k * FC:(k + 1) * FC], op0=ALU.mult, op1=ALU.add)
                    p.inc(ins, "vet", 1)
            p.ve(vf)
            rhs_rdy[slot] = ("vet", p.val("vet"))

        rhs_prep(0, 0)
        bi = 0
        for f in range(NFC):
            slot = f & 1
            if f + 1 < NFC:
                rhs_prep(f + 1, 1 - slot)
            for m in range(NDT):
                ps = banks[bi]

                def tf(e, f=f, m=m, ps=ps, war=bank_war[bi], rt=rhs_rdy[slot], slot=slot):
                    p.wait(e, lrdy[0], lrdy[1])
                    p.wait(e, rt[0], rt[1])
                    if war is not None:
                        p.wait(e, war[0], war[1])
                    for k in range(NDT):
                        ins = e.matmul(
                            out=ps[:, :], lhsT=lt(k)[:, m * 128:(m + 1) * 128],
                            rhs=rbuf[:, (slot * NDT + k) * FC:(slot * NDT + k + 1) * FC],
                            start=(k == 0), stop=(k == NDT - 1))
                    p.inc(ins, "pet", 1)
                p.te(tf)
                mm_t = p.val("pet")
                esl = (f * NDT + m) & 1

                lch = chL if esl == 0 else chL2
                och = chO if esl == 0 else chO2

                def syx(e, m=m, f=f, esl=esl, lch=lch, war=xn_used[esl],
                        g=xn_tick[esl]):
                    if war is not None:
                        p.wait(e, war[0], war[1])
                    elif g is not None:
                        p.wait(e, g[0], g[1])
                    ins = e.dma_start(out=axn[:, esl * FC:(esl + 1) * FC],
                                      in_=scr["xn_f32"][m * 128:(m + 1) * 128,
                                                        f * FC:(f + 1) * FC])
                    p.inc(ins, lch, 16)
                p.sy(syx)
                ax_t = (lch, p.val(lch))
                xn_tick[esl] = ax_t

                def vf2(e, m=m, f=f, esl=esl, mm_t=mm_t, ax_t=ax_t,
                        st_t=ost_tick[esl]):
                    p.wait(e, "pet", mm_t)
                    p.wait(e, ax_t[0], ax_t[1])
                    p.wait(e, bordy[0], bordy[1])
                    if st_t is not None:
                        p.wait(e, st_t[0], st_t[1])
                    p.fence(e, e.tensor_scalar_add(out=stage[:, esl * FC:(esl + 1) * FC],
                                                   in0=ps[:, :], scalar1=bocol(m)), "vet")
                    ins = e.tensor_tensor(out=stage[:, esl * FC:(esl + 1) * FC],
                                          in0=stage[:, esl * FC:(esl + 1) * FC],
                                          in1=axn[:, esl * FC:(esl + 1) * FC], op=ALU.add)
                    p.inc(ins, "vet", 1)
                p.ve(vf2)
                ev_t = p.val("vet")
                bank_war[bi] = ("vet", ev_t)
                xn_used[esl] = ("vet", ev_t)

                def sy2(e, m=m, f=f, esl=esl, ev_t=ev_t, och=och):
                    p.wait(e, "vet", ev_t)
                    ins = e.dma_start(out=scr["acc_a"][m * 128:(m + 1) * 128,
                                                       f * FC:(f + 1) * FC],
                                      in_=stage[:, esl * FC:(esl + 1) * FC])
                    p.inc(ins, och, 16)
                p.sy(sy2)
                ost_tick[esl] = (och, p.val(och))
                p.dguard[och] = ost_tick[esl]
                p.dguard[lch] = xn_tick[esl]
                bi ^= 1
            slot_war[slot] = ("pet", p.val("pet"))
        p.dguard["dra"] = ("pet", p.val("pet"))
        p.dguard["drb"] = ("pet", p.val("pet"))
    p.barrier()


def emit_moe(p, W, C, scr):
    EPS = 1e-5
    mst = ExitStack()
    lg_sb = p.sbuf(mst, "lg_sb", [8, TOK], F32)
    w_sb = p.sbuf(mst, "w_sb", [8, TOK], F32)
    with ExitStack() as ls:
        _, sfn, srdy = load_cols(p, ls, W["ln_scale"], uq("m0s"))
        _, bfn, brdy = load_cols(p, ls, W["ln_bias"], uq("m0b"))

        def join(e):
            p.wait(e, srdy[0], srdy[1])
            p.wait(e, brdy[0], brdy[1])
            ins = e.nop()
            p.inc(ins, "sct", 1)
        p.sc(join)
        emit_ln(p, scr["xn_f32"], scr["xg_f32"], scr["u_bf"], C, uq("gln"),
                gamma_col=sfn, beta_col=bfn)
    p.barrier()
    with ExitStack() as ls:
        gw = p.sbuf(ls, uq("mgw"), [128, NDT * 8], F32)
        grb = p.sbuf(ls, uq("mgr"), [128, NDT * FC], F32)
        ps_g = p.psum(ls, uq("mgp"), [8, FC])

        def syg(e):
            p.wait(e, "dwa")
            for k in range(NDT):
                ins = e.dma_start(out=gw[:, k * 8:(k + 1) * 8],
                                  in_=W["gate_w"][k * 128:(k + 1) * 128, :])
                p.inc(ins, "dwa", 16)
        p.sy(syg)
        gw_t = p.val("dwa")
        for f in range(NFC):
            def syr(e, f=f):
                p.wait(e, "pet")
                p.wait(e, "dra")
                for k in range(NDT):
                    ins = e.dma_start(out=grb[:, k * FC:(k + 1) * FC],
                                      in_=scr["xg_f32"][k * 128:(k + 1) * 128,
                                                        f * FC:(f + 1) * FC])
                    p.inc(ins, "dra", 16)
            p.sy(syr)
            r_t = p.val("dra")

            def tg(e, f=f, r_t=r_t):
                p.wait(e, "dwa", gw_t)
                p.wait(e, "dra", r_t)
                p.wait(e, "vet")
                for k in range(NDT):
                    ins = e.matmul(out=ps_g[0:8, :], lhsT=gw[:, k * 8:(k + 1) * 8],
                                   rhs=grb[:, k * FC:(k + 1) * FC],
                                   start=(k == 0), stop=(k == NDT - 1))
                p.inc(ins, "pet", 1)
            p.te(tg)
            g_t = p.val("pet")

            def vg(e, f=f, g_t=g_t):
                p.wait(e, "pet", g_t)
                ins = e.tensor_copy(out=lg_sb[:, f * FC:(f + 1) * FC], in_=ps_g[0:8, :])
                p.inc(ins, "vet", 1)
            p.ve(vg)
        p.dguard["dra"] = ("pet", p.val("pet"))
    p.barrier()

    with ExitStack() as ls:
        lgt = p.sbuf(ls, uq("m1l"), [128, 8], F32)
        prb = p.sbuf(ls, uq("m1p"), [128, 8], F32)
        ge1 = p.sbuf(ls, uq("m1g"), [128, 8], F32)
        msk = p.sbuf(ls, uq("m1m"), [128, 8], F32)
        wtk = p.sbuf(ls, uq("m1w"), [128, 8], F32)
        colb = p.sbuf(ls, uq("m1c"), [128, 8], F32)
        ps_t = p.psum(ls, uq("m1t"), [128, 8])
        ps_b = p.psum(ls, uq("m1b"), [8, 128])
        for tt in range(TOK // 128):
            def tf(e, tt=tt):
                p.wait(e, "vet")
                ins = e.transpose(out=ps_t[:, :], in_=lg_sb[:, tt * 128:(tt + 1) * 128],
                                  identity=C["c_id"][0:8, 0:8])
                p.inc(ins, "pet", 1)
            p.te(tf)
            t_t = p.val("pet")

            def v1(e, tt=tt, t_t=t_t):
                p.wait(e, "pet", t_t)
                p.fence(e, e.tensor_copy(out=lgt[:, :], in_=ps_t[:, :]), "vet")
                p.fence(e, e.reduce_max(out=colb[:, 0:1], in_=lgt[:, :], axis=AX), "vet")
                p.fence(e, e.tensor_scalar_mul(out=colb[:, 1:2], in0=colb[:, 0:1],
                                               scalar1=-1.0), "vet")
                ins = e.engine_nop()
                p.inc(ins, "vet", 1)
            p.ve(v1)
            v1_t = p.val("vet")

            def s1(e, tt=tt, v1_t=v1_t):
                p.wait(e, "vet", v1_t)
                ins = e.activation(out=prb[:, :], in_=lgt[:, :], func=AF.Exp,
                                   bias=colb[:, 1:2])
                p.inc(ins, "sct", 1)
            p.sc(s1)
            s1_t = p.val("sct")

            def v2(e, tt=tt, s1_t=s1_t):
                p.wait(e, "sct", s1_t)
                p.fence(e, e.reduce_sum(out=colb[:, 2:3], in_=prb[:, :], axis=AX), "vet")
                p.fence(e, e.reduce_max(out=colb[:, 3:4], in_=prb[:, :], axis=AX), "vet")
                p.fence(e, e.tensor_scalar(out=ge1[:, :], in0=prb[:, :],
                                           scalar1=colb[:, 3:4], scalar2=None,
                                           op0=ALU.is_ge), "vet")
                p.fence(e, e.tensor_scalar(out=msk[:, :], in0=ge1[:, :], scalar1=-1.0,
                                           scalar2=1.0, op0=ALU.mult, op1=ALU.add), "vet")
                p.fence(e, e.tensor_tensor(out=msk[:, :], in0=prb[:, :], in1=msk[:, :],
                                           op=ALU.mult), "vet")
                p.fence(e, e.reduce_max(out=colb[:, 4:5], in_=msk[:, :], axis=AX), "vet")
                p.fence(e, e.tensor_scalar(out=msk[:, :], in0=msk[:, :],
                                           scalar1=colb[:, 4:5], scalar2=None,
                                           op0=ALU.is_ge), "vet")
                p.fence(e, e.tensor_tensor(out=msk[:, :], in0=msk[:, :], in1=ge1[:, :],
                                           op=ALU.add), "vet")
                p.fence(e, e.tensor_tensor(out=colb[:, 5:6], in0=colb[:, 3:4],
                                           in1=colb[:, 4:5], op=ALU.add), "vet")
                p.fence(e, e.scalar_tensor_tensor(out=colb[:, 5:6], in0=colb[:, 2:3],
                                                  scalar=EPS, in1=colb[:, 5:6],
                                                  op0=ALU.mult, op1=ALU.add), "vet")
                p.fence(e, e.reciprocal(out=colb[:, 6:7], in_=colb[:, 5:6]), "vet")
                p.fence(e, e.tensor_tensor(out=wtk[:, :], in0=prb[:, :], in1=msk[:, :],
                                           op=ALU.mult), "vet")
                ins = e.tensor_scalar_mul(out=wtk[:, :], in0=wtk[:, :],
                                          scalar1=colb[:, 6:7])
                p.inc(ins, "vet", 1)
            p.ve(v2)
            v2_t = p.val("vet")

            def tf2(e, tt=tt, v2_t=v2_t):
                p.wait(e, "vet", v2_t)
                p.wait(e, "vet")
                ins = e.transpose(out=ps_b[:, :], in_=wtk[:, :],
                                  identity=C["c_id"][:])
                p.inc(ins, "pet", 1)
            p.te(tf2)
            t2_t = p.val("pet")

            def v3(e, tt=tt, t2_t=t2_t):
                p.wait(e, "pet", t2_t)
                ins = e.tensor_copy(out=w_sb[:, tt * 128:(tt + 1) * 128],
                                    in_=ps_b[0:8, :])
                p.inc(ins, "vet", 1)
            p.ve(v3)
    p.barrier()

    for ei in range(9):
        if ei == 0:
            wg_ap, wu_ap, wd_ap = W["sWg"], W["sWu"], W["sWd"]
        else:
            wg_ap, wu_ap, wd_ap = (W["Wg"][ei - 1], W["Wu"][ei - 1], W["Wd"][ei - 1])
        IH = 2048
        NMH = IH // 128
        for half in range(2):
            with ExitStack() as ls:
                mw = IH // 2
                lt, lrdy = load_weight_bf16(p, ls, wg_ap, DM, uq("mwg"), half * mw, mw)
                lt2, lrdy2 = load_weight_bf16(p, ls, wu_ap, DM, uq("mwu"), half * mw, mw)
                rbuf = p.sbuf(ls, uq("mr"), [128, 2 * NDT * FC], BF16)
                psA = p.psum(ls, uq("mA"), [128, FC])
                psB = p.psum(ls, uq("mB"), [128, FC])
                sig = p.sbuf(ls, uq("ms"), [128, FC], F32)
                hb = p.sbuf(ls, uq("mh"), [128, 2 * FC], BF16)
                chS, chS2 = p.dma_pair()
                hst_tick = {0: p.dguard.get(chS), 1: p.dguard.get(chS2)}
                slot_war = [None, None]
                rhs_tick = [None, None]

                def prefetch(f, slot):
                    dch = "dra" if slot == 0 else "drb"
                    guard = slot_war[slot] or p.dguard.get(dch)

                    def syf(e, f=f, slot=slot, dch=dch, guard=guard):
                        if guard is not None:
                            p.wait(e, guard[0], guard[1])
                        for k in range(NDT):
                            ins = e.dma_start(
                                out=rbuf[:, (slot * NDT + k) * FC:(slot * NDT + k + 1) * FC],
                                in_=scr["xn_bf"][k * 128:(k + 1) * 128,
                                                 f * FC:(f + 1) * FC])
                            p.inc(ins, dch, 16)
                    p.sy(syf)
                    rhs_tick[slot] = (dch, p.val(dch))
                prefetch(0, 0)
                for f in range(NFC):
                    slot = f & 1
                    if f + 1 < NFC:
                        prefetch(f + 1, 1 - slot)
                    for m in range(NMH // 2):
                        gm = half * (NMH // 2) + m

                        def tf(e, f=f, m=m, slot=slot, rt=rhs_tick[slot]):
                            p.wait(e, lrdy[0], lrdy[1])
                            p.wait(e, lrdy2[0], lrdy2[1])
                            p.wait(e, rt[0], rt[1])
                            p.wait(e, "vet")
                            p.wait(e, "sct")
                            for k in range(NDT):
                                e.matmul(out=psA[:, :],
                                         lhsT=lt(k)[:, m * 128:(m + 1) * 128],
                                         rhs=rbuf[:, (slot * NDT + k) * FC:(slot * NDT + k + 1) * FC],
                                         start=(k == 0), stop=(k == NDT - 1))
                            for k in range(NDT):
                                ins = e.matmul(out=psB[:, :],
                                               lhsT=lt2(k)[:, m * 128:(m + 1) * 128],
                                               rhs=rbuf[:, (slot * NDT + k) * FC:(slot * NDT + k + 1) * FC],
                                               start=(k == 0), stop=(k == NDT - 1))
                            p.inc(ins, "pet", 1)
                        p.te(tf)
                        mm_t = p.val("pet")

                        def sf(e, mm_t=mm_t):
                            p.wait(e, "pet", mm_t)
                            p.wait(e, "vet")
                            ins = e.activation(out=sig[:, :], in_=psA[:, :],
                                               func=AF.Sigmoid)
                            p.inc(ins, "sct", 1)
                        p.sc(sf)
                        sg_t = p.val("sct")

                        hsl = f & 1
                        hch = chS if hsl == 0 else chS2

                        def vf(e, f=f, sg_t=sg_t, mm_t=mm_t, st_t=hst_tick[hsl],
                               hsl=hsl):
                            p.wait(e, "sct", sg_t)
                            if st_t is not None:
                                p.wait(e, st_t[0], st_t[1])
                            p.fence(e, e.tensor_tensor(out=sig[:, :], in0=psA[:, :],
                                                       in1=sig[:, :], op=ALU.mult), "vet")
                            ins = e.tensor_tensor(out=hb[:, hsl * FC:(hsl + 1) * FC],
                                                  in0=sig[:, :], in1=psB[:, :],
                                                  op=ALU.mult)
                            p.inc(ins, "vet", 1)
                        p.ve(vf)
                        v_t = p.val("vet")

                        def sy2(e, gm=gm, f=f, v_t=v_t, hsl=hsl, hch=hch):
                            p.wait(e, "vet", v_t)
                            ins = e.dma_start(
                                out=scr["h_bf"][gm * 128:(gm + 1) * 128,
                                                f * FC:(f + 1) * FC],
                                in_=hb[:, hsl * FC:(hsl + 1) * FC])
                            p.inc(ins, hch, 16)
                        p.sy(sy2)
                        hst_tick[hsl] = (hch, p.val(hch))
                        p.dguard[hch] = hst_tick[hsl]
                    slot_war[slot] = ("pet", p.val("pet"))
                p.dguard["dra"] = ("pet", p.val("pet"))
                p.dguard["drb"] = ("pet", p.val("pet"))
            p.barrier()
        with ExitStack() as ls:
            lt, lrdy = load_weight_bf16(p, ls, wd_ap, IH, uq("mwd"), 0, DM)
            rbuf = p.sbuf(ls, uq("mdr"), [128, 2 * NMH * FC], BF16)
            wrow = p.sbuf(ls, uq("mwr"), [1, FC], F32)
            wbc = p.sbuf(ls, uq("mwb"), [128, FC], F32)
            acb = p.sbuf(ls, uq("mab"), [128, 2 * FC], F32)
            stage = p.sbuf(ls, uq("mst"), [128, 2 * FC], F32)
            chL, chL2 = p.dma_pair()
            chO, chO2 = p.dma_pair()
            ac_tick = {0: p.dguard.get(chL), 1: p.dguard.get(chL2)}
            ost_tick = {0: p.dguard.get(chO), 1: p.dguard.get(chO2)}
            ev_tick = {0: None, 1: None}
            psA = p.psum(ls, uq("mdA"), [128, FC])
            psB = p.psum(ls, uq("mdB"), [128, FC])
            ps_w = p.psum(ls, uq("mdW"), [128, FC])
            banks = [psA, psB]
            bank_war = [None, None]
            slot_war = [None, None]
            rhs_tick = [None, None]

            def prefetch(f, slot):
                dch = "dra" if slot == 0 else "drb"
                guard = slot_war[slot] or p.dguard.get(dch)

                def syf(e, f=f, slot=slot, dch=dch, guard=guard):
                    if guard is not None:
                        p.wait(e, guard[0], guard[1])
                    for k in range(NMH):
                        ins = e.dma_start(
                            out=rbuf[:, (slot * NMH + k) * FC:(slot * NMH + k + 1) * FC],
                            in_=scr["h_bf"][k * 128:(k + 1) * 128, f * FC:(f + 1) * FC])
                        p.inc(ins, dch, 16)
                p.sy(syf)
                rhs_tick[slot] = (dch, p.val(dch))
            prefetch(0, 0)
            bi = 0
            for f in range(NFC):
                slot = f & 1
                if f + 1 < NFC:
                    prefetch(f + 1, 1 - slot)
                if ei > 0:
                    def syw(e, f=f, ei=ei):
                        p.wait(e, "pet")
                        ins = e.dma_start(out=wrow[0:1, :],
                                          in_=w_sb[ei - 1:ei, f * FC:(f + 1) * FC])
                        p.inc(ins, "dww", 16)
                    p.sy(syw)
                    wr_t = p.val("dww")

                    def tbc(e, f=f, wr_t=wr_t):
                        p.wait(e, "dww", wr_t)
                        p.wait(e, "vet")
                        ins = e.matmul(out=ps_w[:, :], lhsT=C["c_ones_row"][:],
                                       rhs=wrow[0:1, :], start=True, stop=True)
                        p.inc(ins, "pet", 1)
                    p.te(tbc)
                    bc_t = p.val("pet")

                    def vbc(e, f=f, bc_t=bc_t):
                        p.wait(e, "pet", bc_t)
                        ins = e.tensor_copy(out=wbc[:, :], in_=ps_w[:, :])
                        p.inc(ins, "vet", 1)
                    p.ve(vbc)
                    wbc_t = p.val("vet")
                for m in range(NDT):
                    ps = banks[bi]

                    def tf(e, f=f, m=m, ps=ps, war=bank_war[bi], rt=rhs_tick[slot],
                           slot=slot):
                        p.wait(e, lrdy[0], lrdy[1])
                        p.wait(e, rt[0], rt[1])
                        if war is not None:
                            p.wait(e, war[0], war[1])
                        for k in range(NMH):
                            ins = e.matmul(
                                out=ps[:, :], lhsT=lt(k)[:, m * 128:(m + 1) * 128],
                                rhs=rbuf[:, (slot * NMH + k) * FC:(slot * NMH + k + 1) * FC],
                                start=(k == 0), stop=(k == NMH - 1))
                        p.inc(ins, "pet", 1)
                    p.te(tf)
                    mm_t = p.val("pet")
                    esl = (f * NDT + m) & 1
                    lch = chL if esl == 0 else chL2
                    och = chO if esl == 0 else chO2
                    if ei > 0:
                        def sya(e, m=m, f=f, esl=esl, lch=lch, war=ev_tick[esl],
                                g=ac_tick[esl]):
                            if war is not None:
                                p.wait(e, war[0], war[1])
                            elif g is not None:
                                p.wait(e, g[0], g[1])
                            ins = e.dma_start(out=acb[:, esl * FC:(esl + 1) * FC],
                                              in_=scr["acc_b"][m * 128:(m + 1) * 128,
                                                               f * FC:(f + 1) * FC])
                            p.inc(ins, lch, 16)
                        p.sy(sya)
                        ac_t = (lch, p.val(lch))
                        ac_tick[esl] = ac_t

                        def vf(e, m=m, f=f, esl=esl, mm_t=mm_t, ac_t=ac_t, ps=ps,
                               wbc_t=wbc_t, st_t=ost_tick[esl]):
                            p.wait(e, "pet", mm_t)
                            p.wait(e, ac_t[0], ac_t[1])
                            p.wait(e, "vet", wbc_t)
                            if st_t is not None:
                                p.wait(e, st_t[0], st_t[1])
                            p.fence(e, e.tensor_tensor(out=stage[:, esl * FC:(esl + 1) * FC],
                                                       in0=ps[:, :], in1=wbc[:, :],
                                                       op=ALU.mult), "vet")
                            ins = e.tensor_tensor(out=stage[:, esl * FC:(esl + 1) * FC],
                                                  in0=stage[:, esl * FC:(esl + 1) * FC],
                                                  in1=acb[:, esl * FC:(esl + 1) * FC],
                                                  op=ALU.add)
                            p.inc(ins, "vet", 1)
                        p.ve(vf)
                    else:
                        def vf(e, m=m, f=f, esl=esl, mm_t=mm_t, ps=ps,
                               st_t=ost_tick[esl]):
                            p.wait(e, "pet", mm_t)
                            if st_t is not None:
                                p.wait(e, st_t[0], st_t[1])
                            ins = e.tensor_copy(out=stage[:, esl * FC:(esl + 1) * FC],
                                                in_=ps[:, :])
                            p.inc(ins, "vet", 1)
                        p.ve(vf)
                    ev_t = p.val("vet")
                    bank_war[bi] = ("vet", ev_t)
                    ev_tick[esl] = ("vet", ev_t)

                    def sy2(e, m=m, f=f, esl=esl, ev_t=ev_t, och=och):
                        p.wait(e, "vet", ev_t)
                        ins = e.dma_start(out=scr["acc_b"][m * 128:(m + 1) * 128,
                                                           f * FC:(f + 1) * FC],
                                          in_=stage[:, esl * FC:(esl + 1) * FC])
                        p.inc(ins, och, 16)
                    p.sy(sy2)
                    ost_tick[esl] = (och, p.val(och))
                    p.dguard[och] = ost_tick[esl]
                    p.dguard[lch] = ac_tick[esl] if ei > 0 else p.dguard.get(lch)
                    bi ^= 1
                slot_war[slot] = ("pet", p.val("pet"))
            p.dguard["dra"] = ("pet", p.val("pet"))
            p.dguard["drb"] = ("pet", p.val("pet"))
        p.barrier()
    mst.close()


def _emit_layer0(p, W, C, scr, x_in, x_out):
    emit_ln(p, x_in, scr["xn_f32"], scr["xn_bf"], C, uq("ln"))
    p.barrier()
    emit_ssm(p, {k.replace("L0_attn_", ""): W[k] for k in L0_W if "attn" in k},
             C, scr)
    emit_moe(p, {k.replace("L0_ffn_", ""): W[k] for k in L0_W if "ffn" in k},
             C, scr)
    emit_residual_add(p, x_in, scr["acc_a"], scr["acc_b"], x_out)
    p.barrier()


def prep_inputs(inputs):
    t = lambda a: np.ascontiguousarray(np.asarray(a).T)
    a = lambda v: np.ascontiguousarray(np.asarray(v))
    w = {}
    w["L0_attn_log_A"] = a(inputs["L0_attn_log_A"])
    w["L0_attn_B_param"] = a(inputs["L0_attn_B_param"])
    w["L0_attn_C_param"] = a(inputs["L0_attn_C_param"])
    w["L0_attn_D_param"] = a(inputs["L0_attn_D_param"]).reshape(DM, 1)
    w["L0_attn_log_dt"] = a(inputs["L0_attn_log_dt"]).reshape(DM, 1)
    w["L0_attn_ln_gamma"] = a(inputs["L0_attn_ln_gamma"]).reshape(DM, 1)
    w["L0_attn_ln_beta"] = a(inputs["L0_attn_ln_beta"]).reshape(DM, 1)
    w["L0_attn_W_out"] = t(inputs["L0_attn_W_out"])
    w["L0_attn_b_out"] = a(inputs["L0_attn_b_out"]).reshape(DM, 1)
    w["L0_ffn_ln_scale"] = a(inputs["L0_ffn_ln_scale"]).reshape(DM, 1)
    w["L0_ffn_ln_bias"] = a(inputs["L0_ffn_ln_bias"]).reshape(DM, 1)
    w["L0_ffn_gate_w"] = t(inputs["L0_ffn_gate_w"])
    for k in ("Wg", "Wu", "Wd", "sWg", "sWu", "sWd"):
        w[f"L0_ffn_{k}"] = a(inputs[f"L0_ffn_{k}"])
    for k in ("Wq", "Wk", "Wv", "Wo"):
        w[f"L1_attn_{k}"] = a(inputs[f"L1_attn_{k}"])
    w["L1_attn_bo"] = a(inputs["L1_attn_bo"]).reshape(DM, 1)
    w["L1_ffn_w_gate"] = t(inputs["L1_ffn_w_gate"])
    w["L1_ffn_w_up"] = t(inputs["L1_ffn_w_up"])
    w["L1_ffn_w_down"] = t(inputs["L1_ffn_w_down"])
    for k in ("Wq", "Wk", "Wv", "Wo"):
        w[f"L2_attn_{k}"] = t(inputs[f"L2_attn_{k}"])
    for k in ("bq", "bk", "bv", "bo"):
        w[f"L2_attn_{k}"] = a(inputs[f"L2_attn_{k}"]).reshape(DM, 1)
    for k in ("w1_w", "w2_w", "h1_w", "h2_w", "w3_w"):
        w[f"L2_ffn_{k}"] = t(inputs[f"L2_ffn_{k}"])
    for k in ("w1_s", "w2_s", "h1_s", "h2_s", "w3_s"):
        w[f"L2_ffn_{k}"] = a(inputs[f"L2_ffn_{k}"]).reshape(-1, 1)
    for k in ("Wq", "Wk", "Wv", "Wo"):
        w[f"L3_attn_{k}"] = t(inputs[f"L3_attn_{k}"])
    for k in ("cn_w1", "cn_w2", "cn_w3", "cn_w4", "gc_w1", "gc_w2", "gc_w3"):
        w[f"L3_ffn_{k}"] = a(inputs[f"L3_ffn_{k}"])
    for k in ("cn_b1", "cn_b2", "cn_b3", "cn_b4", "gc_b1", "gc_b2", "gc_b3"):
        w[f"L3_ffn_{k}"] = a(inputs[f"L3_ffn_{k}"]).reshape(-1, 1)
    w["L3_ffn_adaptive_range"] = a(inputs["L3_ffn_adaptive_range"]).reshape(1, 1)
    w.update(host_constants())
    return w
